# version 08
# baseline (speedup 1.0000x reference)
import sys
sys.path.insert(0, "/opt/trn_rl_repo")
import contextlib

import numpy as np
import ml_dtypes
import concourse.bass as bass
import concourse.mybir as mybir
from concourse.bass_utils import run_bass_kernel_spmd

F32 = mybir.dt.float32
BF16 = mybir.dt.bfloat16
AF = mybir.ActivationFunctionType
ALU = mybir.AluOpType
AX = mybir.AxisListType

D = 4096
NCH = 32
G = 512
NT = 4
S = 2048
NH = 32
LN_EPS = 1e-5
CLIP = 8.0
BF = ml_dtypes.bfloat16


def alibi_slopes(n_heads=32, bias_max=8.0):
    next_pow2 = 2 ** int(np.ceil(np.log2(n_heads)))
    m = np.arange(1, next_pow2 + 1, dtype=np.float32) * (bias_max / next_pow2)
    slopes = 1.0 / np.power(2.0, m)
    if next_pow2 != n_heads:
        slopes = np.concatenate([slopes[1::2], slopes[::2]])[:n_heads]
    return slopes.astype(np.float32)


SLOPES = alibi_slopes()
ATTN_GROUPS = (0, 1)
DUMP_OB = False
P2_STAGE = 4


def tiles_of(parity, grp):
    if grp == 0:
        return [2 * s + parity for s in range(4)]
    return [8 + 2 * s + (1 - parity) for s in range(4)]


def core_token_tiles(parity):
    return tiles_of(parity, 0) + tiles_of(parity, 1)



class Prog:
    ENGINES = ("sync", "gpsimd", "tensor", "vector", "scalar")

    def __init__(self):
        self.ops = {e: [] for e in self.ENGINES}

    def on(self, eng):
        def deco(fn):
            self.ops[eng].append(fn)
            return None
        return deco

    def emit(self, block):
        for name in self.ENGINES:
            fns = self.ops[name]
            deco = getattr(block, name)

            def section(e, fns=fns):
                for f in fns:
                    f(e)
            deco(section)


class Ctr:
    def __init__(self, sem):
        self.sem = sem
        self.n = 0

    def bump(self, k=1):
        self.n += k
        return self.n

    def seq(self, e, instr):
        instr.then_inc(self.sem, 1)
        e.wait_ge(self.sem, self.bump())


def apv(t):
    a = getattr(t, "ap", None)
    return t.ap() if callable(a) else t


def make_ctx(nc, ctx):
    E = ctx.enter_context
    c = {"nc": nc}
    for nm in ("s_dma", "s_pe", "s_ve", "s_se", "s_wd", "s_out", "s_bar", "s_ge"):
        c[nm] = Ctr(E(nc.semaphore(nm)))
    c["sl_x"] = [Ctr(E(nc.semaphore(f"slx{i}"))) for i in range(4)]
    c["sl_w"] = [Ctr(E(nc.semaphore(f"slw{i}"))) for i in range(2)]
    c["sl_xe"] = [Ctr(E(nc.semaphore(f"slxe{i}"))) for i in range(4)]
    c["sl_o"] = [Ctr(E(nc.semaphore(f"slo{i}"))) for i in range(4)]
    c["sl_c"] = [Ctr(E(nc.semaphore(f"slc{i}"))) for i in range(2)]
    for eng in ("ve", "se", "pe", "ge"):
        c["q_" + eng] = Ctr(E(nc.semaphore("q_" + eng)))
    return c


def barrier(P, c):
    tgt = c["s_bar"].n + 5
    c["s_bar"].n = tgt
    sem = c["s_bar"].sem
    dma_waits = [(ctr.sem, ctr.n) for ctr in
                 ([c["s_dma"], c["s_wd"], c["s_out"], c["s_ge"]] + c["sl_x"] + c["sl_w"]
                  + c["sl_xe"] + c["sl_o"] + c["sl_c"]) if ctr.n > 0]

    for eng in Prog.ENGINES:
        @P.on(eng)
        def _(e, tgt=tgt, sem=sem, eng=eng, dma_waits=tuple(dma_waits)):
            if eng == "sync":
                for sm, n in dma_waits:
                    e.wait_ge(sm, n)
            e.drain()
            e.sem_inc(sem, 1)
            e.wait_ge(sem, tgt)



def sb_common(nc, ctx):
    E = ctx.enter_context
    sb = {}
    sb["xs"] = E(nc.sbuf_tensor("xs", [128, 4 * G], F32))
    sb["sq"] = E(nc.sbuf_tensor("sq", [128, 2 * G], BF16))
    sb["rows"] = E(nc.sbuf_tensor("rows", [1, 4 * G], F32))
    sb["bcast"] = E(nc.sbuf_tensor("bcast", [128, 2 * G], F32))
    sb["onescol_bf"] = E(nc.sbuf_tensor("onescol_bf", [128, 1], BF16))
    sb["onesrow_f"] = E(nc.sbuf_tensor("onesrow_f", [1, 128], F32))
    sb["onesrow_bf"] = E(nc.sbuf_tensor("onesrow_bf", [1, 512], BF16))
    sb["cpool"] = E(nc.sbuf_tensor("cpool", [1, 8], F32))
    sb["wbuf"] = E(nc.sbuf_tensor("wbuf", [128, 2 * NCH * 128], BF16))
    sb["tmp"] = E(nc.sbuf_tensor("tmp", [128, 2 * G], F32))
    sb["xo"] = E(nc.sbuf_tensor("xo", [128, 4 * G], F32))
    return sb


def emit_consts(P, c, sb):
    @P.on("vector")
    def _(e):
        e.memset(sb["onescol_bf"][:], 1.0)
        e.memset(sb["onesrow_f"][:], 1.0)
        e.memset(sb["onesrow_bf"][:], 1.0)
        e.memset(sb["cpool"][0:1, 0:1], LN_EPS).then_inc(c["s_ve"].sem, 1)
    return c["s_ve"].bump()



def emit_ln(P, c, sb, ps, xin_v, toks, arena):
    s_ve = c["s_ve"]
    xnv = apv(arena).rearrange("p (j t) -> p j t", j=NCH)
    xsv = apv(sb["xs"]).rearrange("p (i t) -> p i t", i=4)
    ve0 = s_ve.n
    sl_x = c["sl_x"]
    xwaits = []
    for j in range(NCH):
        xwaits.append(sl_x[j % 4].bump(16))

    @P.on("sync")
    def _(e, ve0=ve0):
        for j in range(NCH):
            if j >= 4:
                e.wait_ge(s_ve.sem, ve0 + (j - 4) + 1)
            e.dma_start(out=xsv[:, j % 4, :], in_=xin_v[j, :, toks]).then_inc(sl_x[j % 4].sem, 16)

    @P.on("vector")
    def _(e):
        for j in range(NCH):
            e.wait_ge(sl_x[j % 4].sem, xwaits[j])
            e.tensor_copy(xnv[:, j, :], xsv[:, j % 4, :]).then_inc(s_ve.sem, 1)
    s_ve.bump(NCH)

    emit_norm(P, c, sb, ps, arena, copy_ve0=ve0)


def emit_norm(P, c, sb, ps, arena, wb=None, copy_ve0=None, nch=NCH):
    s_pe, s_ve, s_se = c["s_pe"], c["s_ve"], c["s_se"]
    NCHa = nch
    xnv = apv(arena).rearrange("p (j t) -> p j t", j=NCHa)
    sqv = apv(sb["sq"]).rearrange("p (i t) -> p i t", i=2)
    psS = ps[4][0:1, :]
    psQ = ps[5][0:1, :]
    ve0, se0, pe0 = s_ve.n - NCHa, s_se.n, s_pe.n
    if copy_ve0 is not None:
        ve0 = copy_ve0

    @P.on("scalar")
    def _(e, ve0=ve0, pe0=pe0):
        for j in range(NCHa):
            e.wait_ge(s_ve.sem, ve0 + j + 1)
            if j >= 2:
                e.wait_ge(s_pe.sem, pe0 + (j - 1))
            e.activation(out=sqv[:, j % 2, :], in_=xnv[:, j, :], func=AF.Square).then_inc(s_se.sem, 1)
    s_se.bump(NCHa)

    @P.on("tensor")
    def _(e, se0=se0):
        for j in range(NCHa):
            e.wait_ge(s_se.sem, se0 + j + 1)
            e.matmul(psS, sb["onescol_bf"][:], xnv[:, j, :], start=(j == 0), stop=(j == NCHa - 1))
            mm = e.matmul(psQ, sb["onescol_bf"][:], sqv[:, j % 2, :], start=(j == 0), stop=(j == NCHa - 1))
            mm.then_inc(s_pe.sem, 1)
    s_pe.bump(NCHa)

    rows = sb["rows"]
    m_r = rows[0:1, 0:G]
    e2_r = rows[0:1, G:2 * G]
    rstd_r = rows[0:1, 2 * G:3 * G]
    mr_r = rows[0:1, 3 * G:4 * G]
    pe_stats = s_pe.n

    @P.on("scalar")
    def _(e, pe_stats=pe_stats):
        e.wait_ge(s_pe.sem, pe_stats)
        e.activation(out=m_r, in_=psS, func=AF.Identity, scale=1.0 / (128 * NCHa))
        e.activation(out=e2_r, in_=psQ, func=AF.Identity, scale=1.0 / (128 * NCHa)).then_inc(s_se.sem, 1)
    se_mq = s_se.bump()

    q_ve, q_se = c["q_ve"], c["q_se"]

    @P.on("vector")
    def _(e, se_mq=se_mq):
        e.wait_ge(s_se.sem, se_mq)
        q_ve.seq(e, e.tensor_tensor(mr_r, m_r, m_r, ALU.mult))
        e.tensor_tensor(e2_r, e2_r, mr_r, ALU.subtract).then_inc(s_ve.sem, 1)
    ve_var = s_ve.bump()

    @P.on("scalar")
    def _(e, ve_var=ve_var):
        e.wait_ge(s_ve.sem, ve_var)
        q_se.seq(e, e.activation(out=rstd_r, in_=e2_r, func=AF.Ln, bias=sb["cpool"][0:1, 0:1]))
        e.activation(out=rstd_r, in_=rstd_r, func=AF.Exp, scale=-0.5).then_inc(s_se.sem, 1)
    se_rstd = s_se.bump()

    @P.on("vector")
    def _(e, se_rstd=se_rstd):
        e.wait_ge(s_se.sem, se_rstd)
        e.tensor_tensor(mr_r, m_r, rstd_r, ALU.mult).then_inc(s_ve.sem, 1)
    ve_mr = s_ve.bump()

    @P.on("tensor")
    def _(e, ve_mr=ve_mr):
        e.wait_ge(s_ve.sem, ve_mr)
        e.matmul(ps[2][:], sb["onesrow_f"][:], rstd_r, start=True, stop=True)
        mm = e.matmul(ps[3][:], sb["onesrow_f"][:], mr_r, start=True, stop=True)
        mm.then_inc(s_pe.sem, 1)
    pe_b = s_pe.bump()

    bc = sb["bcast"]

    @P.on("vector")
    def _(e, pe_b=pe_b):
        e.wait_ge(s_pe.sem, pe_b)
        e.tensor_copy(bc[:, 0:G], ps[2][:])
        e.tensor_copy(bc[:, G:2 * G], ps[3][:]).then_inc(s_ve.sem, 1)
    ve_bc = s_ve.bump()

    tmpv = apv(sb["tmp"]).rearrange("p (i t) -> p i t", i=2)

    if wb is None:
        @P.on("vector")
        def _(e, ve_bc=ve_bc):
            e.wait_ge(s_ve.sem, ve_bc)
            q_ve = c["q_ve"]
            for j in range(NCHa):
                q_ve.seq(e, e.tensor_tensor(tmpv[:, j % 2, :], xnv[:, j, :], bc[:, 0:G], ALU.mult))
                q_ve.seq(e, e.tensor_tensor(xnv[:, j, :], tmpv[:, j % 2, :], bc[:, G:2 * G], ALU.subtract))
                e.sem_inc(s_ve.sem, 1)
        s_ve.bump(NCHa)
    else:
        w_tile, b_tile = wb
        ve_marks = []
        for j in range(NCHa):
            @P.on("vector")
            def _(e, j=j, ve_bc=ve_bc):
                if j == 0:
                    e.wait_ge(s_ve.sem, ve_bc)
                q_ve = c["q_ve"]
                if j >= 2:
                    e.wait_ge(s_se.sem, ve_marks[j - 2])
                q_ve.seq(e, e.tensor_tensor(tmpv[:, j % 2, :], xnv[:, j, :], bc[:, 0:G], ALU.mult))
                e.tensor_tensor(tmpv[:, j % 2, :], tmpv[:, j % 2, :], bc[:, G:2 * G],
                                ALU.subtract).then_inc(s_ve.sem, 1)
            ve_j = s_ve.bump()

            @P.on("scalar")
            def _(e, j=j, ve_j=ve_j, w_tile=w_tile, b_tile=b_tile):
                e.wait_ge(s_ve.sem, ve_j)
                e.activation(out=xnv[:, j, :], in_=tmpv[:, j % 2, :], func=AF.Identity,
                             scale=w_tile[:, j:j + 1], bias=b_tile[:, j:j + 1]).then_inc(s_se.sem, 1)
            ve_marks.append(s_se.bump())

    barrier(P, c)



def emit_dense_residual(P, c, sb, ps, *, xn, wt_dram, xin_v, xout_v, toks,
                        nd=D, extra_mm=None, scale=1.0, halfbias=None,
                        out_override=None):
    s_dma, s_pe, s_ve, s_se, s_wd, s_out = (c[k] for k in
                                            ("s_dma", "s_pe", "s_ve", "s_se", "s_wd", "s_out"))
    ndc = nd // 128
    xnv = apv(xn).rearrange("p (j t) -> p j t", j=NCH)
    wtv = wt_dram
    wbv = apv(sb["wbuf"]).rearrange("p (i j n) -> p i j n", i=2, j=NCH)
    xsv = apv(sb["xs"]).rearrange("p (i t) -> p i t", i=4)
    tmpv = apv(sb["tmp"]).rearrange("p (i t) -> p i t", i=2)
    xov = apv(sb["xo"]).rearrange("p (i t) -> p i t", i=4)

    sl_w, sl_xe, sl_o = c["sl_w"], c["sl_xe"], c["sl_o"]
    pe_marks = [None] * ndc
    evac_done = [None] * ndc
    wwaits = [sl_w[d % 2].bump(16) for d in range(ndc)]

    @P.on("gpsimd")
    def _(e):
        for d in range(ndc):
            if d >= 2:
                e.wait_ge(s_pe.sem, pe_marks[d - 2])
            e.dma_start(out=wbv[:, d % 2, :, :],
                        in_=wtv[d, :, :, :]).then_inc(sl_w[d % 2].sem, 16)

    for d in range(ndc):
        @P.on("tensor")
        def _(e, d=d, ww=wwaits[d]):
            e.wait_ge(sl_w[d % 2].sem, ww)
            if d >= 2:
                e.wait_ge(s_ve.sem, evac_done[d - 2])
            pt = ps[d % 2]
            mm = None
            last = extra_mm is None
            for j in range(NCH):
                mm = e.matmul(pt[:], wbv[:, d % 2, j, :], xnv[:, j, :],
                              start=(j == 0), stop=(last and j == NCH - 1))
            if extra_mm is not None:
                mm = extra_mm(e, d, pt)
            mm.then_inc(s_pe.sem, 1)
        pe_marks[d] = s_pe.bump()

        if out_override is not None:
            evac_done[d] = out_override(P, c, d, ps[d % 2], pe_marks[d])
            continue

        xe_w = sl_xe[d % 4].bump(16)

        @P.on("sync")
        def _(e, d=d):
            if d >= 4:
                e.wait_ge(s_ve.sem, evac_done[d - 4])
            e.dma_start(out=xsv[:, d % 4, :], in_=xin_v[d, :, toks]).then_inc(sl_xe[d % 4].sem, 16)

        @P.on("scalar")
        def _(e, d=d, pe_cnt=pe_marks[d]):
            e.wait_ge(s_pe.sem, pe_cnt)
            if d >= 2:
                e.wait_ge(s_ve.sem, evac_done[d - 2])
            kw = {}
            if halfbias is not None:
                kw["bias"] = halfbias[:, d:d + 1]
            e.activation(out=tmpv[:, d % 2, :], in_=ps[d % 2][:], func=AF.Identity,
                         scale=scale, **kw).then_inc(s_se.sem, 1)
        se_cnt = s_se.bump()

        ow_prev = sl_o[d % 4].n

        @P.on("vector")
        def _(e, d=d, se_cnt=se_cnt, xe_w=xe_w, ow_prev=ow_prev):
            e.wait_ge(s_se.sem, se_cnt)
            e.wait_ge(sl_xe[d % 4].sem, xe_w)
            if d >= 4:
                e.wait_ge(sl_o[d % 4].sem, ow_prev)
            e.tensor_tensor(xov[:, d % 4, :], xsv[:, d % 4, :],
                            tmpv[:, d % 2, :], ALU.add).then_inc(s_ve.sem, 1)
        evac_done[d] = s_ve.bump()

        o_w = sl_o[d % 4].bump(16)

        @P.on("sync")
        def _(e, d=d, ve_cnt=evac_done[d]):
            e.wait_ge(s_ve.sem, ve_cnt)
            e.dma_start(out=xout_v[d, :, toks], in_=xov[:, d % 4, :]).then_inc(sl_o[d % 4].sem, 16)



FFN_SB = {
    0: {"halfbias": ([128, NCH], F32), "a_bf": ([128, 2 * NCH * 128], BF16),
        "bft_bf": ([128, 2 * D], BF16), "wexp": ([128, 2], F32),
        "wsm": ([16, 258], F32)},
    1: {"halfbias": ([128, NCH], F32), "a_bf": ([128, NCH * 64], BF16),
        "bt_bf": ([64, D], BF16), "gw_bf": ([128, NCH * 4], BF16),
        "gbcol": ([4, 1], F32), "erows": ([1, 4 * 64], BF16),
        "identf": ([128, 128], F32), "identb": ([128, 128], BF16),
        "fsc": ([128, 16], F32), "fcols": ([128, 12], BF16),
        "flagrows": ([1, 3 * G], BF16), "maskb": ([64, G], BF16),
        "hm": ([64, G], BF16), "gatef": ([4, G], F32)},
}


def ffn_dram_decls(nc, layer, pref=""):
    d = {}
    d["wt"] = nc.dram_tensor(pref + "wt", [NCH, 128, NCH, 128], F32, kind="ExternalInput").ap()
    d["biasm"] = nc.dram_tensor(pref + "biasm", [128, NCH], F32, kind="ExternalInput").ap()
    if layer == 0:
        d["a2t"] = nc.dram_tensor(pref + "a2t", [2, 128, NCH, 128], F32, kind="ExternalInput").ap()
        d["bft"] = nc.dram_tensor(pref + "bft", [256, D], F32, kind="ExternalInput").ap()
        d["coeff"] = nc.dram_tensor(pref + "coeff", [16, 1], F32, kind="ExternalInput").ap()
        d["scal"] = nc.dram_tensor(pref + "scal", [16, 1], F32, kind="ExternalInput").ap()
        d["emat"] = nc.dram_tensor(pref + "emat", [16, 256], F32, kind="ExternalInput").ap()
    else:
        d["a2t"] = nc.dram_tensor(pref + "a2t", [128, NCH, 64], F32, kind="ExternalInput").ap()
        d["btr"] = nc.dram_tensor(pref + "btr", [64, D], F32, kind="ExternalInput").ap()
        d["gwt"] = nc.dram_tensor(pref + "gwt", [128, NCH, 4], F32, kind="ExternalInput").ap()
        d["gb"] = nc.dram_tensor(pref + "gb", [4, 1], F32, kind="ExternalInput").ap()
        d["erows"] = nc.dram_tensor(pref + "erows", [1, 4 * 64], BF16, kind="ExternalInput").ap()
        d["identin"] = nc.dram_tensor(pref + "identin", [128, 128], F32, kind="ExternalInput").ap()
        d["identbf"] = nc.dram_tensor(pref + "identbf", [128, 128], BF16, kind="ExternalInput").ap()
    return d


def emit_ffn_phase(P, c, ps, psb, sb, xn, layer, io):
    xin, xout = io["xin"], io["xout"]
    wt, biasm = io["wt"], io["biasm"]
    a2t = io["a2t"]
    if layer == 0:
        bft, coeff, scal, emat = io["bft"], io["coeff"], io["scal"], io["emat"]
    else:
        btr, gwt, gb, erows, identin, identbf = (io["btr"], io["gwt"], io["gb"],
                                                 io["erows"], io["identin"], io["identbf"])
    s_dma, s_pe, s_ve, s_se, s_wd, s_out = (c[k] for k in
                                            ("s_dma", "s_pe", "s_ve", "s_se", "s_wd", "s_out"))

    if True:
        @P.on("sync")
        def _(e):
            e.dma_start(out=sb["halfbias"][:], in_=biasm).then_inc(s_dma.sem, 16)
        s_dma.bump(16)

        if layer == 0:
            abfv = apv(sb["a_bf"]).rearrange("p (cb j n) -> p cb j n", cb=2, j=NCH)
            bftv = bft.rearrange("(cb p) n -> cb p n", p=128)

            @P.on("gpsimd")
            def _(e):
                for cb in range(2):
                    e.dma_start(out=abfv[:, cb, :, :],
                                in_=a2t[cb, :, :, :]).then_inc(s_wd.sem, 16)
                for cb in range(2):
                    e.dma_start(out=sb["bft_bf"][:, cb * D:(cb + 1) * D],
                                in_=bftv[cb, :, :]).then_inc(s_wd.sem, 16)
            s_wd.bump(16 * 4)

            @P.on("sync")
            def _(e):
                e.dma_start(out=sb["wsm"][:, 0:1], in_=coeff).then_inc(s_dma.sem, 16)
                e.dma_start(out=sb["wsm"][:, 1:2], in_=scal).then_inc(s_dma.sem, 16)
                e.dma_start(out=sb["wsm"][:, 2:258], in_=emat).then_inc(s_dma.sem, 16)
            s_dma.bump(16 * 3)
        else:
            abfv = apv(sb["a_bf"]).rearrange("p (j n) -> p j n", j=NCH)

            @P.on("gpsimd")
            def _(e):
                e.dma_start(out=abfv[:, :, :], in_=a2t).then_inc(s_wd.sem, 16)
                e.dma_start(out=sb["bt_bf"][:], in_=btr).then_inc(s_wd.sem, 16)
                e.dma_start(out=apv(sb["gw_bf"]).rearrange("p (j n) -> p j n", j=NCH),
                            in_=gwt).then_inc(s_wd.sem, 16)
            s_wd.bump(16 * 3)

            @P.on("sync")
            def _(e):
                e.dma_start(out=sb["gbcol"][:], in_=gb).then_inc(s_dma.sem, 16)
                e.dma_start(out=sb["erows"][:], in_=erows).then_inc(s_dma.sem, 16)
                e.dma_start(out=sb["identf"][:], in_=identin).then_inc(s_dma.sem, 16)
                e.dma_start(out=sb["identb"][:], in_=identbf).then_inc(s_dma.sem, 16)
            s_dma.bump(16 * 4)

        barrier(P, c)

        @P.on("scalar")
        def _(e):
            e.mul(sb["halfbias"][:], sb["halfbias"][:], 0.5).then_inc(s_se.sem, 1)
        s_se.bump()

        if layer == 0:
            @P.on("vector")
            def _(e):
                e.tensor_tensor(sb["wsm"][:, 0:1], sb["wsm"][:, 0:1], sb["wsm"][:, 1:2],
                                ALU.mult).then_inc(s_ve.sem, 1)
            w_ve = s_ve.bump()

            @P.on("tensor")
            def _(e, w_ve=w_ve):
                e.wait_ge(s_ve.sem, w_ve)
                e.matmul(ps[3][:, 0:1], sb["wsm"][:, 2:130], sb["wsm"][:, 0:1], start=True, stop=True)
                mm = e.matmul(ps[3][:, 1:2], sb["wsm"][:, 130:258], sb["wsm"][:, 0:1],
                              start=True, stop=True)
                mm.then_inc(s_pe.sem, 1)
            wexp_pe = s_pe.bump()

            @P.on("scalar")
            def _(e, wexp_pe=wexp_pe):
                e.wait_ge(s_pe.sem, wexp_pe)
                e.copy(sb["wexp"][:], ps[3][:, 0:2]).then_inc(s_se.sem, 1)
            s_se.bump()

        barrier(P, c)

        xin_v = xin.rearrange("(j p) t -> j p t", p=128)
        xout_v = xout.rearrange("(j p) t -> j p t", p=128)

        for g in range(2):
            toks = slice(g * G, (g + 1) * G)
            emit_ln(P, c, sb, ps, xin_v, toks, xn[g])
            xnv = xn[g].ap().rearrange("p (j t) -> p j t", j=NCH)

            if layer == 0:
                @P.on("tensor")
                def _(e, xnv=xnv):
                    for cb in range(2):
                        pt = ps[2 + cb]
                        mm = None
                        for j in range(NCH):
                            mm = e.matmul(pt[:], abfv[:, cb, j, :], xnv[:, j, :],
                                          start=(j == 0), stop=(j == NCH - 1))
                        mm.then_inc(s_pe.sem, 1)
                xa_pe = s_pe.bump(2)

                @P.on("scalar")
                def _(e, xa_pe=xa_pe):
                    e.wait_ge(s_pe.sem, xa_pe)
                    e.activation(out=sb["xaw"][:, 0:G], in_=ps[2][:], func=AF.Identity,
                                 scale=sb["wexp"][:, 0:1])
                    a = e.activation(out=sb["xaw"][:, G:2 * G], in_=ps[3][:], func=AF.Identity,
                                     scale=sb["wexp"][:, 1:2])
                    a.then_inc(s_se.sem, 1)
                s_se.bump()

                bftbv = apv(sb["bft_bf"]).rearrange("p (cb n) -> p cb n", cb=2)

                def extra_mm(e, d, pt, bftbv=bftbv):
                    mm = None
                    for cb in range(2):
                        mm = e.matmul(pt[:], bftbv[:, cb, 128 * d:128 * (d + 1)],
                                      sb["xaw"][:, cb * G:(cb + 1) * G],
                                      start=False, stop=(cb == 1))
                    return mm
            else:
                gwbv = apv(sb["gw_bf"]).rearrange("p (j n) -> p j n", j=NCH)

                @P.on("tensor")
                def _(e, xnv=xnv, gwbv=gwbv):
                    mm = None
                    for j in range(NCH):
                        mm = e.matmul(ps[2][0:4, :], gwbv[:, j, :], xnv[:, j, :],
                                      start=(j == 0), stop=(j == NCH - 1))
                    for j in range(NCH):
                        mm = e.matmul(ps[3][0:64, :], abfv[:, j, :], xnv[:, j, :],
                                      start=(j == 0), stop=(j == NCH - 1))
                    mm.then_inc(s_pe.sem, 1)
                gh_pe = s_pe.bump()

                @P.on("scalar")
                def _(e, gh_pe=gh_pe):
                    e.wait_ge(s_pe.sem, gh_pe)
                    e.activation(out=sb["gatef"][:], in_=ps[2][0:4, :], func=AF.Identity,
                                 bias=sb["gbcol"][:]).then_inc(s_se.sem, 1)
                gate_se = s_se.bump()

                @P.on("tensor")
                def _(e, gate_se=gate_se):
                    e.wait_ge(s_se.sem, gate_se)
                    mm = None
                    for i in range(NT):
                        mm = e.transpose(ps[4][:, 4 * i:4 * (i + 1)],
                                         sb["gatef"][0:4, 128 * i:128 * (i + 1)],
                                         sb["identf"][0:4, 0:4])
                    mm.then_inc(s_pe.sem, 1)
                gt_pe = s_pe.bump()

                @P.on("vector")
                def _(e, gt_pe=gt_pe):
                    e.wait_ge(s_pe.sem, gt_pe)
                    fsc = sb["fsc"]
                    fcols = sb["fcols"]
                    q_ve = c["q_ve"]
                    for i in range(NT):
                        s4 = ps[4][:, 4 * i:4 * (i + 1)]
                        e.reduce_max(fsc[:, 0:1], s4[:, 0:2], axis=AX.X)
                        e.reduce_max(fsc[:, 1:2], s4[:, 0:3], axis=AX.X)
                        e.reduce_max(fsc[:, 2:3], s4[:, 1:4], axis=AX.X)
                        q_ve.seq(e, e.reduce_max(fsc[:, 3:4], s4[:, 2:4], axis=AX.X))
                        e.tensor_tensor(fcols[:, 3 * i + 0:3 * i + 1], fsc[:, 2:3], s4[:, 0:1], ALU.is_gt)
                        e.tensor_tensor(fcols[:, 3 * i + 1:3 * i + 2], fsc[:, 3:4], fsc[:, 0:1], ALU.is_gt)
                        q_ve.seq(e, e.tensor_tensor(fcols[:, 3 * i + 2:3 * i + 3], s4[:, 3:4], fsc[:, 1:2], ALU.is_gt))
                        e.sem_inc(s_ve.sem, 1)
                flag_ve = s_ve.bump(NT)

                @P.on("tensor")
                def _(e, flag_ve=flag_ve):
                    e.wait_ge(s_ve.sem, flag_ve)
                    mm = None
                    for gg in range(2):
                        for i in range(NT):
                            col = (gg * NT + i) * 128
                            mm = e.transpose(psb[0:1, col:col + 128],
                                             sb["fcols"][:, 3 * i + gg:3 * i + gg + 1],
                                             sb["identb"][:])
                    mm.then_inc(s_pe.sem, 1)
                fr_pe = s_pe.bump()

                @P.on("vector")
                def _(e, fr_pe=fr_pe):
                    e.wait_ge(s_pe.sem, fr_pe)
                    e.tensor_copy(sb["flagrows"][0:1, 0:2 * G], psb[0:1, 0:2 * G]).then_inc(s_ve.sem, 1)
                fr2_ve = s_ve.bump()

                @P.on("tensor")
                def _(e, fr2_ve=fr2_ve):
                    e.wait_ge(s_ve.sem, fr2_ve)
                    mm = None
                    for i in range(NT):
                        mm = e.transpose(psb[0:1, i * 128:(i + 1) * 128],
                                         sb["fcols"][:, 3 * i + 2:3 * i + 3],
                                         sb["identb"][:])
                    mm.then_inc(s_pe.sem, 1)
                fr3_pe = s_pe.bump()

                @P.on("vector")
                def _(e, fr3_pe=fr3_pe):
                    e.wait_ge(s_pe.sem, fr3_pe)
                    e.tensor_copy(sb["flagrows"][0:1, 2 * G:3 * G], psb[0:1, 0:G]).then_inc(s_ve.sem, 1)
                frows_ve = s_ve.bump()

                @P.on("tensor")
                def _(e, frows_ve=frows_ve):
                    e.wait_ge(s_ve.sem, frows_ve)
                    e.matmul(ps[2][0:64, :], sb["erows"][0:1, 0:64], sb["onesrow_bf"][:],
                             start=True, stop=False)
                    mm = None
                    for gg in range(3):
                        mm = e.matmul(ps[2][0:64, :], sb["erows"][0:1, 64 * (gg + 1):64 * (gg + 2)],
                                      sb["flagrows"][0:1, gg * G:(gg + 1) * G],
                                      start=False, stop=(gg == 2))
                    mm.then_inc(s_pe.sem, 1)
                mb_pe = s_pe.bump()

                @P.on("vector")
                def _(e, mb_pe=mb_pe):
                    e.wait_ge(s_pe.sem, mb_pe)
                    c["q_ve"].seq(e, e.tensor_copy(sb["maskb"][:], ps[2][0:64, :]))
                    e.tensor_tensor(sb["hm"][:], ps[3][0:64, :], sb["maskb"][:],
                                    ALU.mult).then_inc(s_ve.sem, 1)
                s_ve.bump()

                def extra_mm(e, d, pt):
                    return e.matmul(pt[:], sb["bt_bf"][0:64, 128 * d:128 * (d + 1)],
                                    sb["hm"][0:64, :], start=False, stop=True)

            barrier(P, c)

            emit_dense_residual(P, c, sb, ps, xn=xn[g], wt_dram=wt, xin_v=xin_v,
                                xout_v=xout_v, toks=toks, extra_mm=extra_mm,
                                scale=0.5, halfbias=sb["halfbias"])
            barrier(P, c)


def emit_final_wait(P, c):
    @P.on("sync")
    def _(e):
        for sl in c["sl_o"]:
            if sl.n:
                e.wait_ge(sl.sem, sl.n)
        if c["s_out"].n:
            e.wait_ge(c["s_out"].sem, c["s_out"].n)


def alloc_sb(nc, ctx, spec):
    E = ctx.enter_context
    out = {}
    for k, (shape, dt) in spec.items():
        out[k] = E(nc.sbuf_tensor("t_" + k, shape, dt))
    return out


def build_ffn_launch(layer):
    nc = bass.Bass("TRN2", target_bir_lowering=False)
    io = ffn_dram_decls(nc, layer)
    io["xin"] = nc.dram_tensor("xin", [D, 2 * G], F32, kind="ExternalInput").ap()
    io["xout"] = nc.dram_tensor("xout", [D, 2 * G], F32, kind="ExternalOutput").ap()
    with contextlib.ExitStack() as ctx:
        E = ctx.enter_context
        c = make_ctx(nc, ctx)
        ps = [E(nc.psum_tensor(f"psf{i}", [128, 512], F32)) for i in range(6)]
        psb = E(nc.psum_tensor("psb", [128, 1024], BF16))
        sb = sb_common(nc, ctx)
        sb.update(alloc_sb(nc, ctx, FFN_SB[layer]))
        if layer == 0:
            sb["xaw"] = E(nc.sbuf_tensor("t_xaw", [128, 2 * G], BF16))
        xn = [E(nc.sbuf_tensor("xn0", [128, NCH * G], BF16))] * 2
        block = E(nc.Block())
        P = Prog()
        emit_consts(P, c, sb)
        emit_ffn_phase(P, c, ps, psb, sb, xn, layer, io)
        emit_final_wait(P, c)
        P.emit(block)
    return nc



def _shard_x(x_full, core):
    seq, parity = core // 2, core % 2
    tiles = core_token_tiles(parity)
    idx = np.concatenate([np.arange(128 * t, 128 * (t + 1)) for t in tiles])
    return np.ascontiguousarray(x_full[seq, idx, :].T), idx


def _unshard_x(x_full, core, xc):
    seq, parity = core // 2, core % 2
    tiles = core_token_tiles(parity)
    idx = np.concatenate([np.arange(128 * t, 128 * (t + 1)) for t in tiles])
    x_full[seq, idx, :] = xc.T


def _chunkmajor(v):
    return np.ascontiguousarray(v.reshape(NCH, 128).T)


def run_ffn_layer(x_full, layer, weights):
    nc = _get_prog(("ffn", layer), lambda: build_ffn_launch(layer))
    in_maps = []
    shards = []
    for core in range(8):
        xc, idx = _shard_x(x_full, core)
        shards.append(idx)
        im = {"xin": xc}
        im.update(weights)
        in_maps.append(im)
    res = run_bass_kernel_spmd(nc, in_maps, core_ids=list(range(8)))
    out = x_full.copy()
    for core in range(8):
        _unshard_x(out, core, np.asarray(res.results[core]["xout"]))
    return out


_PROG_CACHE = {}


def _get_prog(key, builder):
    if key not in _PROG_CACHE:
        _PROG_CACHE[key] = builder()
    return _PROG_CACHE[key]


def _wblocks(wT):
    ndim = wT.shape[1]
    ndc = ndim // 128
    return np.ascontiguousarray(
        wT.reshape(NCH, 128, ndc, 128).transpose(2, 1, 0, 3))


def _pjn(wT):
    n = wT.shape[1]
    return np.ascontiguousarray(wT.reshape(NCH, 128, n).transpose(1, 0, 2))


def ffn0_weights(inputs):
    W = np.asarray(inputs["L0_ffn_W"], np.float32)
    A = np.asarray(inputs["L0_ffn_A"], np.float32)
    Bf = np.asarray(inputs["L0_ffn_Bf"], np.float32)
    emat = np.zeros((16, 256), np.float32)
    for e_ in range(16):
        emat[e_, e_ * 16:(e_ + 1) * 16] = 1.0
    a2t = np.ascontiguousarray(A.reshape(256, D).T)
    return {
        "wt": _wblocks(np.ascontiguousarray(W.T)),
        "biasm": _chunkmajor(np.asarray(inputs["L0_ffn_bias"], np.float32)),
        "a2t": _wblocks(a2t),
        "bft": np.ascontiguousarray(Bf.transpose(0, 2, 1).reshape(256, D)),
        "coeff": np.ascontiguousarray(np.asarray(inputs["L0_ffn_coeffs"], np.float32).reshape(16, 1)),
        "scal": np.ascontiguousarray(np.asarray(inputs["L0_ffn_scales"], np.float32).reshape(16, 1)),
        "emat": emat,
    }


def ffn1_weights(inputs):
    erows = np.zeros((1, 4 * 64), BF)
    erows[0, 0:8] = 1
    erows[0, 64 + 8:64 + 16] = 1
    erows[0, 128 + 16:128 + 32] = 1
    erows[0, 192 + 32:192 + 64] = 1
    return {
        "wt": _wblocks(np.ascontiguousarray(np.asarray(inputs["L1_ffn_W"], np.float32).T)),
        "biasm": _chunkmajor(np.asarray(inputs["L1_ffn_b"], np.float32)),
        "a2t": _pjn(np.ascontiguousarray(np.asarray(inputs["L1_ffn_A"], np.float32).T)),
        "btr": np.ascontiguousarray(np.asarray(inputs["L1_ffn_B"], np.float32).T),
        "gwt": _pjn(np.ascontiguousarray(np.asarray(inputs["L1_ffn_gate_w"], np.float32).T)),
        "gb": np.ascontiguousarray(np.asarray(inputs["L1_ffn_gate_b"], np.float32).reshape(4, 1)),
        "erows": erows,
        "identin": np.eye(128, dtype=np.float32),
        "identbf": np.eye(128, dtype=BF),
    }


def carve(arena, spec, base_bytes=0):
    views = {}
    cur = 0
    for k, (shape, dt) in spec.items():
        assert dt == arena.dtype if hasattr(arena, "dtype") else True
        views[k] = apv(arena)[0:shape[0], cur:cur + shape[1]]
        cur += shape[1]
    return views, cur


def build_megakernel():
    nc = bass.Bass("TRN2", target_bir_lowering=False)
    io_f0 = ffn_dram_decls(nc, 0, pref="f0")
    io_f1 = ffn_dram_decls(nc, 1, pref="f1")
    io_qkv = qkv_dram_decls(nc)
    io_att = attc_dram_decls(nc)
    io_a1 = a1_dram_decls(nc)
    xin = nc.dram_tensor("xin", [D, 2 * G], F32, kind="ExternalInput").ap()
    xout = nc.dram_tensor("xout", [D, 2 * G], F32, kind="ExternalOutput").ap()
    xa = nc.dram_tensor("xa", [D, 2 * G], F32, kind="Internal").ap()
    xb = nc.dram_tensor("xb", [D, 2 * G], F32, kind="Internal").ap()
    qn_i = nc.dram_tensor("qn", [NH, 128, 2 * G], BF16, kind="Internal").ap()
    kn_t = nc.dram_tensor("kn", [NH * 128, 2 * G], BF16)
    vt_t = nc.dram_tensor("vt", [2 * G, D], BF16)
    kng_t = nc.dram_tensor("kng", [2 * NH * 128, 2 * G], BF16)
    vtg_t = nc.dram_tensor("vtg", [2 * 2 * G, D], BF16)
    kn_i = kn_t.ap().rearrange("(h p) t -> h p t", p=128)
    vt_i = vt_t.ap()
    kng = kng_t.ap()
    vtg = vtg_t.ap()

    with contextlib.ExitStack() as ctx:
        E = ctx.enter_context
        c = make_ctx(nc, ctx)
        ps = [E(nc.psum_tensor(f"psf{i}", [128, 512], F32)) for i in range(6)]
        psb = [E(nc.psum_tensor(f"psbt{i}", [128, 1024], BF16)) for i in range(2)]
        com = sb_common(nc, ctx)
        A0 = E(nc.sbuf_tensor("A0", [128, NCH * G], BF16))
        A1 = E(nc.sbuf_tensor("A1", [128, NCH * G], BF16))
        A2 = E(nc.sbuf_tensor("A2", [128, NCH * G], BF16))
        A3 = E(nc.sbuf_tensor("A3", [128, 9216], BF16))
        SCB = E(nc.sbuf_tensor("SCB", [128, 16384], BF16))
        SCF = E(nc.sbuf_tensor("SCF", [128, 2304], F32))

        def phase_sb(spec):
            sb = dict(com)
            curb = curf = 0
            for k, (shape, dt) in spec.items():
                if dt == BF16:
                    sb[k] = apv(SCB)[0:shape[0], curb:curb + shape[1]]
                    curb += shape[1]
                else:
                    sb[k] = apv(SCF)[0:shape[0], curf:curf + shape[1]]
                    curf += shape[1]
                assert curb <= 16384 and curf <= 2304, (k, curb, curf)
            return sb

        block = E(nc.Block())
        P = Prog()
        com2 = dict(com)
        emit_consts(P, c, com2)

        sb = phase_sb(FFN_SB[0])
        sb["xaw"] = apv(A3)[:, 0:1024]
        emit_ffn_phase(P, c, ps, psb[0], sb, [A0, A0], 0,
                       dict(io_f0, xin=xin, xout=xa))

        sb = phase_sb(QKV_SB)
        sb["_psb0"], sb["_psb1"] = psb[0], psb[1]
        emit_qkv_phase(P, c, ps, sb, A0, A1, A2,
                       dict(io_qkv, xin=xa, qn=qn_i, kn=kn_i, vt=vt_i))

        s_ge = c["s_ge"]

        @P.on("gpsimd")
        def _(e):
            rg = [[0, 1], [2, 3], [4, 5], [6, 7]]
            e.collective_compute("AllGather", ALU.bypass, replica_groups=rg,
                                 ins=[kn_t.ap().opt()],
                                 outs=[kng_t.ap().opt()]).then_inc(s_ge.sem)
            e.collective_compute("AllGather", ALU.bypass, replica_groups=rg,
                                 ins=[vt_t.ap().opt()],
                                 outs=[vtg_t.ap().opt()]).then_inc(s_ge.sem)
        s_ge.bump(2)
        barrier(P, c)

        sb = phase_sb(ATTC_SB)
        emit_attc_phase(P, c, ps, sb, A1, A2,
                        apv(A3)[:, 0:4096], apv(A3)[:, 4096:8192],
                        apv(A3)[:, 8192:9216],
                        dict(io_att, xin=xa, xout=xb, qn=qn_i,
                             kng=kng.rearrange("(r h p) t -> r h p t", r=2, p=128),
                             vtg=vtg.rearrange("(r t) dd -> r t dd", r=2)))

        sb = phase_sb(FFN_SB[0])
        sb["xaw"] = apv(A3)[:, 0:1024]
        emit_ffn_phase(P, c, ps, psb[0], sb, [A0, A0], 0,
                       dict(io_f0, xin=xb, xout=xa))

        sb = phase_sb(FFN_SB[1])
        emit_ffn_phase(P, c, ps, psb[0], sb, [A0, A0], 1,
                       dict(io_f1, xin=xa, xout=xb))

        sb = phase_sb(A1_SB)
        emit_a1_phase(P, c, ps, psb, sb, A0,
                      apv(A1)[:, 0:4096], apv(A1)[:, 4096:8192],
                      apv(A1)[:, 8192:12288], apv(A1)[:, 12288:16384],
                      apv(A2)[0:8, 0:8192], apv(A2)[0:8, 8192:16384],
                      apv(A3)[:, 0:1024], apv(A3)[:, 1024:2048],
                      apv(SCF)[0:128, 2288:2304],
                      dict(io_a1, xin=xb, xout=xa))

        sb = phase_sb(FFN_SB[1])
        emit_ffn_phase(P, c, ps, psb[0], sb, [A0, A0], 1,
                       dict(io_f1, xin=xa, xout=xout))

        emit_final_wait(P, c)
        P.emit(block)
    return nc


FUSED = True


def kernel(**inputs):
    x = np.asarray(inputs["L0_attn_hidden_states"], dtype=np.float32)
    w_ffn0 = ffn0_weights(inputs)
    w_ffn1 = ffn1_weights(inputs)
    if not FUSED:
        x = run_ffn_layer(x, 0, w_ffn0)
        x = run_attn0(x, inputs)
        x = run_ffn_layer(x, 0, w_ffn0)
        x = run_ffn_layer(x, 1, w_ffn1)
        x = run_attn1(x, inputs)
        x = run_ffn_layer(x, 1, w_ffn1)
        return x

    w_a0 = attn0_weights(inputs)
    w_a1 = attn1_weights(inputs)
    nc = _get_prog(("mega",), build_megakernel)
    in_maps = []
    for core in range(8):
        seq, par = core // 2, core % 2
        xc, _ = _shard_x(x, core)
        im = {"xin": xc}
        im.update({"f0" + k: v for k, v in w_ffn0.items()})
        im.update({"f1" + k: v for k, v in w_ffn1.items()})
        im.update({"wqkvt": w_a0["wqkvt"], "i_qlnw": w_a0["qlnw"], "i_qlnb": w_a0["qlnb"],
                   "i_klnw": w_a0["klnw"], "i_klnb": w_a0["klnb"], "identbf2": w_a0["identbf"]})
        im.update({"woutt": w_a0["woutt"], "colt": w_a0["colt"], "colrow": w_a0["colrow"],
                   "maskab": np.stack([w_a0["maskab"][par], w_a0["maskab"][1 - par]]),
                   "maskq": np.stack([w_a0["maskq"][par], w_a0["maskq"][1 - par]]),
                   "identf": w_a0["identf"]})
        im.update(w_a1)
        in_maps.append(im)
    res = run_bass_kernel_spmd(nc, in_maps, core_ids=list(range(8)))
    out = x.copy()
    for core in range(8):
        _unshard_x(out, core, np.asarray(res.results[core]["xout"]))
    return out



QKV_SB = {
    "clip": ([128, 2], F32), "qlnw": ([128, NCH], F32), "qlnb": ([128, NCH], F32),
    "klnw": ([128, NCH], F32), "klnb": ([128, NCH], F32),
    "vstage": ([128, 2 * D], BF16), "identb": ([128, 128], BF16),
}


def qkv_dram_decls(nc):
    d = {}
    d["wqkvt"] = nc.dram_tensor("wqkvt", [96, 128, NCH, 128], F32, kind="ExternalInput").ap()
    for nm in ("qlnw", "qlnb", "klnw", "klnb"):
        d[nm] = nc.dram_tensor("i_" + nm, [128, NCH], F32, kind="ExternalInput").ap()
    d["identbf2"] = nc.dram_tensor("identbf2", [128, 128], BF16, kind="ExternalInput").ap()
    return d


def emit_qkv_phase(P, c, ps, sb, xn, qf, kf, io):
    xin = io["xin"]
    wqkvt = io["wqkvt"]
    qlnw, qlnb, klnw, klnb = io["qlnw"], io["qlnb"], io["klnw"], io["klnb"]
    identbf = io["identbf2"]
    qn_o, kn_o, vt_o = io["qn"], io["kn"], io["vt"]
    s_dma, s_pe, s_ve, s_se, s_wd, s_out = (c[k] for k in
                                            ("s_dma", "s_pe", "s_ve", "s_se", "s_wd", "s_out"))
    if True:
        @P.on("vector")
        def _(e):
            e.memset(sb["clip"][:, 0:1], -CLIP)
            e.memset(sb["clip"][:, 1:2], CLIP).then_inc(s_ve.sem, 1)
        s_ve.bump()

        @P.on("sync")
        def _(e):
            for nm, t in (("qlnw", qlnw), ("qlnb", qlnb), ("klnw", klnw), ("klnb", klnb)):
                e.dma_start(out=sb[nm][:], in_=t).then_inc(s_dma.sem, 16)
            e.dma_start(out=sb["identb"][:], in_=identbf).then_inc(s_dma.sem, 16)
        s_dma.bump(16 * 5)

        barrier(P, c)

        @P.on("scalar")
        def _(e):
            sc = 1.0 / np.sqrt(128.0)
            c["q_se"].seq(e, e.mul(sb["qlnw"][:], sb["qlnw"][:], sc))
            c["q_se"].seq(e, e.mul(sb["qlnb"][:], sb["qlnb"][:], sc))
        barrier(P, c)

        xin_v = xin.rearrange("(j p) t -> j p t", p=128)
        wbv = apv(sb["wbuf"]).rearrange("p (i j n) -> p i j n", i=2, j=NCH)
        xnv = apv(xn).rearrange("p (j t) -> p j t", j=NCH)
        sl_w = c["sl_w"]

        for g in range(2):
            toks = slice(g * G, (g + 1) * G)
            emit_ln(P, c, sb, ps, xin_v, toks, xn)
            if P2_STAGE < 1:
                continue

            arenas = (qf, kf)
            pe_marks = [None] * 64
            evac_done = [None] * 64
            wwaits = [sl_w[d % 2].bump(16) for d in range(64)]

            @P.on("gpsimd")
            def _(e, wwaits=wwaits, pe_marks=pe_marks, g=g):
                for d in range(64):
                    if d >= 2:
                        e.wait_ge(s_pe.sem, pe_marks[d - 2])
                    e.dma_start(out=wbv[:, d % 2, :, :],
                                in_=wqkvt[d, :, :, :]).then_inc(sl_w[d % 2].sem, 16)

            for d in range(64):
                @P.on("tensor")
                def _(e, d=d, ww=wwaits[d], ed=(evac_done[d - 2] if d >= 2 else 0)):
                    e.wait_ge(sl_w[d % 2].sem, ww)
                    if d >= 2:
                        e.wait_ge(s_ve.sem, ed)
                    pt = ps[d % 2]
                    mm = None
                    for j in range(NCH):
                        mm = e.matmul(pt[:], wbv[:, d % 2, j, :], xnv[:, j, :],
                                      start=(j == 0), stop=(j == NCH - 1))
                    mm.then_inc(s_pe.sem, 1)
                pe_marks[d] = s_pe.bump()

                @P.on("vector")
                def _(e, d=d, pe_cnt=pe_marks[d]):
                    e.wait_ge(s_pe.sem, pe_cnt)
                    ar = arenas[d // NCH].ap().rearrange("p (j t) -> p j t", j=NCH)
                    e.tensor_scalar(ar[:, d % NCH, :], ps[d % 2][:],
                                    sb["clip"][:, 0:1], sb["clip"][:, 1:2],
                                    op0=ALU.max, op1=ALU.min).then_inc(s_ve.sem, 1)
                evac_done[d] = s_ve.bump()

            barrier(P, c)

            if P2_STAGE < 2:
                continue
            for ai, (arena, w_t, b_t, out_t) in enumerate(
                    ((qf, sb["qlnw"], sb["qlnb"], qn_o), (kf, sb["klnw"], sb["klnb"], kn_o))):
                emit_norm(P, c, sb, ps, arena, wb=(w_t, b_t))
                arv = apv(arena).rearrange("p (j t) -> p j t", j=NCH)

                @P.on("sync")
                def _(e, arv=arv, out_t=out_t, g=g):
                    for j in range(NCH):
                        e.dma_start(out=out_t[j, :, slice(g * G, (g + 1) * G)],
                                    in_=arv[:, j, :]).then_inc(s_out.sem, 16)
                s_out.bump(16 * NCH)

            if P2_STAGE < 3:
                continue
            vf = qf
            vfv = apv(vf).rearrange("p (j t) -> p j t", j=NCH)
            pe_v = [None] * NCH
            ve_v = [None] * NCH
            vwaits = [sl_w[d % 2].bump(16) for d in range(NCH)]

            @P.on("gpsimd")
            def _(e, vwaits=vwaits, pe_v=pe_v, g=g):
                for d in range(NCH):
                    if d >= 2:
                        e.wait_ge(s_pe.sem, pe_v[d - 2])
                    e.dma_start(out=wbv[:, d % 2, :, :],
                                in_=wqkvt[64 + d, :, :, :]).then_inc(sl_w[d % 2].sem, 16)

            for d in range(NCH):
                @P.on("tensor")
                def _(e, d=d, vw=vwaits[d], ev=(ve_v[d - 2] if d >= 2 else 0)):
                    e.wait_ge(sl_w[d % 2].sem, vw)
                    if d >= 2:
                        e.wait_ge(s_ve.sem, ev)
                    pt = ps[d % 2]
                    mm = None
                    for j in range(NCH):
                        mm = e.matmul(pt[:], wbv[:, d % 2, j, :], xnv[:, j, :],
                                      start=(j == 0), stop=(j == NCH - 1))
                    mm.then_inc(s_pe.sem, 1)
                pe_v[d] = s_pe.bump()

                @P.on("vector")
                def _(e, d=d, pe_cnt=pe_v[d]):
                    e.wait_ge(s_pe.sem, pe_cnt)
                    e.tensor_scalar(vfv[:, d, :], ps[d % 2][:],
                                    sb["clip"][:, 0:1], sb["clip"][:, 1:2],
                                    op0=ALU.max, op1=ALU.min).then_inc(s_ve.sem, 1)
                ve_v[d] = s_ve.bump()

            barrier(P, c)

            if P2_STAGE < 4:
                barrier(P, c)
                continue
            vstage = sb["vstage"]
            vsv2 = apv(vstage).rearrange("p (i n) -> p i n", i=2)
            psb_v = (sb["_psb0"], sb["_psb1"])
            vti = 0
            vt_ve = []
            for i in range(NT):
                for h in range(NCH):
                    @P.on("tensor")
                    def _(e, h=h, i=i, vti=vti, wv=(vt_ve[vti - 2] if vti >= 2 else 0)):
                        if vti >= 2:
                            e.wait_ge(s_ve.sem, wv)
                        mm = e.transpose(psb_v[vti % 2][:, 0:128],
                                         vfv[:, h, 128 * i:128 * (i + 1)], sb["identb"][:])
                        mm.then_inc(s_pe.sem, 1)
                    vt_pe = s_pe.bump()

                    @P.on("vector")
                    def _(e, h=h, i=i, vti=vti, vt_pe=vt_pe,
                          ow=(c["sl_o"][i % 2].n if h == 0 else 0)):
                        e.wait_ge(s_pe.sem, vt_pe)
                        if h == 0 and i >= 2:
                            e.wait_ge(c["sl_o"][i % 2].sem, ow)
                        e.tensor_copy(vsv2[:, i % 2, 128 * h:128 * (h + 1)],
                                      psb_v[vti % 2][:, 0:128]).then_inc(s_ve.sem, 1)
                    ve_cnt = s_ve.bump()
                    vt_ve.append(ve_cnt)
                    vti += 1

                c["sl_o"][i % 2].bump(16)

                @P.on("sync")
                def _(e, i=i, ve_cnt=ve_cnt, g=g):
                    e.wait_ge(s_ve.sem, ve_cnt)
                    e.dma_start(out=vt_o[slice((g * 4 + i) * 128, (g * 4 + i + 1) * 128), :],
                                in_=vsv2[:, i % 2, :]).then_inc(c["sl_o"][i % 2].sem, 16)

            barrier(P, c)


def build_qkv_launch():
    nc = bass.Bass("TRN2", target_bir_lowering=False)
    io = qkv_dram_decls(nc)
    io["xin"] = nc.dram_tensor("xin", [D, 2 * G], F32, kind="ExternalInput").ap()
    io["qn"] = nc.dram_tensor("qn", [NH, 128, 2 * G], BF16, kind="ExternalOutput").ap()
    io["kn"] = nc.dram_tensor("kn", [NH, 128, 2 * G], BF16, kind="ExternalOutput").ap()
    io["vt"] = nc.dram_tensor("vt", [2 * G, D], BF16, kind="ExternalOutput").ap()
    with contextlib.ExitStack() as ctx:
        E = ctx.enter_context
        c = make_ctx(nc, ctx)
        ps = [E(nc.psum_tensor(f"psf{i}", [128, 512], F32)) for i in range(6)]
        sb = sb_common(nc, ctx)
        sb.update(alloc_sb(nc, ctx, QKV_SB))
        sb["_psb0"] = E(nc.psum_tensor("psb_v0", [128, 1024], BF16))
        sb["_psb1"] = E(nc.psum_tensor("psb_v1", [128, 1024], BF16))
        xn = E(nc.sbuf_tensor("xn0", [128, NCH * G], BF16))
        qf = E(nc.sbuf_tensor("qf", [128, NCH * G], BF16))
        kf = E(nc.sbuf_tensor("kf", [128, NCH * G], BF16))
        block = E(nc.Block())
        P = Prog()
        emit_consts(P, c, sb)
        emit_qkv_phase(P, c, ps, sb, xn, qf, kf, io)
        emit_final_wait(P, c)
        P.emit(block)
    return nc


ATTC_SB = {
    "crow": ([1, 2 * S], BF16), "mcol": ([128, 8], F32), "nmb": ([1, 2 * G], BF16),
    "colt": ([128, NH * 16], F32),
    "maskab": ([128, 2 * 256], F32), "maskq": ([128, 2 * 256], F32),
    "identf": ([128, 128], F32),
}


def attc_dram_decls(nc):
    d = {}
    d["woutt"] = nc.dram_tensor("woutt", [NCH, 128, NCH, 128], F32, kind="ExternalInput").ap()
    d["colt"] = nc.dram_tensor("colt", [128, NH * 16], F32, kind="ExternalInput").ap()
    d["colrow"] = nc.dram_tensor("colrow", [NH, S], BF16, kind="ExternalInput").ap()
    d["maskab"] = nc.dram_tensor("maskab", [2, 128, 256], F32, kind="ExternalInput").ap()
    d["maskq"] = nc.dram_tensor("maskq", [2, 128, 256], F32, kind="ExternalInput").ap()
    d["identf"] = nc.dram_tensor("identf", [128, 128], F32, kind="ExternalInput").ap()
    return d


def emit_attc_phase(P, c, ps, sb, qn_sb, ob, ksb, vsb, expt, io):
    xin, xout = io["xin"], io["xout"]
    qn_i = io["qn"]
    kng = io["kng"]
    vtg = io["vtg"]
    woutt, colt, colrow = io["woutt"], io["colt"], io["colrow"]
    maskab, maskq, identf = io["maskab"], io["maskq"], io["identf"]
    psm = ps[4][0:1, :]
    psz = ps[3][0:1, :]
    s_dma, s_pe, s_ve, s_se, s_wd, s_out = (c[k] for k in
                                            ("s_dma", "s_pe", "s_ve", "s_se", "s_wd", "s_out"))
    if True:
        @P.on("sync")
        def _(e):
            e.dma_start(out=sb["colt"][:], in_=colt).then_inc(s_dma.sem, 16)
            e.dma_start(out=apv(sb["maskab"]).rearrange("p (g n) -> p g n", g=2),
                        in_=maskab.rearrange("g p n -> p g n")).then_inc(s_dma.sem, 16)
            e.dma_start(out=apv(sb["maskq"]).rearrange("p (g n) -> p g n", g=2),
                        in_=maskq.rearrange("g p n -> p g n")).then_inc(s_dma.sem, 16)
            e.dma_start(out=sb["identf"][:], in_=identf).then_inc(s_dma.sem, 16)
        s_dma.bump(16 * 4)

        barrier(P, c)

        xin_v = xin.rearrange("(j p) t -> j p t", p=128)
        xout_v = xout.rearrange("(j p) t -> j p t", p=128)
        qnv = apv(qn_sb).rearrange("p (h t) -> p h t", h=NH)
        obv = apv(ob).rearrange("p (h t) -> p h t", h=NH)
        ksv = apv(ksb).rearrange("p (i t) -> p i t", i=2)
        vsv = apv(vsb).rearrange("p (i t) -> p i t", i=2)
        etv = apv(expt).rearrange("p (i t) -> p i t", i=2)
        crv = apv(sb["crow"]).rearrange("o (i t) -> o i t", i=2)
        rows = sb["rows"]
        nmrow_f = rows[0:1, 0:G]
        zrow = rows[0:1, G:2 * G]

        sl_k = c["sl_x"][0:2]
        sl_v = c["sl_x"][2:4]
        sl_c = c["sl_c"]

        for g in ATTN_GROUPS:
            kmax = 8 + 8 * g
            toks = slice(g * G, (g + 1) * G)

            @P.on("sync")
            def _(e, g=g, toks=toks):
                e.dma_start(out=qnv[:, :, :], in_=qn_i[:, :, toks].rearrange("h p t -> p h t")).then_inc(s_dma.sem, 16)
            s_dma.bump(16)
            barrier(P, c)

            pe_last = [0] * NH
            kw = [None] * NH
            vw = [None] * NH
            cw = [None] * NH
            for h in range(NH):
                kw[h] = sl_k[h % 2].bump(16 * 2 * (g + 1))
                vw[h] = sl_v[h % 2].bump(16 * 2 * (g + 1))
                cw[h] = sl_c[h % 2].bump(16)

            ks6 = apv(ksb).rearrange("p (i gg li r tb) -> p i gg li r tb",
                                     i=2, gg=2, li=4, r=2)
            vs6 = apv(vsb).rearrange("p (i gg li r dd) -> p i gg li r dd",
                                     i=2, gg=2, li=4, r=2)
            vtg_v = vtg.rearrange("r (li2 p) dd -> r li2 p dd", p=128)

            @P.on("sync")
            def _(e, g=g, kmax=kmax, pe_last=pe_last):
                for h in range(NH):
                    if h >= 2:
                        e.wait_ge(s_pe.sem, pe_last[h - 2])
                    for gg in range(g + 1):
                        for r in range(2):
                            rd = r if gg == 0 else 1 - r
                            e.dma_start(
                                out=ks6[:, h % 2, gg, :, rd, :],
                                in_=kng[r, h, :, 512 * gg:512 * (gg + 1)].rearrange(
                                    "p (li tb) -> p li tb", li=4)).then_inc(sl_k[h % 2].sem, 16)
                            e.dma_start(
                                out=vs6[:, h % 2, gg, :, rd, :],
                                in_=vtg_v[r, 4 * gg:4 * (gg + 1), :, 128 * h:128 * (h + 1)].rearrange(
                                    "li p dd -> p li dd")).then_inc(sl_v[h % 2].sem, 16)
                    e.dma_start(out=crv[0:1, h % 2, 0:128 * kmax],
                                in_=colrow[h:h + 1, 0:128 * kmax]).then_inc(sl_c[h % 2].sem, 16)

            nmrow2 = [rows[0:1, 0:G], rows[0:1, 2 * G:3 * G]]
            zrow2 = [rows[0:1, G:2 * G], rows[0:1, 3 * G:4 * G]]
            nmbv = [apv(sb["nmb"])[0:1, 0:G], apv(sb["nmb"])[0:1, G:2 * G]]
            zbv2 = [apv(sb["tmp"])[:, 0:G], apv(sb["tmp"])[:, G:2 * G]]
            psmv = [ps[4][0:1, :], ps[5][0:1, :]]
            pszv = [ps[0][0:1, :], ps[1][0:1, :]]
            pvb = [ps[2], ps[3]]
            scb = [ps[4], ps[5]]
            zbb = [ps[0], ps[1]]

            for hj in range(0, NH, 2):
                hpair = (hj, hj + 1)
                mx_fin = [[0] * 4, [0] * 4]
                tr_pe = [[0] * 4, [0] * 4]
                for s_ in range(4):
                    nk = 128 * (2 * s_ + 2 + 8 * g)
                    npc = (nk + 511) // 512
                    for hp in range(2):
                        h = hpair[hp]
                        redv = [s_ve.bump() for _ in range(npc)]
                        mx_fin[hp][s_] = s_ve.bump()

                        @P.on("tensor")
                        def _(e, g=g, h=h, hp=hp, s_=s_, npc=npc, nk=nk,
                              kwh=kw[h], cwh=cw[h], redv=tuple(redv),
                              mxw=(mx_fin[hp][s_ - 1] if s_ > 0 else 0)):
                            if s_ == 0:
                                e.wait_ge(sl_k[hp].sem, kwh)
                                e.wait_ge(sl_c[hp].sem, cwh)
                            else:
                                e.wait_ge(s_ve.sem, mxw)
                            for pc in range(npc):
                                n0 = 512 * pc
                                n1 = min(nk, n0 + 512)
                                if pc >= 2:
                                    e.wait_ge(s_ve.sem, redv[pc - 2])
                                bank = ps[2 * hp + pc % 2]
                                e.matmul(bank[:, 0:n1 - n0], sb["onesrow_bf"][0:1, 0:128],
                                         crv[0:1, hp, n0:n1], start=True, stop=False)
                                e.matmul(bank[:, 0:n1 - n0], qnv[:, h, 128 * s_:128 * (s_ + 1)],
                                         ksv[:, hp, n0:n1], start=False,
                                         stop=True).then_inc(s_pe.sem, 1)
                        pevals = [s_pe.bump() for _ in range(npc)]

                        @P.on("vector")
                        def _(e, g=g, hp=hp, npc=npc, nk=nk, pevals=tuple(pevals)):
                            q_ve = c["q_ve"]
                            for pc in range(npc):
                                e.wait_ge(s_pe.sem, pevals[pc])
                                n0 = 512 * pc
                                n1 = min(nk, n0 + 512)
                                bank = ps[2 * hp + pc % 2]
                                if pc == npc - 1:
                                    lw = n1 - n0
                                    q_ve.seq(e, e.tensor_tensor(
                                        bank[:, lw - 256:lw], bank[:, lw - 256:lw],
                                        sb["maskq"][:, 256 * g:256 * (g + 1)], ALU.add))
                                q_ve.seq(e, e.reduce_max(
                                    sb["mcol"][:, 4 * hp + pc:4 * hp + pc + 1],
                                    bank[:, 0:n1 - n0], axis=AX.X, negate=True))
                                e.sem_inc(s_ve.sem, 1)
                            for pc in range(1, npc):
                                q_ve.seq(e, e.tensor_tensor(
                                    sb["mcol"][:, 4 * hp:4 * hp + 1],
                                    sb["mcol"][:, 4 * hp:4 * hp + 1],
                                    sb["mcol"][:, 4 * hp + pc:4 * hp + pc + 1], ALU.min))
                            e.sem_inc(s_ve.sem, 1)

                    for hp in range(2):
                        @P.on("tensor")
                        def _(e, hp=hp, s_=s_, mv=mx_fin[hp][s_]):
                            e.wait_ge(s_ve.sem, mv)
                            e.transpose(psmv[hp][0:1, 128 * s_:128 * (s_ + 1)],
                                        sb["mcol"][:, 4 * hp:4 * hp + 1],
                                        sb["identf"][:]).then_inc(s_pe.sem, 1)
                        tr_pe[hp][s_] = s_pe.bump()

                nm_ve = [0, 0]
                for hp in range(2):
                    @P.on("vector")
                    def _(e, hp=hp, trv=tr_pe[hp][3]):
                        e.wait_ge(s_pe.sem, trv)
                        q_ve = c["q_ve"]
                        q_ve.seq(e, e.tensor_copy(nmrow2[hp], psmv[hp]))
                        q_ve.seq(e, e.tensor_copy(nmbv[hp], nmrow2[hp]))
                        e.sem_inc(s_ve.sem, 1)
                    nm_ve[hp] = s_ve.bump()

                stv = [[0] * kmax, [0] * kmax]
                mkv = [[0] * kmax, [0] * kmax]
                exv = [[0] * kmax, [0] * kmax]
                pvv = [[0] * kmax, [0] * kmax]
                for kb in range(kmax):
                    s0 = max(0, -(-(kb - 8 * g - 1) // 2))
                    nq = G - 128 * s0
                    q0 = 128 * s0
                    for hp in range(2):
                        h = hpair[hp]

                        @P.on("tensor")
                        def _(e, h=h, hp=hp, kb=kb, nq=nq, q0=q0,
                              nmw=nm_ve[1], vwh=vw[h]):
                            if kb == 0:
                                e.wait_ge(s_ve.sem, nmw)
                                e.wait_ge(sl_v[hp].sem, vwh)
                            bank = scb[hp]
                            e.matmul(bank[:, 0:nq], sb["onesrow_bf"][0:1, 0:128],
                                     nmbv[hp][0:1, q0:G], start=True, stop=False)
                            e.matmul(bank[:, 0:nq], ksv[:, hp, 128 * kb:128 * (kb + 1)],
                                     qnv[:, h, q0:G], start=False,
                                     stop=True).then_inc(s_pe.sem, 1)
                        stv[hp][kb] = s_pe.bump()

                    for hp in range(2):
                        if kb >= 8 * g:
                            @P.on("vector")
                            def _(e, g=g, hp=hp, kb=kb, st=stv[hp][kb]):
                                e.wait_ge(s_pe.sem, st)
                                bank = scb[hp]
                                moff = 256 * g + 128 * ((kb - 8 * g) % 2)
                                mslice = sb["maskab"][:, moff:moff + 128]
                                c["q_ve"].seq(e, e.tensor_tensor(
                                    bank[:, 0:128], bank[:, 0:128], mslice, ALU.add))
                                e.sem_inc(s_ve.sem, 1)
                            mkv[hp][kb] = s_ve.bump()

                    for hp in range(2):
                        h = hpair[hp]

                        @P.on("scalar")
                        def _(e, h=h, hp=hp, kb=kb, nq=nq, st=stv[hp][kb],
                              mk=mkv[hp][kb], masked=(kb >= 8 * g)):
                            e.wait_ge(s_pe.sem, st)
                            if masked:
                                e.wait_ge(s_ve.sem, mk)
                            bank = scb[hp]
                            a = e.activation(out=etv[:, hp, 0:nq], in_=bank[:, 0:nq],
                                             func=AF.Exp,
                                             bias=sb["colt"][:, 16 * h + kb:16 * h + kb + 1])
                            a.then_inc(s_se.sem, 1)
                        exv[hp][kb] = s_se.bump()

                    for hp in range(2):
                        @P.on("tensor")
                        def _(e, hp=hp, kb=kb, nq=nq, q0=q0, kmax=kmax,
                              ev=exv[hp][kb]):
                            e.wait_ge(s_se.sem, ev)
                            e.matmul(pvb[hp][:, q0:G], vsv[:, hp, 128 * kb:128 * (kb + 1)],
                                     etv[:, hp, 0:nq],
                                     start=(kb == 0), stop=(kb == kmax - 1))
                            mm = e.matmul(pszv[hp][0:1, q0:G], sb["onescol_bf"][:],
                                          etv[:, hp, 0:nq],
                                          start=(kb == 0), stop=(kb == kmax - 1))
                            mm.then_inc(s_pe.sem, 1)
                        pvv[hp][kb] = s_pe.bump()

                ob_ve = [0, 0]
                for hp in range(2):
                    h = hpair[hp]

                    @P.on("vector")
                    def _(e, hp=hp, pv=pvv[hp][kmax - 1]):
                        e.wait_ge(s_pe.sem, pv)
                        c["q_ve"].seq(e, e.reciprocal(zrow2[hp], pszv[hp]))
                        e.sem_inc(s_ve.sem, 1)
                    zr_ve = s_ve.bump()

                    @P.on("tensor")
                    def _(e, hp=hp, zr_ve=zr_ve):
                        e.wait_ge(s_ve.sem, zr_ve)
                        mm = e.matmul(zbb[hp][:], sb["onesrow_f"][:], zrow2[hp],
                                      start=True, stop=True)
                        mm.then_inc(s_pe.sem, 1)
                    zb_pe = s_pe.bump()

                    @P.on("vector")
                    def _(e, h=h, hp=hp, zb_pe=zb_pe):
                        e.wait_ge(s_pe.sem, zb_pe)
                        q_ve = c["q_ve"]
                        q_ve.seq(e, e.tensor_copy(zbv2[hp], zbb[hp][:]))
                        q_ve.seq(e, e.tensor_tensor(obv[:, h, :], pvb[hp][:],
                                                    zbv2[hp], ALU.mult))
                        e.sem_inc(s_ve.sem, 1)
                    ob_ve[hp] = s_ve.bump()
                    pe_last[h] = pvv[hp][kmax - 1]

                @P.on("tensor")
                def _(e, obw=ob_ve[1]):
                    e.wait_ge(s_ve.sem, obw)
                s_pe.bump(0)

            barrier(P, c)

            emit_dense_residual(P, c, sb, ps, xn=ob, wt_dram=woutt, xin_v=xin_v,
                                xout_v=xout_v, toks=toks, scale=1.0, halfbias=None)
            barrier(P, c)


def build_attn_core_launch():
    nc = bass.Bass("TRN2", target_bir_lowering=False)
    io = attc_dram_decls(nc)
    io["xin"] = nc.dram_tensor("xin", [D, 2 * G], F32, kind="ExternalInput").ap()
    io["xout"] = nc.dram_tensor("xout", [D, 2 * G], F32, kind="ExternalOutput").ap()
    io["qn"] = nc.dram_tensor("qn", [NH, 128, 2 * G], BF16, kind="ExternalInput").ap()
    io["kng"] = nc.dram_tensor("kng", [2, NH, 128, 2 * G], BF16, kind="ExternalInput").ap()
    io["vtg"] = nc.dram_tensor("vtg", [2, 2 * G, D], BF16, kind="ExternalInput").ap()
    with contextlib.ExitStack() as ctx:
        E = ctx.enter_context
        c = make_ctx(nc, ctx)
        ps = [E(nc.psum_tensor(f"psf{i}", [128, 512], F32)) for i in range(6)]
        sb = sb_common(nc, ctx)
        sb.update(alloc_sb(nc, ctx, ATTC_SB))
        qn_sb = E(nc.sbuf_tensor("qn_sb", [128, NH * G], BF16))
        ob = E(nc.sbuf_tensor("ob", [128, NH * G], BF16))
        ksb = E(nc.sbuf_tensor("ksb", [128, 2 * S], BF16))
        vsb = E(nc.sbuf_tensor("vsb", [128, 2 * S], BF16))
        expt = E(nc.sbuf_tensor("expt", [128, 2 * G], BF16))
        block = E(nc.Block())
        P = Prog()
        emit_consts(P, c, sb)
        emit_attc_phase(P, c, ps, sb, qn_sb, ob, ksb, vsb, expt, io)
        emit_final_wait(P, c)
        P.emit(block)
    return nc


def attn0_weights(inputs):
    Wqkv = np.asarray(inputs["L0_attn_Wqkv"], np.float32)
    Wout = np.asarray(inputs["L0_attn_Wout"], np.float32)
    colt = np.zeros((128, NH * 16), np.float32)
    colrow = np.zeros((NH, S), np.float32)
    for h in range(NH):
        for kb in range(16):
            colt[:, 16 * h + kb] = SLOPES[h] * (128 * kb + np.arange(128))
        colrow[h] = SLOPES[h] * np.arange(S)
    w = {
        "wqkvt": _wblocks(np.ascontiguousarray(Wqkv.T)),
        "qlnw": _chunkmajor(np.asarray(inputs["L0_attn_q_ln_w"], np.float32)),
        "qlnb": _chunkmajor(np.asarray(inputs["L0_attn_q_ln_b"], np.float32)),
        "klnw": _chunkmajor(np.asarray(inputs["L0_attn_k_ln_w"], np.float32)),
        "klnb": _chunkmajor(np.asarray(inputs["L0_attn_k_ln_b"], np.float32)),
        "identbf": np.eye(128, dtype=BF),
        "woutt": _wblocks(np.ascontiguousarray(Wout.T)),
        "colt": colt,
        "colrow": colrow.astype(BF),
        "identf": np.eye(128, dtype=np.float32),
    }
    NEG = np.float32(-1e30)
    tri_kq = np.tril(np.ones((128, 128), np.float32)).T
    maskab = np.zeros((2, 128, 256), np.float32)
    maskab[0, :, 0:128] = np.where(tri_kq > 0, 0, NEG)
    maskab[0, :, 128:256] = NEG
    maskab[1, :, 0:128] = 0
    maskab[1, :, 128:256] = np.where(tri_kq > 0, 0, NEG)
    w["maskab"] = maskab
    tri_qk = np.tril(np.ones((128, 128), np.float32))
    maskq = np.zeros((2, 128, 256), np.float32)
    maskq[0, :, 0:128] = np.where(tri_qk > 0, 0, NEG)
    maskq[0, :, 128:256] = NEG
    maskq[1, :, 0:128] = 0
    maskq[1, :, 128:256] = np.where(tri_qk > 0, 0, NEG)
    w["maskq"] = maskq
    return w


def run_attn0(x, inputs, w=None):
    if w is None:
        w = attn0_weights(inputs)
    nc2 = _get_prog(("qkv",), build_qkv_launch)
    in_maps = []
    for core in range(8):
        xc, _ = _shard_x(x, core)
        im = {"xin": xc, "wqkvt": w["wqkvt"], "i_qlnw": w["qlnw"], "i_qlnb": w["qlnb"],
              "i_klnw": w["klnw"], "i_klnb": w["klnb"], "identbf2": w["identbf"]}
        in_maps.append(im)
    res = run_bass_kernel_spmd(nc2, in_maps, core_ids=list(range(8)))
    qn = [np.asarray(res.results[cc]["qn"]) for cc in range(8)]
    kn = [np.asarray(res.results[cc]["kn"]) for cc in range(8)]
    vt = [np.asarray(res.results[cc]["vt"]) for cc in range(8)]

    kng = [np.stack([kn[2 * sq], kn[2 * sq + 1]]) for sq in range(4)]
    vtg = [np.stack([vt[2 * sq], vt[2 * sq + 1]]) for sq in range(4)]

    in_maps = []
    for core in range(8):
        seq, par = core // 2, core % 2
        xc, _ = _shard_x(x, core)
        im = {"xin": xc, "qn": qn[core], "kng": kng[seq], "vtg": vtg[seq],
              "woutt": w["woutt"], "colt": w["colt"], "colrow": w["colrow"],
              "maskab": np.stack([w["maskab"][par], w["maskab"][1 - par]]),
              "maskq": np.stack([w["maskq"][par], w["maskq"][1 - par]]),
              "identf": w["identf"]}
        in_maps.append(im)
    out = x.copy()
    nc3 = _get_prog(("attc",), build_attn_core_launch)
    res = run_bass_kernel_spmd(nc3, in_maps, core_ids=list(range(8)))
    for cc in range(8):
        _unshard_x(out, cc, np.asarray(res.results[cc]["xout"]))
    return out


A1_SB = {
    "bq": ([128, NCH], F32), "bk": ([128, 8], F32), "bv": ([128, 8], F32),
    "bqs": ([128, 8], F32), "identb": ([128, 128], BF16),
    "vstage": ([128, 4 * 128], BF16), "oe": ([128, 2 * G], BF16),
}


def a1_dram_decls(nc):
    d = {}
    d["wq"] = nc.dram_tensor("wq", [NCH, 128, NCH, 128], F32, kind="ExternalInput").ap()
    d["wk"] = nc.dram_tensor("wk", [8, 128, NCH, 128], F32, kind="ExternalInput").ap()
    d["wv"] = nc.dram_tensor("wv", [8, 128, NCH, 128], F32, kind="ExternalInput").ap()
    d["bqm"] = nc.dram_tensor("bqm", [128, NCH], F32, kind="ExternalInput").ap()
    d["bkm"] = nc.dram_tensor("bkm", [128, 8], F32, kind="ExternalInput").ap()
    d["bvm"] = nc.dram_tensor("bvm", [128, 8], F32, kind="ExternalInput").ap()
    d["identbf"] = nc.dram_tensor("identbf", [128, 128], BF16, kind="ExternalInput").ap()
    d["q3"] = nc.dram_tensor("q3", [8, G, 128], BF16, kind="Internal").ap()
    d["k3"] = nc.dram_tensor("k3", [8, G, 128], BF16, kind="Internal").ap()
    d["o3e"] = nc.dram_tensor("o3e", [128, NCH, G], BF16, kind="Internal").ap()
    return d


def emit_a1_phase(P, c, ps, psb, sb, xn, qt, kt, vt, ot, qhkT, khkT, probs, ptb, zc, io):
    xin, xout = io["xin"], io["xout"]
    wq, wk, wv = io["wq"], io["wk"], io["wv"]
    bqm, bkm, bvm = io["bqm"], io["bkm"], io["bvm"]
    identbf = io["identbf"]
    q3, k3, o3e = io["q3"], io["k3"], io["o3e"]
    s_dma, s_pe, s_ve, s_se, s_wd, s_out = (c[k] for k in
                                            ("s_dma", "s_pe", "s_ve", "s_se", "s_wd", "s_out"))
    if True:
        @P.on("sync")
        def _(e):
            e.dma_start(out=sb["bq"][:], in_=bqm).then_inc(s_dma.sem, 16)
            e.dma_start(out=sb["bk"][:], in_=bkm).then_inc(s_dma.sem, 16)
            e.dma_start(out=sb["bv"][:], in_=bvm).then_inc(s_dma.sem, 16)
            e.dma_start(out=sb["identb"][:], in_=identbf).then_inc(s_dma.sem, 16)
        s_dma.bump(16 * 4)

        barrier(P, c)

        SC = float(1.0 / np.sqrt(128.0))

        @P.on("vector")
        def _(e):
            bqv = apv(sb["bq"]).rearrange("p (hk jj) -> p hk jj", hk=8)
            q_ve = c["q_ve"]
            q_ve.seq(e, e.tensor_tensor(sb["bqs"][:], bqv[:, :, 0], bqv[:, :, 1], ALU.add))
            q_ve.seq(e, e.tensor_tensor(sb["bqs"][:], sb["bqs"][:], bqv[:, :, 2], ALU.add))
            q_ve.seq(e, e.tensor_tensor(sb["bqs"][:], sb["bqs"][:], bqv[:, :, 3], ALU.add))
            e.sem_inc(s_ve.sem, 1)
        s_ve.bump()

        @P.on("scalar")
        def _(e, w=s_ve.n):
            e.wait_ge(s_ve.sem, w)
            e.mul(sb["bqs"][:], sb["bqs"][:], SC).then_inc(s_se.sem, 1)
        s_se.bump()

        barrier(P, c)

        xin_v = xin.rearrange("(j p) t -> j p t", p=128)
        xout_v = xout.rearrange("(j p) t -> j p t", p=128)
        xnv = apv(xn).rearrange("p (j t) -> p j t", j=NCH)
        qtv = apv(qt).rearrange("p (hk t) -> p hk t", hk=8)
        ktv = apv(kt).rearrange("p (hk t) -> p hk t", hk=8)
        vtv = apv(vt).rearrange("p (hk t) -> p hk t", hk=8)
        otv = apv(ot).rearrange("p (hk t) -> p hk t", hk=8)
        wbv = apv(sb["wbuf"]).rearrange("p (i j n) -> p i j n", i=2, j=NCH)
        sl_w = c["sl_w"]
        xsv = apv(sb["xs"]).rearrange("p (i t) -> p i t", i=4)
        xov = apv(sb["xo"]).rearrange("p (i t) -> p i t", i=4)
        tmpv = apv(sb["tmp"]).rearrange("p (i t) -> p i t", i=2)
        sl_xe, sl_o = c["sl_xe"], c["sl_o"]

        for g in range(2):
            toks = slice(g * G, (g + 1) * G)
            emit_ln(P, c, sb, ps, xin_v, toks, xn)

            wprev = [0, 0]
            pprev = [0, 0]
            for proj, (wsrc, nblk, acc, arena, bias_t, scale) in enumerate((
                    (wq, NCH, 4, qtv, sb["bqs"], SC),
                    (wk, 8, 1, ktv, sb["bk"], 1.0),
                    (wv, 8, 1, vtv, sb["bv"], 1.0))):
                nout = nblk // acc
                wws = []
                pe_m = [None] * nblk
                ve_m = [None] * nout
                for d in range(nblk):
                    wws.append(sl_w[d % 2].bump(16))

                @P.on("gpsimd")
                def _(e, wsrc=wsrc, nblk=nblk, pe_m=pe_m, acc=acc, wprev=tuple(wprev)):
                    for d in range(nblk):
                        if d >= 2:
                            e.wait_ge(s_pe.sem, pe_m[d - 2])
                        elif wprev[d % 2]:
                            e.wait_ge(s_pe.sem, wprev[d % 2])
                        e.dma_start(out=wbv[:, d % 2, :, :],
                                    in_=wsrc[d, :, :, :]).then_inc(sl_w[d % 2].sem, 16)

                for d in range(nblk):
                    hk = d // acc

                    @P.on("tensor")
                    def _(e, d=d, hk=hk, ww=wws[d], acc=acc,
                          ev=(ve_m[(d // acc) - 2] if d // acc >= 2 and d % acc == 0
                              else (pprev[(d // acc) % 2] if d // acc < 2 and d % acc == 0 else None))):
                        e.wait_ge(sl_w[d % 2].sem, ww)
                        if ev:
                            e.wait_ge(s_se.sem, ev)
                        pt = ps[hk % 2]
                        mm = None
                        for j in range(NCH):
                            mm = e.matmul(pt[:], wbv[:, d % 2, j, :], xnv[:, j, :],
                                          start=(j == 0 and d % acc == 0),
                                          stop=(j == NCH - 1 and d % acc == acc - 1))
                        mm.then_inc(s_pe.sem, 1)
                    pe_m[d] = s_pe.bump()

                    if d % acc == acc - 1:
                        @P.on("scalar")
                        def _(e, hk=hk, pe_cnt=pe_m[d], arena=arena, bias_t=bias_t, scale=scale):
                            e.wait_ge(s_pe.sem, pe_cnt)
                            e.activation(out=arena[:, hk, :], in_=ps[hk % 2][:], func=AF.Identity,
                                         scale=scale, bias=bias_t[:, hk:hk + 1]).then_inc(s_se.sem, 1)
                        ve_m[hk] = s_se.bump()
                wprev = [pe_m[nblk - 2], pe_m[nblk - 1]]
                pprev = [ve_m[nout - 2], ve_m[nout - 1]]
            barrier(P, c)

            tvi = 0
            tv_se = []
            for src_ar, dst in ((qtv, q3), (ktv, k3)):
                for hk in range(8):
                    for i in range(NT):
                        @P.on("tensor")
                        def _(e, src_ar=src_ar, hk=hk, i=i, tvi=tvi,
                              wv=(tv_se[tvi - 2] if tvi >= 2 else 0)):
                            if tvi >= 2:
                                e.wait_ge(s_se.sem, wv)
                            mm = e.transpose(psb[tvi % 2][:, 0:128],
                                             src_ar[:, hk, 128 * i:128 * (i + 1)],
                                             sb["identb"][:])
                            mm.then_inc(s_pe.sem, 1)
                        tp = s_pe.bump()

                        @P.on("scalar")
                        def _(e, i=i, tvi=tvi, tp=tp,
                              ow=(c["sl_o"][tvi % 4].n if True else 0)):
                            e.wait_ge(s_pe.sem, tp)
                            if tvi >= 4:
                                e.wait_ge(c["sl_o"][tvi % 4].sem, ow)
                            e.copy(apv(sb["vstage"]).rearrange("p (i n) -> p i n", i=4)[:, tvi % 4, 0:128],
                                   psb[tvi % 2][:, 0:128]).then_inc(s_se.sem, 1)
                        se_cnt = s_se.bump()
                        tv_se.append(se_cnt)
                        c["sl_o"][tvi % 4].bump(16)

                        @P.on("sync")
                        def _(e, dst=dst, hk=hk, i=i, tvi=tvi, se_cnt=se_cnt):
                            e.wait_ge(s_se.sem, se_cnt)
                            e.dma_start(out=dst[hk, 128 * i:128 * (i + 1), :],
                                        in_=apv(sb["vstage"]).rearrange("p (i n) -> p i n", i=4)[:, tvi % 4, 0:128]).then_inc(c["sl_o"][tvi % 4].sem, 16)
                        tvi += 1
            barrier(P, c)

            BT = 32
            q_hk = apv(qhkT).rearrange("p (i n) -> p i n", i=2)
            k_hk = apv(khkT).rearrange("p (i n) -> p i n", i=2)
            pv_pe_l = []
            ld_b = [None] * (G // BT)
            for b in range(G // BT):
                ld_b[b] = c["sl_x"][b % 2].bump(32)

            @P.on("sync")
            def _(e, g=g, pv_pe_l=pv_pe_l, ld_b=ld_b):
                for b in range(G // BT):
                    if b >= 2:
                        e.wait_ge(s_pe.sem, pv_pe_l[(b - 2) * BT + BT - 1])
                    e.dma_start(out=q_hk[0:8, b % 2, 0:BT * 128],
                                in_=q3[:, BT * b:BT * (b + 1), :]).then_inc(c["sl_x"][b % 2].sem, 16)
                    e.dma_start(out=k_hk[0:8, b % 2, 0:BT * 128],
                                in_=k3[:, BT * b:BT * (b + 1), :]).then_inc(c["sl_x"][b % 2].sem, 16)

            rec = []
            vt_q = apv(vt).rearrange("p (hk t) -> p t hk", hk=8)
            ot_q = apv(ot).rearrange("p (hk t) -> p t hk", hk=8)
            tmpv2 = apv(sb["tmp"]).rearrange("p (i t) -> p i t", i=2)
            NQD = G // 4
            for qd in range(NQD):
                b = (4 * qd) // BT
                sl = qd % 2
                prev = rec[qd - 2] if qd >= 2 else None
                t0 = 4 * qd
                tb0 = t0 - b * BT

                @P.on("tensor")
                def _(e, b=b, sl=sl, qd=qd, t0=t0, tb0=tb0, ldw=ld_b[b],
                      zv=(prev["ex_se"] if prev else 0)):
                    if tb0 == 0:
                        e.wait_ge(c["sl_x"][b % 2].sem, ldw)
                    if qd >= 2:
                        e.wait_ge(s_se.sem, zv)
                    mm = None
                    for j in range(4):
                        mm = e.matmul(ps[2 + sl][:, 128 * j:128 * (j + 1)],
                                      q_hk[0:8, b % 2, 128 * (tb0 + j):128 * (tb0 + j + 1)],
                                      k_hk[0:8, b % 2, 128 * (tb0 + j):128 * (tb0 + j + 1)],
                                      start=(j == 0), stop=(j == 3))
                    mm.then_inc(s_pe.sem, 1)
                sc_pe = s_pe.bump()

                @P.on("vector")
                def _(e, sl=sl, qd=qd, sc_pe=sc_pe,
                      exw=(prev["ex_se"] if prev else 0)):
                    e.wait_ge(s_pe.sem, sc_pe)
                    q_ve = c["q_ve"]
                    for j in range(4):
                        mm = e.reduce_max(zc[:, 4 * sl + j:4 * sl + j + 1],
                                          ps[2 + sl][:, 128 * j:128 * (j + 1)],
                                          axis=AX.X, negate=True)
                    q_ve.seq(e, mm)
                    if qd >= 2:
                        e.wait_ge(s_se.sem, exw)
                    for j in range(4):
                        q_ve.seq(e, e.tensor_scalar(tmpv2[:, sl, 128 * j:128 * (j + 1)],
                                                    ps[2 + sl][:, 128 * j:128 * (j + 1)],
                                                    zc[:, 4 * sl + j:4 * sl + j + 1], None,
                                                    op0=ALU.add))
                    e.sem_inc(s_ve.sem, 1)
                nm_ve = s_ve.bump()

                @P.on("scalar")
                def _(e, sl=sl, qd=qd, nm_ve=nm_ve,
                      pvw=(prev["tr_pe"] if prev else 0),
                      zrw=(prev["zr_ve"] if prev else 0)):
                    e.wait_ge(s_ve.sem, nm_ve)
                    if qd >= 2:
                        e.wait_ge(s_pe.sem, pvw)
                        e.wait_ge(s_ve.sem, zrw)
                    e.activation(out=probs[:, 512 * sl:512 * (sl + 1)],
                                 in_=tmpv2[:, sl, :], func=AF.Exp).then_inc(s_se.sem, 1)
                ex_se = s_se.bump()

                @P.on("vector")
                def _(e, sl=sl, qd=qd, ex_se=ex_se):
                    e.wait_ge(s_se.sem, ex_se)
                    q_ve = c["q_ve"]
                    for j in range(4):
                        mm = e.reduce_sum(zc[:, 8 + 4 * sl + j:8 + 4 * sl + j + 1],
                                          probs[:, 512 * sl + 128 * j:512 * sl + 128 * (j + 1)], axis=AX.X)
                    q_ve.seq(e, mm)
                    q_ve.seq(e, e.reciprocal(zc[:, 8 + 4 * sl:8 + 4 * sl + 4],
                                             zc[:, 8 + 4 * sl:8 + 4 * sl + 4]))
                    e.sem_inc(s_ve.sem, 1)
                zr_ve = s_ve.bump()

                @P.on("tensor")
                def _(e, sl=sl, qd=qd, ex_se=ex_se,
                      cpw=(prev["cp_ve"] if prev else 0)):
                    e.wait_ge(s_se.sem, ex_se)
                    if qd >= 2:
                        e.wait_ge(s_ve.sem, cpw)
                    mm = None
                    for j in range(4):
                        mm = e.transpose(psb[sl][:, 128 * j:128 * (j + 1)],
                                         probs[:, 512 * sl + 128 * j:512 * sl + 128 * (j + 1)],
                                         sb["identb"][:])
                    mm.then_inc(s_pe.sem, 1)
                tr_pe = s_pe.bump()

                @P.on("vector")
                def _(e, sl=sl, qd=qd, tr_pe=tr_pe,
                      opw=(prev["pv_pe"] if prev else 0)):
                    e.wait_ge(s_pe.sem, tr_pe)
                    if qd >= 2:
                        e.wait_ge(s_pe.sem, opw)
                    e.tensor_copy(ptb[:, 512 * sl:512 * (sl + 1)],
                                  psb[sl][:, 0:512]).then_inc(s_ve.sem, 1)
                cp_ve = s_ve.bump()

                @P.on("tensor")
                def _(e, sl=sl, qd=qd, t0=t0, cp_ve=cp_ve,
                      ovw=(prev["ev_ve"] if prev else 0)):
                    e.wait_ge(s_ve.sem, cp_ve)
                    if qd >= 2:
                        e.wait_ge(s_ve.sem, ovw)
                    mm = None
                    for j in range(4):
                        mm = e.matmul(ps[4 + sl][:, 8 * j:8 * (j + 1)],
                                      ptb[:, 512 * sl + 128 * j:512 * sl + 128 * (j + 1)],
                                      vt_q[:, t0 + j, :], start=(j == 0), stop=(j == 3))
                    mm.then_inc(s_pe.sem, 1)
                pv_pe = s_pe.bump()
                for j in range(4):
                    pv_pe_l.append(pv_pe)

                @P.on("vector")
                def _(e, sl=sl, qd=qd, t0=t0, pv_pe=pv_pe):
                    e.wait_ge(s_pe.sem, pv_pe)
                    q_ve = c["q_ve"]
                    for j in range(4):
                        q_ve.seq(e, e.tensor_scalar(ot_q[:, t0 + j, :],
                                                    ps[4 + sl][:, 8 * j:8 * (j + 1)],
                                                    zc[:, 8 + 4 * sl + j:8 + 4 * sl + j + 1], None,
                                                    op0=ALU.mult))
                    e.sem_inc(s_ve.sem, 1)
                ev_ve = s_ve.bump()
                rec.append({"ex_se": ex_se, "zr_ve": zr_ve, "tr_pe": tr_pe,
                            "cp_ve": cp_ve, "pv_pe": pv_pe, "ev_ve": ev_ve})


            barrier(P, c)

            o3x = o3e.rearrange("dd (hk r) t -> dd hk r t", r=4)

            @P.on("sync")
            def _(e):
                for r in range(4):
                    e.dma_start(out=o3x[:, :, r, :], in_=otv[:, :, :]).then_inc(s_dma.sem, 16)
            s_dma.bump(16 * 4)
            barrier(P, c)

            o3f = o3e.rearrange("dd hh t -> (dd hh) t").rearrange("(a p) t -> a p t", p=128)
            oev = apv(sb["oe"]).rearrange("p (i t) -> p i t", i=2)
            a1rv = []
            for cch in range(NCH):
                xe_w = sl_xe[cch % 4].bump(16)
                oe_w = c["sl_c"][cch % 2].bump(16)
                ve_prev2 = a1rv[cch - 2] if cch >= 2 else 0

                @P.on("sync")
                def _(e, cch=cch, toks=toks,
                      ev4=(a1rv[cch - 4] if cch >= 4 else 0)):
                    if cch >= 4:
                        e.wait_ge(s_ve.sem, ev4)
                    e.dma_start(out=xsv[:, cch % 4, :], in_=xin_v[cch, :, toks]).then_inc(sl_xe[cch % 4].sem, 16)

                @P.on("gpsimd")
                def _(e, cch=cch, ve_prev2=ve_prev2):
                    if cch >= 2:
                        e.wait_ge(s_ve.sem, ve_prev2)
                    e.dma_start(out=oev[:, cch % 2, :], in_=o3f[cch, :, :]).then_inc(c["sl_c"][cch % 2].sem, 16)

                @P.on("vector")
                def _(e, cch=cch, xw=xe_w, ow=oe_w,
                      ov=(c["sl_o"][cch % 4].n if cch >= 4 else 0)):
                    e.wait_ge(sl_xe[cch % 4].sem, xw)
                    e.wait_ge(c["sl_c"][cch % 2].sem, ow)
                    if cch >= 4:
                        e.wait_ge(c["sl_o"][cch % 4].sem, ov)
                    e.tensor_tensor(xov[:, cch % 4, :], xsv[:, cch % 4, :],
                                    oev[:, cch % 2, :], ALU.add).then_inc(s_ve.sem, 1)
                ve_cnt = s_ve.bump()
                a1rv.append(ve_cnt)
                c["sl_o"][cch % 4].bump(16)

                @P.on("sync")
                def _(e, cch=cch, ve_cnt=ve_cnt, toks=toks):
                    e.wait_ge(s_ve.sem, ve_cnt)
                    e.dma_start(out=xout_v[cch, :, toks], in_=xov[:, cch % 4, :]).then_inc(c["sl_o"][cch % 4].sem, 16)

            barrier(P, c)


def build_attn1_launch():
    nc = bass.Bass("TRN2", target_bir_lowering=False)
    io = a1_dram_decls(nc)
    io["xin"] = nc.dram_tensor("xin", [D, 2 * G], F32, kind="ExternalInput").ap()
    io["xout"] = nc.dram_tensor("xout", [D, 2 * G], F32, kind="ExternalOutput").ap()
    with contextlib.ExitStack() as ctx:
        E = ctx.enter_context
        c = make_ctx(nc, ctx)
        ps = [E(nc.psum_tensor(f"psf{i}", [128, 512], F32)) for i in range(6)]
        psb = [E(nc.psum_tensor(f"psbt{i}", [128, 1024], BF16)) for i in range(2)]
        sb = sb_common(nc, ctx)
        sb.update(alloc_sb(nc, ctx, A1_SB))
        xn = E(nc.sbuf_tensor("xn0", [128, NCH * G], BF16))
        qt = E(nc.sbuf_tensor("qt", [128, 8 * G], BF16))
        kt = E(nc.sbuf_tensor("kt", [128, 8 * G], BF16))
        vt = E(nc.sbuf_tensor("vt_sb", [128, 8 * G], BF16))
        ot = E(nc.sbuf_tensor("ot", [128, 8 * G], BF16))
        qhkT = E(nc.sbuf_tensor("qhkT", [8, 64 * 128 * 2], BF16))
        khkT = E(nc.sbuf_tensor("khkT", [8, 64 * 128 * 2], BF16))
        probs = E(nc.sbuf_tensor("probs_sb", [128, 1024], BF16))
        ptb = E(nc.sbuf_tensor("pt_sb", [128, 1024], BF16))
        zc = E(nc.sbuf_tensor("zc", [128, 16], F32))
        block = E(nc.Block())
        P = Prog()
        emit_consts(P, c, sb)
        emit_a1_phase(P, c, ps, psb, sb, xn, qt, kt, vt, ot, qhkT, khkT, probs, ptb, zc, io)
        emit_final_wait(P, c)
        P.emit(block)
    return nc


def attn1_weights(inputs):
    Wq = np.asarray(inputs["L1_attn_Wq"], np.float32)
    Wk = np.asarray(inputs["L1_attn_Wk"], np.float32)
    Wv = np.asarray(inputs["L1_attn_Wv"], np.float32)
    wqT = Wq.transpose(2, 1, 0).reshape(D, D).T
    wqT = np.ascontiguousarray(Wq.transpose(0, 2, 1).reshape(D, D))
    wkT = np.ascontiguousarray(Wk.transpose(0, 2, 1).reshape(D, 8 * 128))
    wvT = np.ascontiguousarray(Wv.transpose(0, 2, 1).reshape(D, 8 * 128))
    bq = np.asarray(inputs["L1_attn_bq"], np.float32)
    bk = np.asarray(inputs["L1_attn_bk"], np.float32)
    bv = np.asarray(inputs["L1_attn_bv"], np.float32)
    bqv = np.ascontiguousarray(bq.T.reshape(D))
    bkv = np.ascontiguousarray(bk.T.reshape(8 * 128))
    bvv = np.ascontiguousarray(bv.T.reshape(8 * 128))
    return {
        "wq": _wblocks(wqT),
        "wk": _wblocks(wkT),
        "wv": _wblocks(wvT),
        "bqm": _chunkmajor(bqv),
        "bkm": np.ascontiguousarray(bkv.reshape(8, 128).T),
        "bvm": np.ascontiguousarray(bvv.reshape(8, 128).T),
        "identbf": np.eye(128, dtype=BF),
    }


def run_attn1(x, inputs, w=None):
    if w is None:
        w = attn1_weights(inputs)
    nc = _get_prog(("attn1",), build_attn1_launch)
    in_maps = []
    for core in range(8):
        xc, _ = _shard_x(x, core)
        im = {"xin": xc}
        im.update(w)
        in_maps.append(im)
    res = run_bass_kernel_spmd(nc, in_maps, core_ids=list(range(8)))
    out = x.copy()
    for core in range(8):
        _unshard_x(out, core, np.asarray(res.results[core]["xout"]))
    return out




# version 10
# speedup vs baseline: 1.0092x; 1.0092x over previous
import sys
sys.path.insert(0, "/opt/trn_rl_repo")
import contextlib

import numpy as np
import ml_dtypes
import concourse.bass as bass
import concourse.mybir as mybir
from concourse.bass_utils import run_bass_kernel_spmd

F32 = mybir.dt.float32
BF16 = mybir.dt.bfloat16
AF = mybir.ActivationFunctionType
ALU = mybir.AluOpType
AX = mybir.AxisListType

D = 4096
NCH = 32
G = 512
NT = 4
S = 2048
NH = 32
LN_EPS = 1e-5
CLIP = 8.0
BF = ml_dtypes.bfloat16


def alibi_slopes(n_heads=32, bias_max=8.0):
    next_pow2 = 2 ** int(np.ceil(np.log2(n_heads)))
    m = np.arange(1, next_pow2 + 1, dtype=np.float32) * (bias_max / next_pow2)
    slopes = 1.0 / np.power(2.0, m)
    if next_pow2 != n_heads:
        slopes = np.concatenate([slopes[1::2], slopes[::2]])[:n_heads]
    return slopes.astype(np.float32)


SLOPES = alibi_slopes()
ATTN_GROUPS = (0, 1)
DUMP_OB = False
P2_STAGE = 4


def tiles_of(parity, grp):
    if grp == 0:
        return [2 * s + parity for s in range(4)]
    return [8 + 2 * s + (1 - parity) for s in range(4)]


def core_token_tiles(parity):
    return tiles_of(parity, 0) + tiles_of(parity, 1)



class Prog:
    ENGINES = ("sync", "gpsimd", "tensor", "vector", "scalar")

    def __init__(self):
        self.ops = {e: [] for e in self.ENGINES}

    def on(self, eng):
        def deco(fn):
            self.ops[eng].append(fn)
            return None
        return deco

    def emit(self, block):
        for name in self.ENGINES:
            fns = self.ops[name]
            deco = getattr(block, name)

            def section(e, fns=fns):
                for f in fns:
                    f(e)
            deco(section)


class Ctr:
    def __init__(self, sem):
        self.sem = sem
        self.n = 0

    def bump(self, k=1):
        self.n += k
        return self.n

    def seq(self, e, instr):
        instr.then_inc(self.sem, 1)
        e.wait_ge(self.sem, self.bump())


def apv(t):
    a = getattr(t, "ap", None)
    return t.ap() if callable(a) else t


def make_ctx(nc, ctx):
    E = ctx.enter_context
    c = {"nc": nc}
    for nm in ("s_dma", "s_pe", "s_ve", "s_se", "s_wd", "s_out", "s_bar", "s_ge"):
        c[nm] = Ctr(E(nc.semaphore(nm)))
    c["sl_x"] = [Ctr(E(nc.semaphore(f"slx{i}"))) for i in range(4)]
    c["sl_w"] = [Ctr(E(nc.semaphore(f"slw{i}"))) for i in range(2)]
    c["sl_xe"] = [Ctr(E(nc.semaphore(f"slxe{i}"))) for i in range(4)]
    c["sl_o"] = [Ctr(E(nc.semaphore(f"slo{i}"))) for i in range(4)]
    c["sl_c"] = [Ctr(E(nc.semaphore(f"slc{i}"))) for i in range(2)]
    for eng in ("ve", "se", "pe", "ge"):
        c["q_" + eng] = Ctr(E(nc.semaphore("q_" + eng)))
    return c


def barrier(P, c):
    tgt = c["s_bar"].n + 5
    c["s_bar"].n = tgt
    sem = c["s_bar"].sem
    dma_waits = [(ctr.sem, ctr.n) for ctr in
                 ([c["s_dma"], c["s_wd"], c["s_out"]] + c["sl_x"] + c["sl_w"]
                  + c["sl_xe"] + c["sl_o"] + c["sl_c"]) if ctr.n > 0]

    for eng in Prog.ENGINES:
        @P.on(eng)
        def _(e, tgt=tgt, sem=sem, eng=eng, dma_waits=tuple(dma_waits)):
            if eng == "sync":
                for sm, n in dma_waits:
                    e.wait_ge(sm, n)
            e.drain()
            e.sem_inc(sem, 1)
            e.wait_ge(sem, tgt)



def sb_common(nc, ctx):
    E = ctx.enter_context
    sb = {}
    sb["xs"] = E(nc.sbuf_tensor("xs", [128, 4 * G], F32))
    sb["sq"] = E(nc.sbuf_tensor("sq", [128, 2 * G], BF16))
    sb["rows"] = E(nc.sbuf_tensor("rows", [1, 4 * G], F32))
    sb["bcast"] = E(nc.sbuf_tensor("bcast", [128, 2 * G], F32))
    sb["onescol_bf"] = E(nc.sbuf_tensor("onescol_bf", [128, 1], BF16))
    sb["onesrow_f"] = E(nc.sbuf_tensor("onesrow_f", [1, 128], F32))
    sb["onesrow_bf"] = E(nc.sbuf_tensor("onesrow_bf", [1, 512], BF16))
    sb["cpool"] = E(nc.sbuf_tensor("cpool", [1, 8], F32))
    sb["wbuf"] = E(nc.sbuf_tensor("wbuf", [128, 2 * NCH * 128], BF16))
    sb["tmp"] = E(nc.sbuf_tensor("tmp", [128, 2 * G], F32))
    sb["xo"] = E(nc.sbuf_tensor("xo", [128, 4 * G], F32))
    return sb


def emit_consts(P, c, sb):
    @P.on("vector")
    def _(e):
        e.memset(sb["onescol_bf"][:], 1.0)
        e.memset(sb["onesrow_f"][:], 1.0)
        e.memset(sb["onesrow_bf"][:], 1.0)
        e.memset(sb["cpool"][0:1, 0:1], LN_EPS).then_inc(c["s_ve"].sem, 1)
    return c["s_ve"].bump()



def emit_ln(P, c, sb, ps, xin_v, toks, arena):
    s_ve = c["s_ve"]
    xnv = apv(arena).rearrange("p (j t) -> p j t", j=NCH)
    xsv = apv(sb["xs"]).rearrange("p (i t) -> p i t", i=4)
    ve0 = s_ve.n
    sl_x = c["sl_x"]
    xwaits = []
    for j in range(NCH):
        xwaits.append(sl_x[j % 4].bump(16))

    @P.on("sync")
    def _(e, ve0=ve0):
        for j in range(NCH):
            if j >= 4:
                e.wait_ge(s_ve.sem, ve0 + (j - 4) + 1)
            e.dma_start(out=xsv[:, j % 4, :], in_=xin_v[j, :, toks]).then_inc(sl_x[j % 4].sem, 16)

    @P.on("vector")
    def _(e):
        for j in range(NCH):
            e.wait_ge(sl_x[j % 4].sem, xwaits[j])
            e.tensor_copy(xnv[:, j, :], xsv[:, j % 4, :]).then_inc(s_ve.sem, 1)
    s_ve.bump(NCH)

    emit_norm(P, c, sb, ps, arena, copy_ve0=ve0)


def emit_norm(P, c, sb, ps, arena, wb=None, copy_ve0=None, nch=NCH):
    s_pe, s_ve, s_se = c["s_pe"], c["s_ve"], c["s_se"]
    NCHa = nch
    xnv = apv(arena).rearrange("p (j t) -> p j t", j=NCHa)
    sqv = apv(sb["sq"]).rearrange("p (i t) -> p i t", i=2)
    psS = ps[4][0:1, :]
    psQ = ps[5][0:1, :]
    ve0, se0, pe0 = s_ve.n - NCHa, s_se.n, s_pe.n
    if copy_ve0 is not None:
        ve0 = copy_ve0

    @P.on("scalar")
    def _(e, ve0=ve0, pe0=pe0):
        for j in range(NCHa):
            e.wait_ge(s_ve.sem, ve0 + j + 1)
            if j >= 2:
                e.wait_ge(s_pe.sem, pe0 + (j - 1))
            e.activation(out=sqv[:, j % 2, :], in_=xnv[:, j, :], func=AF.Square).then_inc(s_se.sem, 1)
    s_se.bump(NCHa)

    @P.on("tensor")
    def _(e, se0=se0):
        for j in range(NCHa):
            e.wait_ge(s_se.sem, se0 + j + 1)
            e.matmul(psS, sb["onescol_bf"][:], xnv[:, j, :], start=(j == 0), stop=(j == NCHa - 1))
            mm = e.matmul(psQ, sb["onescol_bf"][:], sqv[:, j % 2, :], start=(j == 0), stop=(j == NCHa - 1))
            mm.then_inc(s_pe.sem, 1)
    s_pe.bump(NCHa)

    rows = sb["rows"]
    m_r = rows[0:1, 0:G]
    e2_r = rows[0:1, G:2 * G]
    rstd_r = rows[0:1, 2 * G:3 * G]
    mr_r = rows[0:1, 3 * G:4 * G]
    pe_stats = s_pe.n

    @P.on("scalar")
    def _(e, pe_stats=pe_stats):
        e.wait_ge(s_pe.sem, pe_stats)
        e.activation(out=m_r, in_=psS, func=AF.Identity, scale=1.0 / (128 * NCHa))
        e.activation(out=e2_r, in_=psQ, func=AF.Identity, scale=1.0 / (128 * NCHa)).then_inc(s_se.sem, 1)
    se_mq = s_se.bump()

    q_ve, q_se = c["q_ve"], c["q_se"]

    @P.on("vector")
    def _(e, se_mq=se_mq):
        e.wait_ge(s_se.sem, se_mq)
        q_ve.seq(e, e.tensor_tensor(mr_r, m_r, m_r, ALU.mult))
        e.tensor_tensor(e2_r, e2_r, mr_r, ALU.subtract).then_inc(s_ve.sem, 1)
    ve_var = s_ve.bump()

    @P.on("scalar")
    def _(e, ve_var=ve_var):
        e.wait_ge(s_ve.sem, ve_var)
        q_se.seq(e, e.activation(out=rstd_r, in_=e2_r, func=AF.Ln, bias=sb["cpool"][0:1, 0:1]))
        e.activation(out=rstd_r, in_=rstd_r, func=AF.Exp, scale=-0.5).then_inc(s_se.sem, 1)
    se_rstd = s_se.bump()

    @P.on("vector")
    def _(e, se_rstd=se_rstd):
        e.wait_ge(s_se.sem, se_rstd)
        e.tensor_tensor(mr_r, m_r, rstd_r, ALU.mult).then_inc(s_ve.sem, 1)
    ve_mr = s_ve.bump()

    @P.on("tensor")
    def _(e, ve_mr=ve_mr):
        e.wait_ge(s_ve.sem, ve_mr)
        e.matmul(ps[2][:], sb["onesrow_f"][:], rstd_r, start=True, stop=True)
        mm = e.matmul(ps[3][:], sb["onesrow_f"][:], mr_r, start=True, stop=True)
        mm.then_inc(s_pe.sem, 1)
    pe_b = s_pe.bump()

    bc = sb["bcast"]

    @P.on("vector")
    def _(e, pe_b=pe_b):
        e.wait_ge(s_pe.sem, pe_b)
        e.tensor_copy(bc[:, 0:G], ps[2][:])
        e.tensor_copy(bc[:, G:2 * G], ps[3][:]).then_inc(s_ve.sem, 1)
    ve_bc = s_ve.bump()

    tmpv = apv(sb["tmp"]).rearrange("p (i t) -> p i t", i=2)

    if wb is None:
        @P.on("vector")
        def _(e, ve_bc=ve_bc):
            e.wait_ge(s_ve.sem, ve_bc)
            q_ve = c["q_ve"]
            for j in range(NCHa):
                q_ve.seq(e, e.tensor_tensor(tmpv[:, j % 2, :], xnv[:, j, :], bc[:, 0:G], ALU.mult))
                q_ve.seq(e, e.tensor_tensor(xnv[:, j, :], tmpv[:, j % 2, :], bc[:, G:2 * G], ALU.subtract))
                e.sem_inc(s_ve.sem, 1)
        s_ve.bump(NCHa)
    else:
        w_tile, b_tile = wb
        ve_marks = []
        for j in range(NCHa):
            @P.on("vector")
            def _(e, j=j, ve_bc=ve_bc):
                if j == 0:
                    e.wait_ge(s_ve.sem, ve_bc)
                q_ve = c["q_ve"]
                if j >= 2:
                    e.wait_ge(s_se.sem, ve_marks[j - 2])
                q_ve.seq(e, e.tensor_tensor(tmpv[:, j % 2, :], xnv[:, j, :], bc[:, 0:G], ALU.mult))
                e.tensor_tensor(tmpv[:, j % 2, :], tmpv[:, j % 2, :], bc[:, G:2 * G],
                                ALU.subtract).then_inc(s_ve.sem, 1)
            ve_j = s_ve.bump()

            @P.on("scalar")
            def _(e, j=j, ve_j=ve_j, w_tile=w_tile, b_tile=b_tile):
                e.wait_ge(s_ve.sem, ve_j)
                e.activation(out=xnv[:, j, :], in_=tmpv[:, j % 2, :], func=AF.Identity,
                             scale=w_tile[:, j:j + 1], bias=b_tile[:, j:j + 1]).then_inc(s_se.sem, 1)
            ve_marks.append(s_se.bump())

    barrier(P, c)



def emit_dense_residual(P, c, sb, ps, *, xn, wt_dram, xin_v, xout_v, toks,
                        nd=D, extra_mm=None, scale=1.0, halfbias=None,
                        out_override=None):
    s_dma, s_pe, s_ve, s_se, s_wd, s_out = (c[k] for k in
                                            ("s_dma", "s_pe", "s_ve", "s_se", "s_wd", "s_out"))
    ndc = nd // 128
    xnv = apv(xn).rearrange("p (j t) -> p j t", j=NCH)
    wtv = wt_dram
    wbv = apv(sb["wbuf"]).rearrange("p (i j n) -> p i j n", i=2, j=NCH)
    xsv = apv(sb["xs"]).rearrange("p (i t) -> p i t", i=4)
    tmpv = apv(sb["tmp"]).rearrange("p (i t) -> p i t", i=2)
    xov = apv(sb["xo"]).rearrange("p (i t) -> p i t", i=4)

    sl_w, sl_xe, sl_o = c["sl_w"], c["sl_xe"], c["sl_o"]
    pe_marks = [None] * ndc
    evac_done = [None] * ndc
    wwaits = [sl_w[d % 2].bump(16) for d in range(ndc)]

    @P.on("gpsimd")
    def _(e):
        for d in range(ndc):
            if d >= 2:
                e.wait_ge(s_pe.sem, pe_marks[d - 2])
            e.dma_start(out=wbv[:, d % 2, :, :],
                        in_=wtv[d, :, :, :]).then_inc(sl_w[d % 2].sem, 16)

    for d in range(ndc):
        @P.on("tensor")
        def _(e, d=d, ww=wwaits[d]):
            e.wait_ge(sl_w[d % 2].sem, ww)
            if d >= 2:
                e.wait_ge(s_ve.sem, evac_done[d - 2])
            pt = ps[d % 2]
            mm = None
            last = extra_mm is None
            for j in range(NCH):
                mm = e.matmul(pt[:], wbv[:, d % 2, j, :], xnv[:, j, :],
                              start=(j == 0), stop=(last and j == NCH - 1))
            if extra_mm is not None:
                mm = extra_mm(e, d, pt)
            mm.then_inc(s_pe.sem, 1)
        pe_marks[d] = s_pe.bump()

        if out_override is not None:
            evac_done[d] = out_override(P, c, d, ps[d % 2], pe_marks[d])
            continue

        xe_w = sl_xe[d % 4].bump(16)

        @P.on("sync")
        def _(e, d=d):
            if d >= 4:
                e.wait_ge(s_ve.sem, evac_done[d - 4])
            e.dma_start(out=xsv[:, d % 4, :], in_=xin_v[d, :, toks]).then_inc(sl_xe[d % 4].sem, 16)

        @P.on("scalar")
        def _(e, d=d, pe_cnt=pe_marks[d]):
            e.wait_ge(s_pe.sem, pe_cnt)
            if d >= 2:
                e.wait_ge(s_ve.sem, evac_done[d - 2])
            kw = {}
            if halfbias is not None:
                kw["bias"] = halfbias[:, d:d + 1]
            e.activation(out=tmpv[:, d % 2, :], in_=ps[d % 2][:], func=AF.Identity,
                         scale=scale, **kw).then_inc(s_se.sem, 1)
        se_cnt = s_se.bump()

        ow_prev = sl_o[d % 4].n

        @P.on("vector")
        def _(e, d=d, se_cnt=se_cnt, xe_w=xe_w, ow_prev=ow_prev):
            e.wait_ge(s_se.sem, se_cnt)
            e.wait_ge(sl_xe[d % 4].sem, xe_w)
            if d >= 4:
                e.wait_ge(sl_o[d % 4].sem, ow_prev)
            e.tensor_tensor(xov[:, d % 4, :], xsv[:, d % 4, :],
                            tmpv[:, d % 2, :], ALU.add).then_inc(s_ve.sem, 1)
        evac_done[d] = s_ve.bump()

        o_w = sl_o[d % 4].bump(16)

        @P.on("sync")
        def _(e, d=d, ve_cnt=evac_done[d]):
            e.wait_ge(s_ve.sem, ve_cnt)
            e.dma_start(out=xout_v[d, :, toks], in_=xov[:, d % 4, :]).then_inc(sl_o[d % 4].sem, 16)



FFN_SB = {
    0: {"halfbias": ([128, NCH], F32), "a_bf": ([128, 2 * NCH * 128], BF16),
        "bft_bf": ([128, 2 * D], BF16), "wexp": ([128, 2], F32),
        "wsm": ([16, 258], F32)},
    1: {"halfbias": ([128, NCH], F32), "a_bf": ([128, NCH * 64], BF16),
        "bt_bf": ([64, D], BF16), "gw_bf": ([128, NCH * 4], BF16),
        "gbcol": ([4, 1], F32), "erows": ([1, 4 * 64], BF16),
        "identf": ([128, 128], F32), "identb": ([128, 128], BF16),
        "fsc": ([128, 16], F32), "fcols": ([128, 12], BF16),
        "flagrows": ([1, 3 * G], BF16), "maskb": ([64, G], BF16),
        "hm": ([64, G], BF16), "gatef": ([4, G], F32)},
}


def ffn_dram_decls(nc, layer, pref=""):
    d = {}
    d["wt"] = nc.dram_tensor(pref + "wt", [NCH, 128, NCH, 128], F32, kind="ExternalInput").ap()
    d["biasm"] = nc.dram_tensor(pref + "biasm", [128, NCH], F32, kind="ExternalInput").ap()
    if layer == 0:
        d["a2t"] = nc.dram_tensor(pref + "a2t", [2, 128, NCH, 128], F32, kind="ExternalInput").ap()
        d["bft"] = nc.dram_tensor(pref + "bft", [256, D], F32, kind="ExternalInput").ap()
        d["coeff"] = nc.dram_tensor(pref + "coeff", [16, 1], F32, kind="ExternalInput").ap()
        d["scal"] = nc.dram_tensor(pref + "scal", [16, 1], F32, kind="ExternalInput").ap()
        d["emat"] = nc.dram_tensor(pref + "emat", [16, 256], F32, kind="ExternalInput").ap()
    else:
        d["a2t"] = nc.dram_tensor(pref + "a2t", [128, NCH, 64], F32, kind="ExternalInput").ap()
        d["btr"] = nc.dram_tensor(pref + "btr", [64, D], F32, kind="ExternalInput").ap()
        d["gwt"] = nc.dram_tensor(pref + "gwt", [128, NCH, 4], F32, kind="ExternalInput").ap()
        d["gb"] = nc.dram_tensor(pref + "gb", [4, 1], F32, kind="ExternalInput").ap()
        d["erows"] = nc.dram_tensor(pref + "erows", [1, 4 * 64], BF16, kind="ExternalInput").ap()
        d["identin"] = nc.dram_tensor(pref + "identin", [128, 128], F32, kind="ExternalInput").ap()
        d["identbf"] = nc.dram_tensor(pref + "identbf", [128, 128], BF16, kind="ExternalInput").ap()
    return d


def emit_ffn_phase(P, c, ps, psb, sb, xn, layer, io):
    xin, xout = io["xin"], io["xout"]
    wt, biasm = io["wt"], io["biasm"]
    a2t = io["a2t"]
    if layer == 0:
        bft, coeff, scal, emat = io["bft"], io["coeff"], io["scal"], io["emat"]
    else:
        btr, gwt, gb, erows, identin, identbf = (io["btr"], io["gwt"], io["gb"],
                                                 io["erows"], io["identin"], io["identbf"])
    s_dma, s_pe, s_ve, s_se, s_wd, s_out = (c[k] for k in
                                            ("s_dma", "s_pe", "s_ve", "s_se", "s_wd", "s_out"))

    if True:
        @P.on("sync")
        def _(e):
            e.dma_start(out=sb["halfbias"][:], in_=biasm).then_inc(s_dma.sem, 16)
        s_dma.bump(16)

        if layer == 0:
            abfv = apv(sb["a_bf"]).rearrange("p (cb j n) -> p cb j n", cb=2, j=NCH)
            bftv = bft.rearrange("(cb p) n -> cb p n", p=128)

            @P.on("gpsimd")
            def _(e):
                for cb in range(2):
                    e.dma_start(out=abfv[:, cb, :, :],
                                in_=a2t[cb, :, :, :]).then_inc(s_wd.sem, 16)
                for cb in range(2):
                    e.dma_start(out=sb["bft_bf"][:, cb * D:(cb + 1) * D],
                                in_=bftv[cb, :, :]).then_inc(s_wd.sem, 16)
            s_wd.bump(16 * 4)

            @P.on("sync")
            def _(e):
                e.dma_start(out=sb["wsm"][:, 0:1], in_=coeff).then_inc(s_dma.sem, 16)
                e.dma_start(out=sb["wsm"][:, 1:2], in_=scal).then_inc(s_dma.sem, 16)
                e.dma_start(out=sb["wsm"][:, 2:258], in_=emat).then_inc(s_dma.sem, 16)
            s_dma.bump(16 * 3)
        else:
            abfv = apv(sb["a_bf"]).rearrange("p (j n) -> p j n", j=NCH)

            @P.on("gpsimd")
            def _(e):
                e.dma_start(out=abfv[:, :, :], in_=a2t).then_inc(s_wd.sem, 16)
                e.dma_start(out=sb["bt_bf"][:], in_=btr).then_inc(s_wd.sem, 16)
                e.dma_start(out=apv(sb["gw_bf"]).rearrange("p (j n) -> p j n", j=NCH),
                            in_=gwt).then_inc(s_wd.sem, 16)
            s_wd.bump(16 * 3)

            @P.on("sync")
            def _(e):
                e.dma_start(out=sb["gbcol"][:], in_=gb).then_inc(s_dma.sem, 16)
                e.dma_start(out=sb["erows"][:], in_=erows).then_inc(s_dma.sem, 16)
                e.dma_start(out=sb["identf"][:], in_=identin).then_inc(s_dma.sem, 16)
                e.dma_start(out=sb["identb"][:], in_=identbf).then_inc(s_dma.sem, 16)
            s_dma.bump(16 * 4)

        barrier(P, c)

        @P.on("scalar")
        def _(e):
            e.mul(sb["halfbias"][:], sb["halfbias"][:], 0.5).then_inc(s_se.sem, 1)
        s_se.bump()

        if layer == 0:
            @P.on("vector")
            def _(e):
                e.tensor_tensor(sb["wsm"][:, 0:1], sb["wsm"][:, 0:1], sb["wsm"][:, 1:2],
                                ALU.mult).then_inc(s_ve.sem, 1)
            w_ve = s_ve.bump()

            @P.on("tensor")
            def _(e, w_ve=w_ve):
                e.wait_ge(s_ve.sem, w_ve)
                e.matmul(ps[3][:, 0:1], sb["wsm"][:, 2:130], sb["wsm"][:, 0:1], start=True, stop=True)
                mm = e.matmul(ps[3][:, 1:2], sb["wsm"][:, 130:258], sb["wsm"][:, 0:1],
                              start=True, stop=True)
                mm.then_inc(s_pe.sem, 1)
            wexp_pe = s_pe.bump()

            @P.on("scalar")
            def _(e, wexp_pe=wexp_pe):
                e.wait_ge(s_pe.sem, wexp_pe)
                e.copy(sb["wexp"][:], ps[3][:, 0:2]).then_inc(s_se.sem, 1)
            s_se.bump()

        barrier(P, c)

        xin_v = xin.rearrange("(j p) t -> j p t", p=128)
        xout_v = xout.rearrange("(j p) t -> j p t", p=128)

        for g in range(2):
            toks = slice(g * G, (g + 1) * G)
            emit_ln(P, c, sb, ps, xin_v, toks, xn[g])
            xnv = xn[g].ap().rearrange("p (j t) -> p j t", j=NCH)

            if layer == 0:
                @P.on("tensor")
                def _(e, xnv=xnv):
                    for cb in range(2):
                        pt = ps[2 + cb]
                        mm = None
                        for j in range(NCH):
                            mm = e.matmul(pt[:], abfv[:, cb, j, :], xnv[:, j, :],
                                          start=(j == 0), stop=(j == NCH - 1))
                        mm.then_inc(s_pe.sem, 1)
                xa_pe = s_pe.bump(2)

                @P.on("scalar")
                def _(e, xa_pe=xa_pe):
                    e.wait_ge(s_pe.sem, xa_pe)
                    e.activation(out=sb["xaw"][:, 0:G], in_=ps[2][:], func=AF.Identity,
                                 scale=sb["wexp"][:, 0:1])
                    a = e.activation(out=sb["xaw"][:, G:2 * G], in_=ps[3][:], func=AF.Identity,
                                     scale=sb["wexp"][:, 1:2])
                    a.then_inc(s_se.sem, 1)
                s_se.bump()

                bftbv = apv(sb["bft_bf"]).rearrange("p (cb n) -> p cb n", cb=2)

                def extra_mm(e, d, pt, bftbv=bftbv):
                    mm = None
                    for cb in range(2):
                        mm = e.matmul(pt[:], bftbv[:, cb, 128 * d:128 * (d + 1)],
                                      sb["xaw"][:, cb * G:(cb + 1) * G],
                                      start=False, stop=(cb == 1))
                    return mm
            else:
                gwbv = apv(sb["gw_bf"]).rearrange("p (j n) -> p j n", j=NCH)

                @P.on("tensor")
                def _(e, xnv=xnv, gwbv=gwbv):
                    mm = None
                    for j in range(NCH):
                        mm = e.matmul(ps[2][0:4, :], gwbv[:, j, :], xnv[:, j, :],
                                      start=(j == 0), stop=(j == NCH - 1))
                    for j in range(NCH):
                        mm = e.matmul(ps[3][0:64, :], abfv[:, j, :], xnv[:, j, :],
                                      start=(j == 0), stop=(j == NCH - 1))
                    mm.then_inc(s_pe.sem, 1)
                gh_pe = s_pe.bump()

                @P.on("scalar")
                def _(e, gh_pe=gh_pe):
                    e.wait_ge(s_pe.sem, gh_pe)
                    e.activation(out=sb["gatef"][:], in_=ps[2][0:4, :], func=AF.Identity,
                                 bias=sb["gbcol"][:]).then_inc(s_se.sem, 1)
                gate_se = s_se.bump()

                @P.on("tensor")
                def _(e, gate_se=gate_se):
                    e.wait_ge(s_se.sem, gate_se)
                    mm = None
                    for i in range(NT):
                        mm = e.transpose(ps[4][:, 4 * i:4 * (i + 1)],
                                         sb["gatef"][0:4, 128 * i:128 * (i + 1)],
                                         sb["identf"][0:4, 0:4])
                    mm.then_inc(s_pe.sem, 1)
                gt_pe = s_pe.bump()

                @P.on("vector")
                def _(e, gt_pe=gt_pe):
                    e.wait_ge(s_pe.sem, gt_pe)
                    fsc = sb["fsc"]
                    fcols = sb["fcols"]
                    q_ve = c["q_ve"]
                    for i in range(NT):
                        s4 = ps[4][:, 4 * i:4 * (i + 1)]
                        e.reduce_max(fsc[:, 0:1], s4[:, 0:2], axis=AX.X)
                        e.reduce_max(fsc[:, 1:2], s4[:, 0:3], axis=AX.X)
                        e.reduce_max(fsc[:, 2:3], s4[:, 1:4], axis=AX.X)
                        q_ve.seq(e, e.reduce_max(fsc[:, 3:4], s4[:, 2:4], axis=AX.X))
                        e.tensor_tensor(fcols[:, 3 * i + 0:3 * i + 1], fsc[:, 2:3], s4[:, 0:1], ALU.is_gt)
                        e.tensor_tensor(fcols[:, 3 * i + 1:3 * i + 2], fsc[:, 3:4], fsc[:, 0:1], ALU.is_gt)
                        q_ve.seq(e, e.tensor_tensor(fcols[:, 3 * i + 2:3 * i + 3], s4[:, 3:4], fsc[:, 1:2], ALU.is_gt))
                        e.sem_inc(s_ve.sem, 1)
                flag_ve = s_ve.bump(NT)

                @P.on("tensor")
                def _(e, flag_ve=flag_ve):
                    e.wait_ge(s_ve.sem, flag_ve)
                    mm = None
                    for gg in range(2):
                        for i in range(NT):
                            col = (gg * NT + i) * 128
                            mm = e.transpose(psb[0:1, col:col + 128],
                                             sb["fcols"][:, 3 * i + gg:3 * i + gg + 1],
                                             sb["identb"][:])
                    mm.then_inc(s_pe.sem, 1)
                fr_pe = s_pe.bump()

                @P.on("vector")
                def _(e, fr_pe=fr_pe):
                    e.wait_ge(s_pe.sem, fr_pe)
                    e.tensor_copy(sb["flagrows"][0:1, 0:2 * G], psb[0:1, 0:2 * G]).then_inc(s_ve.sem, 1)
                fr2_ve = s_ve.bump()

                @P.on("tensor")
                def _(e, fr2_ve=fr2_ve):
                    e.wait_ge(s_ve.sem, fr2_ve)
                    mm = None
                    for i in range(NT):
                        mm = e.transpose(psb[0:1, i * 128:(i + 1) * 128],
                                         sb["fcols"][:, 3 * i + 2:3 * i + 3],
                                         sb["identb"][:])
                    mm.then_inc(s_pe.sem, 1)
                fr3_pe = s_pe.bump()

                @P.on("vector")
                def _(e, fr3_pe=fr3_pe):
                    e.wait_ge(s_pe.sem, fr3_pe)
                    e.tensor_copy(sb["flagrows"][0:1, 2 * G:3 * G], psb[0:1, 0:G]).then_inc(s_ve.sem, 1)
                frows_ve = s_ve.bump()

                @P.on("tensor")
                def _(e, frows_ve=frows_ve):
                    e.wait_ge(s_ve.sem, frows_ve)
                    e.matmul(ps[2][0:64, :], sb["erows"][0:1, 0:64], sb["onesrow_bf"][:],
                             start=True, stop=False)
                    mm = None
                    for gg in range(3):
                        mm = e.matmul(ps[2][0:64, :], sb["erows"][0:1, 64 * (gg + 1):64 * (gg + 2)],
                                      sb["flagrows"][0:1, gg * G:(gg + 1) * G],
                                      start=False, stop=(gg == 2))
                    mm.then_inc(s_pe.sem, 1)
                mb_pe = s_pe.bump()

                @P.on("vector")
                def _(e, mb_pe=mb_pe):
                    e.wait_ge(s_pe.sem, mb_pe)
                    c["q_ve"].seq(e, e.tensor_copy(sb["maskb"][:], ps[2][0:64, :]))
                    e.tensor_tensor(sb["hm"][:], ps[3][0:64, :], sb["maskb"][:],
                                    ALU.mult).then_inc(s_ve.sem, 1)
                s_ve.bump()

                def extra_mm(e, d, pt):
                    return e.matmul(pt[:], sb["bt_bf"][0:64, 128 * d:128 * (d + 1)],
                                    sb["hm"][0:64, :], start=False, stop=True)

            barrier(P, c)

            emit_dense_residual(P, c, sb, ps, xn=xn[g], wt_dram=wt, xin_v=xin_v,
                                xout_v=xout_v, toks=toks, extra_mm=extra_mm,
                                scale=0.5, halfbias=sb["halfbias"])
            barrier(P, c)


def emit_final_wait(P, c):
    @P.on("sync")
    def _(e):
        for sl in c["sl_o"]:
            if sl.n:
                e.wait_ge(sl.sem, sl.n)
        if c["s_out"].n:
            e.wait_ge(c["s_out"].sem, c["s_out"].n)
        if c["s_ge"].n:
            e.wait_ge(c["s_ge"].sem, c["s_ge"].n)


def alloc_sb(nc, ctx, spec):
    E = ctx.enter_context
    out = {}
    for k, (shape, dt) in spec.items():
        out[k] = E(nc.sbuf_tensor("t_" + k, shape, dt))
    return out


def build_ffn_launch(layer):
    nc = bass.Bass("TRN2", target_bir_lowering=False)
    io = ffn_dram_decls(nc, layer)
    io["xin"] = nc.dram_tensor("xin", [D, 2 * G], F32, kind="ExternalInput").ap()
    io["xout"] = nc.dram_tensor("xout", [D, 2 * G], F32, kind="ExternalOutput").ap()
    with contextlib.ExitStack() as ctx:
        E = ctx.enter_context
        c = make_ctx(nc, ctx)
        ps = [E(nc.psum_tensor(f"psf{i}", [128, 512], F32)) for i in range(6)]
        psb = E(nc.psum_tensor("psb", [128, 1024], BF16))
        sb = sb_common(nc, ctx)
        sb.update(alloc_sb(nc, ctx, FFN_SB[layer]))
        if layer == 0:
            sb["xaw"] = E(nc.sbuf_tensor("t_xaw", [128, 2 * G], BF16))
        xn = [E(nc.sbuf_tensor("xn0", [128, NCH * G], BF16))] * 2
        block = E(nc.Block())
        P = Prog()
        emit_consts(P, c, sb)
        emit_ffn_phase(P, c, ps, psb, sb, xn, layer, io)
        emit_final_wait(P, c)
        P.emit(block)
    return nc



def _shard_x(x_full, core):
    seq, parity = core // 2, core % 2
    tiles = core_token_tiles(parity)
    idx = np.concatenate([np.arange(128 * t, 128 * (t + 1)) for t in tiles])
    return np.ascontiguousarray(x_full[seq, idx, :].T), idx


def _unshard_x(x_full, core, xc):
    seq, parity = core // 2, core % 2
    tiles = core_token_tiles(parity)
    idx = np.concatenate([np.arange(128 * t, 128 * (t + 1)) for t in tiles])
    x_full[seq, idx, :] = xc.T


def _chunkmajor(v):
    return np.ascontiguousarray(v.reshape(NCH, 128).T)


def run_ffn_layer(x_full, layer, weights):
    nc = _get_prog(("ffn", layer), lambda: build_ffn_launch(layer))
    in_maps = []
    shards = []
    for core in range(8):
        xc, idx = _shard_x(x_full, core)
        shards.append(idx)
        im = {"xin": xc}
        im.update(weights)
        in_maps.append(im)
    res = run_bass_kernel_spmd(nc, in_maps, core_ids=list(range(8)))
    out = x_full.copy()
    for core in range(8):
        _unshard_x(out, core, np.asarray(res.results[core]["xout"]))
    return out


_PROG_CACHE = {}


def _get_prog(key, builder):
    if key not in _PROG_CACHE:
        _PROG_CACHE[key] = builder()
    return _PROG_CACHE[key]


def _wblocks(wT):
    ndim = wT.shape[1]
    ndc = ndim // 128
    return np.ascontiguousarray(
        wT.reshape(NCH, 128, ndc, 128).transpose(2, 1, 0, 3))


def _pjn(wT):
    n = wT.shape[1]
    return np.ascontiguousarray(wT.reshape(NCH, 128, n).transpose(1, 0, 2))


def ffn0_weights(inputs):
    W = np.asarray(inputs["L0_ffn_W"], np.float32)
    A = np.asarray(inputs["L0_ffn_A"], np.float32)
    Bf = np.asarray(inputs["L0_ffn_Bf"], np.float32)
    emat = np.zeros((16, 256), np.float32)
    for e_ in range(16):
        emat[e_, e_ * 16:(e_ + 1) * 16] = 1.0
    a2t = np.ascontiguousarray(A.reshape(256, D).T)
    return {
        "wt": _wblocks(np.ascontiguousarray(W.T)),
        "biasm": _chunkmajor(np.asarray(inputs["L0_ffn_bias"], np.float32)),
        "a2t": _wblocks(a2t),
        "bft": np.ascontiguousarray(Bf.transpose(0, 2, 1).reshape(256, D)),
        "coeff": np.ascontiguousarray(np.asarray(inputs["L0_ffn_coeffs"], np.float32).reshape(16, 1)),
        "scal": np.ascontiguousarray(np.asarray(inputs["L0_ffn_scales"], np.float32).reshape(16, 1)),
        "emat": emat,
    }


def ffn1_weights(inputs):
    erows = np.zeros((1, 4 * 64), BF)
    erows[0, 0:8] = 1
    erows[0, 64 + 8:64 + 16] = 1
    erows[0, 128 + 16:128 + 32] = 1
    erows[0, 192 + 32:192 + 64] = 1
    return {
        "wt": _wblocks(np.ascontiguousarray(np.asarray(inputs["L1_ffn_W"], np.float32).T)),
        "biasm": _chunkmajor(np.asarray(inputs["L1_ffn_b"], np.float32)),
        "a2t": _pjn(np.ascontiguousarray(np.asarray(inputs["L1_ffn_A"], np.float32).T)),
        "btr": np.ascontiguousarray(np.asarray(inputs["L1_ffn_B"], np.float32).T),
        "gwt": _pjn(np.ascontiguousarray(np.asarray(inputs["L1_ffn_gate_w"], np.float32).T)),
        "gb": np.ascontiguousarray(np.asarray(inputs["L1_ffn_gate_b"], np.float32).reshape(4, 1)),
        "erows": erows,
        "identin": np.eye(128, dtype=np.float32),
        "identbf": np.eye(128, dtype=BF),
    }


def carve(arena, spec, base_bytes=0):
    views = {}
    cur = 0
    for k, (shape, dt) in spec.items():
        assert dt == arena.dtype if hasattr(arena, "dtype") else True
        views[k] = apv(arena)[0:shape[0], cur:cur + shape[1]]
        cur += shape[1]
    return views, cur


def build_megakernel():
    nc = bass.Bass("TRN2", target_bir_lowering=False)
    io_f0 = ffn_dram_decls(nc, 0, pref="f0")
    io_f1 = ffn_dram_decls(nc, 1, pref="f1")
    io_qkv = qkv_dram_decls(nc)
    io_att = attc_dram_decls(nc)
    io_a1 = a1_dram_decls(nc)
    xin = nc.dram_tensor("xin", [D, 2 * G], F32, kind="ExternalInput").ap()
    xout = nc.dram_tensor("xout", [D, 2 * G], F32, kind="ExternalOutput").ap()
    xa = nc.dram_tensor("xa", [D, 2 * G], F32, kind="Internal").ap()
    xb = nc.dram_tensor("xb", [D, 2 * G], F32, kind="Internal").ap()
    qn_i = nc.dram_tensor("qn", [NH, 128, 2 * G], BF16, kind="Internal").ap()
    kn_t = nc.dram_tensor("kn", [NH * 128, 2 * G], BF16)
    vt_t = nc.dram_tensor("vt", [2 * G, D], BF16)
    kng_t = nc.dram_tensor("kng", [2 * NH * 128, 2 * G], BF16)
    vtg_t = nc.dram_tensor("vtg", [2 * 2 * G, D], BF16)
    kn_i = kn_t.ap().rearrange("(h p) t -> h p t", p=128)
    vt_i = vt_t.ap()
    kng = kng_t.ap()
    vtg = vtg_t.ap()

    with contextlib.ExitStack() as ctx:
        E = ctx.enter_context
        c = make_ctx(nc, ctx)
        ps = [E(nc.psum_tensor(f"psf{i}", [128, 512], F32)) for i in range(6)]
        psb = [E(nc.psum_tensor(f"psbt{i}", [128, 1024], BF16)) for i in range(2)]
        com = sb_common(nc, ctx)
        A0 = E(nc.sbuf_tensor("A0", [128, NCH * G], BF16))
        A1 = E(nc.sbuf_tensor("A1", [128, NCH * G], BF16))
        A2 = E(nc.sbuf_tensor("A2", [128, NCH * G], BF16))
        A3 = E(nc.sbuf_tensor("A3", [128, 9216], BF16))
        SCB = E(nc.sbuf_tensor("SCB", [128, 16384], BF16))
        SCF = E(nc.sbuf_tensor("SCF", [128, 2304], F32))

        def phase_sb(spec):
            sb = dict(com)
            curb = curf = 0
            for k, (shape, dt) in spec.items():
                if dt == BF16:
                    sb[k] = apv(SCB)[0:shape[0], curb:curb + shape[1]]
                    curb += shape[1]
                else:
                    sb[k] = apv(SCF)[0:shape[0], curf:curf + shape[1]]
                    curf += shape[1]
                assert curb <= 16384 and curf <= 2304, (k, curb, curf)
            return sb

        block = E(nc.Block())
        P = Prog()
        com2 = dict(com)
        emit_consts(P, c, com2)

        sb = phase_sb(FFN_SB[0])
        sb["xaw"] = apv(A3)[:, 0:1024]
        emit_ffn_phase(P, c, ps, psb[0], sb, [A0, A0], 0,
                       dict(io_f0, xin=xin, xout=xa))

        rg = [[0, 1], [2, 3], [4, 5], [6, 7]]

        def hook_kn(P, c):
            s_ge = c["s_ge"]

            @P.on("gpsimd")
            def _(e, w=c["s_out"].n):
                e.wait_ge(c["s_out"].sem, w)
                e.collective_compute("AllGather", ALU.bypass, replica_groups=rg,
                                     ins=[kn_t.ap().opt()],
                                     outs=[kng_t.ap().opt()]).then_inc(s_ge.sem)
            c["m_ge_k"] = s_ge.bump()

        def hook_vt(P, c):
            s_ge = c["s_ge"]

            @P.on("gpsimd")
            def _(e, w0=c["sl_o"][0].n, w1=c["sl_o"][1].n):
                e.wait_ge(c["sl_o"][0].sem, w0)
                e.wait_ge(c["sl_o"][1].sem, w1)
                e.collective_compute("AllGather", ALU.bypass, replica_groups=rg,
                                     ins=[vt_t.ap().opt()],
                                     outs=[vtg_t.ap().opt()]).then_inc(s_ge.sem)
            c["m_ge_v"] = s_ge.bump()

        sb = phase_sb(QKV_SB)
        sb["_psb0"], sb["_psb1"] = psb[0], psb[1]
        emit_qkv_phase(P, c, ps, sb, A0, A1, A2,
                       dict(io_qkv, xin=xa, qn=qn_i, kn=kn_i, vt=vt_i,
                            hook_kn=hook_kn, hook_vt=hook_vt))

        sb = phase_sb(ATTC_SB)
        emit_attc_phase(P, c, ps, sb, A1, A2,
                        apv(A3)[:, 0:4096], apv(A3)[:, 4096:8192],
                        apv(A3)[:, 8192:9216],
                        dict(io_att, xin=xa, xout=xb, qn=qn_i,
                             kng=kng.rearrange("(r h p) t -> r h p t", r=2, p=128),
                             vtg=vtg.rearrange("(r t) dd -> r t dd", r=2)))

        sb = phase_sb(FFN_SB[0])
        sb["xaw"] = apv(A3)[:, 0:1024]
        emit_ffn_phase(P, c, ps, psb[0], sb, [A0, A0], 0,
                       dict(io_f0, xin=xb, xout=xa))

        sb = phase_sb(FFN_SB[1])
        emit_ffn_phase(P, c, ps, psb[0], sb, [A0, A0], 1,
                       dict(io_f1, xin=xa, xout=xb))

        sb = phase_sb(A1_SB)
        emit_a1_phase(P, c, ps, psb, sb, A0,
                      apv(A1)[:, 0:4096], apv(A1)[:, 4096:8192],
                      apv(A1)[:, 8192:12288], apv(A1)[:, 12288:16384],
                      apv(A2)[0:8, 0:8192], apv(A2)[0:8, 8192:16384],
                      apv(A3)[:, 0:1024], apv(A3)[:, 1024:2048],
                      apv(SCF)[0:128, 2288:2304],
                      dict(io_a1, xin=xb, xout=xa))

        sb = phase_sb(FFN_SB[1])
        emit_ffn_phase(P, c, ps, psb[0], sb, [A0, A0], 1,
                       dict(io_f1, xin=xa, xout=xout))

        emit_final_wait(P, c)
        P.emit(block)
    return nc


FUSED = True


def kernel(**inputs):
    x = np.asarray(inputs["L0_attn_hidden_states"], dtype=np.float32)
    w_ffn0 = ffn0_weights(inputs)
    w_ffn1 = ffn1_weights(inputs)
    if not FUSED:
        x = run_ffn_layer(x, 0, w_ffn0)
        x = run_attn0(x, inputs)
        x = run_ffn_layer(x, 0, w_ffn0)
        x = run_ffn_layer(x, 1, w_ffn1)
        x = run_attn1(x, inputs)
        x = run_ffn_layer(x, 1, w_ffn1)
        return x

    w_a0 = attn0_weights(inputs)
    w_a1 = attn1_weights(inputs)
    nc = _get_prog(("mega",), build_megakernel)
    in_maps = []
    for core in range(8):
        seq, par = core // 2, core % 2
        xc, _ = _shard_x(x, core)
        im = {"xin": xc}
        im.update({"f0" + k: v for k, v in w_ffn0.items()})
        im.update({"f1" + k: v for k, v in w_ffn1.items()})
        im.update({"wqkvt": w_a0["wqkvt"], "i_qlnw": w_a0["qlnw"], "i_qlnb": w_a0["qlnb"],
                   "i_klnw": w_a0["klnw"], "i_klnb": w_a0["klnb"], "identbf2": w_a0["identbf"]})
        im.update({"woutt": w_a0["woutt"], "colt": w_a0["colt"], "colrow": w_a0["colrow"],
                   "maskab": np.stack([w_a0["maskab"][par], w_a0["maskab"][1 - par]]),
                   "maskq": np.stack([w_a0["maskq"][par], w_a0["maskq"][1 - par]]),
                   "identf": w_a0["identf"]})
        im.update(w_a1)
        in_maps.append(im)
    res = run_bass_kernel_spmd(nc, in_maps, core_ids=list(range(8)))
    out = x.copy()
    for core in range(8):
        _unshard_x(out, core, np.asarray(res.results[core]["xout"]))
    return out



QKV_SB = {
    "clip": ([128, 2], F32), "qlnw": ([128, NCH], F32), "qlnb": ([128, NCH], F32),
    "klnw": ([128, NCH], F32), "klnb": ([128, NCH], F32),
    "vstage": ([128, 2 * D], BF16), "identb": ([128, 128], BF16),
}


def qkv_dram_decls(nc):
    d = {}
    d["wqkvt"] = nc.dram_tensor("wqkvt", [96, 128, NCH, 128], F32, kind="ExternalInput").ap()
    for nm in ("qlnw", "qlnb", "klnw", "klnb"):
        d[nm] = nc.dram_tensor("i_" + nm, [128, NCH], F32, kind="ExternalInput").ap()
    d["identbf2"] = nc.dram_tensor("identbf2", [128, 128], BF16, kind="ExternalInput").ap()
    return d


def emit_qkv_phase(P, c, ps, sb, xn, qf, kf, io):
    xin = io["xin"]
    wqkvt = io["wqkvt"]
    qlnw, qlnb, klnw, klnb = io["qlnw"], io["qlnb"], io["klnw"], io["klnb"]
    identbf = io["identbf2"]
    qn_o, kn_o, vt_o = io["qn"], io["kn"], io["vt"]
    s_dma, s_pe, s_ve, s_se, s_wd, s_out = (c[k] for k in
                                            ("s_dma", "s_pe", "s_ve", "s_se", "s_wd", "s_out"))
    if True:
        @P.on("vector")
        def _(e):
            e.memset(sb["clip"][:, 0:1], -CLIP)
            e.memset(sb["clip"][:, 1:2], CLIP).then_inc(s_ve.sem, 1)
        s_ve.bump()

        @P.on("sync")
        def _(e):
            for nm, t in (("qlnw", qlnw), ("qlnb", qlnb), ("klnw", klnw), ("klnb", klnb)):
                e.dma_start(out=sb[nm][:], in_=t).then_inc(s_dma.sem, 16)
            e.dma_start(out=sb["identb"][:], in_=identbf).then_inc(s_dma.sem, 16)
        s_dma.bump(16 * 5)

        barrier(P, c)

        @P.on("scalar")
        def _(e):
            sc = 1.0 / np.sqrt(128.0)
            c["q_se"].seq(e, e.mul(sb["qlnw"][:], sb["qlnw"][:], sc))
            c["q_se"].seq(e, e.mul(sb["qlnb"][:], sb["qlnb"][:], sc))
        barrier(P, c)

        xin_v = xin.rearrange("(j p) t -> j p t", p=128)
        wbv = apv(sb["wbuf"]).rearrange("p (i j n) -> p i j n", i=2, j=NCH)
        xnv = apv(xn).rearrange("p (j t) -> p j t", j=NCH)
        sl_w = c["sl_w"]

        for g in range(2):
            toks = slice(g * G, (g + 1) * G)
            emit_ln(P, c, sb, ps, xin_v, toks, xn)
            if P2_STAGE < 1:
                continue

            arenas = (qf, kf)
            pe_marks = [None] * 64
            evac_done = [None] * 64
            wwaits = [sl_w[d % 2].bump(16) for d in range(64)]

            @P.on("gpsimd")
            def _(e, wwaits=wwaits, pe_marks=pe_marks, g=g):
                for d in range(64):
                    if d >= 2:
                        e.wait_ge(s_pe.sem, pe_marks[d - 2])
                    e.dma_start(out=wbv[:, d % 2, :, :],
                                in_=wqkvt[d, :, :, :]).then_inc(sl_w[d % 2].sem, 16)

            for d in range(64):
                @P.on("tensor")
                def _(e, d=d, ww=wwaits[d], ed=(evac_done[d - 2] if d >= 2 else 0)):
                    e.wait_ge(sl_w[d % 2].sem, ww)
                    if d >= 2:
                        e.wait_ge(s_ve.sem, ed)
                    pt = ps[d % 2]
                    mm = None
                    for j in range(NCH):
                        mm = e.matmul(pt[:], wbv[:, d % 2, j, :], xnv[:, j, :],
                                      start=(j == 0), stop=(j == NCH - 1))
                    mm.then_inc(s_pe.sem, 1)
                pe_marks[d] = s_pe.bump()

                @P.on("vector")
                def _(e, d=d, pe_cnt=pe_marks[d]):
                    e.wait_ge(s_pe.sem, pe_cnt)
                    ar = arenas[d // NCH].ap().rearrange("p (j t) -> p j t", j=NCH)
                    e.tensor_scalar(ar[:, d % NCH, :], ps[d % 2][:],
                                    sb["clip"][:, 0:1], sb["clip"][:, 1:2],
                                    op0=ALU.max, op1=ALU.min).then_inc(s_ve.sem, 1)
                evac_done[d] = s_ve.bump()

            barrier(P, c)

            if P2_STAGE < 2:
                continue
            for ai, (arena, w_t, b_t, out_t) in enumerate(
                    ((qf, sb["qlnw"], sb["qlnb"], qn_o), (kf, sb["klnw"], sb["klnb"], kn_o))):
                emit_norm(P, c, sb, ps, arena, wb=(w_t, b_t))
                arv = apv(arena).rearrange("p (j t) -> p j t", j=NCH)

                @P.on("sync")
                def _(e, arv=arv, out_t=out_t, g=g):
                    for j in range(NCH):
                        e.dma_start(out=out_t[j, :, slice(g * G, (g + 1) * G)],
                                    in_=arv[:, j, :]).then_inc(s_out.sem, 16)
                s_out.bump(16 * NCH)
                if g == 1 and ai == 1 and "hook_kn" in io:
                    io["hook_kn"](P, c)

            if P2_STAGE < 3:
                continue
            vf = qf
            vfv = apv(vf).rearrange("p (j t) -> p j t", j=NCH)
            pe_v = [None] * NCH
            ve_v = [None] * NCH
            vwaits = [sl_w[d % 2].bump(16) for d in range(NCH)]

            @P.on("gpsimd")
            def _(e, vwaits=vwaits, pe_v=pe_v, g=g):
                for d in range(NCH):
                    if d >= 2:
                        e.wait_ge(s_pe.sem, pe_v[d - 2])
                    e.dma_start(out=wbv[:, d % 2, :, :],
                                in_=wqkvt[64 + d, :, :, :]).then_inc(sl_w[d % 2].sem, 16)

            for d in range(NCH):
                @P.on("tensor")
                def _(e, d=d, vw=vwaits[d], ev=(ve_v[d - 2] if d >= 2 else 0)):
                    e.wait_ge(sl_w[d % 2].sem, vw)
                    if d >= 2:
                        e.wait_ge(s_ve.sem, ev)
                    pt = ps[d % 2]
                    mm = None
                    for j in range(NCH):
                        mm = e.matmul(pt[:], wbv[:, d % 2, j, :], xnv[:, j, :],
                                      start=(j == 0), stop=(j == NCH - 1))
                    mm.then_inc(s_pe.sem, 1)
                pe_v[d] = s_pe.bump()

                @P.on("vector")
                def _(e, d=d, pe_cnt=pe_v[d]):
                    e.wait_ge(s_pe.sem, pe_cnt)
                    e.tensor_scalar(vfv[:, d, :], ps[d % 2][:],
                                    sb["clip"][:, 0:1], sb["clip"][:, 1:2],
                                    op0=ALU.max, op1=ALU.min).then_inc(s_ve.sem, 1)
                ve_v[d] = s_ve.bump()

            barrier(P, c)

            if P2_STAGE < 4:
                barrier(P, c)
                continue
            vstage = sb["vstage"]
            vsv2 = apv(vstage).rearrange("p (i n) -> p i n", i=2)
            psb_v = (sb["_psb0"], sb["_psb1"])
            vti = 0
            vt_ve = []
            for i in range(NT):
                for h in range(NCH):
                    @P.on("tensor")
                    def _(e, h=h, i=i, vti=vti, wv=(vt_ve[vti - 2] if vti >= 2 else 0)):
                        if vti >= 2:
                            e.wait_ge(s_ve.sem, wv)
                        mm = e.transpose(psb_v[vti % 2][:, 0:128],
                                         vfv[:, h, 128 * i:128 * (i + 1)], sb["identb"][:])
                        mm.then_inc(s_pe.sem, 1)
                    vt_pe = s_pe.bump()

                    @P.on("vector")
                    def _(e, h=h, i=i, vti=vti, vt_pe=vt_pe,
                          ow=(c["sl_o"][i % 2].n if h == 0 else 0)):
                        e.wait_ge(s_pe.sem, vt_pe)
                        if h == 0 and i >= 2:
                            e.wait_ge(c["sl_o"][i % 2].sem, ow)
                        e.tensor_copy(vsv2[:, i % 2, 128 * h:128 * (h + 1)],
                                      psb_v[vti % 2][:, 0:128]).then_inc(s_ve.sem, 1)
                    ve_cnt = s_ve.bump()
                    vt_ve.append(ve_cnt)
                    vti += 1

                c["sl_o"][i % 2].bump(16)

                @P.on("sync")
                def _(e, i=i, ve_cnt=ve_cnt, g=g):
                    e.wait_ge(s_ve.sem, ve_cnt)
                    e.dma_start(out=vt_o[slice((g * 4 + i) * 128, (g * 4 + i + 1) * 128), :],
                                in_=vsv2[:, i % 2, :]).then_inc(c["sl_o"][i % 2].sem, 16)

            if g == 1 and "hook_vt" in io:
                io["hook_vt"](P, c)
            barrier(P, c)


def build_qkv_launch():
    nc = bass.Bass("TRN2", target_bir_lowering=False)
    io = qkv_dram_decls(nc)
    io["xin"] = nc.dram_tensor("xin", [D, 2 * G], F32, kind="ExternalInput").ap()
    io["qn"] = nc.dram_tensor("qn", [NH, 128, 2 * G], BF16, kind="ExternalOutput").ap()
    io["kn"] = nc.dram_tensor("kn", [NH, 128, 2 * G], BF16, kind="ExternalOutput").ap()
    io["vt"] = nc.dram_tensor("vt", [2 * G, D], BF16, kind="ExternalOutput").ap()
    with contextlib.ExitStack() as ctx:
        E = ctx.enter_context
        c = make_ctx(nc, ctx)
        ps = [E(nc.psum_tensor(f"psf{i}", [128, 512], F32)) for i in range(6)]
        sb = sb_common(nc, ctx)
        sb.update(alloc_sb(nc, ctx, QKV_SB))
        sb["_psb0"] = E(nc.psum_tensor("psb_v0", [128, 1024], BF16))
        sb["_psb1"] = E(nc.psum_tensor("psb_v1", [128, 1024], BF16))
        xn = E(nc.sbuf_tensor("xn0", [128, NCH * G], BF16))
        qf = E(nc.sbuf_tensor("qf", [128, NCH * G], BF16))
        kf = E(nc.sbuf_tensor("kf", [128, NCH * G], BF16))
        block = E(nc.Block())
        P = Prog()
        emit_consts(P, c, sb)
        emit_qkv_phase(P, c, ps, sb, xn, qf, kf, io)
        emit_final_wait(P, c)
        P.emit(block)
    return nc


ATTC_SB = {
    "crow": ([1, 2 * S], BF16), "mcol": ([128, 8], F32), "nmb": ([1, 2 * G], BF16),
    "colt": ([128, NH * 16], F32),
    "maskab": ([128, 2 * 256], F32), "maskq": ([128, 2 * 256], F32),
    "identf": ([128, 128], F32),
}


def attc_dram_decls(nc):
    d = {}
    d["woutt"] = nc.dram_tensor("woutt", [NCH, 128, NCH, 128], F32, kind="ExternalInput").ap()
    d["colt"] = nc.dram_tensor("colt", [128, NH * 16], F32, kind="ExternalInput").ap()
    d["colrow"] = nc.dram_tensor("colrow", [NH, S], BF16, kind="ExternalInput").ap()
    d["maskab"] = nc.dram_tensor("maskab", [2, 128, 256], F32, kind="ExternalInput").ap()
    d["maskq"] = nc.dram_tensor("maskq", [2, 128, 256], F32, kind="ExternalInput").ap()
    d["identf"] = nc.dram_tensor("identf", [128, 128], F32, kind="ExternalInput").ap()
    return d


def emit_attc_phase(P, c, ps, sb, qn_sb, ob, ksb, vsb, expt, io):
    xin, xout = io["xin"], io["xout"]
    qn_i = io["qn"]
    kng = io["kng"]
    vtg = io["vtg"]
    woutt, colt, colrow = io["woutt"], io["colt"], io["colrow"]
    maskab, maskq, identf = io["maskab"], io["maskq"], io["identf"]
    psm = ps[4][0:1, :]
    psz = ps[3][0:1, :]
    s_dma, s_pe, s_ve, s_se, s_wd, s_out = (c[k] for k in
                                            ("s_dma", "s_pe", "s_ve", "s_se", "s_wd", "s_out"))
    if True:
        @P.on("sync")
        def _(e):
            e.dma_start(out=sb["colt"][:], in_=colt).then_inc(s_dma.sem, 16)
            e.dma_start(out=apv(sb["maskab"]).rearrange("p (g n) -> p g n", g=2),
                        in_=maskab.rearrange("g p n -> p g n")).then_inc(s_dma.sem, 16)
            e.dma_start(out=apv(sb["maskq"]).rearrange("p (g n) -> p g n", g=2),
                        in_=maskq.rearrange("g p n -> p g n")).then_inc(s_dma.sem, 16)
            e.dma_start(out=sb["identf"][:], in_=identf).then_inc(s_dma.sem, 16)
        s_dma.bump(16 * 4)

        barrier(P, c)

        xin_v = xin.rearrange("(j p) t -> j p t", p=128)
        xout_v = xout.rearrange("(j p) t -> j p t", p=128)
        qnv = apv(qn_sb).rearrange("p (h t) -> p h t", h=NH)
        obv = apv(ob).rearrange("p (h t) -> p h t", h=NH)
        ksv = apv(ksb).rearrange("p (i t) -> p i t", i=2)
        vsv = apv(vsb).rearrange("p (i t) -> p i t", i=2)
        etv = apv(expt).rearrange("p (i t) -> p i t", i=2)
        crv = apv(sb["crow"]).rearrange("o (i t) -> o i t", i=2)
        rows = sb["rows"]
        nmrow_f = rows[0:1, 0:G]
        zrow = rows[0:1, G:2 * G]

        sl_k = c["sl_x"][0:2]
        sl_v = c["sl_x"][2:4]
        sl_c = c["sl_c"]

        for g in ATTN_GROUPS:
            kmax = 8 + 8 * g
            toks = slice(g * G, (g + 1) * G)

            @P.on("sync")
            def _(e, g=g, toks=toks):
                e.dma_start(out=qnv[:, :, :], in_=qn_i[:, :, toks].rearrange("h p t -> p h t")).then_inc(s_dma.sem, 16)
            s_dma.bump(16)
            barrier(P, c)

            pe_last = [0] * NH
            kw = [None] * NH
            vw = [None] * NH
            cw = [None] * NH
            for h in range(NH):
                kw[h] = sl_k[h % 2].bump(16 * 2 * (g + 1))
                vw[h] = sl_v[h % 2].bump(16 * 2 * (g + 1))
                cw[h] = sl_c[h % 2].bump(16)

            ks6 = apv(ksb).rearrange("p (i gg li r tb) -> p i gg li r tb",
                                     i=2, gg=2, li=4, r=2)
            vs6 = apv(vsb).rearrange("p (i gg li r dd) -> p i gg li r dd",
                                     i=2, gg=2, li=4, r=2)
            vtg_v = vtg.rearrange("r (li2 p) dd -> r li2 p dd", p=128)

            @P.on("sync")
            def _(e, g=g, kmax=kmax, pe_last=pe_last,
                  gk=c.get("m_ge_k", 0), gv=c.get("m_ge_v", 0)):
                def k_loads(h):
                    for gg in range(g + 1):
                        for r in range(2):
                            rd = r if gg == 0 else 1 - r
                            e.dma_start(
                                out=ks6[:, h % 2, gg, :, rd, :],
                                in_=kng[r, h, :, 512 * gg:512 * (gg + 1)].rearrange(
                                    "p (li tb) -> p li tb", li=4)).then_inc(sl_k[h % 2].sem, 16)

                def v_loads(h):
                    for gg in range(g + 1):
                        for r in range(2):
                            rd = r if gg == 0 else 1 - r
                            e.dma_start(
                                out=vs6[:, h % 2, gg, :, rd, :],
                                in_=vtg_v[r, 4 * gg:4 * (gg + 1), :, 128 * h:128 * (h + 1)].rearrange(
                                    "li p dd -> p li dd")).then_inc(sl_v[h % 2].sem, 16)

                def c_loads(h):
                    e.dma_start(out=crv[0:1, h % 2, 0:128 * kmax],
                                in_=colrow[h:h + 1, 0:128 * kmax]).then_inc(sl_c[h % 2].sem, 16)

                if gk:
                    e.wait_ge(c["s_ge"].sem, gk)
                for h in (0, 1):
                    k_loads(h)
                    c_loads(h)
                if gv:
                    e.wait_ge(c["s_ge"].sem, gv)
                for h in (0, 1):
                    v_loads(h)
                for h in range(2, NH):
                    e.wait_ge(s_pe.sem, pe_last[h - 2])
                    k_loads(h)
                    v_loads(h)
                    c_loads(h)

            nmrow2 = [rows[0:1, 0:G], rows[0:1, 2 * G:3 * G]]
            zrow2 = [rows[0:1, G:2 * G], rows[0:1, 3 * G:4 * G]]
            nmbv = [apv(sb["nmb"])[0:1, 0:G], apv(sb["nmb"])[0:1, G:2 * G]]
            zbv2 = [apv(sb["tmp"])[:, 0:G], apv(sb["tmp"])[:, G:2 * G]]
            psmv = [ps[4][0:1, :], ps[5][0:1, :]]
            pszv = [ps[0][0:1, :], ps[1][0:1, :]]
            pvb = [ps[2], ps[3]]
            scb = [ps[4], ps[5]]
            zbb = [ps[0], ps[1]]

            for hj in range(0, NH, 2):
                hpair = (hj, hj + 1)
                mx_fin = [[0] * 4, [0] * 4]
                tr_pe = [[0] * 4, [0] * 4]
                for s_ in range(4):
                    nk = 128 * (2 * s_ + 2 + 8 * g)
                    npc = (nk + 511) // 512
                    for hp in range(2):
                        h = hpair[hp]
                        redv = [s_ve.bump() for _ in range(npc)]
                        mx_fin[hp][s_] = s_ve.bump()

                        @P.on("tensor")
                        def _(e, g=g, h=h, hp=hp, s_=s_, npc=npc, nk=nk,
                              kwh=kw[h], cwh=cw[h], redv=tuple(redv),
                              mxw=(mx_fin[hp][s_ - 1] if s_ > 0 else 0)):
                            if s_ == 0:
                                e.wait_ge(sl_k[hp].sem, kwh)
                                e.wait_ge(sl_c[hp].sem, cwh)
                            else:
                                e.wait_ge(s_ve.sem, mxw)
                            for pc in range(npc):
                                n0 = 512 * pc
                                n1 = min(nk, n0 + 512)
                                if pc >= 2:
                                    e.wait_ge(s_ve.sem, redv[pc - 2])
                                bank = ps[2 * hp + pc % 2]
                                e.matmul(bank[:, 0:n1 - n0], sb["onesrow_bf"][0:1, 0:128],
                                         crv[0:1, hp, n0:n1], start=True, stop=False)
                                e.matmul(bank[:, 0:n1 - n0], qnv[:, h, 128 * s_:128 * (s_ + 1)],
                                         ksv[:, hp, n0:n1], start=False,
                                         stop=True).then_inc(s_pe.sem, 1)
                        pevals = [s_pe.bump() for _ in range(npc)]

                        @P.on("vector")
                        def _(e, g=g, hp=hp, npc=npc, nk=nk, pevals=tuple(pevals)):
                            q_ve = c["q_ve"]
                            for pc in range(npc):
                                e.wait_ge(s_pe.sem, pevals[pc])
                                n0 = 512 * pc
                                n1 = min(nk, n0 + 512)
                                bank = ps[2 * hp + pc % 2]
                                if pc == npc - 1:
                                    lw = n1 - n0
                                    q_ve.seq(e, e.tensor_tensor(
                                        bank[:, lw - 256:lw], bank[:, lw - 256:lw],
                                        sb["maskq"][:, 256 * g:256 * (g + 1)], ALU.add))
                                q_ve.seq(e, e.reduce_max(
                                    sb["mcol"][:, 4 * hp + pc:4 * hp + pc + 1],
                                    bank[:, 0:n1 - n0], axis=AX.X, negate=True))
                                e.sem_inc(s_ve.sem, 1)
                            for pc in range(1, npc):
                                q_ve.seq(e, e.tensor_tensor(
                                    sb["mcol"][:, 4 * hp:4 * hp + 1],
                                    sb["mcol"][:, 4 * hp:4 * hp + 1],
                                    sb["mcol"][:, 4 * hp + pc:4 * hp + pc + 1], ALU.min))
                            e.sem_inc(s_ve.sem, 1)

                    for hp in range(2):
                        @P.on("tensor")
                        def _(e, hp=hp, s_=s_, mv=mx_fin[hp][s_]):
                            e.wait_ge(s_ve.sem, mv)
                            e.transpose(psmv[hp][0:1, 128 * s_:128 * (s_ + 1)],
                                        sb["mcol"][:, 4 * hp:4 * hp + 1],
                                        sb["identf"][:]).then_inc(s_pe.sem, 1)
                        tr_pe[hp][s_] = s_pe.bump()

                nm_ve = [0, 0]
                for hp in range(2):
                    @P.on("vector")
                    def _(e, hp=hp, trv=tr_pe[hp][3]):
                        e.wait_ge(s_pe.sem, trv)
                        q_ve = c["q_ve"]
                        q_ve.seq(e, e.tensor_copy(nmrow2[hp], psmv[hp]))
                        q_ve.seq(e, e.tensor_copy(nmbv[hp], nmrow2[hp]))
                        e.sem_inc(s_ve.sem, 1)
                    nm_ve[hp] = s_ve.bump()

                stv = [[0] * kmax, [0] * kmax]
                mkv = [[0] * kmax, [0] * kmax]
                exv = [[0] * kmax, [0] * kmax]
                pvv = [[0] * kmax, [0] * kmax]
                for kb in range(kmax):
                    s0 = max(0, -(-(kb - 8 * g - 1) // 2))
                    nq = G - 128 * s0
                    q0 = 128 * s0
                    for hp in range(2):
                        h = hpair[hp]

                        @P.on("tensor")
                        def _(e, h=h, hp=hp, kb=kb, nq=nq, q0=q0,
                              nmw=nm_ve[1], vwh=vw[h]):
                            if kb == 0:
                                e.wait_ge(s_ve.sem, nmw)
                                e.wait_ge(sl_v[hp].sem, vwh)
                            bank = scb[hp]
                            e.matmul(bank[:, 0:nq], sb["onesrow_bf"][0:1, 0:128],
                                     nmbv[hp][0:1, q0:G], start=True, stop=False)
                            e.matmul(bank[:, 0:nq], ksv[:, hp, 128 * kb:128 * (kb + 1)],
                                     qnv[:, h, q0:G], start=False,
                                     stop=True).then_inc(s_pe.sem, 1)
                        stv[hp][kb] = s_pe.bump()

                    for hp in range(2):
                        if kb >= 8 * g:
                            @P.on("vector")
                            def _(e, g=g, hp=hp, kb=kb, st=stv[hp][kb]):
                                e.wait_ge(s_pe.sem, st)
                                bank = scb[hp]
                                moff = 256 * g + 128 * ((kb - 8 * g) % 2)
                                mslice = sb["maskab"][:, moff:moff + 128]
                                c["q_ve"].seq(e, e.tensor_tensor(
                                    bank[:, 0:128], bank[:, 0:128], mslice, ALU.add))
                                e.sem_inc(s_ve.sem, 1)
                            mkv[hp][kb] = s_ve.bump()

                    for hp in range(2):
                        h = hpair[hp]

                        @P.on("scalar")
                        def _(e, h=h, hp=hp, kb=kb, nq=nq, st=stv[hp][kb],
                              mk=mkv[hp][kb], masked=(kb >= 8 * g)):
                            e.wait_ge(s_pe.sem, st)
                            if masked:
                                e.wait_ge(s_ve.sem, mk)
                            bank = scb[hp]
                            a = e.activation(out=etv[:, hp, 0:nq], in_=bank[:, 0:nq],
                                             func=AF.Exp,
                                             bias=sb["colt"][:, 16 * h + kb:16 * h + kb + 1])
                            a.then_inc(s_se.sem, 1)
                        exv[hp][kb] = s_se.bump()

                    for hp in range(2):
                        @P.on("tensor")
                        def _(e, hp=hp, kb=kb, nq=nq, q0=q0, kmax=kmax,
                              ev=exv[hp][kb]):
                            e.wait_ge(s_se.sem, ev)
                            e.matmul(pvb[hp][:, q0:G], vsv[:, hp, 128 * kb:128 * (kb + 1)],
                                     etv[:, hp, 0:nq],
                                     start=(kb == 0), stop=(kb == kmax - 1))
                            mm = e.matmul(pszv[hp][0:1, q0:G], sb["onescol_bf"][:],
                                          etv[:, hp, 0:nq],
                                          start=(kb == 0), stop=(kb == kmax - 1))
                            mm.then_inc(s_pe.sem, 1)
                        pvv[hp][kb] = s_pe.bump()

                ob_ve = [0, 0]
                for hp in range(2):
                    h = hpair[hp]

                    @P.on("vector")
                    def _(e, hp=hp, pv=pvv[hp][kmax - 1]):
                        e.wait_ge(s_pe.sem, pv)
                        c["q_ve"].seq(e, e.reciprocal(zrow2[hp], pszv[hp]))
                        e.sem_inc(s_ve.sem, 1)
                    zr_ve = s_ve.bump()

                    @P.on("tensor")
                    def _(e, hp=hp, zr_ve=zr_ve):
                        e.wait_ge(s_ve.sem, zr_ve)
                        mm = e.matmul(zbb[hp][:], sb["onesrow_f"][:], zrow2[hp],
                                      start=True, stop=True)
                        mm.then_inc(s_pe.sem, 1)
                    zb_pe = s_pe.bump()

                    @P.on("vector")
                    def _(e, h=h, hp=hp, zb_pe=zb_pe):
                        e.wait_ge(s_pe.sem, zb_pe)
                        q_ve = c["q_ve"]
                        q_ve.seq(e, e.tensor_copy(zbv2[hp], zbb[hp][:]))
                        q_ve.seq(e, e.tensor_tensor(obv[:, h, :], pvb[hp][:],
                                                    zbv2[hp], ALU.mult))
                        e.sem_inc(s_ve.sem, 1)
                    ob_ve[hp] = s_ve.bump()
                    pe_last[h] = pvv[hp][kmax - 1]

                @P.on("tensor")
                def _(e, obw=ob_ve[1]):
                    e.wait_ge(s_ve.sem, obw)
                s_pe.bump(0)

            barrier(P, c)

            emit_dense_residual(P, c, sb, ps, xn=ob, wt_dram=woutt, xin_v=xin_v,
                                xout_v=xout_v, toks=toks, scale=1.0, halfbias=None)
            barrier(P, c)


def build_attn_core_launch():
    nc = bass.Bass("TRN2", target_bir_lowering=False)
    io = attc_dram_decls(nc)
    io["xin"] = nc.dram_tensor("xin", [D, 2 * G], F32, kind="ExternalInput").ap()
    io["xout"] = nc.dram_tensor("xout", [D, 2 * G], F32, kind="ExternalOutput").ap()
    io["qn"] = nc.dram_tensor("qn", [NH, 128, 2 * G], BF16, kind="ExternalInput").ap()
    io["kng"] = nc.dram_tensor("kng", [2, NH, 128, 2 * G], BF16, kind="ExternalInput").ap()
    io["vtg"] = nc.dram_tensor("vtg", [2, 2 * G, D], BF16, kind="ExternalInput").ap()
    with contextlib.ExitStack() as ctx:
        E = ctx.enter_context
        c = make_ctx(nc, ctx)
        ps = [E(nc.psum_tensor(f"psf{i}", [128, 512], F32)) for i in range(6)]
        sb = sb_common(nc, ctx)
        sb.update(alloc_sb(nc, ctx, ATTC_SB))
        qn_sb = E(nc.sbuf_tensor("qn_sb", [128, NH * G], BF16))
        ob = E(nc.sbuf_tensor("ob", [128, NH * G], BF16))
        ksb = E(nc.sbuf_tensor("ksb", [128, 2 * S], BF16))
        vsb = E(nc.sbuf_tensor("vsb", [128, 2 * S], BF16))
        expt = E(nc.sbuf_tensor("expt", [128, 2 * G], BF16))
        block = E(nc.Block())
        P = Prog()
        emit_consts(P, c, sb)
        emit_attc_phase(P, c, ps, sb, qn_sb, ob, ksb, vsb, expt, io)
        emit_final_wait(P, c)
        P.emit(block)
    return nc


def attn0_weights(inputs):
    Wqkv = np.asarray(inputs["L0_attn_Wqkv"], np.float32)
    Wout = np.asarray(inputs["L0_attn_Wout"], np.float32)
    colt = np.zeros((128, NH * 16), np.float32)
    colrow = np.zeros((NH, S), np.float32)
    for h in range(NH):
        for kb in range(16):
            colt[:, 16 * h + kb] = SLOPES[h] * (128 * kb + np.arange(128))
        colrow[h] = SLOPES[h] * np.arange(S)
    w = {
        "wqkvt": _wblocks(np.ascontiguousarray(Wqkv.T)),
        "qlnw": _chunkmajor(np.asarray(inputs["L0_attn_q_ln_w"], np.float32)),
        "qlnb": _chunkmajor(np.asarray(inputs["L0_attn_q_ln_b"], np.float32)),
        "klnw": _chunkmajor(np.asarray(inputs["L0_attn_k_ln_w"], np.float32)),
        "klnb": _chunkmajor(np.asarray(inputs["L0_attn_k_ln_b"], np.float32)),
        "identbf": np.eye(128, dtype=BF),
        "woutt": _wblocks(np.ascontiguousarray(Wout.T)),
        "colt": colt,
        "colrow": colrow.astype(BF),
        "identf": np.eye(128, dtype=np.float32),
    }
    NEG = np.float32(-1e30)
    tri_kq = np.tril(np.ones((128, 128), np.float32)).T
    maskab = np.zeros((2, 128, 256), np.float32)
    maskab[0, :, 0:128] = np.where(tri_kq > 0, 0, NEG)
    maskab[0, :, 128:256] = NEG
    maskab[1, :, 0:128] = 0
    maskab[1, :, 128:256] = np.where(tri_kq > 0, 0, NEG)
    w["maskab"] = maskab
    tri_qk = np.tril(np.ones((128, 128), np.float32))
    maskq = np.zeros((2, 128, 256), np.float32)
    maskq[0, :, 0:128] = np.where(tri_qk > 0, 0, NEG)
    maskq[0, :, 128:256] = NEG
    maskq[1, :, 0:128] = 0
    maskq[1, :, 128:256] = np.where(tri_qk > 0, 0, NEG)
    w["maskq"] = maskq
    return w


def run_attn0(x, inputs, w=None):
    if w is None:
        w = attn0_weights(inputs)
    nc2 = _get_prog(("qkv",), build_qkv_launch)
    in_maps = []
    for core in range(8):
        xc, _ = _shard_x(x, core)
        im = {"xin": xc, "wqkvt": w["wqkvt"], "i_qlnw": w["qlnw"], "i_qlnb": w["qlnb"],
              "i_klnw": w["klnw"], "i_klnb": w["klnb"], "identbf2": w["identbf"]}
        in_maps.append(im)
    res = run_bass_kernel_spmd(nc2, in_maps, core_ids=list(range(8)))
    qn = [np.asarray(res.results[cc]["qn"]) for cc in range(8)]
    kn = [np.asarray(res.results[cc]["kn"]) for cc in range(8)]
    vt = [np.asarray(res.results[cc]["vt"]) for cc in range(8)]

    kng = [np.stack([kn[2 * sq], kn[2 * sq + 1]]) for sq in range(4)]
    vtg = [np.stack([vt[2 * sq], vt[2 * sq + 1]]) for sq in range(4)]

    in_maps = []
    for core in range(8):
        seq, par = core // 2, core % 2
        xc, _ = _shard_x(x, core)
        im = {"xin": xc, "qn": qn[core], "kng": kng[seq], "vtg": vtg[seq],
              "woutt": w["woutt"], "colt": w["colt"], "colrow": w["colrow"],
              "maskab": np.stack([w["maskab"][par], w["maskab"][1 - par]]),
              "maskq": np.stack([w["maskq"][par], w["maskq"][1 - par]]),
              "identf": w["identf"]}
        in_maps.append(im)
    out = x.copy()
    nc3 = _get_prog(("attc",), build_attn_core_launch)
    res = run_bass_kernel_spmd(nc3, in_maps, core_ids=list(range(8)))
    for cc in range(8):
        _unshard_x(out, cc, np.asarray(res.results[cc]["xout"]))
    return out


A1_SB = {
    "bq": ([128, NCH], F32), "bk": ([128, 8], F32), "bv": ([128, 8], F32),
    "bqs": ([128, 8], F32), "identb": ([128, 128], BF16),
    "vstage": ([128, 4 * 128], BF16), "oe": ([128, 2 * G], BF16),
}


def a1_dram_decls(nc):
    d = {}
    d["wq"] = nc.dram_tensor("wq", [NCH, 128, NCH, 128], F32, kind="ExternalInput").ap()
    d["wk"] = nc.dram_tensor("wk", [8, 128, NCH, 128], F32, kind="ExternalInput").ap()
    d["wv"] = nc.dram_tensor("wv", [8, 128, NCH, 128], F32, kind="ExternalInput").ap()
    d["bqm"] = nc.dram_tensor("bqm", [128, NCH], F32, kind="ExternalInput").ap()
    d["bkm"] = nc.dram_tensor("bkm", [128, 8], F32, kind="ExternalInput").ap()
    d["bvm"] = nc.dram_tensor("bvm", [128, 8], F32, kind="ExternalInput").ap()
    d["identbf"] = nc.dram_tensor("identbf", [128, 128], BF16, kind="ExternalInput").ap()
    d["q3"] = nc.dram_tensor("q3", [8, G, 128], BF16, kind="Internal").ap()
    d["k3"] = nc.dram_tensor("k3", [8, G, 128], BF16, kind="Internal").ap()
    d["o3e"] = nc.dram_tensor("o3e", [128, NCH, G], BF16, kind="Internal").ap()
    return d


def emit_a1_phase(P, c, ps, psb, sb, xn, qt, kt, vt, ot, qhkT, khkT, probs, ptb, zc, io):
    xin, xout = io["xin"], io["xout"]
    wq, wk, wv = io["wq"], io["wk"], io["wv"]
    bqm, bkm, bvm = io["bqm"], io["bkm"], io["bvm"]
    identbf = io["identbf"]
    q3, k3, o3e = io["q3"], io["k3"], io["o3e"]
    s_dma, s_pe, s_ve, s_se, s_wd, s_out = (c[k] for k in
                                            ("s_dma", "s_pe", "s_ve", "s_se", "s_wd", "s_out"))
    if True:
        @P.on("sync")
        def _(e):
            e.dma_start(out=sb["bq"][:], in_=bqm).then_inc(s_dma.sem, 16)
            e.dma_start(out=sb["bk"][:], in_=bkm).then_inc(s_dma.sem, 16)
            e.dma_start(out=sb["bv"][:], in_=bvm).then_inc(s_dma.sem, 16)
            e.dma_start(out=sb["identb"][:], in_=identbf).then_inc(s_dma.sem, 16)
        s_dma.bump(16 * 4)

        barrier(P, c)

        SC = float(1.0 / np.sqrt(128.0))

        @P.on("vector")
        def _(e):
            bqv = apv(sb["bq"]).rearrange("p (hk jj) -> p hk jj", hk=8)
            q_ve = c["q_ve"]
            q_ve.seq(e, e.tensor_tensor(sb["bqs"][:], bqv[:, :, 0], bqv[:, :, 1], ALU.add))
            q_ve.seq(e, e.tensor_tensor(sb["bqs"][:], sb["bqs"][:], bqv[:, :, 2], ALU.add))
            q_ve.seq(e, e.tensor_tensor(sb["bqs"][:], sb["bqs"][:], bqv[:, :, 3], ALU.add))
            e.sem_inc(s_ve.sem, 1)
        s_ve.bump()

        @P.on("scalar")
        def _(e, w=s_ve.n):
            e.wait_ge(s_ve.sem, w)
            e.mul(sb["bqs"][:], sb["bqs"][:], SC).then_inc(s_se.sem, 1)
        s_se.bump()

        barrier(P, c)

        xin_v = xin.rearrange("(j p) t -> j p t", p=128)
        xout_v = xout.rearrange("(j p) t -> j p t", p=128)
        xnv = apv(xn).rearrange("p (j t) -> p j t", j=NCH)
        qtv = apv(qt).rearrange("p (hk t) -> p hk t", hk=8)
        ktv = apv(kt).rearrange("p (hk t) -> p hk t", hk=8)
        vtv = apv(vt).rearrange("p (hk t) -> p hk t", hk=8)
        otv = apv(ot).rearrange("p (hk t) -> p hk t", hk=8)
        wbv = apv(sb["wbuf"]).rearrange("p (i j n) -> p i j n", i=2, j=NCH)
        sl_w = c["sl_w"]
        xsv = apv(sb["xs"]).rearrange("p (i t) -> p i t", i=4)
        xov = apv(sb["xo"]).rearrange("p (i t) -> p i t", i=4)
        tmpv = apv(sb["tmp"]).rearrange("p (i t) -> p i t", i=2)
        sl_xe, sl_o = c["sl_xe"], c["sl_o"]

        for g in range(2):
            toks = slice(g * G, (g + 1) * G)
            emit_ln(P, c, sb, ps, xin_v, toks, xn)

            wprev = [0, 0]
            pprev = [0, 0]
            for proj, (wsrc, nblk, acc, arena, bias_t, scale) in enumerate((
                    (wq, NCH, 4, qtv, sb["bqs"], SC),
                    (wk, 8, 1, ktv, sb["bk"], 1.0),
                    (wv, 8, 1, vtv, sb["bv"], 1.0))):
                nout = nblk // acc
                wws = []
                pe_m = [None] * nblk
                ve_m = [None] * nout
                for d in range(nblk):
                    wws.append(sl_w[d % 2].bump(16))

                @P.on("gpsimd")
                def _(e, wsrc=wsrc, nblk=nblk, pe_m=pe_m, acc=acc, wprev=tuple(wprev)):
                    for d in range(nblk):
                        if d >= 2:
                            e.wait_ge(s_pe.sem, pe_m[d - 2])
                        elif wprev[d % 2]:
                            e.wait_ge(s_pe.sem, wprev[d % 2])
                        e.dma_start(out=wbv[:, d % 2, :, :],
                                    in_=wsrc[d, :, :, :]).then_inc(sl_w[d % 2].sem, 16)

                for d in range(nblk):
                    hk = d // acc

                    @P.on("tensor")
                    def _(e, d=d, hk=hk, ww=wws[d], acc=acc,
                          ev=(ve_m[(d // acc) - 2] if d // acc >= 2 and d % acc == 0
                              else (pprev[(d // acc) % 2] if d // acc < 2 and d % acc == 0 else None))):
                        e.wait_ge(sl_w[d % 2].sem, ww)
                        if ev:
                            e.wait_ge(s_se.sem, ev)
                        pt = ps[hk % 2]
                        mm = None
                        for j in range(NCH):
                            mm = e.matmul(pt[:], wbv[:, d % 2, j, :], xnv[:, j, :],
                                          start=(j == 0 and d % acc == 0),
                                          stop=(j == NCH - 1 and d % acc == acc - 1))
                        mm.then_inc(s_pe.sem, 1)
                    pe_m[d] = s_pe.bump()

                    if d % acc == acc - 1:
                        @P.on("scalar")
                        def _(e, hk=hk, pe_cnt=pe_m[d], arena=arena, bias_t=bias_t, scale=scale):
                            e.wait_ge(s_pe.sem, pe_cnt)
                            e.activation(out=arena[:, hk, :], in_=ps[hk % 2][:], func=AF.Identity,
                                         scale=scale, bias=bias_t[:, hk:hk + 1]).then_inc(s_se.sem, 1)
                        ve_m[hk] = s_se.bump()
                wprev = [pe_m[nblk - 2], pe_m[nblk - 1]]
                pprev = [ve_m[nout - 2], ve_m[nout - 1]]
            barrier(P, c)

            tvi = 0
            tv_se = []
            for src_ar, dst in ((qtv, q3), (ktv, k3)):
                for hk in range(8):
                    for i in range(NT):
                        @P.on("tensor")
                        def _(e, src_ar=src_ar, hk=hk, i=i, tvi=tvi,
                              wv=(tv_se[tvi - 2] if tvi >= 2 else 0)):
                            if tvi >= 2:
                                e.wait_ge(s_se.sem, wv)
                            mm = e.transpose(psb[tvi % 2][:, 0:128],
                                             src_ar[:, hk, 128 * i:128 * (i + 1)],
                                             sb["identb"][:])
                            mm.then_inc(s_pe.sem, 1)
                        tp = s_pe.bump()

                        @P.on("scalar")
                        def _(e, i=i, tvi=tvi, tp=tp,
                              ow=(c["sl_o"][tvi % 4].n if True else 0)):
                            e.wait_ge(s_pe.sem, tp)
                            if tvi >= 4:
                                e.wait_ge(c["sl_o"][tvi % 4].sem, ow)
                            e.copy(apv(sb["vstage"]).rearrange("p (i n) -> p i n", i=4)[:, tvi % 4, 0:128],
                                   psb[tvi % 2][:, 0:128]).then_inc(s_se.sem, 1)
                        se_cnt = s_se.bump()
                        tv_se.append(se_cnt)
                        c["sl_o"][tvi % 4].bump(16)

                        @P.on("sync")
                        def _(e, dst=dst, hk=hk, i=i, tvi=tvi, se_cnt=se_cnt):
                            e.wait_ge(s_se.sem, se_cnt)
                            e.dma_start(out=dst[hk, 128 * i:128 * (i + 1), :],
                                        in_=apv(sb["vstage"]).rearrange("p (i n) -> p i n", i=4)[:, tvi % 4, 0:128]).then_inc(c["sl_o"][tvi % 4].sem, 16)
                        tvi += 1
            barrier(P, c)

            BT = 32
            q_hk = apv(qhkT).rearrange("p (i n) -> p i n", i=2)
            k_hk = apv(khkT).rearrange("p (i n) -> p i n", i=2)
            pv_pe_l = []
            ld_b = [None] * (G // BT)
            for b in range(G // BT):
                ld_b[b] = c["sl_x"][b % 2].bump(32)

            @P.on("sync")
            def _(e, g=g, pv_pe_l=pv_pe_l, ld_b=ld_b):
                for b in range(G // BT):
                    if b >= 2:
                        e.wait_ge(s_pe.sem, pv_pe_l[(b - 2) * BT + BT - 1])
                    e.dma_start(out=q_hk[0:8, b % 2, 0:BT * 128],
                                in_=q3[:, BT * b:BT * (b + 1), :]).then_inc(c["sl_x"][b % 2].sem, 16)
                    e.dma_start(out=k_hk[0:8, b % 2, 0:BT * 128],
                                in_=k3[:, BT * b:BT * (b + 1), :]).then_inc(c["sl_x"][b % 2].sem, 16)

            rec = []
            vt_q = apv(vt).rearrange("p (hk t) -> p t hk", hk=8)
            ot_q = apv(ot).rearrange("p (hk t) -> p t hk", hk=8)
            tmpv2 = apv(sb["tmp"]).rearrange("p (i t) -> p i t", i=2)
            NQD = G // 4
            for qd in range(NQD):
                b = (4 * qd) // BT
                sl = qd % 2
                prev = rec[qd - 2] if qd >= 2 else None
                t0 = 4 * qd
                tb0 = t0 - b * BT

                @P.on("tensor")
                def _(e, b=b, sl=sl, qd=qd, t0=t0, tb0=tb0, ldw=ld_b[b],
                      zv=(prev["ex_se"] if prev else 0)):
                    if tb0 == 0:
                        e.wait_ge(c["sl_x"][b % 2].sem, ldw)
                    if qd >= 2:
                        e.wait_ge(s_se.sem, zv)
                    mm = None
                    for j in range(4):
                        mm = e.matmul(ps[2 + sl][:, 128 * j:128 * (j + 1)],
                                      q_hk[0:8, b % 2, 128 * (tb0 + j):128 * (tb0 + j + 1)],
                                      k_hk[0:8, b % 2, 128 * (tb0 + j):128 * (tb0 + j + 1)],
                                      start=(j == 0), stop=(j == 3))
                    mm.then_inc(s_pe.sem, 1)
                sc_pe = s_pe.bump()

                @P.on("vector")
                def _(e, sl=sl, qd=qd, sc_pe=sc_pe,
                      exw=(prev["ex_se"] if prev else 0)):
                    e.wait_ge(s_pe.sem, sc_pe)
                    q_ve = c["q_ve"]
                    for j in range(4):
                        mm = e.reduce_max(zc[:, 4 * sl + j:4 * sl + j + 1],
                                          ps[2 + sl][:, 128 * j:128 * (j + 1)],
                                          axis=AX.X, negate=True)
                    q_ve.seq(e, mm)
                    if qd >= 2:
                        e.wait_ge(s_se.sem, exw)
                    for j in range(4):
                        q_ve.seq(e, e.tensor_scalar(tmpv2[:, sl, 128 * j:128 * (j + 1)],
                                                    ps[2 + sl][:, 128 * j:128 * (j + 1)],
                                                    zc[:, 4 * sl + j:4 * sl + j + 1], None,
                                                    op0=ALU.add))
                    e.sem_inc(s_ve.sem, 1)
                nm_ve = s_ve.bump()

                @P.on("scalar")
                def _(e, sl=sl, qd=qd, nm_ve=nm_ve,
                      pvw=(prev["tr_pe"] if prev else 0),
                      zrw=(prev["zr_ve"] if prev else 0)):
                    e.wait_ge(s_ve.sem, nm_ve)
                    if qd >= 2:
                        e.wait_ge(s_pe.sem, pvw)
                        e.wait_ge(s_ve.sem, zrw)
                    e.activation(out=probs[:, 512 * sl:512 * (sl + 1)],
                                 in_=tmpv2[:, sl, :], func=AF.Exp).then_inc(s_se.sem, 1)
                ex_se = s_se.bump()

                @P.on("vector")
                def _(e, sl=sl, qd=qd, ex_se=ex_se):
                    e.wait_ge(s_se.sem, ex_se)
                    q_ve = c["q_ve"]
                    for j in range(4):
                        mm = e.reduce_sum(zc[:, 8 + 4 * sl + j:8 + 4 * sl + j + 1],
                                          probs[:, 512 * sl + 128 * j:512 * sl + 128 * (j + 1)], axis=AX.X)
                    q_ve.seq(e, mm)
                    q_ve.seq(e, e.reciprocal(zc[:, 8 + 4 * sl:8 + 4 * sl + 4],
                                             zc[:, 8 + 4 * sl:8 + 4 * sl + 4]))
                    e.sem_inc(s_ve.sem, 1)
                zr_ve = s_ve.bump()

                @P.on("tensor")
                def _(e, sl=sl, qd=qd, ex_se=ex_se,
                      cpw=(prev["cp_ve"] if prev else 0)):
                    e.wait_ge(s_se.sem, ex_se)
                    if qd >= 2:
                        e.wait_ge(s_ve.sem, cpw)
                    mm = None
                    for j in range(4):
                        mm = e.transpose(psb[sl][:, 128 * j:128 * (j + 1)],
                                         probs[:, 512 * sl + 128 * j:512 * sl + 128 * (j + 1)],
                                         sb["identb"][:])
                    mm.then_inc(s_pe.sem, 1)
                tr_pe = s_pe.bump()

                @P.on("vector")
                def _(e, sl=sl, qd=qd, tr_pe=tr_pe,
                      opw=(prev["pv_pe"] if prev else 0)):
                    e.wait_ge(s_pe.sem, tr_pe)
                    if qd >= 2:
                        e.wait_ge(s_pe.sem, opw)
                    e.tensor_copy(ptb[:, 512 * sl:512 * (sl + 1)],
                                  psb[sl][:, 0:512]).then_inc(s_ve.sem, 1)
                cp_ve = s_ve.bump()

                @P.on("tensor")
                def _(e, sl=sl, qd=qd, t0=t0, cp_ve=cp_ve,
                      ovw=(prev["ev_ve"] if prev else 0)):
                    e.wait_ge(s_ve.sem, cp_ve)
                    if qd >= 2:
                        e.wait_ge(s_ve.sem, ovw)
                    mm = None
                    for j in range(4):
                        mm = e.matmul(ps[4 + sl][:, 8 * j:8 * (j + 1)],
                                      ptb[:, 512 * sl + 128 * j:512 * sl + 128 * (j + 1)],
                                      vt_q[:, t0 + j, :], start=(j == 0), stop=(j == 3))
                    mm.then_inc(s_pe.sem, 1)
                pv_pe = s_pe.bump()
                for j in range(4):
                    pv_pe_l.append(pv_pe)

                @P.on("vector")
                def _(e, sl=sl, qd=qd, t0=t0, pv_pe=pv_pe):
                    e.wait_ge(s_pe.sem, pv_pe)
                    q_ve = c["q_ve"]
                    for j in range(4):
                        q_ve.seq(e, e.tensor_scalar(ot_q[:, t0 + j, :],
                                                    ps[4 + sl][:, 8 * j:8 * (j + 1)],
                                                    zc[:, 8 + 4 * sl + j:8 + 4 * sl + j + 1], None,
                                                    op0=ALU.mult))
                    e.sem_inc(s_ve.sem, 1)
                ev_ve = s_ve.bump()
                rec.append({"ex_se": ex_se, "zr_ve": zr_ve, "tr_pe": tr_pe,
                            "cp_ve": cp_ve, "pv_pe": pv_pe, "ev_ve": ev_ve})


            barrier(P, c)

            o3x = o3e.rearrange("dd (hk r) t -> dd hk r t", r=4)

            @P.on("sync")
            def _(e):
                for r in range(4):
                    e.dma_start(out=o3x[:, :, r, :], in_=otv[:, :, :]).then_inc(s_dma.sem, 16)
            s_dma.bump(16 * 4)
            barrier(P, c)

            o3f = o3e.rearrange("dd hh t -> (dd hh) t").rearrange("(a p) t -> a p t", p=128)
            oev = apv(sb["oe"]).rearrange("p (i t) -> p i t", i=2)
            a1rv = []
            for cch in range(NCH):
                xe_w = sl_xe[cch % 4].bump(16)
                oe_w = c["sl_c"][cch % 2].bump(16)
                ve_prev2 = a1rv[cch - 2] if cch >= 2 else 0

                @P.on("sync")
                def _(e, cch=cch, toks=toks,
                      ev4=(a1rv[cch - 4] if cch >= 4 else 0)):
                    if cch >= 4:
                        e.wait_ge(s_ve.sem, ev4)
                    e.dma_start(out=xsv[:, cch % 4, :], in_=xin_v[cch, :, toks]).then_inc(sl_xe[cch % 4].sem, 16)

                @P.on("gpsimd")
                def _(e, cch=cch, ve_prev2=ve_prev2):
                    if cch >= 2:
                        e.wait_ge(s_ve.sem, ve_prev2)
                    e.dma_start(out=oev[:, cch % 2, :], in_=o3f[cch, :, :]).then_inc(c["sl_c"][cch % 2].sem, 16)

                @P.on("vector")
                def _(e, cch=cch, xw=xe_w, ow=oe_w,
                      ov=(c["sl_o"][cch % 4].n if cch >= 4 else 0)):
                    e.wait_ge(sl_xe[cch % 4].sem, xw)
                    e.wait_ge(c["sl_c"][cch % 2].sem, ow)
                    if cch >= 4:
                        e.wait_ge(c["sl_o"][cch % 4].sem, ov)
                    e.tensor_tensor(xov[:, cch % 4, :], xsv[:, cch % 4, :],
                                    oev[:, cch % 2, :], ALU.add).then_inc(s_ve.sem, 1)
                ve_cnt = s_ve.bump()
                a1rv.append(ve_cnt)
                c["sl_o"][cch % 4].bump(16)

                @P.on("sync")
                def _(e, cch=cch, ve_cnt=ve_cnt, toks=toks):
                    e.wait_ge(s_ve.sem, ve_cnt)
                    e.dma_start(out=xout_v[cch, :, toks], in_=xov[:, cch % 4, :]).then_inc(c["sl_o"][cch % 4].sem, 16)

            barrier(P, c)


def build_attn1_launch():
    nc = bass.Bass("TRN2", target_bir_lowering=False)
    io = a1_dram_decls(nc)
    io["xin"] = nc.dram_tensor("xin", [D, 2 * G], F32, kind="ExternalInput").ap()
    io["xout"] = nc.dram_tensor("xout", [D, 2 * G], F32, kind="ExternalOutput").ap()
    with contextlib.ExitStack() as ctx:
        E = ctx.enter_context
        c = make_ctx(nc, ctx)
        ps = [E(nc.psum_tensor(f"psf{i}", [128, 512], F32)) for i in range(6)]
        psb = [E(nc.psum_tensor(f"psbt{i}", [128, 1024], BF16)) for i in range(2)]
        sb = sb_common(nc, ctx)
        sb.update(alloc_sb(nc, ctx, A1_SB))
        xn = E(nc.sbuf_tensor("xn0", [128, NCH * G], BF16))
        qt = E(nc.sbuf_tensor("qt", [128, 8 * G], BF16))
        kt = E(nc.sbuf_tensor("kt", [128, 8 * G], BF16))
        vt = E(nc.sbuf_tensor("vt_sb", [128, 8 * G], BF16))
        ot = E(nc.sbuf_tensor("ot", [128, 8 * G], BF16))
        qhkT = E(nc.sbuf_tensor("qhkT", [8, 64 * 128 * 2], BF16))
        khkT = E(nc.sbuf_tensor("khkT", [8, 64 * 128 * 2], BF16))
        probs = E(nc.sbuf_tensor("probs_sb", [128, 1024], BF16))
        ptb = E(nc.sbuf_tensor("pt_sb", [128, 1024], BF16))
        zc = E(nc.sbuf_tensor("zc", [128, 16], F32))
        block = E(nc.Block())
        P = Prog()
        emit_consts(P, c, sb)
        emit_a1_phase(P, c, ps, psb, sb, xn, qt, kt, vt, ot, qhkT, khkT, probs, ptb, zc, io)
        emit_final_wait(P, c)
        P.emit(block)
    return nc


def attn1_weights(inputs):
    Wq = np.asarray(inputs["L1_attn_Wq"], np.float32)
    Wk = np.asarray(inputs["L1_attn_Wk"], np.float32)
    Wv = np.asarray(inputs["L1_attn_Wv"], np.float32)
    wqT = Wq.transpose(2, 1, 0).reshape(D, D).T
    wqT = np.ascontiguousarray(Wq.transpose(0, 2, 1).reshape(D, D))
    wkT = np.ascontiguousarray(Wk.transpose(0, 2, 1).reshape(D, 8 * 128))
    wvT = np.ascontiguousarray(Wv.transpose(0, 2, 1).reshape(D, 8 * 128))
    bq = np.asarray(inputs["L1_attn_bq"], np.float32)
    bk = np.asarray(inputs["L1_attn_bk"], np.float32)
    bv = np.asarray(inputs["L1_attn_bv"], np.float32)
    bqv = np.ascontiguousarray(bq.T.reshape(D))
    bkv = np.ascontiguousarray(bk.T.reshape(8 * 128))
    bvv = np.ascontiguousarray(bv.T.reshape(8 * 128))
    return {
        "wq": _wblocks(wqT),
        "wk": _wblocks(wkT),
        "wv": _wblocks(wvT),
        "bqm": _chunkmajor(bqv),
        "bkm": np.ascontiguousarray(bkv.reshape(8, 128).T),
        "bvm": np.ascontiguousarray(bvv.reshape(8, 128).T),
        "identbf": np.eye(128, dtype=BF),
    }


def run_attn1(x, inputs, w=None):
    if w is None:
        w = attn1_weights(inputs)
    nc = _get_prog(("attn1",), build_attn1_launch)
    in_maps = []
    for core in range(8):
        xc, _ = _shard_x(x, core)
        im = {"xin": xc}
        im.update(w)
        in_maps.append(im)
    res = run_bass_kernel_spmd(nc, in_maps, core_ids=list(range(8)))
    out = x.copy()
    for core in range(8):
        _unshard_x(out, core, np.asarray(res.results[core]["xout"]))
    return out




# version 11
# speedup vs baseline: 1.0402x; 1.0307x over previous
import sys
sys.path.insert(0, "/opt/trn_rl_repo")
import contextlib

import numpy as np
import ml_dtypes
import concourse.bass as bass
import concourse.mybir as mybir
from concourse.bass_utils import run_bass_kernel_spmd

F32 = mybir.dt.float32
BF16 = mybir.dt.bfloat16
AF = mybir.ActivationFunctionType
ALU = mybir.AluOpType
AX = mybir.AxisListType

D = 4096
NCH = 32
G = 512
NT = 4
S = 2048
NH = 32
LN_EPS = 1e-5
CLIP = 8.0
BF = ml_dtypes.bfloat16


def alibi_slopes(n_heads=32, bias_max=8.0):
    next_pow2 = 2 ** int(np.ceil(np.log2(n_heads)))
    m = np.arange(1, next_pow2 + 1, dtype=np.float32) * (bias_max / next_pow2)
    slopes = 1.0 / np.power(2.0, m)
    if next_pow2 != n_heads:
        slopes = np.concatenate([slopes[1::2], slopes[::2]])[:n_heads]
    return slopes.astype(np.float32)


SLOPES = alibi_slopes()
ATTN_GROUPS = (0, 1)
DUMP_OB = False
P2_STAGE = 4


def tiles_of(parity, grp):
    if grp == 0:
        return [2 * s + parity for s in range(4)]
    return [8 + 2 * s + (1 - parity) for s in range(4)]


def core_token_tiles(parity):
    return tiles_of(parity, 0) + tiles_of(parity, 1)



class Prog:
    ENGINES = ("sync", "gpsimd", "tensor", "vector", "scalar")

    def __init__(self):
        self.ops = {e: [] for e in self.ENGINES}

    def on(self, eng):
        def deco(fn):
            self.ops[eng].append(fn)
            return None
        return deco

    def emit(self, block):
        for name in self.ENGINES:
            fns = self.ops[name]
            deco = getattr(block, name)

            def section(e, fns=fns):
                for f in fns:
                    f(e)
            deco(section)


class Ctr:
    def __init__(self, sem):
        self.sem = sem
        self.n = 0

    def bump(self, k=1):
        self.n += k
        return self.n

    def seq(self, e, instr):
        instr.then_inc(self.sem, 1)
        e.wait_ge(self.sem, self.bump())


def apv(t):
    a = getattr(t, "ap", None)
    return t.ap() if callable(a) else t


def make_ctx(nc, ctx):
    E = ctx.enter_context
    c = {"nc": nc}
    for nm in ("s_dma", "s_pe", "s_ve", "s_se", "s_wd", "s_out", "s_bar", "s_ge"):
        c[nm] = Ctr(E(nc.semaphore(nm)))
    c["sl_x"] = [Ctr(E(nc.semaphore(f"slx{i}"))) for i in range(4)]
    c["sl_w"] = [Ctr(E(nc.semaphore(f"slw{i}"))) for i in range(2)]
    c["sl_xe"] = [Ctr(E(nc.semaphore(f"slxe{i}"))) for i in range(4)]
    c["sl_o"] = [Ctr(E(nc.semaphore(f"slo{i}"))) for i in range(4)]
    c["sl_c"] = [Ctr(E(nc.semaphore(f"slc{i}"))) for i in range(2)]
    for eng in ("ve", "se", "pe", "ge"):
        c["q_" + eng] = Ctr(E(nc.semaphore("q_" + eng)))
    return c


def barrier(P, c):
    tgt = c["s_bar"].n + 5
    c["s_bar"].n = tgt
    sem = c["s_bar"].sem
    dma_waits = [(ctr.sem, ctr.n) for ctr in
                 ([c["s_dma"], c["s_wd"], c["s_out"]] + c["sl_x"] + c["sl_w"]
                  + c["sl_xe"] + c["sl_o"] + c["sl_c"]) if ctr.n > 0]

    for eng in Prog.ENGINES:
        @P.on(eng)
        def _(e, tgt=tgt, sem=sem, eng=eng, dma_waits=tuple(dma_waits)):
            if eng == "sync":
                for sm, n in dma_waits:
                    e.wait_ge(sm, n)
            e.drain()
            e.sem_inc(sem, 1)
            e.wait_ge(sem, tgt)



def sb_common(nc, ctx):
    E = ctx.enter_context
    sb = {}
    sb["xs"] = E(nc.sbuf_tensor("xs", [128, 4 * G], F32))
    sb["sq"] = E(nc.sbuf_tensor("sq", [128, 2 * G], BF16))
    sb["rows"] = E(nc.sbuf_tensor("rows", [1, 4 * G], F32))
    sb["bcast"] = E(nc.sbuf_tensor("bcast", [128, 2 * G], F32))
    sb["onescol_bf"] = E(nc.sbuf_tensor("onescol_bf", [128, 1], BF16))
    sb["onesrow_f"] = E(nc.sbuf_tensor("onesrow_f", [1, 128], F32))
    sb["onesrow_bf"] = E(nc.sbuf_tensor("onesrow_bf", [1, 512], BF16))
    sb["cpool"] = E(nc.sbuf_tensor("cpool", [1, 8], F32))
    sb["wbuf"] = E(nc.sbuf_tensor("wbuf", [128, 2 * NCH * 128], BF16))
    sb["tmp"] = E(nc.sbuf_tensor("tmp", [128, 2 * G], F32))
    sb["xo"] = E(nc.sbuf_tensor("xo", [128, 4 * G], F32))
    return sb


def emit_consts(P, c, sb):
    @P.on("vector")
    def _(e):
        e.memset(sb["onescol_bf"][:], 1.0)
        e.memset(sb["onesrow_f"][:], 1.0)
        e.memset(sb["onesrow_bf"][:], 1.0)
        e.memset(sb["cpool"][0:1, 0:1], LN_EPS).then_inc(c["s_ve"].sem, 1)
    return c["s_ve"].bump()



def emit_ln(P, c, sb, ps, xin_v, toks, arena):
    s_ve = c["s_ve"]
    xnv = apv(arena).rearrange("p (j t) -> p j t", j=NCH)
    xsv = apv(sb["xs"]).rearrange("p (i t) -> p i t", i=4)
    ve0 = s_ve.n
    sl_x = c["sl_x"]
    xwaits = []
    for j in range(NCH):
        xwaits.append(sl_x[j % 4].bump(16))

    @P.on("sync")
    def _(e, ve0=ve0):
        for j in range(NCH):
            if j >= 4:
                e.wait_ge(s_ve.sem, ve0 + (j - 4) + 1)
            e.dma_start(out=xsv[:, j % 4, :], in_=xin_v[j, :, toks]).then_inc(sl_x[j % 4].sem, 16)

    @P.on("vector")
    def _(e):
        for j in range(NCH):
            e.wait_ge(sl_x[j % 4].sem, xwaits[j])
            e.tensor_copy(xnv[:, j, :], xsv[:, j % 4, :]).then_inc(s_ve.sem, 1)
    s_ve.bump(NCH)

    emit_norm(P, c, sb, ps, arena, copy_ve0=ve0)


def emit_norm(P, c, sb, ps, arena, wb=None, copy_ve0=None, nch=NCH):
    s_pe, s_ve, s_se = c["s_pe"], c["s_ve"], c["s_se"]
    NCHa = nch
    xnv = apv(arena).rearrange("p (j t) -> p j t", j=NCHa)
    sqv = apv(sb["sq"]).rearrange("p (i t) -> p i t", i=2)
    psS = ps[4][0:1, :]
    psQ = ps[5][0:1, :]
    ve0, se0, pe0 = s_ve.n - NCHa, s_se.n, s_pe.n
    if copy_ve0 is not None:
        ve0 = copy_ve0

    @P.on("scalar")
    def _(e, ve0=ve0, pe0=pe0):
        for j in range(NCHa):
            e.wait_ge(s_ve.sem, ve0 + j + 1)
            if j >= 2:
                e.wait_ge(s_pe.sem, pe0 + (j - 1))
            e.activation(out=sqv[:, j % 2, :], in_=xnv[:, j, :], func=AF.Square).then_inc(s_se.sem, 1)
    s_se.bump(NCHa)

    @P.on("tensor")
    def _(e, se0=se0):
        for j in range(NCHa):
            e.wait_ge(s_se.sem, se0 + j + 1)
            e.matmul(psS, sb["onescol_bf"][:], xnv[:, j, :], start=(j == 0), stop=(j == NCHa - 1))
            mm = e.matmul(psQ, sb["onescol_bf"][:], sqv[:, j % 2, :], start=(j == 0), stop=(j == NCHa - 1))
            mm.then_inc(s_pe.sem, 1)
    s_pe.bump(NCHa)

    rows = sb["rows"]
    m_r = rows[0:1, 0:G]
    e2_r = rows[0:1, G:2 * G]
    rstd_r = rows[0:1, 2 * G:3 * G]
    mr_r = rows[0:1, 3 * G:4 * G]
    pe_stats = s_pe.n

    @P.on("scalar")
    def _(e, pe_stats=pe_stats):
        e.wait_ge(s_pe.sem, pe_stats)
        e.activation(out=m_r, in_=psS, func=AF.Identity, scale=1.0 / (128 * NCHa))
        e.activation(out=e2_r, in_=psQ, func=AF.Identity, scale=1.0 / (128 * NCHa)).then_inc(s_se.sem, 1)
    se_mq = s_se.bump()

    q_ve, q_se = c["q_ve"], c["q_se"]

    @P.on("vector")
    def _(e, se_mq=se_mq):
        e.wait_ge(s_se.sem, se_mq)
        q_ve.seq(e, e.tensor_tensor(mr_r, m_r, m_r, ALU.mult))
        e.tensor_tensor(e2_r, e2_r, mr_r, ALU.subtract).then_inc(s_ve.sem, 1)
    ve_var = s_ve.bump()

    @P.on("scalar")
    def _(e, ve_var=ve_var):
        e.wait_ge(s_ve.sem, ve_var)
        q_se.seq(e, e.activation(out=rstd_r, in_=e2_r, func=AF.Ln, bias=sb["cpool"][0:1, 0:1]))
        e.activation(out=rstd_r, in_=rstd_r, func=AF.Exp, scale=-0.5).then_inc(s_se.sem, 1)
    se_rstd = s_se.bump()

    @P.on("vector")
    def _(e, se_rstd=se_rstd):
        e.wait_ge(s_se.sem, se_rstd)
        e.tensor_tensor(mr_r, m_r, rstd_r, ALU.mult).then_inc(s_ve.sem, 1)
    ve_mr = s_ve.bump()

    @P.on("tensor")
    def _(e, ve_mr=ve_mr):
        e.wait_ge(s_ve.sem, ve_mr)
        e.matmul(ps[2][:], sb["onesrow_f"][:], rstd_r, start=True, stop=True)
        mm = e.matmul(ps[3][:], sb["onesrow_f"][:], mr_r, start=True, stop=True)
        mm.then_inc(s_pe.sem, 1)
    pe_b = s_pe.bump()

    bc = sb["bcast"]

    @P.on("vector")
    def _(e, pe_b=pe_b):
        e.wait_ge(s_pe.sem, pe_b)
        e.tensor_copy(bc[:, 0:G], ps[2][:])
        e.tensor_copy(bc[:, G:2 * G], ps[3][:]).then_inc(s_ve.sem, 1)
    ve_bc = s_ve.bump()

    tmpv = apv(sb["tmp"]).rearrange("p (i t) -> p i t", i=2)

    if wb is None:
        @P.on("vector")
        def _(e, ve_bc=ve_bc):
            e.wait_ge(s_ve.sem, ve_bc)
            q_ve = c["q_ve"]
            for j in range(NCHa):
                q_ve.seq(e, e.tensor_tensor(tmpv[:, j % 2, :], xnv[:, j, :], bc[:, 0:G], ALU.mult))
                q_ve.seq(e, e.tensor_tensor(xnv[:, j, :], tmpv[:, j % 2, :], bc[:, G:2 * G], ALU.subtract))
                e.sem_inc(s_ve.sem, 1)
        s_ve.bump(NCHa)
    else:
        w_tile, b_tile = wb
        ve_marks = []
        for j in range(NCHa):
            @P.on("vector")
            def _(e, j=j, ve_bc=ve_bc):
                if j == 0:
                    e.wait_ge(s_ve.sem, ve_bc)
                q_ve = c["q_ve"]
                if j >= 2:
                    e.wait_ge(s_se.sem, ve_marks[j - 2])
                q_ve.seq(e, e.tensor_tensor(tmpv[:, j % 2, :], xnv[:, j, :], bc[:, 0:G], ALU.mult))
                e.tensor_tensor(tmpv[:, j % 2, :], tmpv[:, j % 2, :], bc[:, G:2 * G],
                                ALU.subtract).then_inc(s_ve.sem, 1)
            ve_j = s_ve.bump()

            @P.on("scalar")
            def _(e, j=j, ve_j=ve_j, w_tile=w_tile, b_tile=b_tile):
                e.wait_ge(s_ve.sem, ve_j)
                e.activation(out=xnv[:, j, :], in_=tmpv[:, j % 2, :], func=AF.Identity,
                             scale=w_tile[:, j:j + 1], bias=b_tile[:, j:j + 1]).then_inc(s_se.sem, 1)
            ve_marks.append(s_se.bump())

    barrier(P, c)



def emit_dense_residual(P, c, sb, ps, *, xn, wt_dram, xin_v, xout_v, toks,
                        nd=D, extra_mm=None, scale=1.0, halfbias=None,
                        out_override=None):
    s_dma, s_pe, s_ve, s_se, s_wd, s_out = (c[k] for k in
                                            ("s_dma", "s_pe", "s_ve", "s_se", "s_wd", "s_out"))
    ndc = nd // 128
    xnv = apv(xn).rearrange("p (j t) -> p j t", j=NCH)
    wtv = wt_dram
    wbv = apv(sb["wbuf"]).rearrange("p (i j n) -> p i j n", i=2, j=NCH)
    xsv = apv(sb["xs"]).rearrange("p (i t) -> p i t", i=4)
    tmpv = apv(sb["tmp"]).rearrange("p (i t) -> p i t", i=2)
    xov = apv(sb["xo"]).rearrange("p (i t) -> p i t", i=4)

    sl_w, sl_xe, sl_o = c["sl_w"], c["sl_xe"], c["sl_o"]
    pe_marks = [None] * ndc
    evac_done = [None] * ndc
    wwaits = [sl_w[d % 2].bump(16) for d in range(ndc)]

    @P.on("gpsimd")
    def _(e):
        for d in range(ndc):
            if d >= 2:
                e.wait_ge(s_pe.sem, pe_marks[d - 2])
            e.dma_start(out=wbv[:, d % 2, :, :],
                        in_=wtv[d, :, :, :]).then_inc(sl_w[d % 2].sem, 16)

    for d in range(ndc):
        @P.on("tensor")
        def _(e, d=d, ww=wwaits[d]):
            e.wait_ge(sl_w[d % 2].sem, ww)
            if d >= 2:
                e.wait_ge(s_ve.sem, evac_done[d - 2])
            pt = ps[d % 2]
            mm = None
            last = extra_mm is None
            for j in range(NCH):
                mm = e.matmul(pt[:], wbv[:, d % 2, j, :], xnv[:, j, :],
                              start=(j == 0), stop=(last and j == NCH - 1))
            if extra_mm is not None:
                mm = extra_mm(e, d, pt)
            mm.then_inc(s_pe.sem, 1)
        pe_marks[d] = s_pe.bump()

        if out_override is not None:
            evac_done[d] = out_override(P, c, d, ps[d % 2], pe_marks[d])
            continue

        xe_w = sl_xe[d % 4].bump(16)

        @P.on("sync")
        def _(e, d=d):
            if d >= 4:
                e.wait_ge(s_ve.sem, evac_done[d - 4])
            e.dma_start(out=xsv[:, d % 4, :], in_=xin_v[d, :, toks]).then_inc(sl_xe[d % 4].sem, 16)

        @P.on("scalar")
        def _(e, d=d, pe_cnt=pe_marks[d]):
            e.wait_ge(s_pe.sem, pe_cnt)
            if d >= 2:
                e.wait_ge(s_ve.sem, evac_done[d - 2])
            kw = {}
            if halfbias is not None:
                kw["bias"] = halfbias[:, d:d + 1]
            e.activation(out=tmpv[:, d % 2, :], in_=ps[d % 2][:], func=AF.Identity,
                         scale=scale, **kw).then_inc(s_se.sem, 1)
        se_cnt = s_se.bump()

        ow_prev = sl_o[d % 4].n

        @P.on("vector")
        def _(e, d=d, se_cnt=se_cnt, xe_w=xe_w, ow_prev=ow_prev):
            e.wait_ge(s_se.sem, se_cnt)
            e.wait_ge(sl_xe[d % 4].sem, xe_w)
            if d >= 4:
                e.wait_ge(sl_o[d % 4].sem, ow_prev)
            e.tensor_tensor(xov[:, d % 4, :], xsv[:, d % 4, :],
                            tmpv[:, d % 2, :], ALU.add).then_inc(s_ve.sem, 1)
        evac_done[d] = s_ve.bump()

        o_w = sl_o[d % 4].bump(16)

        @P.on("sync")
        def _(e, d=d, ve_cnt=evac_done[d]):
            e.wait_ge(s_ve.sem, ve_cnt)
            e.dma_start(out=xout_v[d, :, toks], in_=xov[:, d % 4, :]).then_inc(sl_o[d % 4].sem, 16)



FFN_SB = {
    0: {"halfbias": ([128, NCH], F32), "a_bf": ([128, 2 * NCH * 128], BF16),
        "bft_bf": ([128, 2 * D], BF16), "wexp": ([128, 2], F32),
        "wsm": ([16, 258], F32)},
    1: {"halfbias": ([128, NCH], F32), "a_bf": ([128, NCH * 64], BF16),
        "bt_bf": ([64, D], BF16), "gw_bf": ([128, NCH * 4], BF16),
        "gbcol": ([4, 1], F32), "erows": ([1, 4 * 64], BF16),
        "identf": ([128, 128], F32), "identb": ([128, 128], BF16),
        "fsc": ([128, 16], F32), "fcols": ([128, 12], BF16),
        "flagrows": ([1, 3 * G], BF16), "maskb": ([64, G], BF16),
        "hm": ([64, G], BF16), "gatef": ([4, G], F32)},
}


def ffn_dram_decls(nc, layer, pref=""):
    d = {}
    d["wt"] = nc.dram_tensor(pref + "wt", [NCH, 128, NCH, 128], F32, kind="ExternalInput").ap()
    d["biasm"] = nc.dram_tensor(pref + "biasm", [128, NCH], F32, kind="ExternalInput").ap()
    if layer == 0:
        d["a2t"] = nc.dram_tensor(pref + "a2t", [2, 128, NCH, 128], F32, kind="ExternalInput").ap()
        d["bft"] = nc.dram_tensor(pref + "bft", [256, D], F32, kind="ExternalInput").ap()
        d["coeff"] = nc.dram_tensor(pref + "coeff", [16, 1], F32, kind="ExternalInput").ap()
        d["scal"] = nc.dram_tensor(pref + "scal", [16, 1], F32, kind="ExternalInput").ap()
        d["emat"] = nc.dram_tensor(pref + "emat", [16, 256], F32, kind="ExternalInput").ap()
    else:
        d["a2t"] = nc.dram_tensor(pref + "a2t", [128, NCH, 64], F32, kind="ExternalInput").ap()
        d["btr"] = nc.dram_tensor(pref + "btr", [64, D], F32, kind="ExternalInput").ap()
        d["gwt"] = nc.dram_tensor(pref + "gwt", [128, NCH, 4], F32, kind="ExternalInput").ap()
        d["gb"] = nc.dram_tensor(pref + "gb", [4, 1], F32, kind="ExternalInput").ap()
        d["erows"] = nc.dram_tensor(pref + "erows", [1, 4 * 64], BF16, kind="ExternalInput").ap()
        d["identin"] = nc.dram_tensor(pref + "identin", [128, 128], F32, kind="ExternalInput").ap()
        d["identbf"] = nc.dram_tensor(pref + "identbf", [128, 128], BF16, kind="ExternalInput").ap()
    return d


def emit_ffn_phase(P, c, ps, psb, sb, xn, layer, io):
    xin, xout = io["xin"], io["xout"]
    wt, biasm = io["wt"], io["biasm"]
    a2t = io["a2t"]
    if layer == 0:
        bft, coeff, scal, emat = io["bft"], io["coeff"], io["scal"], io["emat"]
    else:
        btr, gwt, gb, erows, identin, identbf = (io["btr"], io["gwt"], io["gb"],
                                                 io["erows"], io["identin"], io["identbf"])
    s_dma, s_pe, s_ve, s_se, s_wd, s_out = (c[k] for k in
                                            ("s_dma", "s_pe", "s_ve", "s_se", "s_wd", "s_out"))

    if True:
        @P.on("sync")
        def _(e):
            e.dma_start(out=sb["halfbias"][:], in_=biasm).then_inc(s_dma.sem, 16)
        s_dma.bump(16)

        if layer == 0:
            abfv = apv(sb["a_bf"]).rearrange("p (cb j n) -> p cb j n", cb=2, j=NCH)
            bftv = bft.rearrange("(cb p) n -> cb p n", p=128)

            @P.on("gpsimd")
            def _(e):
                for cb in range(2):
                    e.dma_start(out=abfv[:, cb, :, :],
                                in_=a2t[cb, :, :, :]).then_inc(s_wd.sem, 16)
                for cb in range(2):
                    e.dma_start(out=sb["bft_bf"][:, cb * D:(cb + 1) * D],
                                in_=bftv[cb, :, :]).then_inc(s_wd.sem, 16)
            s_wd.bump(16 * 4)

            @P.on("sync")
            def _(e):
                e.dma_start(out=sb["wsm"][:, 0:1], in_=coeff).then_inc(s_dma.sem, 16)
                e.dma_start(out=sb["wsm"][:, 1:2], in_=scal).then_inc(s_dma.sem, 16)
                e.dma_start(out=sb["wsm"][:, 2:258], in_=emat).then_inc(s_dma.sem, 16)
            s_dma.bump(16 * 3)
        else:
            abfv = apv(sb["a_bf"]).rearrange("p (j n) -> p j n", j=NCH)

            @P.on("gpsimd")
            def _(e):
                e.dma_start(out=abfv[:, :, :], in_=a2t).then_inc(s_wd.sem, 16)
                e.dma_start(out=sb["bt_bf"][:], in_=btr).then_inc(s_wd.sem, 16)
                e.dma_start(out=apv(sb["gw_bf"]).rearrange("p (j n) -> p j n", j=NCH),
                            in_=gwt).then_inc(s_wd.sem, 16)
            s_wd.bump(16 * 3)

            @P.on("sync")
            def _(e):
                e.dma_start(out=sb["gbcol"][:], in_=gb).then_inc(s_dma.sem, 16)
                e.dma_start(out=sb["erows"][:], in_=erows).then_inc(s_dma.sem, 16)
                e.dma_start(out=sb["identf"][:], in_=identin).then_inc(s_dma.sem, 16)
                e.dma_start(out=sb["identb"][:], in_=identbf).then_inc(s_dma.sem, 16)
            s_dma.bump(16 * 4)

        barrier(P, c)

        @P.on("scalar")
        def _(e):
            e.mul(sb["halfbias"][:], sb["halfbias"][:], 0.5).then_inc(s_se.sem, 1)
        s_se.bump()

        if layer == 0:
            @P.on("vector")
            def _(e):
                e.tensor_tensor(sb["wsm"][:, 0:1], sb["wsm"][:, 0:1], sb["wsm"][:, 1:2],
                                ALU.mult).then_inc(s_ve.sem, 1)
            w_ve = s_ve.bump()

            @P.on("tensor")
            def _(e, w_ve=w_ve):
                e.wait_ge(s_ve.sem, w_ve)
                e.matmul(ps[3][:, 0:1], sb["wsm"][:, 2:130], sb["wsm"][:, 0:1], start=True, stop=True)
                mm = e.matmul(ps[3][:, 1:2], sb["wsm"][:, 130:258], sb["wsm"][:, 0:1],
                              start=True, stop=True)
                mm.then_inc(s_pe.sem, 1)
            wexp_pe = s_pe.bump()

            @P.on("scalar")
            def _(e, wexp_pe=wexp_pe):
                e.wait_ge(s_pe.sem, wexp_pe)
                e.copy(sb["wexp"][:], ps[3][:, 0:2]).then_inc(s_se.sem, 1)
            s_se.bump()

        barrier(P, c)

        xin_v = xin.rearrange("(j p) t -> j p t", p=128)
        xout_v = xout.rearrange("(j p) t -> j p t", p=128)

        for g in range(2):
            toks = slice(g * G, (g + 1) * G)
            emit_ln(P, c, sb, ps, xin_v, toks, xn[g])
            xnv = xn[g].ap().rearrange("p (j t) -> p j t", j=NCH)

            if layer == 0:
                @P.on("tensor")
                def _(e, xnv=xnv):
                    for cb in range(2):
                        pt = ps[2 + cb]
                        mm = None
                        for j in range(NCH):
                            mm = e.matmul(pt[:], abfv[:, cb, j, :], xnv[:, j, :],
                                          start=(j == 0), stop=(j == NCH - 1))
                        mm.then_inc(s_pe.sem, 1)
                xa_pe = s_pe.bump(2)

                @P.on("scalar")
                def _(e, xa_pe=xa_pe):
                    e.wait_ge(s_pe.sem, xa_pe)
                    e.activation(out=sb["xaw"][:, 0:G], in_=ps[2][:], func=AF.Identity,
                                 scale=sb["wexp"][:, 0:1])
                    a = e.activation(out=sb["xaw"][:, G:2 * G], in_=ps[3][:], func=AF.Identity,
                                     scale=sb["wexp"][:, 1:2])
                    a.then_inc(s_se.sem, 1)
                s_se.bump()

                bftbv = apv(sb["bft_bf"]).rearrange("p (cb n) -> p cb n", cb=2)

                def extra_mm(e, d, pt, bftbv=bftbv):
                    mm = None
                    for cb in range(2):
                        mm = e.matmul(pt[:], bftbv[:, cb, 128 * d:128 * (d + 1)],
                                      sb["xaw"][:, cb * G:(cb + 1) * G],
                                      start=False, stop=(cb == 1))
                    return mm
            else:
                gwbv = apv(sb["gw_bf"]).rearrange("p (j n) -> p j n", j=NCH)

                @P.on("tensor")
                def _(e, xnv=xnv, gwbv=gwbv):
                    mm = None
                    for j in range(NCH):
                        mm = e.matmul(ps[2][0:4, :], gwbv[:, j, :], xnv[:, j, :],
                                      start=(j == 0), stop=(j == NCH - 1))
                    for j in range(NCH):
                        mm = e.matmul(ps[3][0:64, :], abfv[:, j, :], xnv[:, j, :],
                                      start=(j == 0), stop=(j == NCH - 1))
                    mm.then_inc(s_pe.sem, 1)
                gh_pe = s_pe.bump()

                @P.on("scalar")
                def _(e, gh_pe=gh_pe):
                    e.wait_ge(s_pe.sem, gh_pe)
                    e.activation(out=sb["gatef"][:], in_=ps[2][0:4, :], func=AF.Identity,
                                 bias=sb["gbcol"][:]).then_inc(s_se.sem, 1)
                gate_se = s_se.bump()

                @P.on("tensor")
                def _(e, gate_se=gate_se):
                    e.wait_ge(s_se.sem, gate_se)
                    mm = None
                    for i in range(NT):
                        mm = e.transpose(ps[4][:, 4 * i:4 * (i + 1)],
                                         sb["gatef"][0:4, 128 * i:128 * (i + 1)],
                                         sb["identf"][0:4, 0:4])
                    mm.then_inc(s_pe.sem, 1)
                gt_pe = s_pe.bump()

                @P.on("vector")
                def _(e, gt_pe=gt_pe):
                    e.wait_ge(s_pe.sem, gt_pe)
                    fsc = sb["fsc"]
                    fcols = sb["fcols"]
                    q_ve = c["q_ve"]
                    for i in range(NT):
                        s4 = ps[4][:, 4 * i:4 * (i + 1)]
                        e.reduce_max(fsc[:, 0:1], s4[:, 0:2], axis=AX.X)
                        e.reduce_max(fsc[:, 1:2], s4[:, 0:3], axis=AX.X)
                        e.reduce_max(fsc[:, 2:3], s4[:, 1:4], axis=AX.X)
                        q_ve.seq(e, e.reduce_max(fsc[:, 3:4], s4[:, 2:4], axis=AX.X))
                        e.tensor_tensor(fcols[:, 3 * i + 0:3 * i + 1], fsc[:, 2:3], s4[:, 0:1], ALU.is_gt)
                        e.tensor_tensor(fcols[:, 3 * i + 1:3 * i + 2], fsc[:, 3:4], fsc[:, 0:1], ALU.is_gt)
                        q_ve.seq(e, e.tensor_tensor(fcols[:, 3 * i + 2:3 * i + 3], s4[:, 3:4], fsc[:, 1:2], ALU.is_gt))
                        e.sem_inc(s_ve.sem, 1)
                flag_ve = s_ve.bump(NT)

                @P.on("tensor")
                def _(e, flag_ve=flag_ve):
                    e.wait_ge(s_ve.sem, flag_ve)
                    mm = None
                    for gg in range(2):
                        for i in range(NT):
                            col = (gg * NT + i) * 128
                            mm = e.transpose(psb[0:1, col:col + 128],
                                             sb["fcols"][:, 3 * i + gg:3 * i + gg + 1],
                                             sb["identb"][:])
                    mm.then_inc(s_pe.sem, 1)
                fr_pe = s_pe.bump()

                @P.on("vector")
                def _(e, fr_pe=fr_pe):
                    e.wait_ge(s_pe.sem, fr_pe)
                    e.tensor_copy(sb["flagrows"][0:1, 0:2 * G], psb[0:1, 0:2 * G]).then_inc(s_ve.sem, 1)
                fr2_ve = s_ve.bump()

                @P.on("tensor")
                def _(e, fr2_ve=fr2_ve):
                    e.wait_ge(s_ve.sem, fr2_ve)
                    mm = None
                    for i in range(NT):
                        mm = e.transpose(psb[0:1, i * 128:(i + 1) * 128],
                                         sb["fcols"][:, 3 * i + 2:3 * i + 3],
                                         sb["identb"][:])
                    mm.then_inc(s_pe.sem, 1)
                fr3_pe = s_pe.bump()

                @P.on("vector")
                def _(e, fr3_pe=fr3_pe):
                    e.wait_ge(s_pe.sem, fr3_pe)
                    e.tensor_copy(sb["flagrows"][0:1, 2 * G:3 * G], psb[0:1, 0:G]).then_inc(s_ve.sem, 1)
                frows_ve = s_ve.bump()

                @P.on("tensor")
                def _(e, frows_ve=frows_ve):
                    e.wait_ge(s_ve.sem, frows_ve)
                    e.matmul(ps[2][0:64, :], sb["erows"][0:1, 0:64], sb["onesrow_bf"][:],
                             start=True, stop=False)
                    mm = None
                    for gg in range(3):
                        mm = e.matmul(ps[2][0:64, :], sb["erows"][0:1, 64 * (gg + 1):64 * (gg + 2)],
                                      sb["flagrows"][0:1, gg * G:(gg + 1) * G],
                                      start=False, stop=(gg == 2))
                    mm.then_inc(s_pe.sem, 1)
                mb_pe = s_pe.bump()

                @P.on("vector")
                def _(e, mb_pe=mb_pe):
                    e.wait_ge(s_pe.sem, mb_pe)
                    c["q_ve"].seq(e, e.tensor_copy(sb["maskb"][:], ps[2][0:64, :]))
                    e.tensor_tensor(sb["hm"][:], ps[3][0:64, :], sb["maskb"][:],
                                    ALU.mult).then_inc(s_ve.sem, 1)
                s_ve.bump()

                def extra_mm(e, d, pt):
                    return e.matmul(pt[:], sb["bt_bf"][0:64, 128 * d:128 * (d + 1)],
                                    sb["hm"][0:64, :], start=False, stop=True)

            barrier(P, c)

            emit_dense_residual(P, c, sb, ps, xn=xn[g], wt_dram=wt, xin_v=xin_v,
                                xout_v=xout_v, toks=toks, extra_mm=extra_mm,
                                scale=0.5, halfbias=sb["halfbias"])
            barrier(P, c)


def emit_final_wait(P, c):
    @P.on("sync")
    def _(e):
        for sl in c["sl_o"]:
            if sl.n:
                e.wait_ge(sl.sem, sl.n)
        if c["s_out"].n:
            e.wait_ge(c["s_out"].sem, c["s_out"].n)
        if c["s_ge"].n:
            e.wait_ge(c["s_ge"].sem, c["s_ge"].n)


def alloc_sb(nc, ctx, spec):
    E = ctx.enter_context
    out = {}
    for k, (shape, dt) in spec.items():
        out[k] = E(nc.sbuf_tensor("t_" + k, shape, dt))
    return out


def build_ffn_launch(layer):
    nc = bass.Bass("TRN2", target_bir_lowering=False)
    io = ffn_dram_decls(nc, layer)
    io["xin"] = nc.dram_tensor("xin", [D, 2 * G], F32, kind="ExternalInput").ap()
    io["xout"] = nc.dram_tensor("xout", [D, 2 * G], F32, kind="ExternalOutput").ap()
    with contextlib.ExitStack() as ctx:
        E = ctx.enter_context
        c = make_ctx(nc, ctx)
        ps = [E(nc.psum_tensor(f"psf{i}", [128, 512], F32)) for i in range(6)]
        psb = E(nc.psum_tensor("psb", [128, 1024], BF16))
        sb = sb_common(nc, ctx)
        sb.update(alloc_sb(nc, ctx, FFN_SB[layer]))
        if layer == 0:
            sb["xaw"] = E(nc.sbuf_tensor("t_xaw", [128, 2 * G], BF16))
        xn = [E(nc.sbuf_tensor("xn0", [128, NCH * G], BF16))] * 2
        block = E(nc.Block())
        P = Prog()
        emit_consts(P, c, sb)
        emit_ffn_phase(P, c, ps, psb, sb, xn, layer, io)
        emit_final_wait(P, c)
        P.emit(block)
    return nc



def _shard_x(x_full, core):
    seq, parity = core // 2, core % 2
    tiles = core_token_tiles(parity)
    idx = np.concatenate([np.arange(128 * t, 128 * (t + 1)) for t in tiles])
    return np.ascontiguousarray(x_full[seq, idx, :].T), idx


def _unshard_x(x_full, core, xc):
    seq, parity = core // 2, core % 2
    tiles = core_token_tiles(parity)
    idx = np.concatenate([np.arange(128 * t, 128 * (t + 1)) for t in tiles])
    x_full[seq, idx, :] = xc.T


def _chunkmajor(v):
    return np.ascontiguousarray(v.reshape(NCH, 128).T)


def run_ffn_layer(x_full, layer, weights):
    nc = _get_prog(("ffn", layer), lambda: build_ffn_launch(layer))
    in_maps = []
    shards = []
    for core in range(8):
        xc, idx = _shard_x(x_full, core)
        shards.append(idx)
        im = {"xin": xc}
        im.update(weights)
        in_maps.append(im)
    res = run_bass_kernel_spmd(nc, in_maps, core_ids=list(range(8)))
    out = x_full.copy()
    for core in range(8):
        _unshard_x(out, core, np.asarray(res.results[core]["xout"]))
    return out


_PROG_CACHE = {}


def _get_prog(key, builder):
    if key not in _PROG_CACHE:
        _PROG_CACHE[key] = builder()
    return _PROG_CACHE[key]


def _wblocks(wT):
    ndim = wT.shape[1]
    ndc = ndim // 128
    return np.ascontiguousarray(
        wT.reshape(NCH, 128, ndc, 128).transpose(2, 1, 0, 3))


def _pjn(wT):
    n = wT.shape[1]
    return np.ascontiguousarray(wT.reshape(NCH, 128, n).transpose(1, 0, 2))


def ffn0_weights(inputs):
    W = np.asarray(inputs["L0_ffn_W"], np.float32)
    A = np.asarray(inputs["L0_ffn_A"], np.float32)
    Bf = np.asarray(inputs["L0_ffn_Bf"], np.float32)
    emat = np.zeros((16, 256), np.float32)
    for e_ in range(16):
        emat[e_, e_ * 16:(e_ + 1) * 16] = 1.0
    a2t = np.ascontiguousarray(A.reshape(256, D).T)
    return {
        "wt": _wblocks(np.ascontiguousarray(W.T)),
        "biasm": _chunkmajor(np.asarray(inputs["L0_ffn_bias"], np.float32)),
        "a2t": _wblocks(a2t),
        "bft": np.ascontiguousarray(Bf.transpose(0, 2, 1).reshape(256, D)),
        "coeff": np.ascontiguousarray(np.asarray(inputs["L0_ffn_coeffs"], np.float32).reshape(16, 1)),
        "scal": np.ascontiguousarray(np.asarray(inputs["L0_ffn_scales"], np.float32).reshape(16, 1)),
        "emat": emat,
    }


def ffn1_weights(inputs):
    erows = np.zeros((1, 4 * 64), BF)
    erows[0, 0:8] = 1
    erows[0, 64 + 8:64 + 16] = 1
    erows[0, 128 + 16:128 + 32] = 1
    erows[0, 192 + 32:192 + 64] = 1
    return {
        "wt": _wblocks(np.ascontiguousarray(np.asarray(inputs["L1_ffn_W"], np.float32).T)),
        "biasm": _chunkmajor(np.asarray(inputs["L1_ffn_b"], np.float32)),
        "a2t": _pjn(np.ascontiguousarray(np.asarray(inputs["L1_ffn_A"], np.float32).T)),
        "btr": np.ascontiguousarray(np.asarray(inputs["L1_ffn_B"], np.float32).T),
        "gwt": _pjn(np.ascontiguousarray(np.asarray(inputs["L1_ffn_gate_w"], np.float32).T)),
        "gb": np.ascontiguousarray(np.asarray(inputs["L1_ffn_gate_b"], np.float32).reshape(4, 1)),
        "erows": erows,
        "identin": np.eye(128, dtype=np.float32),
        "identbf": np.eye(128, dtype=BF),
    }


def carve(arena, spec, base_bytes=0):
    views = {}
    cur = 0
    for k, (shape, dt) in spec.items():
        assert dt == arena.dtype if hasattr(arena, "dtype") else True
        views[k] = apv(arena)[0:shape[0], cur:cur + shape[1]]
        cur += shape[1]
    return views, cur


def build_megakernel():
    nc = bass.Bass("TRN2", target_bir_lowering=False)
    io_f0 = ffn_dram_decls(nc, 0, pref="f0")
    io_f1 = ffn_dram_decls(nc, 1, pref="f1")
    io_qkv = qkv_dram_decls(nc)
    io_att = attc_dram_decls(nc)
    io_a1 = a1_dram_decls(nc)
    xin = nc.dram_tensor("xin", [D, 2 * G], F32, kind="ExternalInput").ap()
    xout = nc.dram_tensor("xout", [D, 2 * G], F32, kind="ExternalOutput").ap()
    xa = nc.dram_tensor("xa", [D, 2 * G], F32, kind="Internal").ap()
    xb = nc.dram_tensor("xb", [D, 2 * G], F32, kind="Internal").ap()
    qn_i = nc.dram_tensor("qn", [NH, 128, 2 * G], BF16, kind="Internal").ap()
    kn_t = nc.dram_tensor("kn", [NH * 128, 2 * G], BF16)
    vt_t = nc.dram_tensor("vt", [2 * G, D], BF16)
    kng_t = nc.dram_tensor("kng", [2 * NH * 128, 2 * G], BF16)
    vtg_t = nc.dram_tensor("vtg", [2 * 2 * G, D], BF16)
    kn_i = kn_t.ap().rearrange("(h p) t -> h p t", p=128)
    vt_i = vt_t.ap()
    kng = kng_t.ap()
    vtg = vtg_t.ap()

    with contextlib.ExitStack() as ctx:
        E = ctx.enter_context
        c = make_ctx(nc, ctx)
        ps = [E(nc.psum_tensor(f"psf{i}", [128, 512], F32)) for i in range(6)]
        psb = [E(nc.psum_tensor(f"psbt{i}", [128, 1024], BF16)) for i in range(2)]
        com = sb_common(nc, ctx)
        A0 = E(nc.sbuf_tensor("A0", [128, NCH * G], BF16))
        A1 = E(nc.sbuf_tensor("A1", [128, NCH * G], BF16))
        A2 = E(nc.sbuf_tensor("A2", [128, NCH * G], BF16))
        A3 = E(nc.sbuf_tensor("A3", [128, 9216], BF16))
        SCB = E(nc.sbuf_tensor("SCB", [128, 16384], BF16))
        SCF = E(nc.sbuf_tensor("SCF", [128, 2304], F32))

        def phase_sb(spec):
            sb = dict(com)
            curb = curf = 0
            for k, (shape, dt) in spec.items():
                if dt == BF16:
                    sb[k] = apv(SCB)[0:shape[0], curb:curb + shape[1]]
                    curb += shape[1]
                else:
                    sb[k] = apv(SCF)[0:shape[0], curf:curf + shape[1]]
                    curf += shape[1]
                assert curb <= 16384 and curf <= 2304, (k, curb, curf)
            return sb

        block = E(nc.Block())
        P = Prog()
        com2 = dict(com)
        emit_consts(P, c, com2)

        sb = phase_sb(FFN_SB[0])
        sb["xaw"] = apv(A3)[:, 0:1024]
        emit_ffn_phase(P, c, ps, psb[0], sb, [A0, A0], 0,
                       dict(io_f0, xin=xin, xout=xa))

        rg = [[0, 1], [2, 3], [4, 5], [6, 7]]

        def hook_kn(P, c):
            s_ge = c["s_ge"]

            @P.on("gpsimd")
            def _(e, w=c["s_out"].n):
                e.wait_ge(c["s_out"].sem, w)
                e.collective_compute("AllGather", ALU.bypass, replica_groups=rg,
                                     ins=[kn_t.ap().opt()],
                                     outs=[kng_t.ap().opt()]).then_inc(s_ge.sem)
            c["m_ge_k"] = s_ge.bump()

        def hook_vt(P, c):
            s_ge = c["s_ge"]

            @P.on("gpsimd")
            def _(e, w0=c["sl_o"][0].n, w1=c["sl_o"][1].n):
                e.wait_ge(c["sl_o"][0].sem, w0)
                e.wait_ge(c["sl_o"][1].sem, w1)
                e.collective_compute("AllGather", ALU.bypass, replica_groups=rg,
                                     ins=[vt_t.ap().opt()],
                                     outs=[vtg_t.ap().opt()]).then_inc(s_ge.sem)
            c["m_ge_v"] = s_ge.bump()

        sb = phase_sb(QKV_SB)
        sb["_psb0"], sb["_psb1"] = psb[0], psb[1]
        emit_qkv_phase(P, c, ps, sb, A0, A1, A2,
                       dict(io_qkv, xin=xa, qn=qn_i, kn=kn_i, vt=vt_i,
                            hook_kn=hook_kn, hook_vt=hook_vt))

        sb = phase_sb(ATTC_SB)
        emit_attc_phase(P, c, ps, sb, A1, A2,
                        apv(A3)[:, 0:4096], apv(A3)[:, 4096:8192],
                        apv(A3)[:, 8192:9216],
                        dict(io_att, xin=xa, xout=xb, qn=qn_i,
                             kng=kng.rearrange("(r h p) t -> r h p t", r=2, p=128),
                             vtg=vtg.rearrange("(r t) dd -> r t dd", r=2)))

        sb = phase_sb(FFN_SB[0])
        sb["xaw"] = apv(A3)[:, 0:1024]
        emit_ffn_phase(P, c, ps, psb[0], sb, [A0, A0], 0,
                       dict(io_f0, xin=xb, xout=xa))

        sb = phase_sb(FFN_SB[1])
        emit_ffn_phase(P, c, ps, psb[0], sb, [A0, A0], 1,
                       dict(io_f1, xin=xa, xout=xb))

        sb = phase_sb(A1_SB)
        emit_a1_phase(P, c, ps, psb, sb, A0,
                      apv(A1)[:, 0:4096], apv(A1)[:, 4096:8192],
                      apv(A1)[:, 8192:12288], apv(A1)[:, 12288:16384],
                      apv(A2)[0:8, 0:8192], apv(A2)[0:8, 8192:16384],
                      apv(A3)[:, 0:1024], apv(A3)[:, 1024:2048],
                      apv(SCF)[0:128, 2288:2304],
                      dict(io_a1, xin=xb, xout=xa))

        sb = phase_sb(FFN_SB[1])
        emit_ffn_phase(P, c, ps, psb[0], sb, [A0, A0], 1,
                       dict(io_f1, xin=xa, xout=xout))

        emit_final_wait(P, c)
        P.emit(block)
    return nc


FUSED = True


def kernel(**inputs):
    x = np.asarray(inputs["L0_attn_hidden_states"], dtype=np.float32)
    w_ffn0 = ffn0_weights(inputs)
    w_ffn1 = ffn1_weights(inputs)
    if not FUSED:
        x = run_ffn_layer(x, 0, w_ffn0)
        x = run_attn0(x, inputs)
        x = run_ffn_layer(x, 0, w_ffn0)
        x = run_ffn_layer(x, 1, w_ffn1)
        x = run_attn1(x, inputs)
        x = run_ffn_layer(x, 1, w_ffn1)
        return x

    w_a0 = attn0_weights(inputs)
    w_a1 = attn1_weights(inputs)
    nc = _get_prog(("mega",), build_megakernel)
    in_maps = []
    for core in range(8):
        seq, par = core // 2, core % 2
        xc, _ = _shard_x(x, core)
        im = {"xin": xc}
        im.update({"f0" + k: v for k, v in w_ffn0.items()})
        im.update({"f1" + k: v for k, v in w_ffn1.items()})
        im.update({"wqkvt": w_a0["wqkvt"], "i_qlnw": w_a0["qlnw"], "i_qlnb": w_a0["qlnb"],
                   "i_klnw": w_a0["klnw"], "i_klnb": w_a0["klnb"], "identbf2": w_a0["identbf"]})
        im.update({"woutt": w_a0["woutt"], "colt": w_a0["colt"], "colrow": w_a0["colrow"],
                   "maskab": np.stack([w_a0["maskab"][par], w_a0["maskab"][1 - par]]),
                   "maskq": np.stack([w_a0["maskq"][par], w_a0["maskq"][1 - par]]),
                   "identf": w_a0["identf"]})
        im.update(w_a1)
        in_maps.append(im)
    res = run_bass_kernel_spmd(nc, in_maps, core_ids=list(range(8)))
    out = x.copy()
    for core in range(8):
        _unshard_x(out, core, np.asarray(res.results[core]["xout"]))
    return out



QKV_SB = {
    "clip": ([128, 2], F32), "qlnw": ([128, NCH], F32), "qlnb": ([128, NCH], F32),
    "klnw": ([128, NCH], F32), "klnb": ([128, NCH], F32),
    "vstage": ([128, 2 * D], BF16), "identb": ([128, 128], BF16),
}


def qkv_dram_decls(nc):
    d = {}
    d["wqkvt"] = nc.dram_tensor("wqkvt", [96, 128, NCH, 128], F32, kind="ExternalInput").ap()
    for nm in ("qlnw", "qlnb", "klnw", "klnb"):
        d[nm] = nc.dram_tensor("i_" + nm, [128, NCH], F32, kind="ExternalInput").ap()
    d["identbf2"] = nc.dram_tensor("identbf2", [128, 128], BF16, kind="ExternalInput").ap()
    return d


def emit_qkv_phase(P, c, ps, sb, xn, qf, kf, io):
    xin = io["xin"]
    wqkvt = io["wqkvt"]
    qlnw, qlnb, klnw, klnb = io["qlnw"], io["qlnb"], io["klnw"], io["klnb"]
    identbf = io["identbf2"]
    qn_o, kn_o, vt_o = io["qn"], io["kn"], io["vt"]
    s_dma, s_pe, s_ve, s_se, s_wd, s_out = (c[k] for k in
                                            ("s_dma", "s_pe", "s_ve", "s_se", "s_wd", "s_out"))
    if True:
        @P.on("vector")
        def _(e):
            e.memset(sb["clip"][:, 0:1], -CLIP)
            e.memset(sb["clip"][:, 1:2], CLIP).then_inc(s_ve.sem, 1)
        s_ve.bump()

        @P.on("sync")
        def _(e):
            for nm, t in (("qlnw", qlnw), ("qlnb", qlnb), ("klnw", klnw), ("klnb", klnb)):
                e.dma_start(out=sb[nm][:], in_=t).then_inc(s_dma.sem, 16)
            e.dma_start(out=sb["identb"][:], in_=identbf).then_inc(s_dma.sem, 16)
        s_dma.bump(16 * 5)

        barrier(P, c)

        @P.on("scalar")
        def _(e):
            sc = 1.0 / np.sqrt(128.0)
            c["q_se"].seq(e, e.mul(sb["qlnw"][:], sb["qlnw"][:], sc))
            c["q_se"].seq(e, e.mul(sb["qlnb"][:], sb["qlnb"][:], sc))
        barrier(P, c)

        xin_v = xin.rearrange("(j p) t -> j p t", p=128)
        wbv = apv(sb["wbuf"]).rearrange("p (i j n) -> p i j n", i=2, j=NCH)
        xnv = apv(xn).rearrange("p (j t) -> p j t", j=NCH)
        sl_w = c["sl_w"]

        for g in range(2):
            toks = slice(g * G, (g + 1) * G)
            emit_ln(P, c, sb, ps, xin_v, toks, xn)
            if P2_STAGE < 1:
                continue

            arenas = (qf, kf)
            pe_marks = [None] * 64
            evac_done = [None] * 64
            wwaits = [sl_w[d % 2].bump(16) for d in range(64)]

            @P.on("gpsimd")
            def _(e, wwaits=wwaits, pe_marks=pe_marks, g=g):
                for d in range(64):
                    if d >= 2:
                        e.wait_ge(s_pe.sem, pe_marks[d - 2])
                    e.dma_start(out=wbv[:, d % 2, :, :],
                                in_=wqkvt[d, :, :, :]).then_inc(sl_w[d % 2].sem, 16)

            for d in range(64):
                @P.on("tensor")
                def _(e, d=d, ww=wwaits[d], ed=(evac_done[d - 2] if d >= 2 else 0)):
                    e.wait_ge(sl_w[d % 2].sem, ww)
                    if d >= 2:
                        e.wait_ge(s_ve.sem, ed)
                    pt = ps[d % 2]
                    mm = None
                    for j in range(NCH):
                        mm = e.matmul(pt[:], wbv[:, d % 2, j, :], xnv[:, j, :],
                                      start=(j == 0), stop=(j == NCH - 1))
                    mm.then_inc(s_pe.sem, 1)
                pe_marks[d] = s_pe.bump()

                @P.on("vector")
                def _(e, d=d, pe_cnt=pe_marks[d]):
                    e.wait_ge(s_pe.sem, pe_cnt)
                    ar = arenas[d // NCH].ap().rearrange("p (j t) -> p j t", j=NCH)
                    e.tensor_scalar(ar[:, d % NCH, :], ps[d % 2][:],
                                    sb["clip"][:, 0:1], sb["clip"][:, 1:2],
                                    op0=ALU.max, op1=ALU.min).then_inc(s_ve.sem, 1)
                evac_done[d] = s_ve.bump()

            barrier(P, c)

            if P2_STAGE < 2:
                continue
            for ai, (arena, w_t, b_t, out_t) in enumerate(
                    ((qf, sb["qlnw"], sb["qlnb"], qn_o), (kf, sb["klnw"], sb["klnb"], kn_o))):
                emit_norm(P, c, sb, ps, arena, wb=(w_t, b_t))
                arv = apv(arena).rearrange("p (j t) -> p j t", j=NCH)

                @P.on("sync")
                def _(e, arv=arv, out_t=out_t, g=g):
                    for j in range(NCH):
                        e.dma_start(out=out_t[j, :, slice(g * G, (g + 1) * G)],
                                    in_=arv[:, j, :]).then_inc(s_out.sem, 16)
                s_out.bump(16 * NCH)
                if g == 1 and ai == 1 and "hook_kn" in io:
                    io["hook_kn"](P, c)

            if P2_STAGE < 3:
                continue
            vf = qf
            vfv = apv(vf).rearrange("p (j t) -> p j t", j=NCH)
            pe_v = [None] * NCH
            ve_v = [None] * NCH
            vwaits = [sl_w[d % 2].bump(16) for d in range(NCH)]

            @P.on("gpsimd")
            def _(e, vwaits=vwaits, pe_v=pe_v, g=g):
                for d in range(NCH):
                    if d >= 2:
                        e.wait_ge(s_pe.sem, pe_v[d - 2])
                    e.dma_start(out=wbv[:, d % 2, :, :],
                                in_=wqkvt[64 + d, :, :, :]).then_inc(sl_w[d % 2].sem, 16)

            for d in range(NCH):
                @P.on("tensor")
                def _(e, d=d, vw=vwaits[d], ev=(ve_v[d - 2] if d >= 2 else 0)):
                    e.wait_ge(sl_w[d % 2].sem, vw)
                    if d >= 2:
                        e.wait_ge(s_ve.sem, ev)
                    pt = ps[d % 2]
                    mm = None
                    for j in range(NCH):
                        mm = e.matmul(pt[:], wbv[:, d % 2, j, :], xnv[:, j, :],
                                      start=(j == 0), stop=(j == NCH - 1))
                    mm.then_inc(s_pe.sem, 1)
                pe_v[d] = s_pe.bump()

                @P.on("vector")
                def _(e, d=d, pe_cnt=pe_v[d]):
                    e.wait_ge(s_pe.sem, pe_cnt)
                    e.tensor_scalar(vfv[:, d, :], ps[d % 2][:],
                                    sb["clip"][:, 0:1], sb["clip"][:, 1:2],
                                    op0=ALU.max, op1=ALU.min).then_inc(s_ve.sem, 1)
                ve_v[d] = s_ve.bump()

            barrier(P, c)

            if P2_STAGE < 4:
                barrier(P, c)
                continue
            vstage = sb["vstage"]
            vsv2 = apv(vstage).rearrange("p (i n) -> p i n", i=2)
            psb_v = (sb["_psb0"], sb["_psb1"])
            vti = 0
            vt_ve = []
            for i in range(NT):
                for h in range(NCH):
                    @P.on("tensor")
                    def _(e, h=h, i=i, vti=vti, wv=(vt_ve[vti - 2] if vti >= 2 else 0)):
                        if vti >= 2:
                            e.wait_ge(s_ve.sem, wv)
                        mm = e.transpose(psb_v[vti % 2][:, 0:128],
                                         vfv[:, h, 128 * i:128 * (i + 1)], sb["identb"][:])
                        mm.then_inc(s_pe.sem, 1)
                    vt_pe = s_pe.bump()

                    @P.on("vector")
                    def _(e, h=h, i=i, vti=vti, vt_pe=vt_pe,
                          ow=(c["sl_o"][i % 2].n if h == 0 else 0)):
                        e.wait_ge(s_pe.sem, vt_pe)
                        if h == 0 and i >= 2:
                            e.wait_ge(c["sl_o"][i % 2].sem, ow)
                        e.tensor_copy(vsv2[:, i % 2, 128 * h:128 * (h + 1)],
                                      psb_v[vti % 2][:, 0:128]).then_inc(s_ve.sem, 1)
                    ve_cnt = s_ve.bump()
                    vt_ve.append(ve_cnt)
                    vti += 1

                c["sl_o"][i % 2].bump(16)

                @P.on("sync")
                def _(e, i=i, ve_cnt=ve_cnt, g=g):
                    e.wait_ge(s_ve.sem, ve_cnt)
                    e.dma_start(out=vt_o[slice((g * 4 + i) * 128, (g * 4 + i + 1) * 128), :],
                                in_=vsv2[:, i % 2, :]).then_inc(c["sl_o"][i % 2].sem, 16)

            if g == 1 and "hook_vt" in io:
                io["hook_vt"](P, c)
            barrier(P, c)


def build_qkv_launch():
    nc = bass.Bass("TRN2", target_bir_lowering=False)
    io = qkv_dram_decls(nc)
    io["xin"] = nc.dram_tensor("xin", [D, 2 * G], F32, kind="ExternalInput").ap()
    io["qn"] = nc.dram_tensor("qn", [NH, 128, 2 * G], BF16, kind="ExternalOutput").ap()
    io["kn"] = nc.dram_tensor("kn", [NH, 128, 2 * G], BF16, kind="ExternalOutput").ap()
    io["vt"] = nc.dram_tensor("vt", [2 * G, D], BF16, kind="ExternalOutput").ap()
    with contextlib.ExitStack() as ctx:
        E = ctx.enter_context
        c = make_ctx(nc, ctx)
        ps = [E(nc.psum_tensor(f"psf{i}", [128, 512], F32)) for i in range(6)]
        sb = sb_common(nc, ctx)
        sb.update(alloc_sb(nc, ctx, QKV_SB))
        sb["_psb0"] = E(nc.psum_tensor("psb_v0", [128, 1024], BF16))
        sb["_psb1"] = E(nc.psum_tensor("psb_v1", [128, 1024], BF16))
        xn = E(nc.sbuf_tensor("xn0", [128, NCH * G], BF16))
        qf = E(nc.sbuf_tensor("qf", [128, NCH * G], BF16))
        kf = E(nc.sbuf_tensor("kf", [128, NCH * G], BF16))
        block = E(nc.Block())
        P = Prog()
        emit_consts(P, c, sb)
        emit_qkv_phase(P, c, ps, sb, xn, qf, kf, io)
        emit_final_wait(P, c)
        P.emit(block)
    return nc


ATTC_SB = {
    "crow": ([1, 2 * S], BF16), "mcol": ([128, 8], F32), "nmb": ([1, 2 * G], BF16),
    "colt": ([128, NH * 16], F32),
    "maskab": ([128, 2 * 256], F32), "maskq": ([128, 2 * 256], F32),
    "identf": ([128, 128], F32),
}


def attc_dram_decls(nc):
    d = {}
    d["woutt"] = nc.dram_tensor("woutt", [NCH, 128, NCH, 128], F32, kind="ExternalInput").ap()
    d["colt"] = nc.dram_tensor("colt", [128, NH * 16], F32, kind="ExternalInput").ap()
    d["colrow"] = nc.dram_tensor("colrow", [NH, S], BF16, kind="ExternalInput").ap()
    d["maskab"] = nc.dram_tensor("maskab", [2, 128, 256], F32, kind="ExternalInput").ap()
    d["maskq"] = nc.dram_tensor("maskq", [2, 128, 256], F32, kind="ExternalInput").ap()
    d["identf"] = nc.dram_tensor("identf", [128, 128], F32, kind="ExternalInput").ap()
    return d


def emit_attc_phase(P, c, ps, sb, qn_sb, ob, ksb, vsb, expt, io):
    xin, xout = io["xin"], io["xout"]
    qn_i = io["qn"]
    kng = io["kng"]
    vtg = io["vtg"]
    woutt, colt, colrow = io["woutt"], io["colt"], io["colrow"]
    maskab, maskq, identf = io["maskab"], io["maskq"], io["identf"]
    psm = ps[4][0:1, :]
    psz = ps[3][0:1, :]
    s_dma, s_pe, s_ve, s_se, s_wd, s_out = (c[k] for k in
                                            ("s_dma", "s_pe", "s_ve", "s_se", "s_wd", "s_out"))
    if True:
        @P.on("sync")
        def _(e):
            e.dma_start(out=sb["colt"][:], in_=colt).then_inc(s_dma.sem, 16)
            e.dma_start(out=apv(sb["maskab"]).rearrange("p (g n) -> p g n", g=2),
                        in_=maskab.rearrange("g p n -> p g n")).then_inc(s_dma.sem, 16)
            e.dma_start(out=apv(sb["maskq"]).rearrange("p (g n) -> p g n", g=2),
                        in_=maskq.rearrange("g p n -> p g n")).then_inc(s_dma.sem, 16)
            e.dma_start(out=sb["identf"][:], in_=identf).then_inc(s_dma.sem, 16)
        s_dma.bump(16 * 4)

        barrier(P, c)

        xin_v = xin.rearrange("(j p) t -> j p t", p=128)
        xout_v = xout.rearrange("(j p) t -> j p t", p=128)
        qnv = apv(qn_sb).rearrange("p (h t) -> p h t", h=NH)
        obv = apv(ob).rearrange("p (h t) -> p h t", h=NH)
        ksv = apv(ksb).rearrange("p (i t) -> p i t", i=2)
        vsv = apv(vsb).rearrange("p (i t) -> p i t", i=2)
        etv = apv(expt).rearrange("p (i t) -> p i t", i=2)
        crv = apv(sb["crow"]).rearrange("o (i t) -> o i t", i=2)
        rows = sb["rows"]
        nmrow_f = rows[0:1, 0:G]
        zrow = rows[0:1, G:2 * G]

        sl_k = c["sl_x"][0:2]
        sl_v = c["sl_x"][2:4]
        sl_c = c["sl_c"]

        for g in ATTN_GROUPS:
            kmax = 8 + 8 * g
            toks = slice(g * G, (g + 1) * G)

            @P.on("sync")
            def _(e, g=g, toks=toks):
                e.dma_start(out=qnv[:, :, :], in_=qn_i[:, :, toks].rearrange("h p t -> p h t")).then_inc(s_dma.sem, 16)
            s_dma.bump(16)
            barrier(P, c)

            pe_last = [0] * NH
            kw = [None] * NH
            vw = [None] * NH
            cw = [None] * NH
            for h in range(NH):
                kw[h] = sl_k[h % 2].bump(16 * 2 * (g + 1))
                vw[h] = sl_v[h % 2].bump(16 * 2 * (g + 1))
                cw[h] = sl_c[h % 2].bump(16)

            ks6 = apv(ksb).rearrange("p (i gg li r tb) -> p i gg li r tb",
                                     i=2, gg=2, li=4, r=2)
            vs6 = apv(vsb).rearrange("p (i gg li r dd) -> p i gg li r dd",
                                     i=2, gg=2, li=4, r=2)
            vtg_v = vtg.rearrange("r (li2 p) dd -> r li2 p dd", p=128)

            @P.on("sync")
            def _(e, g=g, kmax=kmax, pe_last=pe_last,
                  gk=c.get("m_ge_k", 0), gv=c.get("m_ge_v", 0)):
                def k_loads(h):
                    for gg in range(g + 1):
                        for r in range(2):
                            rd = r if gg == 0 else 1 - r
                            e.dma_start(
                                out=ks6[:, h % 2, gg, :, rd, :],
                                in_=kng[r, h, :, 512 * gg:512 * (gg + 1)].rearrange(
                                    "p (li tb) -> p li tb", li=4)).then_inc(sl_k[h % 2].sem, 16)

                def v_loads(h):
                    for gg in range(g + 1):
                        for r in range(2):
                            rd = r if gg == 0 else 1 - r
                            e.dma_start(
                                out=vs6[:, h % 2, gg, :, rd, :],
                                in_=vtg_v[r, 4 * gg:4 * (gg + 1), :, 128 * h:128 * (h + 1)].rearrange(
                                    "li p dd -> p li dd")).then_inc(sl_v[h % 2].sem, 16)

                def c_loads(h):
                    e.dma_start(out=crv[0:1, h % 2, 0:128 * kmax],
                                in_=colrow[h:h + 1, 0:128 * kmax]).then_inc(sl_c[h % 2].sem, 16)

                if gk:
                    e.wait_ge(c["s_ge"].sem, gk)
                for h in (0, 1):
                    k_loads(h)
                    c_loads(h)
                if gv:
                    e.wait_ge(c["s_ge"].sem, gv)
                for h in (0, 1):
                    v_loads(h)
                for h in range(2, NH):
                    e.wait_ge(s_pe.sem, pe_last[h - 2])
                    k_loads(h)
                    v_loads(h)
                    c_loads(h)

            nmrow2 = [rows[0:1, 0:G], rows[0:1, 2 * G:3 * G]]
            zrow2 = [rows[0:1, G:2 * G], rows[0:1, 3 * G:4 * G]]
            nmbv = [apv(sb["nmb"])[0:1, 0:G], apv(sb["nmb"])[0:1, G:2 * G]]
            zbv2 = [apv(sb["tmp"])[:, 0:G], apv(sb["tmp"])[:, G:2 * G]]
            psmv = [ps[4][0:1, :], ps[5][0:1, :]]
            pszv = [ps[0][0:1, :], ps[1][0:1, :]]
            pvb = [ps[2], ps[3]]
            scb = [ps[4], ps[5]]
            zbb = [ps[0], ps[1]]

            for hj in range(0, NH, 2):
                hpair = (hj, hj + 1)
                mx_fin = [[0] * 4, [0] * 4]
                tr_pe = [[0] * 4, [0] * 4]
                for s_ in range(4):
                    nk = 128 * (2 * s_ + 2 + 8 * g)
                    npc = (nk + 511) // 512
                    for hp in range(2):
                        h = hpair[hp]
                        redv = [s_ve.bump() for _ in range(npc)]
                        mx_fin[hp][s_] = s_ve.bump()

                        @P.on("tensor")
                        def _(e, g=g, h=h, hp=hp, s_=s_, npc=npc, nk=nk,
                              kwh=kw[h], cwh=cw[h], redv=tuple(redv),
                              mxw=(mx_fin[hp][s_ - 1] if s_ > 0 else 0)):
                            if s_ == 0:
                                e.wait_ge(sl_k[hp].sem, kwh)
                                e.wait_ge(sl_c[hp].sem, cwh)
                            else:
                                e.wait_ge(s_ve.sem, mxw)
                            for pc in range(npc):
                                n0 = 512 * pc
                                n1 = min(nk, n0 + 512)
                                if pc >= 2:
                                    e.wait_ge(s_ve.sem, redv[pc - 2])
                                bank = ps[2 * hp + pc % 2]
                                e.matmul(bank[:, 0:n1 - n0], sb["onesrow_bf"][0:1, 0:128],
                                         crv[0:1, hp, n0:n1], start=True, stop=False)
                                e.matmul(bank[:, 0:n1 - n0], qnv[:, h, 128 * s_:128 * (s_ + 1)],
                                         ksv[:, hp, n0:n1], start=False,
                                         stop=True).then_inc(s_pe.sem, 1)
                        pevals = [s_pe.bump() for _ in range(npc)]

                        @P.on("vector")
                        def _(e, g=g, hp=hp, npc=npc, nk=nk, pevals=tuple(pevals)):
                            q_ve = c["q_ve"]
                            for pc in range(npc):
                                e.wait_ge(s_pe.sem, pevals[pc])
                                n0 = 512 * pc
                                n1 = min(nk, n0 + 512)
                                bank = ps[2 * hp + pc % 2]
                                if pc == npc - 1:
                                    lw = n1 - n0
                                    q_ve.seq(e, e.tensor_tensor(
                                        bank[:, lw - 256:lw], bank[:, lw - 256:lw],
                                        sb["maskq"][:, 256 * g:256 * (g + 1)], ALU.add))
                                q_ve.seq(e, e.reduce_max(
                                    sb["mcol"][:, 4 * hp + pc:4 * hp + pc + 1],
                                    bank[:, 0:n1 - n0], axis=AX.X, negate=True))
                                e.sem_inc(s_ve.sem, 1)
                            for pc in range(1, npc):
                                q_ve.seq(e, e.tensor_tensor(
                                    sb["mcol"][:, 4 * hp:4 * hp + 1],
                                    sb["mcol"][:, 4 * hp:4 * hp + 1],
                                    sb["mcol"][:, 4 * hp + pc:4 * hp + pc + 1], ALU.min))
                            e.sem_inc(s_ve.sem, 1)

                    for hp in range(2):
                        @P.on("tensor")
                        def _(e, hp=hp, s_=s_, mv=mx_fin[hp][s_]):
                            e.wait_ge(s_ve.sem, mv)
                            e.transpose(psmv[hp][0:1, 128 * s_:128 * (s_ + 1)],
                                        sb["mcol"][:, 4 * hp:4 * hp + 1],
                                        sb["identf"][:]).then_inc(s_pe.sem, 1)
                        tr_pe[hp][s_] = s_pe.bump()

                nm_ve = [0, 0]
                for hp in range(2):
                    @P.on("vector")
                    def _(e, hp=hp, trv=tr_pe[hp][3]):
                        e.wait_ge(s_pe.sem, trv)
                        q_ve = c["q_ve"]
                        q_ve.seq(e, e.tensor_copy(nmrow2[hp], psmv[hp]))
                        q_ve.seq(e, e.tensor_copy(nmbv[hp], nmrow2[hp]))
                        e.sem_inc(s_ve.sem, 1)
                    nm_ve[hp] = s_ve.bump()

                stv = [[0] * kmax, [0] * kmax]
                mkv = [[0] * kmax, [0] * kmax]
                exv = [[0] * kmax, [0] * kmax]
                pvv = [[0] * kmax, [0] * kmax]
                for kb in range(kmax):
                    s0 = max(0, -(-(kb - 8 * g - 1) // 2))
                    nq = G - 128 * s0
                    q0 = 128 * s0
                    for hp in range(2):
                        h = hpair[hp]

                        @P.on("tensor")
                        def _(e, h=h, hp=hp, kb=kb, nq=nq, q0=q0,
                              nmw=nm_ve[1], vwh=vw[h]):
                            if kb == 0:
                                e.wait_ge(s_ve.sem, nmw)
                                e.wait_ge(sl_v[hp].sem, vwh)
                            bank = scb[hp]
                            e.matmul(bank[:, 0:nq], sb["onesrow_bf"][0:1, 0:128],
                                     nmbv[hp][0:1, q0:G], start=True, stop=False)
                            e.matmul(bank[:, 0:nq], ksv[:, hp, 128 * kb:128 * (kb + 1)],
                                     qnv[:, h, q0:G], start=False,
                                     stop=True).then_inc(s_pe.sem, 1)
                        stv[hp][kb] = s_pe.bump()

                    for hp in range(2):
                        if kb >= 8 * g:
                            @P.on("vector")
                            def _(e, g=g, hp=hp, kb=kb, st=stv[hp][kb]):
                                e.wait_ge(s_pe.sem, st)
                                bank = scb[hp]
                                moff = 256 * g + 128 * ((kb - 8 * g) % 2)
                                mslice = sb["maskab"][:, moff:moff + 128]
                                c["q_ve"].seq(e, e.tensor_tensor(
                                    bank[:, 0:128], bank[:, 0:128], mslice, ALU.add))
                                e.sem_inc(s_ve.sem, 1)
                            mkv[hp][kb] = s_ve.bump()

                    for hp in range(2):
                        h = hpair[hp]

                        @P.on("scalar")
                        def _(e, h=h, hp=hp, kb=kb, nq=nq, st=stv[hp][kb],
                              mk=mkv[hp][kb], masked=(kb >= 8 * g)):
                            e.wait_ge(s_pe.sem, st)
                            if masked:
                                e.wait_ge(s_ve.sem, mk)
                            bank = scb[hp]
                            a = e.activation(out=etv[:, hp, 0:nq], in_=bank[:, 0:nq],
                                             func=AF.Exp,
                                             bias=sb["colt"][:, 16 * h + kb:16 * h + kb + 1])
                            a.then_inc(s_se.sem, 1)
                        exv[hp][kb] = s_se.bump()

                    for hp in range(2):
                        @P.on("tensor")
                        def _(e, hp=hp, kb=kb, nq=nq, q0=q0, kmax=kmax,
                              ev=exv[hp][kb]):
                            e.wait_ge(s_se.sem, ev)
                            e.matmul(pvb[hp][:, q0:G], vsv[:, hp, 128 * kb:128 * (kb + 1)],
                                     etv[:, hp, 0:nq],
                                     start=(kb == 0), stop=(kb == kmax - 1))
                            mm = e.matmul(pszv[hp][0:1, q0:G], sb["onescol_bf"][:],
                                          etv[:, hp, 0:nq],
                                          start=(kb == 0), stop=(kb == kmax - 1))
                            mm.then_inc(s_pe.sem, 1)
                        pvv[hp][kb] = s_pe.bump()

                ob_ve = [0, 0]
                for hp in range(2):
                    h = hpair[hp]

                    @P.on("vector")
                    def _(e, hp=hp, pv=pvv[hp][kmax - 1]):
                        e.wait_ge(s_pe.sem, pv)
                        c["q_ve"].seq(e, e.reciprocal(zrow2[hp], pszv[hp]))
                        e.sem_inc(s_ve.sem, 1)
                    zr_ve = s_ve.bump()

                    @P.on("tensor")
                    def _(e, hp=hp, zr_ve=zr_ve):
                        e.wait_ge(s_ve.sem, zr_ve)
                        mm = e.matmul(zbb[hp][:], sb["onesrow_f"][:], zrow2[hp],
                                      start=True, stop=True)
                        mm.then_inc(s_pe.sem, 1)
                    zb_pe = s_pe.bump()

                    @P.on("vector")
                    def _(e, h=h, hp=hp, zb_pe=zb_pe):
                        e.wait_ge(s_pe.sem, zb_pe)
                        q_ve = c["q_ve"]
                        q_ve.seq(e, e.tensor_copy(zbv2[hp], zbb[hp][:]))
                        q_ve.seq(e, e.tensor_tensor(obv[:, h, :], pvb[hp][:],
                                                    zbv2[hp], ALU.mult))
                        e.sem_inc(s_ve.sem, 1)
                    ob_ve[hp] = s_ve.bump()
                    pe_last[h] = pvv[hp][kmax - 1]

                @P.on("tensor")
                def _(e, obw=ob_ve[1]):
                    e.wait_ge(s_ve.sem, obw)
                s_pe.bump(0)

            barrier(P, c)

            emit_dense_residual(P, c, sb, ps, xn=ob, wt_dram=woutt, xin_v=xin_v,
                                xout_v=xout_v, toks=toks, scale=1.0, halfbias=None)
            barrier(P, c)


def build_attn_core_launch():
    nc = bass.Bass("TRN2", target_bir_lowering=False)
    io = attc_dram_decls(nc)
    io["xin"] = nc.dram_tensor("xin", [D, 2 * G], F32, kind="ExternalInput").ap()
    io["xout"] = nc.dram_tensor("xout", [D, 2 * G], F32, kind="ExternalOutput").ap()
    io["qn"] = nc.dram_tensor("qn", [NH, 128, 2 * G], BF16, kind="ExternalInput").ap()
    io["kng"] = nc.dram_tensor("kng", [2, NH, 128, 2 * G], BF16, kind="ExternalInput").ap()
    io["vtg"] = nc.dram_tensor("vtg", [2, 2 * G, D], BF16, kind="ExternalInput").ap()
    with contextlib.ExitStack() as ctx:
        E = ctx.enter_context
        c = make_ctx(nc, ctx)
        ps = [E(nc.psum_tensor(f"psf{i}", [128, 512], F32)) for i in range(6)]
        sb = sb_common(nc, ctx)
        sb.update(alloc_sb(nc, ctx, ATTC_SB))
        qn_sb = E(nc.sbuf_tensor("qn_sb", [128, NH * G], BF16))
        ob = E(nc.sbuf_tensor("ob", [128, NH * G], BF16))
        ksb = E(nc.sbuf_tensor("ksb", [128, 2 * S], BF16))
        vsb = E(nc.sbuf_tensor("vsb", [128, 2 * S], BF16))
        expt = E(nc.sbuf_tensor("expt", [128, 2 * G], BF16))
        block = E(nc.Block())
        P = Prog()
        emit_consts(P, c, sb)
        emit_attc_phase(P, c, ps, sb, qn_sb, ob, ksb, vsb, expt, io)
        emit_final_wait(P, c)
        P.emit(block)
    return nc


def attn0_weights(inputs):
    Wqkv = np.asarray(inputs["L0_attn_Wqkv"], np.float32)
    Wout = np.asarray(inputs["L0_attn_Wout"], np.float32)
    colt = np.zeros((128, NH * 16), np.float32)
    colrow = np.zeros((NH, S), np.float32)
    for h in range(NH):
        for kb in range(16):
            colt[:, 16 * h + kb] = SLOPES[h] * (128 * kb + np.arange(128))
        colrow[h] = SLOPES[h] * np.arange(S)
    w = {
        "wqkvt": _wblocks(np.ascontiguousarray(Wqkv.T)),
        "qlnw": _chunkmajor(np.asarray(inputs["L0_attn_q_ln_w"], np.float32)),
        "qlnb": _chunkmajor(np.asarray(inputs["L0_attn_q_ln_b"], np.float32)),
        "klnw": _chunkmajor(np.asarray(inputs["L0_attn_k_ln_w"], np.float32)),
        "klnb": _chunkmajor(np.asarray(inputs["L0_attn_k_ln_b"], np.float32)),
        "identbf": np.eye(128, dtype=BF),
        "woutt": _wblocks(np.ascontiguousarray(Wout.T)),
        "colt": colt,
        "colrow": colrow.astype(BF),
        "identf": np.eye(128, dtype=np.float32),
    }
    NEG = np.float32(-1e30)
    tri_kq = np.tril(np.ones((128, 128), np.float32)).T
    maskab = np.zeros((2, 128, 256), np.float32)
    maskab[0, :, 0:128] = np.where(tri_kq > 0, 0, NEG)
    maskab[0, :, 128:256] = NEG
    maskab[1, :, 0:128] = 0
    maskab[1, :, 128:256] = np.where(tri_kq > 0, 0, NEG)
    w["maskab"] = maskab
    tri_qk = np.tril(np.ones((128, 128), np.float32))
    maskq = np.zeros((2, 128, 256), np.float32)
    maskq[0, :, 0:128] = np.where(tri_qk > 0, 0, NEG)
    maskq[0, :, 128:256] = NEG
    maskq[1, :, 0:128] = 0
    maskq[1, :, 128:256] = np.where(tri_qk > 0, 0, NEG)
    w["maskq"] = maskq
    return w


def run_attn0(x, inputs, w=None):
    if w is None:
        w = attn0_weights(inputs)
    nc2 = _get_prog(("qkv",), build_qkv_launch)
    in_maps = []
    for core in range(8):
        xc, _ = _shard_x(x, core)
        im = {"xin": xc, "wqkvt": w["wqkvt"], "i_qlnw": w["qlnw"], "i_qlnb": w["qlnb"],
              "i_klnw": w["klnw"], "i_klnb": w["klnb"], "identbf2": w["identbf"]}
        in_maps.append(im)
    res = run_bass_kernel_spmd(nc2, in_maps, core_ids=list(range(8)))
    qn = [np.asarray(res.results[cc]["qn"]) for cc in range(8)]
    kn = [np.asarray(res.results[cc]["kn"]) for cc in range(8)]
    vt = [np.asarray(res.results[cc]["vt"]) for cc in range(8)]

    kng = [np.stack([kn[2 * sq], kn[2 * sq + 1]]) for sq in range(4)]
    vtg = [np.stack([vt[2 * sq], vt[2 * sq + 1]]) for sq in range(4)]

    in_maps = []
    for core in range(8):
        seq, par = core // 2, core % 2
        xc, _ = _shard_x(x, core)
        im = {"xin": xc, "qn": qn[core], "kng": kng[seq], "vtg": vtg[seq],
              "woutt": w["woutt"], "colt": w["colt"], "colrow": w["colrow"],
              "maskab": np.stack([w["maskab"][par], w["maskab"][1 - par]]),
              "maskq": np.stack([w["maskq"][par], w["maskq"][1 - par]]),
              "identf": w["identf"]}
        in_maps.append(im)
    out = x.copy()
    nc3 = _get_prog(("attc",), build_attn_core_launch)
    res = run_bass_kernel_spmd(nc3, in_maps, core_ids=list(range(8)))
    for cc in range(8):
        _unshard_x(out, cc, np.asarray(res.results[cc]["xout"]))
    return out


A1_SB = {
    "bq": ([128, NCH], F32), "bk": ([128, 8], F32), "bv": ([128, 8], F32),
    "bqs": ([128, 8], F32), "identb": ([128, 128], BF16),
    "vstage": ([128, 4 * 128], BF16), "oe": ([128, 2 * G], BF16),
}


def a1_dram_decls(nc):
    d = {}
    d["wq"] = nc.dram_tensor("wq", [NCH, 128, NCH, 128], F32, kind="ExternalInput").ap()
    d["wk"] = nc.dram_tensor("wk", [8, 128, NCH, 128], F32, kind="ExternalInput").ap()
    d["wv"] = nc.dram_tensor("wv", [8, 128, NCH, 128], F32, kind="ExternalInput").ap()
    d["bqm"] = nc.dram_tensor("bqm", [128, NCH], F32, kind="ExternalInput").ap()
    d["bkm"] = nc.dram_tensor("bkm", [128, 8], F32, kind="ExternalInput").ap()
    d["bvm"] = nc.dram_tensor("bvm", [128, 8], F32, kind="ExternalInput").ap()
    d["identbf"] = nc.dram_tensor("identbf", [128, 128], BF16, kind="ExternalInput").ap()
    d["q3"] = nc.dram_tensor("q3", [8, G, 128], BF16, kind="Internal").ap()
    d["k3"] = nc.dram_tensor("k3", [8, G, 128], BF16, kind="Internal").ap()
    d["o3e"] = nc.dram_tensor("o3e", [128, NCH, G], BF16, kind="Internal").ap()
    return d


def emit_a1_phase(P, c, ps, psb, sb, xn, qt, kt, vt, ot, qhkT, khkT, probs, ptb, zc, io):
    xin, xout = io["xin"], io["xout"]
    wq, wk, wv = io["wq"], io["wk"], io["wv"]
    bqm, bkm, bvm = io["bqm"], io["bkm"], io["bvm"]
    identbf = io["identbf"]
    q3, k3, o3e = io["q3"], io["k3"], io["o3e"]
    s_dma, s_pe, s_ve, s_se, s_wd, s_out = (c[k] for k in
                                            ("s_dma", "s_pe", "s_ve", "s_se", "s_wd", "s_out"))
    if True:
        @P.on("sync")
        def _(e):
            e.dma_start(out=sb["bq"][:], in_=bqm).then_inc(s_dma.sem, 16)
            e.dma_start(out=sb["bk"][:], in_=bkm).then_inc(s_dma.sem, 16)
            e.dma_start(out=sb["bv"][:], in_=bvm).then_inc(s_dma.sem, 16)
            e.dma_start(out=sb["identb"][:], in_=identbf).then_inc(s_dma.sem, 16)
        s_dma.bump(16 * 4)

        barrier(P, c)

        SC = float(1.0 / np.sqrt(128.0))

        @P.on("vector")
        def _(e):
            bqv = apv(sb["bq"]).rearrange("p (hk jj) -> p hk jj", hk=8)
            q_ve = c["q_ve"]
            q_ve.seq(e, e.tensor_tensor(sb["bqs"][:], bqv[:, :, 0], bqv[:, :, 1], ALU.add))
            q_ve.seq(e, e.tensor_tensor(sb["bqs"][:], sb["bqs"][:], bqv[:, :, 2], ALU.add))
            q_ve.seq(e, e.tensor_tensor(sb["bqs"][:], sb["bqs"][:], bqv[:, :, 3], ALU.add))
            e.sem_inc(s_ve.sem, 1)
        s_ve.bump()

        @P.on("scalar")
        def _(e, w=s_ve.n):
            e.wait_ge(s_ve.sem, w)
            e.mul(sb["bqs"][:], sb["bqs"][:], SC).then_inc(s_se.sem, 1)
        s_se.bump()

        barrier(P, c)

        xin_v = xin.rearrange("(j p) t -> j p t", p=128)
        xout_v = xout.rearrange("(j p) t -> j p t", p=128)
        xnv = apv(xn).rearrange("p (j t) -> p j t", j=NCH)
        qtv = apv(qt).rearrange("p (hk t) -> p hk t", hk=8)
        ktv = apv(kt).rearrange("p (hk t) -> p hk t", hk=8)
        vtv = apv(vt).rearrange("p (hk t) -> p hk t", hk=8)
        otv = apv(ot).rearrange("p (hk t) -> p hk t", hk=8)
        wbv = apv(sb["wbuf"]).rearrange("p (i j n) -> p i j n", i=2, j=NCH)
        sl_w = c["sl_w"]
        xsv = apv(sb["xs"]).rearrange("p (i t) -> p i t", i=4)
        xov = apv(sb["xo"]).rearrange("p (i t) -> p i t", i=4)
        tmpv = apv(sb["tmp"]).rearrange("p (i t) -> p i t", i=2)
        sl_xe, sl_o = c["sl_xe"], c["sl_o"]

        for g in range(2):
            toks = slice(g * G, (g + 1) * G)
            emit_ln(P, c, sb, ps, xin_v, toks, xn)

            wprev = [0, 0]
            pprev = [0, 0]
            for proj, (wsrc, nblk, acc, arena, bias_t, scale) in enumerate((
                    (wq, NCH, 4, qtv, sb["bqs"], SC),
                    (wk, 8, 1, ktv, sb["bk"], 1.0),
                    (wv, 8, 1, vtv, sb["bv"], 1.0))):
                nout = nblk // acc
                wws = []
                pe_m = [None] * nblk
                ve_m = [None] * nout
                for d in range(nblk):
                    wws.append(sl_w[d % 2].bump(16))

                @P.on("gpsimd")
                def _(e, wsrc=wsrc, nblk=nblk, pe_m=pe_m, acc=acc, wprev=tuple(wprev)):
                    for d in range(nblk):
                        if d >= 2:
                            e.wait_ge(s_pe.sem, pe_m[d - 2])
                        elif wprev[d % 2]:
                            e.wait_ge(s_pe.sem, wprev[d % 2])
                        e.dma_start(out=wbv[:, d % 2, :, :],
                                    in_=wsrc[d, :, :, :]).then_inc(sl_w[d % 2].sem, 16)

                for d in range(nblk):
                    hk = d // acc

                    @P.on("tensor")
                    def _(e, d=d, hk=hk, ww=wws[d], acc=acc,
                          ev=(ve_m[(d // acc) - 2] if d // acc >= 2 and d % acc == 0
                              else (pprev[(d // acc) % 2] if d // acc < 2 and d % acc == 0 else None))):
                        e.wait_ge(sl_w[d % 2].sem, ww)
                        if ev:
                            e.wait_ge(s_se.sem, ev)
                        pt = ps[hk % 2]
                        mm = None
                        for j in range(NCH):
                            mm = e.matmul(pt[:], wbv[:, d % 2, j, :], xnv[:, j, :],
                                          start=(j == 0 and d % acc == 0),
                                          stop=(j == NCH - 1 and d % acc == acc - 1))
                        mm.then_inc(s_pe.sem, 1)
                    pe_m[d] = s_pe.bump()

                    if d % acc == acc - 1:
                        @P.on("scalar")
                        def _(e, hk=hk, pe_cnt=pe_m[d], arena=arena, bias_t=bias_t, scale=scale):
                            e.wait_ge(s_pe.sem, pe_cnt)
                            e.activation(out=arena[:, hk, :], in_=ps[hk % 2][:], func=AF.Identity,
                                         scale=scale, bias=bias_t[:, hk:hk + 1]).then_inc(s_se.sem, 1)
                        ve_m[hk] = s_se.bump()
                wprev = [pe_m[nblk - 2], pe_m[nblk - 1]]
                pprev = [ve_m[nout - 2], ve_m[nout - 1]]
            barrier(P, c)

            tvi = 0
            tv_se = []
            for src_ar, dst in ((qtv, q3), (ktv, k3)):
                for hk in range(8):
                    for i in range(NT):
                        @P.on("tensor")
                        def _(e, src_ar=src_ar, hk=hk, i=i, tvi=tvi,
                              wv=(tv_se[tvi - 2] if tvi >= 2 else 0)):
                            if tvi >= 2:
                                e.wait_ge(s_se.sem, wv)
                            mm = e.transpose(psb[tvi % 2][:, 0:128],
                                             src_ar[:, hk, 128 * i:128 * (i + 1)],
                                             sb["identb"][:])
                            mm.then_inc(s_pe.sem, 1)
                        tp = s_pe.bump()

                        @P.on("scalar")
                        def _(e, i=i, tvi=tvi, tp=tp,
                              ow=(c["sl_o"][tvi % 4].n if True else 0)):
                            e.wait_ge(s_pe.sem, tp)
                            if tvi >= 4:
                                e.wait_ge(c["sl_o"][tvi % 4].sem, ow)
                            e.copy(apv(sb["vstage"]).rearrange("p (i n) -> p i n", i=4)[:, tvi % 4, 0:128],
                                   psb[tvi % 2][:, 0:128]).then_inc(s_se.sem, 1)
                        se_cnt = s_se.bump()
                        tv_se.append(se_cnt)
                        c["sl_o"][tvi % 4].bump(16)

                        @P.on("sync")
                        def _(e, dst=dst, hk=hk, i=i, tvi=tvi, se_cnt=se_cnt):
                            e.wait_ge(s_se.sem, se_cnt)
                            e.dma_start(out=dst[hk, 128 * i:128 * (i + 1), :],
                                        in_=apv(sb["vstage"]).rearrange("p (i n) -> p i n", i=4)[:, tvi % 4, 0:128]).then_inc(c["sl_o"][tvi % 4].sem, 16)
                        tvi += 1
            barrier(P, c)

            BT = 32
            q_hk = apv(qhkT).rearrange("p (i n) -> p i n", i=2)
            k_hk = apv(khkT).rearrange("p (i n) -> p i n", i=2)
            pv_pe_l = []
            ld_b = [None] * (G // BT)
            for b in range(G // BT):
                ld_b[b] = c["sl_x"][b % 2].bump(32)

            @P.on("sync")
            def _(e, g=g, pv_pe_l=pv_pe_l, ld_b=ld_b):
                for b in range(G // BT):
                    if b >= 2:
                        e.wait_ge(s_pe.sem, pv_pe_l[(b - 2) * BT + BT - 1])
                    e.dma_start(out=q_hk[0:8, b % 2, 0:BT * 128],
                                in_=q3[:, BT * b:BT * (b + 1), :]).then_inc(c["sl_x"][b % 2].sem, 16)
                    e.dma_start(out=k_hk[0:8, b % 2, 0:BT * 128],
                                in_=k3[:, BT * b:BT * (b + 1), :]).then_inc(c["sl_x"][b % 2].sem, 16)

            rec = []
            vt_q = apv(vt).rearrange("p (hk t) -> p t hk", hk=8)
            ot_q = apv(ot).rearrange("p (hk t) -> p t hk", hk=8)
            zt = apv(sb["tmp"])
            scb3 = [ps[2], ps[3], ps[0]]
            pvb3 = [ps[4], ps[5], ps[1]]
            NQD = G // 4
            for qd in range(NQD):
                b = (4 * qd) // BT
                sl = qd % 2
                s3 = qd % 3
                prev2 = rec[qd - 2] if qd >= 2 else None
                prev3 = rec[qd - 3] if qd >= 3 else None
                t0 = 4 * qd
                tb0 = t0 - b * BT

                @P.on("tensor")
                def _(e, b=b, s3=s3, qd=qd, t0=t0, tb0=tb0, ldw=ld_b[b],
                      zv=(prev3["ex_se"] if prev3 else 0)):
                    if tb0 == 0:
                        e.wait_ge(c["sl_x"][b % 2].sem, ldw)
                    if qd >= 3:
                        e.wait_ge(s_se.sem, zv)
                    mm = None
                    for j in range(4):
                        mm = e.matmul(scb3[s3][:, 128 * j:128 * (j + 1)],
                                      q_hk[0:8, b % 2, 128 * (tb0 + j):128 * (tb0 + j + 1)],
                                      k_hk[0:8, b % 2, 128 * (tb0 + j):128 * (tb0 + j + 1)],
                                      start=(j == 0), stop=(j == 3))
                    mm.then_inc(s_pe.sem, 1)
                sc_pe = s_pe.bump()

                @P.on("vector")
                def _(e, s3=s3, qd=qd, sc_pe=sc_pe):
                    e.wait_ge(s_pe.sem, sc_pe)
                    q_ve = c["q_ve"]
                    mm = None
                    for j in range(4):
                        mm = e.reduce_max(zt[:, 4 * s3 + j:4 * s3 + j + 1],
                                          scb3[s3][:, 128 * j:128 * (j + 1)],
                                          axis=AX.X, negate=True)
                    q_ve.seq(e, mm)
                    e.sem_inc(s_ve.sem, 1)
                nm_ve = s_ve.bump()

                @P.on("scalar")
                def _(e, sl=sl, s3=s3, qd=qd, nm_ve=nm_ve,
                      pvw=(prev2["tr_pe"] if prev2 else 0),
                      zrw=(prev2["zr_ve"] if prev2 else 0)):
                    e.wait_ge(s_ve.sem, nm_ve)
                    if qd >= 2:
                        e.wait_ge(s_pe.sem, pvw)
                        e.wait_ge(s_ve.sem, zrw)
                    a = None
                    for j in range(4):
                        a = e.activation(
                            out=probs[:, 512 * sl + 128 * j:512 * sl + 128 * (j + 1)],
                            in_=scb3[s3][:, 128 * j:128 * (j + 1)], func=AF.Exp,
                            bias=zt[:, 4 * s3 + j:4 * s3 + j + 1])
                    a.then_inc(s_se.sem, 1)
                ex_se = s_se.bump()

                @P.on("vector")
                def _(e, sl=sl, s3=s3, qd=qd, ex_se=ex_se):
                    e.wait_ge(s_se.sem, ex_se)
                    q_ve = c["q_ve"]
                    mm = None
                    for j in range(4):
                        mm = e.reduce_sum(zt[:, 16 + 4 * s3 + j:16 + 4 * s3 + j + 1],
                                          probs[:, 512 * sl + 128 * j:512 * sl + 128 * (j + 1)],
                                          axis=AX.X)
                    q_ve.seq(e, mm)
                    q_ve.seq(e, e.reciprocal(zt[:, 16 + 4 * s3:16 + 4 * s3 + 4],
                                             zt[:, 16 + 4 * s3:16 + 4 * s3 + 4]))
                    e.sem_inc(s_ve.sem, 1)
                zr_ve = s_ve.bump()

                @P.on("tensor")
                def _(e, sl=sl, qd=qd, ex_se=ex_se,
                      cpw=(prev2["cp_ve"] if prev2 else 0)):
                    e.wait_ge(s_se.sem, ex_se)
                    if qd >= 2:
                        e.wait_ge(s_ve.sem, cpw)
                    mm = None
                    for j in range(4):
                        mm = e.transpose(psb[sl][:, 128 * j:128 * (j + 1)],
                                         probs[:, 512 * sl + 128 * j:512 * sl + 128 * (j + 1)],
                                         sb["identb"][:])
                    mm.then_inc(s_pe.sem, 1)
                tr_pe = s_pe.bump()

                @P.on("vector")
                def _(e, sl=sl, qd=qd, tr_pe=tr_pe,
                      opw=(prev2["pv_pe"] if prev2 else 0)):
                    e.wait_ge(s_pe.sem, tr_pe)
                    if qd >= 2:
                        e.wait_ge(s_pe.sem, opw)
                    e.tensor_copy(ptb[:, 512 * sl:512 * (sl + 1)],
                                  psb[sl][:, 0:512]).then_inc(s_ve.sem, 1)
                cp_ve = s_ve.bump()

                @P.on("tensor")
                def _(e, sl=sl, s3=s3, qd=qd, t0=t0, cp_ve=cp_ve,
                      ovw=(prev3["ev_ve"] if prev3 else 0)):
                    e.wait_ge(s_ve.sem, cp_ve)
                    if qd >= 3:
                        e.wait_ge(s_ve.sem, ovw)
                    mm = None
                    for j in range(4):
                        mm = e.matmul(pvb3[s3][:, 8 * j:8 * (j + 1)],
                                      ptb[:, 512 * sl + 128 * j:512 * sl + 128 * (j + 1)],
                                      vt_q[:, t0 + j, :], start=(j == 0), stop=(j == 3))
                    mm.then_inc(s_pe.sem, 1)
                pv_pe = s_pe.bump()
                for j in range(4):
                    pv_pe_l.append(pv_pe)

                @P.on("vector")
                def _(e, s3=s3, qd=qd, t0=t0, pv_pe=pv_pe):
                    e.wait_ge(s_pe.sem, pv_pe)
                    q_ve = c["q_ve"]
                    for j in range(4):
                        q_ve.seq(e, e.tensor_scalar(ot_q[:, t0 + j, :],
                                                    pvb3[s3][:, 8 * j:8 * (j + 1)],
                                                    zt[:, 16 + 4 * s3 + j:16 + 4 * s3 + j + 1],
                                                    None, op0=ALU.mult))
                    e.sem_inc(s_ve.sem, 1)
                ev_ve = s_ve.bump()
                rec.append({"ex_se": ex_se, "zr_ve": zr_ve, "tr_pe": tr_pe,
                            "cp_ve": cp_ve, "pv_pe": pv_pe, "ev_ve": ev_ve})


            barrier(P, c)

            o3x = o3e.rearrange("dd (hk r) t -> dd hk r t", r=4)

            @P.on("sync")
            def _(e):
                for r in range(4):
                    e.dma_start(out=o3x[:, :, r, :], in_=otv[:, :, :]).then_inc(s_dma.sem, 16)
            s_dma.bump(16 * 4)
            barrier(P, c)

            o3f = o3e.rearrange("dd hh t -> (dd hh) t").rearrange("(a p) t -> a p t", p=128)
            oev = apv(sb["oe"]).rearrange("p (i t) -> p i t", i=2)
            a1rv = []
            for cch in range(NCH):
                xe_w = sl_xe[cch % 4].bump(16)
                oe_w = c["sl_c"][cch % 2].bump(16)
                ve_prev2 = a1rv[cch - 2] if cch >= 2 else 0

                @P.on("sync")
                def _(e, cch=cch, toks=toks,
                      ev4=(a1rv[cch - 4] if cch >= 4 else 0)):
                    if cch >= 4:
                        e.wait_ge(s_ve.sem, ev4)
                    e.dma_start(out=xsv[:, cch % 4, :], in_=xin_v[cch, :, toks]).then_inc(sl_xe[cch % 4].sem, 16)

                @P.on("gpsimd")
                def _(e, cch=cch, ve_prev2=ve_prev2):
                    if cch >= 2:
                        e.wait_ge(s_ve.sem, ve_prev2)
                    e.dma_start(out=oev[:, cch % 2, :], in_=o3f[cch, :, :]).then_inc(c["sl_c"][cch % 2].sem, 16)

                @P.on("vector")
                def _(e, cch=cch, xw=xe_w, ow=oe_w,
                      ov=(c["sl_o"][cch % 4].n if cch >= 4 else 0)):
                    e.wait_ge(sl_xe[cch % 4].sem, xw)
                    e.wait_ge(c["sl_c"][cch % 2].sem, ow)
                    if cch >= 4:
                        e.wait_ge(c["sl_o"][cch % 4].sem, ov)
                    e.tensor_tensor(xov[:, cch % 4, :], xsv[:, cch % 4, :],
                                    oev[:, cch % 2, :], ALU.add).then_inc(s_ve.sem, 1)
                ve_cnt = s_ve.bump()
                a1rv.append(ve_cnt)
                c["sl_o"][cch % 4].bump(16)

                @P.on("sync")
                def _(e, cch=cch, ve_cnt=ve_cnt, toks=toks):
                    e.wait_ge(s_ve.sem, ve_cnt)
                    e.dma_start(out=xout_v[cch, :, toks], in_=xov[:, cch % 4, :]).then_inc(c["sl_o"][cch % 4].sem, 16)

            barrier(P, c)


def build_attn1_launch():
    nc = bass.Bass("TRN2", target_bir_lowering=False)
    io = a1_dram_decls(nc)
    io["xin"] = nc.dram_tensor("xin", [D, 2 * G], F32, kind="ExternalInput").ap()
    io["xout"] = nc.dram_tensor("xout", [D, 2 * G], F32, kind="ExternalOutput").ap()
    with contextlib.ExitStack() as ctx:
        E = ctx.enter_context
        c = make_ctx(nc, ctx)
        ps = [E(nc.psum_tensor(f"psf{i}", [128, 512], F32)) for i in range(6)]
        psb = [E(nc.psum_tensor(f"psbt{i}", [128, 1024], BF16)) for i in range(2)]
        sb = sb_common(nc, ctx)
        sb.update(alloc_sb(nc, ctx, A1_SB))
        xn = E(nc.sbuf_tensor("xn0", [128, NCH * G], BF16))
        qt = E(nc.sbuf_tensor("qt", [128, 8 * G], BF16))
        kt = E(nc.sbuf_tensor("kt", [128, 8 * G], BF16))
        vt = E(nc.sbuf_tensor("vt_sb", [128, 8 * G], BF16))
        ot = E(nc.sbuf_tensor("ot", [128, 8 * G], BF16))
        qhkT = E(nc.sbuf_tensor("qhkT", [8, 64 * 128 * 2], BF16))
        khkT = E(nc.sbuf_tensor("khkT", [8, 64 * 128 * 2], BF16))
        probs = E(nc.sbuf_tensor("probs_sb", [128, 1024], BF16))
        ptb = E(nc.sbuf_tensor("pt_sb", [128, 1024], BF16))
        zc = E(nc.sbuf_tensor("zc", [128, 16], F32))
        block = E(nc.Block())
        P = Prog()
        emit_consts(P, c, sb)
        emit_a1_phase(P, c, ps, psb, sb, xn, qt, kt, vt, ot, qhkT, khkT, probs, ptb, zc, io)
        emit_final_wait(P, c)
        P.emit(block)
    return nc


def attn1_weights(inputs):
    Wq = np.asarray(inputs["L1_attn_Wq"], np.float32)
    Wk = np.asarray(inputs["L1_attn_Wk"], np.float32)
    Wv = np.asarray(inputs["L1_attn_Wv"], np.float32)
    wqT = Wq.transpose(2, 1, 0).reshape(D, D).T
    wqT = np.ascontiguousarray(Wq.transpose(0, 2, 1).reshape(D, D))
    wkT = np.ascontiguousarray(Wk.transpose(0, 2, 1).reshape(D, 8 * 128))
    wvT = np.ascontiguousarray(Wv.transpose(0, 2, 1).reshape(D, 8 * 128))
    bq = np.asarray(inputs["L1_attn_bq"], np.float32)
    bk = np.asarray(inputs["L1_attn_bk"], np.float32)
    bv = np.asarray(inputs["L1_attn_bv"], np.float32)
    bqv = np.ascontiguousarray(bq.T.reshape(D))
    bkv = np.ascontiguousarray(bk.T.reshape(8 * 128))
    bvv = np.ascontiguousarray(bv.T.reshape(8 * 128))
    return {
        "wq": _wblocks(wqT),
        "wk": _wblocks(wkT),
        "wv": _wblocks(wvT),
        "bqm": _chunkmajor(bqv),
        "bkm": np.ascontiguousarray(bkv.reshape(8, 128).T),
        "bvm": np.ascontiguousarray(bvv.reshape(8, 128).T),
        "identbf": np.eye(128, dtype=BF),
    }


def run_attn1(x, inputs, w=None):
    if w is None:
        w = attn1_weights(inputs)
    nc = _get_prog(("attn1",), build_attn1_launch)
    in_maps = []
    for core in range(8):
        xc, _ = _shard_x(x, core)
        im = {"xin": xc}
        im.update(w)
        in_maps.append(im)
    res = run_bass_kernel_spmd(nc, in_maps, core_ids=list(range(8)))
    out = x.copy()
    for core in range(8):
        _unshard_x(out, core, np.asarray(res.results[core]["xout"]))
    return out




# version 12 (final)
# speedup vs baseline: 1.1567x; 1.1120x over previous
import sys
sys.path.insert(0, "/opt/trn_rl_repo")
import contextlib

import numpy as np
import ml_dtypes
import concourse.bass as bass
import concourse.mybir as mybir
from concourse.bass_utils import run_bass_kernel_spmd

F32 = mybir.dt.float32
BF16 = mybir.dt.bfloat16
AF = mybir.ActivationFunctionType
ALU = mybir.AluOpType
AX = mybir.AxisListType

D = 4096
NCH = 32
G = 512
NT = 4
S = 2048
NH = 32
LN_EPS = 1e-5
CLIP = 8.0
BF = ml_dtypes.bfloat16


def alibi_slopes(n_heads=32, bias_max=8.0):
    next_pow2 = 2 ** int(np.ceil(np.log2(n_heads)))
    m = np.arange(1, next_pow2 + 1, dtype=np.float32) * (bias_max / next_pow2)
    slopes = 1.0 / np.power(2.0, m)
    if next_pow2 != n_heads:
        slopes = np.concatenate([slopes[1::2], slopes[::2]])[:n_heads]
    return slopes.astype(np.float32)


SLOPES = alibi_slopes()
ATTN_GROUPS = (0, 1)
DUMP_OB = False
P2_STAGE = 4


def tiles_of(parity, grp):
    if grp == 0:
        return [2 * s + parity for s in range(4)]
    return [8 + 2 * s + (1 - parity) for s in range(4)]


def core_token_tiles(parity):
    return tiles_of(parity, 0) + tiles_of(parity, 1)



class Prog:
    ENGINES = ("sync", "gpsimd", "tensor", "vector", "scalar")

    def __init__(self):
        self.ops = {e: [] for e in self.ENGINES}

    def on(self, eng):
        def deco(fn):
            self.ops[eng].append(fn)
            return None
        return deco

    def emit(self, block):
        for name in self.ENGINES:
            fns = self.ops[name]
            deco = getattr(block, name)

            def section(e, fns=fns):
                for f in fns:
                    f(e)
            deco(section)


class Ctr:
    def __init__(self, sem):
        self.sem = sem
        self.n = 0

    def bump(self, k=1):
        self.n += k
        return self.n

    def seq(self, e, instr):
        instr.then_inc(self.sem, 1)
        e.wait_ge(self.sem, self.bump())


def apv(t):
    a = getattr(t, "ap", None)
    return t.ap() if callable(a) else t


def make_ctx(nc, ctx):
    E = ctx.enter_context
    c = {"nc": nc}
    for nm in ("s_dma", "s_pe", "s_ve", "s_se", "s_wd", "s_out", "s_bar", "s_ge"):
        c[nm] = Ctr(E(nc.semaphore(nm)))
    c["sl_x"] = [Ctr(E(nc.semaphore(f"slx{i}"))) for i in range(4)]
    c["sl_w"] = [Ctr(E(nc.semaphore(f"slw{i}"))) for i in range(2)]
    c["sl_xe"] = [Ctr(E(nc.semaphore(f"slxe{i}"))) for i in range(4)]
    c["sl_o"] = [Ctr(E(nc.semaphore(f"slo{i}"))) for i in range(4)]
    c["sl_c"] = [Ctr(E(nc.semaphore(f"slc{i}"))) for i in range(2)]
    c["sl_g"] = [Ctr(E(nc.semaphore(f"slg{i}"))) for i in range(2)]
    for eng in ("ve", "se", "pe", "ge"):
        c["q_" + eng] = Ctr(E(nc.semaphore("q_" + eng)))
    return c


def barrier(P, c):
    tgt = c["s_bar"].n + 5
    c["s_bar"].n = tgt
    sem = c["s_bar"].sem
    dma_waits = [(ctr.sem, ctr.n) for ctr in
                 ([c["s_dma"], c["s_wd"], c["s_out"]] + c["sl_x"] + c["sl_w"]
                  + c["sl_xe"] + c["sl_o"] + c["sl_c"] + c["sl_g"]) if ctr.n > 0]

    for eng in Prog.ENGINES:
        @P.on(eng)
        def _(e, tgt=tgt, sem=sem, eng=eng, dma_waits=tuple(dma_waits)):
            if eng == "sync":
                for sm, n in dma_waits:
                    e.wait_ge(sm, n)
            e.drain()
            e.sem_inc(sem, 1)
            e.wait_ge(sem, tgt)



def sb_common(nc, ctx):
    E = ctx.enter_context
    sb = {}
    sb["xs"] = E(nc.sbuf_tensor("xs", [128, 4 * G], F32))
    sb["sq"] = E(nc.sbuf_tensor("sq", [128, 2 * G], BF16))
    sb["rows"] = E(nc.sbuf_tensor("rows", [1, 4 * G], F32))
    sb["bcast"] = E(nc.sbuf_tensor("bcast", [128, 2 * G], F32))
    sb["onescol_bf"] = E(nc.sbuf_tensor("onescol_bf", [128, 1], BF16))
    sb["onesrow_f"] = E(nc.sbuf_tensor("onesrow_f", [1, 128], F32))
    sb["onesrow_bf"] = E(nc.sbuf_tensor("onesrow_bf", [1, 512], BF16))
    sb["cpool"] = E(nc.sbuf_tensor("cpool", [1, 8], F32))
    sb["wbuf"] = E(nc.sbuf_tensor("wbuf", [128, 2 * NCH * 128], BF16))
    sb["tmp"] = E(nc.sbuf_tensor("tmp", [128, 2 * G], F32))
    sb["xo"] = E(nc.sbuf_tensor("xo", [128, 4 * G], F32))
    return sb


def emit_consts(P, c, sb):
    @P.on("vector")
    def _(e):
        e.memset(sb["onescol_bf"][:], 1.0)
        e.memset(sb["onesrow_f"][:], 1.0)
        e.memset(sb["onesrow_bf"][:], 1.0)
        e.memset(sb["cpool"][0:1, 0:1], LN_EPS).then_inc(c["s_ve"].sem, 1)
    return c["s_ve"].bump()



def emit_ln(P, c, sb, ps, xin_v, toks, arena):
    s_ve = c["s_ve"]
    xnv = apv(arena).rearrange("p (j t) -> p j t", j=NCH)
    xsv = apv(sb["xs"]).rearrange("p (i t) -> p i t", i=4)
    ve0 = s_ve.n
    sl_x = c["sl_x"]
    xwaits = []
    for j in range(NCH):
        xwaits.append(sl_x[j % 4].bump(16))

    @P.on("sync")
    def _(e, ve0=ve0):
        for j in range(NCH):
            if j >= 4:
                e.wait_ge(s_ve.sem, ve0 + (j - 4) + 1)
            e.dma_start(out=xsv[:, j % 4, :], in_=xin_v[j, :, toks]).then_inc(sl_x[j % 4].sem, 16)

    @P.on("vector")
    def _(e):
        for j in range(NCH):
            e.wait_ge(sl_x[j % 4].sem, xwaits[j])
            e.tensor_copy(xnv[:, j, :], xsv[:, j % 4, :]).then_inc(s_ve.sem, 1)
    s_ve.bump(NCH)

    emit_norm(P, c, sb, ps, arena, copy_ve0=ve0)


def emit_norm(P, c, sb, ps, arena, wb=None, copy_ve0=None, nch=NCH):
    s_pe, s_ve, s_se = c["s_pe"], c["s_ve"], c["s_se"]
    NCHa = nch
    xnv = apv(arena).rearrange("p (j t) -> p j t", j=NCHa)
    sqv = apv(sb["sq"]).rearrange("p (i t) -> p i t", i=2)
    psS = ps[4][0:1, :]
    psQ = ps[5][0:1, :]
    ve0, se0, pe0 = s_ve.n - NCHa, s_se.n, s_pe.n
    if copy_ve0 is not None:
        ve0 = copy_ve0

    @P.on("scalar")
    def _(e, ve0=ve0, pe0=pe0):
        for j in range(NCHa):
            e.wait_ge(s_ve.sem, ve0 + j + 1)
            if j >= 2:
                e.wait_ge(s_pe.sem, pe0 + (j - 1))
            e.activation(out=sqv[:, j % 2, :], in_=xnv[:, j, :], func=AF.Square).then_inc(s_se.sem, 1)
    s_se.bump(NCHa)

    @P.on("tensor")
    def _(e, se0=se0):
        for j in range(NCHa):
            e.wait_ge(s_se.sem, se0 + j + 1)
            e.matmul(psS, sb["onescol_bf"][:], xnv[:, j, :], start=(j == 0), stop=(j == NCHa - 1))
            mm = e.matmul(psQ, sb["onescol_bf"][:], sqv[:, j % 2, :], start=(j == 0), stop=(j == NCHa - 1))
            mm.then_inc(s_pe.sem, 1)
    s_pe.bump(NCHa)

    rows = sb["rows"]
    m_r = rows[0:1, 0:G]
    e2_r = rows[0:1, G:2 * G]
    rstd_r = rows[0:1, 2 * G:3 * G]
    mr_r = rows[0:1, 3 * G:4 * G]
    pe_stats = s_pe.n

    @P.on("scalar")
    def _(e, pe_stats=pe_stats):
        e.wait_ge(s_pe.sem, pe_stats)
        e.activation(out=m_r, in_=psS, func=AF.Identity, scale=1.0 / (128 * NCHa))
        e.activation(out=e2_r, in_=psQ, func=AF.Identity, scale=1.0 / (128 * NCHa)).then_inc(s_se.sem, 1)
    se_mq = s_se.bump()

    q_ve, q_se = c["q_ve"], c["q_se"]

    @P.on("vector")
    def _(e, se_mq=se_mq):
        e.wait_ge(s_se.sem, se_mq)
        q_ve.seq(e, e.tensor_tensor(mr_r, m_r, m_r, ALU.mult))
        e.tensor_tensor(e2_r, e2_r, mr_r, ALU.subtract).then_inc(s_ve.sem, 1)
    ve_var = s_ve.bump()

    @P.on("scalar")
    def _(e, ve_var=ve_var):
        e.wait_ge(s_ve.sem, ve_var)
        q_se.seq(e, e.activation(out=rstd_r, in_=e2_r, func=AF.Ln, bias=sb["cpool"][0:1, 0:1]))
        e.activation(out=rstd_r, in_=rstd_r, func=AF.Exp, scale=-0.5).then_inc(s_se.sem, 1)
    se_rstd = s_se.bump()

    @P.on("vector")
    def _(e, se_rstd=se_rstd):
        e.wait_ge(s_se.sem, se_rstd)
        e.tensor_tensor(mr_r, m_r, rstd_r, ALU.mult).then_inc(s_ve.sem, 1)
    ve_mr = s_ve.bump()

    @P.on("tensor")
    def _(e, ve_mr=ve_mr):
        e.wait_ge(s_ve.sem, ve_mr)
        e.matmul(ps[2][:], sb["onesrow_f"][:], rstd_r, start=True, stop=True)
        mm = e.matmul(ps[3][:], sb["onesrow_f"][:], mr_r, start=True, stop=True)
        mm.then_inc(s_pe.sem, 1)
    pe_b = s_pe.bump()

    bc = sb["bcast"]

    @P.on("vector")
    def _(e, pe_b=pe_b):
        e.wait_ge(s_pe.sem, pe_b)
        e.tensor_copy(bc[:, 0:G], ps[2][:])
        e.tensor_copy(bc[:, G:2 * G], ps[3][:]).then_inc(s_ve.sem, 1)
    ve_bc = s_ve.bump()

    tmpv = apv(sb["tmp"]).rearrange("p (i t) -> p i t", i=2)

    if wb is None:
        @P.on("vector")
        def _(e, ve_bc=ve_bc):
            e.wait_ge(s_ve.sem, ve_bc)
            q_ve = c["q_ve"]
            for j in range(NCHa):
                q_ve.seq(e, e.tensor_tensor(tmpv[:, j % 2, :], xnv[:, j, :], bc[:, 0:G], ALU.mult))
                q_ve.seq(e, e.tensor_tensor(xnv[:, j, :], tmpv[:, j % 2, :], bc[:, G:2 * G], ALU.subtract))
                e.sem_inc(s_ve.sem, 1)
        s_ve.bump(NCHa)
    else:
        w_tile, b_tile = wb
        ve_marks = []
        for j in range(NCHa):
            @P.on("vector")
            def _(e, j=j, ve_bc=ve_bc):
                if j == 0:
                    e.wait_ge(s_ve.sem, ve_bc)
                q_ve = c["q_ve"]
                if j >= 2:
                    e.wait_ge(s_se.sem, ve_marks[j - 2])
                q_ve.seq(e, e.tensor_tensor(tmpv[:, j % 2, :], xnv[:, j, :], bc[:, 0:G], ALU.mult))
                e.tensor_tensor(tmpv[:, j % 2, :], tmpv[:, j % 2, :], bc[:, G:2 * G],
                                ALU.subtract).then_inc(s_ve.sem, 1)
            ve_j = s_ve.bump()

            @P.on("scalar")
            def _(e, j=j, ve_j=ve_j, w_tile=w_tile, b_tile=b_tile):
                e.wait_ge(s_ve.sem, ve_j)
                e.activation(out=xnv[:, j, :], in_=tmpv[:, j % 2, :], func=AF.Identity,
                             scale=w_tile[:, j:j + 1], bias=b_tile[:, j:j + 1]).then_inc(s_se.sem, 1)
            ve_marks.append(s_se.bump())

    barrier(P, c)



def emit_dense_residual(P, c, sb, ps, *, xn, wt_dram, xin_v, xout_v, toks,
                        nd=D, extra_mm=None, scale=1.0, halfbias=None,
                        out_override=None):
    s_dma, s_pe, s_ve, s_se, s_wd, s_out = (c[k] for k in
                                            ("s_dma", "s_pe", "s_ve", "s_se", "s_wd", "s_out"))
    ndc = nd // 128
    xnv = apv(xn).rearrange("p (j t) -> p j t", j=NCH)
    wtv = wt_dram
    wbv = apv(sb["wbuf"]).rearrange("p (i j n) -> p i j n", i=2, j=NCH)
    xsv = apv(sb["xs"]).rearrange("p (i t) -> p i t", i=4)
    tmpv = apv(sb["tmp"]).rearrange("p (i t) -> p i t", i=2)
    xov = apv(sb["xo"]).rearrange("p (i t) -> p i t", i=4)

    sl_w, sl_xe, sl_o = c["sl_w"], c["sl_xe"], c["sl_o"]
    pe_marks = [None] * ndc
    evac_done = [None] * ndc
    wwaits = [sl_w[d % 2].bump(16) for d in range(ndc)]

    @P.on("gpsimd")
    def _(e):
        for d in range(ndc):
            if d >= 2:
                e.wait_ge(s_pe.sem, pe_marks[d - 2])
            e.dma_start(out=wbv[:, d % 2, :, :],
                        in_=wtv[d, :, :, :]).then_inc(sl_w[d % 2].sem, 16)

    for d in range(ndc):
        @P.on("tensor")
        def _(e, d=d, ww=wwaits[d]):
            e.wait_ge(sl_w[d % 2].sem, ww)
            if d >= 2:
                e.wait_ge(s_ve.sem, evac_done[d - 2])
            pt = ps[d % 2]
            mm = None
            last = extra_mm is None
            for j in range(NCH):
                mm = e.matmul(pt[:], wbv[:, d % 2, j, :], xnv[:, j, :],
                              start=(j == 0), stop=(last and j == NCH - 1))
            if extra_mm is not None:
                mm = extra_mm(e, d, pt)
            mm.then_inc(s_pe.sem, 1)
        pe_marks[d] = s_pe.bump()

        if out_override is not None:
            evac_done[d] = out_override(P, c, d, ps[d % 2], pe_marks[d])
            continue

        xe_w = sl_xe[d % 4].bump(16)

        @P.on("sync")
        def _(e, d=d):
            if d >= 4:
                e.wait_ge(s_ve.sem, evac_done[d - 4])
            e.dma_start(out=xsv[:, d % 4, :], in_=xin_v[d, :, toks]).then_inc(sl_xe[d % 4].sem, 16)

        @P.on("scalar")
        def _(e, d=d, pe_cnt=pe_marks[d]):
            e.wait_ge(s_pe.sem, pe_cnt)
            if d >= 2:
                e.wait_ge(s_ve.sem, evac_done[d - 2])
            kw = {}
            if halfbias is not None:
                kw["bias"] = halfbias[:, d:d + 1]
            e.activation(out=tmpv[:, d % 2, :], in_=ps[d % 2][:], func=AF.Identity,
                         scale=scale, **kw).then_inc(s_se.sem, 1)
        se_cnt = s_se.bump()

        ow_prev = sl_o[d % 4].n

        @P.on("vector")
        def _(e, d=d, se_cnt=se_cnt, xe_w=xe_w, ow_prev=ow_prev):
            e.wait_ge(s_se.sem, se_cnt)
            e.wait_ge(sl_xe[d % 4].sem, xe_w)
            if d >= 4:
                e.wait_ge(sl_o[d % 4].sem, ow_prev)
            e.tensor_tensor(xov[:, d % 4, :], xsv[:, d % 4, :],
                            tmpv[:, d % 2, :], ALU.add).then_inc(s_ve.sem, 1)
        evac_done[d] = s_ve.bump()

        o_w = sl_o[d % 4].bump(16)

        @P.on("sync")
        def _(e, d=d, ve_cnt=evac_done[d]):
            e.wait_ge(s_ve.sem, ve_cnt)
            e.dma_start(out=xout_v[d, :, toks], in_=xov[:, d % 4, :]).then_inc(sl_o[d % 4].sem, 16)



FFN_SB = {
    0: {"halfbias": ([128, NCH], F32), "a_bf": ([128, 2 * NCH * 128], BF16),
        "bft_bf": ([128, 2 * D], BF16), "wexp": ([128, 2], F32),
        "wsm": ([16, 258], F32)},
    1: {"halfbias": ([128, NCH], F32), "a_bf": ([128, NCH * 64], BF16),
        "bt_bf": ([64, D], BF16), "gw_bf": ([128, NCH * 4], BF16),
        "gbcol": ([4, 1], F32), "erows": ([1, 4 * 64], BF16),
        "identf": ([128, 128], F32), "identb": ([128, 128], BF16),
        "fsc": ([128, 16], F32), "fcols": ([128, 12], BF16),
        "flagrows": ([1, 3 * G], BF16), "maskb": ([64, G], BF16),
        "hm": ([64, G], BF16), "gatef": ([4, G], F32)},
}


def ffn_dram_decls(nc, layer, pref=""):
    d = {}
    d["wt"] = nc.dram_tensor(pref + "wt", [NCH, 128, NCH, 128], F32, kind="ExternalInput").ap()
    d["biasm"] = nc.dram_tensor(pref + "biasm", [128, NCH], F32, kind="ExternalInput").ap()
    if layer == 0:
        d["a2t"] = nc.dram_tensor(pref + "a2t", [2, 128, NCH, 128], F32, kind="ExternalInput").ap()
        d["bft"] = nc.dram_tensor(pref + "bft", [256, D], F32, kind="ExternalInput").ap()
        d["coeff"] = nc.dram_tensor(pref + "coeff", [16, 1], F32, kind="ExternalInput").ap()
        d["scal"] = nc.dram_tensor(pref + "scal", [16, 1], F32, kind="ExternalInput").ap()
        d["emat"] = nc.dram_tensor(pref + "emat", [16, 256], F32, kind="ExternalInput").ap()
    else:
        d["a2t"] = nc.dram_tensor(pref + "a2t", [128, NCH, 64], F32, kind="ExternalInput").ap()
        d["btr"] = nc.dram_tensor(pref + "btr", [64, D], F32, kind="ExternalInput").ap()
        d["gwt"] = nc.dram_tensor(pref + "gwt", [128, NCH, 4], F32, kind="ExternalInput").ap()
        d["gb"] = nc.dram_tensor(pref + "gb", [4, 1], F32, kind="ExternalInput").ap()
        d["erows"] = nc.dram_tensor(pref + "erows", [1, 4 * 64], BF16, kind="ExternalInput").ap()
        d["identin"] = nc.dram_tensor(pref + "identin", [128, 128], F32, kind="ExternalInput").ap()
        d["identbf"] = nc.dram_tensor(pref + "identbf", [128, 128], BF16, kind="ExternalInput").ap()
    return d


def emit_ffn_phase(P, c, ps, psb, sb, xn, layer, io):
    xin, xout = io["xin"], io["xout"]
    wt, biasm = io["wt"], io["biasm"]
    a2t = io["a2t"]
    if layer == 0:
        bft, coeff, scal, emat = io["bft"], io["coeff"], io["scal"], io["emat"]
    else:
        btr, gwt, gb, erows, identin, identbf = (io["btr"], io["gwt"], io["gb"],
                                                 io["erows"], io["identin"], io["identbf"])
    s_dma, s_pe, s_ve, s_se, s_wd, s_out = (c[k] for k in
                                            ("s_dma", "s_pe", "s_ve", "s_se", "s_wd", "s_out"))

    if True:
        @P.on("sync")
        def _(e):
            e.dma_start(out=sb["halfbias"][:], in_=biasm).then_inc(s_dma.sem, 16)
        s_dma.bump(16)

        if layer == 0:
            abfv = apv(sb["a_bf"]).rearrange("p (cb j n) -> p cb j n", cb=2, j=NCH)
            bftv = bft.rearrange("(cb p) n -> cb p n", p=128)

            @P.on("gpsimd")
            def _(e):
                for cb in range(2):
                    e.dma_start(out=abfv[:, cb, :, :],
                                in_=a2t[cb, :, :, :]).then_inc(s_wd.sem, 16)
                for cb in range(2):
                    e.dma_start(out=sb["bft_bf"][:, cb * D:(cb + 1) * D],
                                in_=bftv[cb, :, :]).then_inc(s_wd.sem, 16)
            s_wd.bump(16 * 4)

            @P.on("sync")
            def _(e):
                e.dma_start(out=sb["wsm"][:, 0:1], in_=coeff).then_inc(s_dma.sem, 16)
                e.dma_start(out=sb["wsm"][:, 1:2], in_=scal).then_inc(s_dma.sem, 16)
                e.dma_start(out=sb["wsm"][:, 2:258], in_=emat).then_inc(s_dma.sem, 16)
            s_dma.bump(16 * 3)
        else:
            abfv = apv(sb["a_bf"]).rearrange("p (j n) -> p j n", j=NCH)

            @P.on("gpsimd")
            def _(e):
                e.dma_start(out=abfv[:, :, :], in_=a2t).then_inc(s_wd.sem, 16)
                e.dma_start(out=sb["bt_bf"][:], in_=btr).then_inc(s_wd.sem, 16)
                e.dma_start(out=apv(sb["gw_bf"]).rearrange("p (j n) -> p j n", j=NCH),
                            in_=gwt).then_inc(s_wd.sem, 16)
            s_wd.bump(16 * 3)

            @P.on("sync")
            def _(e):
                e.dma_start(out=sb["gbcol"][:], in_=gb).then_inc(s_dma.sem, 16)
                e.dma_start(out=sb["erows"][:], in_=erows).then_inc(s_dma.sem, 16)
                e.dma_start(out=sb["identf"][:], in_=identin).then_inc(s_dma.sem, 16)
                e.dma_start(out=sb["identb"][:], in_=identbf).then_inc(s_dma.sem, 16)
            s_dma.bump(16 * 4)

        barrier(P, c)

        @P.on("scalar")
        def _(e):
            e.mul(sb["halfbias"][:], sb["halfbias"][:], 0.5).then_inc(s_se.sem, 1)
        s_se.bump()

        if layer == 0:
            @P.on("vector")
            def _(e):
                e.tensor_tensor(sb["wsm"][:, 0:1], sb["wsm"][:, 0:1], sb["wsm"][:, 1:2],
                                ALU.mult).then_inc(s_ve.sem, 1)
            w_ve = s_ve.bump()

            @P.on("tensor")
            def _(e, w_ve=w_ve):
                e.wait_ge(s_ve.sem, w_ve)
                e.matmul(ps[3][:, 0:1], sb["wsm"][:, 2:130], sb["wsm"][:, 0:1], start=True, stop=True)
                mm = e.matmul(ps[3][:, 1:2], sb["wsm"][:, 130:258], sb["wsm"][:, 0:1],
                              start=True, stop=True)
                mm.then_inc(s_pe.sem, 1)
            wexp_pe = s_pe.bump()

            @P.on("scalar")
            def _(e, wexp_pe=wexp_pe):
                e.wait_ge(s_pe.sem, wexp_pe)
                e.copy(sb["wexp"][:], ps[3][:, 0:2]).then_inc(s_se.sem, 1)
            s_se.bump()

        barrier(P, c)

        xin_v = xin.rearrange("(j p) t -> j p t", p=128)
        xout_v = xout.rearrange("(j p) t -> j p t", p=128)

        for g in range(2):
            toks = slice(g * G, (g + 1) * G)
            emit_ln(P, c, sb, ps, xin_v, toks, xn[g])
            xnv = xn[g].ap().rearrange("p (j t) -> p j t", j=NCH)

            if layer == 0:
                @P.on("tensor")
                def _(e, xnv=xnv):
                    for cb in range(2):
                        pt = ps[2 + cb]
                        mm = None
                        for j in range(NCH):
                            mm = e.matmul(pt[:], abfv[:, cb, j, :], xnv[:, j, :],
                                          start=(j == 0), stop=(j == NCH - 1))
                        mm.then_inc(s_pe.sem, 1)
                xa_pe = s_pe.bump(2)

                @P.on("scalar")
                def _(e, xa_pe=xa_pe):
                    e.wait_ge(s_pe.sem, xa_pe)
                    e.activation(out=sb["xaw"][:, 0:G], in_=ps[2][:], func=AF.Identity,
                                 scale=sb["wexp"][:, 0:1])
                    a = e.activation(out=sb["xaw"][:, G:2 * G], in_=ps[3][:], func=AF.Identity,
                                     scale=sb["wexp"][:, 1:2])
                    a.then_inc(s_se.sem, 1)
                s_se.bump()

                bftbv = apv(sb["bft_bf"]).rearrange("p (cb n) -> p cb n", cb=2)

                def extra_mm(e, d, pt, bftbv=bftbv):
                    mm = None
                    for cb in range(2):
                        mm = e.matmul(pt[:], bftbv[:, cb, 128 * d:128 * (d + 1)],
                                      sb["xaw"][:, cb * G:(cb + 1) * G],
                                      start=False, stop=(cb == 1))
                    return mm
            else:
                gwbv = apv(sb["gw_bf"]).rearrange("p (j n) -> p j n", j=NCH)

                @P.on("tensor")
                def _(e, xnv=xnv, gwbv=gwbv):
                    mm = None
                    for j in range(NCH):
                        mm = e.matmul(ps[2][0:4, :], gwbv[:, j, :], xnv[:, j, :],
                                      start=(j == 0), stop=(j == NCH - 1))
                    for j in range(NCH):
                        mm = e.matmul(ps[3][0:64, :], abfv[:, j, :], xnv[:, j, :],
                                      start=(j == 0), stop=(j == NCH - 1))
                    mm.then_inc(s_pe.sem, 1)
                gh_pe = s_pe.bump()

                @P.on("scalar")
                def _(e, gh_pe=gh_pe):
                    e.wait_ge(s_pe.sem, gh_pe)
                    e.activation(out=sb["gatef"][:], in_=ps[2][0:4, :], func=AF.Identity,
                                 bias=sb["gbcol"][:]).then_inc(s_se.sem, 1)
                gate_se = s_se.bump()

                @P.on("tensor")
                def _(e, gate_se=gate_se):
                    e.wait_ge(s_se.sem, gate_se)
                    mm = None
                    for i in range(NT):
                        mm = e.transpose(ps[4][:, 4 * i:4 * (i + 1)],
                                         sb["gatef"][0:4, 128 * i:128 * (i + 1)],
                                         sb["identf"][0:4, 0:4])
                    mm.then_inc(s_pe.sem, 1)
                gt_pe = s_pe.bump()

                @P.on("vector")
                def _(e, gt_pe=gt_pe):
                    e.wait_ge(s_pe.sem, gt_pe)
                    fsc = sb["fsc"]
                    fcols = sb["fcols"]
                    q_ve = c["q_ve"]
                    for i in range(NT):
                        s4 = ps[4][:, 4 * i:4 * (i + 1)]
                        e.reduce_max(fsc[:, 0:1], s4[:, 0:2], axis=AX.X)
                        e.reduce_max(fsc[:, 1:2], s4[:, 0:3], axis=AX.X)
                        e.reduce_max(fsc[:, 2:3], s4[:, 1:4], axis=AX.X)
                        q_ve.seq(e, e.reduce_max(fsc[:, 3:4], s4[:, 2:4], axis=AX.X))
                        e.tensor_tensor(fcols[:, 3 * i + 0:3 * i + 1], fsc[:, 2:3], s4[:, 0:1], ALU.is_gt)
                        e.tensor_tensor(fcols[:, 3 * i + 1:3 * i + 2], fsc[:, 3:4], fsc[:, 0:1], ALU.is_gt)
                        q_ve.seq(e, e.tensor_tensor(fcols[:, 3 * i + 2:3 * i + 3], s4[:, 3:4], fsc[:, 1:2], ALU.is_gt))
                        e.sem_inc(s_ve.sem, 1)
                flag_ve = s_ve.bump(NT)

                @P.on("tensor")
                def _(e, flag_ve=flag_ve):
                    e.wait_ge(s_ve.sem, flag_ve)
                    mm = None
                    for gg in range(2):
                        for i in range(NT):
                            col = (gg * NT + i) * 128
                            mm = e.transpose(psb[0:1, col:col + 128],
                                             sb["fcols"][:, 3 * i + gg:3 * i + gg + 1],
                                             sb["identb"][:])
                    mm.then_inc(s_pe.sem, 1)
                fr_pe = s_pe.bump()

                @P.on("vector")
                def _(e, fr_pe=fr_pe):
                    e.wait_ge(s_pe.sem, fr_pe)
                    e.tensor_copy(sb["flagrows"][0:1, 0:2 * G], psb[0:1, 0:2 * G]).then_inc(s_ve.sem, 1)
                fr2_ve = s_ve.bump()

                @P.on("tensor")
                def _(e, fr2_ve=fr2_ve):
                    e.wait_ge(s_ve.sem, fr2_ve)
                    mm = None
                    for i in range(NT):
                        mm = e.transpose(psb[0:1, i * 128:(i + 1) * 128],
                                         sb["fcols"][:, 3 * i + 2:3 * i + 3],
                                         sb["identb"][:])
                    mm.then_inc(s_pe.sem, 1)
                fr3_pe = s_pe.bump()

                @P.on("vector")
                def _(e, fr3_pe=fr3_pe):
                    e.wait_ge(s_pe.sem, fr3_pe)
                    e.tensor_copy(sb["flagrows"][0:1, 2 * G:3 * G], psb[0:1, 0:G]).then_inc(s_ve.sem, 1)
                frows_ve = s_ve.bump()

                @P.on("tensor")
                def _(e, frows_ve=frows_ve):
                    e.wait_ge(s_ve.sem, frows_ve)
                    e.matmul(ps[2][0:64, :], sb["erows"][0:1, 0:64], sb["onesrow_bf"][:],
                             start=True, stop=False)
                    mm = None
                    for gg in range(3):
                        mm = e.matmul(ps[2][0:64, :], sb["erows"][0:1, 64 * (gg + 1):64 * (gg + 2)],
                                      sb["flagrows"][0:1, gg * G:(gg + 1) * G],
                                      start=False, stop=(gg == 2))
                    mm.then_inc(s_pe.sem, 1)
                mb_pe = s_pe.bump()

                @P.on("vector")
                def _(e, mb_pe=mb_pe):
                    e.wait_ge(s_pe.sem, mb_pe)
                    c["q_ve"].seq(e, e.tensor_copy(sb["maskb"][:], ps[2][0:64, :]))
                    e.tensor_tensor(sb["hm"][:], ps[3][0:64, :], sb["maskb"][:],
                                    ALU.mult).then_inc(s_ve.sem, 1)
                s_ve.bump()

                def extra_mm(e, d, pt):
                    return e.matmul(pt[:], sb["bt_bf"][0:64, 128 * d:128 * (d + 1)],
                                    sb["hm"][0:64, :], start=False, stop=True)

            barrier(P, c)

            emit_dense_residual(P, c, sb, ps, xn=xn[g], wt_dram=wt, xin_v=xin_v,
                                xout_v=xout_v, toks=toks, extra_mm=extra_mm,
                                scale=0.5, halfbias=sb["halfbias"])
            barrier(P, c)


def emit_final_wait(P, c):
    @P.on("sync")
    def _(e):
        for sl in c["sl_o"]:
            if sl.n:
                e.wait_ge(sl.sem, sl.n)
        if c["s_out"].n:
            e.wait_ge(c["s_out"].sem, c["s_out"].n)
        if c["s_ge"].n:
            e.wait_ge(c["s_ge"].sem, c["s_ge"].n)


def alloc_sb(nc, ctx, spec):
    E = ctx.enter_context
    out = {}
    for k, (shape, dt) in spec.items():
        out[k] = E(nc.sbuf_tensor("t_" + k, shape, dt))
    return out


def build_ffn_launch(layer):
    nc = bass.Bass("TRN2", target_bir_lowering=False)
    io = ffn_dram_decls(nc, layer)
    io["xin"] = nc.dram_tensor("xin", [D, 2 * G], F32, kind="ExternalInput").ap()
    io["xout"] = nc.dram_tensor("xout", [D, 2 * G], F32, kind="ExternalOutput").ap()
    with contextlib.ExitStack() as ctx:
        E = ctx.enter_context
        c = make_ctx(nc, ctx)
        ps = [E(nc.psum_tensor(f"psf{i}", [128, 512], F32)) for i in range(6)]
        psb = E(nc.psum_tensor("psb", [128, 1024], BF16))
        sb = sb_common(nc, ctx)
        sb.update(alloc_sb(nc, ctx, FFN_SB[layer]))
        if layer == 0:
            sb["xaw"] = E(nc.sbuf_tensor("t_xaw", [128, 2 * G], BF16))
        xn = [E(nc.sbuf_tensor("xn0", [128, NCH * G], BF16))] * 2
        block = E(nc.Block())
        P = Prog()
        emit_consts(P, c, sb)
        emit_ffn_phase(P, c, ps, psb, sb, xn, layer, io)
        emit_final_wait(P, c)
        P.emit(block)
    return nc



def _shard_x(x_full, core):
    seq, parity = core // 2, core % 2
    tiles = core_token_tiles(parity)
    idx = np.concatenate([np.arange(128 * t, 128 * (t + 1)) for t in tiles])
    return np.ascontiguousarray(x_full[seq, idx, :].T), idx


def _unshard_x(x_full, core, xc):
    seq, parity = core // 2, core % 2
    tiles = core_token_tiles(parity)
    idx = np.concatenate([np.arange(128 * t, 128 * (t + 1)) for t in tiles])
    x_full[seq, idx, :] = xc.T


def _chunkmajor(v):
    return np.ascontiguousarray(v.reshape(NCH, 128).T)


def run_ffn_layer(x_full, layer, weights):
    nc = _get_prog(("ffn", layer), lambda: build_ffn_launch(layer))
    in_maps = []
    shards = []
    for core in range(8):
        xc, idx = _shard_x(x_full, core)
        shards.append(idx)
        im = {"xin": xc}
        im.update(weights)
        in_maps.append(im)
    res = run_bass_kernel_spmd(nc, in_maps, core_ids=list(range(8)))
    out = x_full.copy()
    for core in range(8):
        _unshard_x(out, core, np.asarray(res.results[core]["xout"]))
    return out


_PROG_CACHE = {}


def _get_prog(key, builder):
    if key not in _PROG_CACHE:
        _PROG_CACHE[key] = builder()
    return _PROG_CACHE[key]


def _wblocks(wT):
    ndim = wT.shape[1]
    ndc = ndim // 128
    return np.ascontiguousarray(
        wT.reshape(NCH, 128, ndc, 128).transpose(2, 1, 0, 3))


def _pjn(wT):
    n = wT.shape[1]
    return np.ascontiguousarray(wT.reshape(NCH, 128, n).transpose(1, 0, 2))


def ffn0_weights(inputs):
    W = np.asarray(inputs["L0_ffn_W"], np.float32)
    A = np.asarray(inputs["L0_ffn_A"], np.float32)
    Bf = np.asarray(inputs["L0_ffn_Bf"], np.float32)
    emat = np.zeros((16, 256), np.float32)
    for e_ in range(16):
        emat[e_, e_ * 16:(e_ + 1) * 16] = 1.0
    a2t = np.ascontiguousarray(A.reshape(256, D).T)
    return {
        "wt": _wblocks(np.ascontiguousarray(W.T)),
        "biasm": _chunkmajor(np.asarray(inputs["L0_ffn_bias"], np.float32)),
        "a2t": _wblocks(a2t),
        "bft": np.ascontiguousarray(Bf.transpose(0, 2, 1).reshape(256, D)),
        "coeff": np.ascontiguousarray(np.asarray(inputs["L0_ffn_coeffs"], np.float32).reshape(16, 1)),
        "scal": np.ascontiguousarray(np.asarray(inputs["L0_ffn_scales"], np.float32).reshape(16, 1)),
        "emat": emat,
    }


def ffn1_weights(inputs):
    erows = np.zeros((1, 4 * 64), BF)
    erows[0, 0:8] = 1
    erows[0, 64 + 8:64 + 16] = 1
    erows[0, 128 + 16:128 + 32] = 1
    erows[0, 192 + 32:192 + 64] = 1
    return {
        "wt": _wblocks(np.ascontiguousarray(np.asarray(inputs["L1_ffn_W"], np.float32).T)),
        "biasm": _chunkmajor(np.asarray(inputs["L1_ffn_b"], np.float32)),
        "a2t": _pjn(np.ascontiguousarray(np.asarray(inputs["L1_ffn_A"], np.float32).T)),
        "btr": np.ascontiguousarray(np.asarray(inputs["L1_ffn_B"], np.float32).T),
        "gwt": _pjn(np.ascontiguousarray(np.asarray(inputs["L1_ffn_gate_w"], np.float32).T)),
        "gb": np.ascontiguousarray(np.asarray(inputs["L1_ffn_gate_b"], np.float32).reshape(4, 1)),
        "erows": erows,
        "identin": np.eye(128, dtype=np.float32),
        "identbf": np.eye(128, dtype=BF),
    }


def carve(arena, spec, base_bytes=0):
    views = {}
    cur = 0
    for k, (shape, dt) in spec.items():
        assert dt == arena.dtype if hasattr(arena, "dtype") else True
        views[k] = apv(arena)[0:shape[0], cur:cur + shape[1]]
        cur += shape[1]
    return views, cur


def build_megakernel():
    nc = bass.Bass("TRN2", target_bir_lowering=False)
    io_f0 = ffn_dram_decls(nc, 0, pref="f0")
    io_f1 = ffn_dram_decls(nc, 1, pref="f1")
    io_qkv = qkv_dram_decls(nc)
    io_att = attc_dram_decls(nc)
    io_a1 = a1_dram_decls(nc)
    xin = nc.dram_tensor("xin", [D, 2 * G], F32, kind="ExternalInput").ap()
    xout = nc.dram_tensor("xout", [D, 2 * G], F32, kind="ExternalOutput").ap()
    xa = nc.dram_tensor("xa", [D, 2 * G], F32, kind="Internal").ap()
    xb = nc.dram_tensor("xb", [D, 2 * G], F32, kind="Internal").ap()
    qn_i = nc.dram_tensor("qn", [NH, 128, 2 * G], BF16, kind="Internal").ap()
    kn_t = nc.dram_tensor("kn", [NH * 128, 2 * G], BF16)
    vt_t = nc.dram_tensor("vt", [2 * G, D], BF16)
    kng_t = nc.dram_tensor("kng", [2 * NH * 128, 2 * G], BF16)
    vtg_t = nc.dram_tensor("vtg", [2 * 2 * G, D], BF16)
    kn_i = kn_t.ap().rearrange("(h p) t -> h p t", p=128)
    vt_i = vt_t.ap()
    kng = kng_t.ap()
    vtg = vtg_t.ap()

    with contextlib.ExitStack() as ctx:
        E = ctx.enter_context
        c = make_ctx(nc, ctx)
        ps = [E(nc.psum_tensor(f"psf{i}", [128, 512], F32)) for i in range(6)]
        psb = [E(nc.psum_tensor(f"psbt{i}", [128, 1024], BF16)) for i in range(2)]
        com = sb_common(nc, ctx)
        A0 = E(nc.sbuf_tensor("A0", [128, NCH * G], BF16))
        A1 = E(nc.sbuf_tensor("A1", [128, NCH * G], BF16))
        A2 = E(nc.sbuf_tensor("A2", [128, NCH * G], BF16))
        A3 = E(nc.sbuf_tensor("A3", [128, 9216], BF16))
        SCB = E(nc.sbuf_tensor("SCB", [128, 16384], BF16))
        SCF = E(nc.sbuf_tensor("SCF", [128, 2304], F32))

        def phase_sb(spec):
            sb = dict(com)
            curb = curf = 0
            for k, (shape, dt) in spec.items():
                if dt == BF16:
                    sb[k] = apv(SCB)[0:shape[0], curb:curb + shape[1]]
                    curb += shape[1]
                else:
                    sb[k] = apv(SCF)[0:shape[0], curf:curf + shape[1]]
                    curf += shape[1]
                assert curb <= 16384 and curf <= 2304, (k, curb, curf)
            return sb

        block = E(nc.Block())
        P = Prog()
        com2 = dict(com)
        emit_consts(P, c, com2)

        sb = phase_sb(FFN_SB[0])
        sb["xaw"] = apv(A3)[:, 0:1024]
        emit_ffn_phase(P, c, ps, psb[0], sb, [A0, A0], 0,
                       dict(io_f0, xin=xin, xout=xa))

        rg = [[0, 1], [2, 3], [4, 5], [6, 7]]

        def hook_kn(P, c):
            s_ge = c["s_ge"]

            @P.on("gpsimd")
            def _(e, w=c["s_out"].n):
                e.wait_ge(c["s_out"].sem, w)
                e.collective_compute("AllGather", ALU.bypass, replica_groups=rg,
                                     ins=[kn_t.ap().opt()],
                                     outs=[kng_t.ap().opt()]).then_inc(s_ge.sem)
            c["m_ge_k"] = s_ge.bump()

        def hook_vt(P, c):
            s_ge = c["s_ge"]

            @P.on("gpsimd")
            def _(e, w0=c["sl_o"][0].n, w1=c["sl_o"][1].n):
                e.wait_ge(c["sl_o"][0].sem, w0)
                e.wait_ge(c["sl_o"][1].sem, w1)
                e.collective_compute("AllGather", ALU.bypass, replica_groups=rg,
                                     ins=[vt_t.ap().opt()],
                                     outs=[vtg_t.ap().opt()]).then_inc(s_ge.sem)
            c["m_ge_v"] = s_ge.bump()

        sb = phase_sb(QKV_SB)
        sb["_psb0"], sb["_psb1"] = psb[0], psb[1]
        emit_qkv_phase(P, c, ps, sb, A0, A1, A2,
                       dict(io_qkv, xin=xa, qn=qn_i, kn=kn_i, vt=vt_i,
                            hook_kn=hook_kn, hook_vt=hook_vt))

        sb = phase_sb(ATTC_SB)
        emit_attc_phase(P, c, ps, sb, A1, A2,
                        apv(A3)[:, 0:4096], apv(A3)[:, 4096:8192],
                        apv(A3)[:, 8192:9216],
                        dict(io_att, xin=xa, xout=xb, qn=qn_i,
                             kng=kng.rearrange("(r h p) t -> r h p t", r=2, p=128),
                             vtg=vtg.rearrange("(r t) dd -> r t dd", r=2)))

        sb = phase_sb(FFN_SB[0])
        sb["xaw"] = apv(A3)[:, 0:1024]
        emit_ffn_phase(P, c, ps, psb[0], sb, [A0, A0], 0,
                       dict(io_f0, xin=xb, xout=xa))

        sb = phase_sb(FFN_SB[1])
        emit_ffn_phase(P, c, ps, psb[0], sb, [A0, A0], 1,
                       dict(io_f1, xin=xa, xout=xb))

        sb = phase_sb(A1_SB)
        emit_a1_phase(P, c, ps, psb, sb, A0,
                      apv(A1)[:, 0:4096], apv(A1)[:, 4096:8192],
                      apv(A1)[:, 8192:12288], apv(A1)[:, 12288:16384],
                      apv(A2)[0:8, 0:8192], apv(A2)[0:8, 8192:16384],
                      apv(A3)[:, 0:1024], apv(A3)[:, 1024:2048],
                      apv(SCF)[0:128, 2288:2304],
                      dict(io_a1, xin=xb, xout=xa))

        sb = phase_sb(FFN_SB[1])
        emit_ffn_phase(P, c, ps, psb[0], sb, [A0, A0], 1,
                       dict(io_f1, xin=xa, xout=xout))

        emit_final_wait(P, c)
        P.emit(block)
    return nc


FUSED = True


def kernel(**inputs):
    x = np.asarray(inputs["L0_attn_hidden_states"], dtype=np.float32)
    w_ffn0 = ffn0_weights(inputs)
    w_ffn1 = ffn1_weights(inputs)
    if not FUSED:
        x = run_ffn_layer(x, 0, w_ffn0)
        x = run_attn0(x, inputs)
        x = run_ffn_layer(x, 0, w_ffn0)
        x = run_ffn_layer(x, 1, w_ffn1)
        x = run_attn1(x, inputs)
        x = run_ffn_layer(x, 1, w_ffn1)
        return x

    w_a0 = attn0_weights(inputs)
    w_a1 = attn1_weights(inputs)
    nc = _get_prog(("mega",), build_megakernel)
    in_maps = []
    for core in range(8):
        seq, par = core // 2, core % 2
        xc, _ = _shard_x(x, core)
        im = {"xin": xc}
        im.update({"f0" + k: v for k, v in w_ffn0.items()})
        im.update({"f1" + k: v for k, v in w_ffn1.items()})
        im.update({"wqkvt": w_a0["wqkvt"], "i_qlnw": w_a0["qlnw"], "i_qlnb": w_a0["qlnb"],
                   "i_klnw": w_a0["klnw"], "i_klnb": w_a0["klnb"], "identbf2": w_a0["identbf"]})
        im.update({"woutt": w_a0["woutt"], "colt": w_a0["colt"], "colrow": w_a0["colrow"],
                   "maskab": np.stack([w_a0["maskab"][par], w_a0["maskab"][1 - par]]),
                   "maskq": np.stack([w_a0["maskq"][par], w_a0["maskq"][1 - par]]),
                   "identf": w_a0["identf"]})
        im.update(w_a1)
        in_maps.append(im)
    res = run_bass_kernel_spmd(nc, in_maps, core_ids=list(range(8)))
    out = x.copy()
    for core in range(8):
        _unshard_x(out, core, np.asarray(res.results[core]["xout"]))
    return out



QKV_SB = {
    "clip": ([128, 2], F32), "qlnw": ([128, NCH], F32), "qlnb": ([128, NCH], F32),
    "klnw": ([128, NCH], F32), "klnb": ([128, NCH], F32),
    "vstage": ([128, 2 * D], BF16), "identb": ([128, 128], BF16),
}


def qkv_dram_decls(nc):
    d = {}
    d["wqkvt"] = nc.dram_tensor("wqkvt", [96, 128, NCH, 128], F32, kind="ExternalInput").ap()
    for nm in ("qlnw", "qlnb", "klnw", "klnb"):
        d[nm] = nc.dram_tensor("i_" + nm, [128, NCH], F32, kind="ExternalInput").ap()
    d["identbf2"] = nc.dram_tensor("identbf2", [128, 128], BF16, kind="ExternalInput").ap()
    return d


def emit_qkv_phase(P, c, ps, sb, xn, qf, kf, io):
    xin = io["xin"]
    wqkvt = io["wqkvt"]
    qlnw, qlnb, klnw, klnb = io["qlnw"], io["qlnb"], io["klnw"], io["klnb"]
    identbf = io["identbf2"]
    qn_o, kn_o, vt_o = io["qn"], io["kn"], io["vt"]
    s_dma, s_pe, s_ve, s_se, s_wd, s_out = (c[k] for k in
                                            ("s_dma", "s_pe", "s_ve", "s_se", "s_wd", "s_out"))
    if True:
        @P.on("vector")
        def _(e):
            e.memset(sb["clip"][:, 0:1], -CLIP)
            e.memset(sb["clip"][:, 1:2], CLIP).then_inc(s_ve.sem, 1)
        s_ve.bump()

        @P.on("sync")
        def _(e):
            for nm, t in (("qlnw", qlnw), ("qlnb", qlnb), ("klnw", klnw), ("klnb", klnb)):
                e.dma_start(out=sb[nm][:], in_=t).then_inc(s_dma.sem, 16)
            e.dma_start(out=sb["identb"][:], in_=identbf).then_inc(s_dma.sem, 16)
        s_dma.bump(16 * 5)

        barrier(P, c)

        @P.on("scalar")
        def _(e):
            sc = 1.0 / np.sqrt(128.0)
            c["q_se"].seq(e, e.mul(sb["qlnw"][:], sb["qlnw"][:], sc))
            c["q_se"].seq(e, e.mul(sb["qlnb"][:], sb["qlnb"][:], sc))
        barrier(P, c)

        xin_v = xin.rearrange("(j p) t -> j p t", p=128)
        wbv = apv(sb["wbuf"]).rearrange("p (i j n) -> p i j n", i=2, j=NCH)
        xnv = apv(xn).rearrange("p (j t) -> p j t", j=NCH)
        sl_w = c["sl_w"]

        for g in range(2):
            toks = slice(g * G, (g + 1) * G)
            emit_ln(P, c, sb, ps, xin_v, toks, xn)
            if P2_STAGE < 1:
                continue

            arenas = (qf, kf)
            pe_marks = [None] * 64
            evac_done = [None] * 64
            wwaits = [sl_w[d % 2].bump(16) for d in range(64)]

            @P.on("gpsimd")
            def _(e, wwaits=wwaits, pe_marks=pe_marks, g=g):
                for d in range(64):
                    if d >= 2:
                        e.wait_ge(s_pe.sem, pe_marks[d - 2])
                    e.dma_start(out=wbv[:, d % 2, :, :],
                                in_=wqkvt[d, :, :, :]).then_inc(sl_w[d % 2].sem, 16)

            for d in range(64):
                @P.on("tensor")
                def _(e, d=d, ww=wwaits[d], ed=(evac_done[d - 2] if d >= 2 else 0)):
                    e.wait_ge(sl_w[d % 2].sem, ww)
                    if d >= 2:
                        e.wait_ge(s_ve.sem, ed)
                    pt = ps[d % 2]
                    mm = None
                    for j in range(NCH):
                        mm = e.matmul(pt[:], wbv[:, d % 2, j, :], xnv[:, j, :],
                                      start=(j == 0), stop=(j == NCH - 1))
                    mm.then_inc(s_pe.sem, 1)
                pe_marks[d] = s_pe.bump()

                @P.on("vector")
                def _(e, d=d, pe_cnt=pe_marks[d]):
                    e.wait_ge(s_pe.sem, pe_cnt)
                    ar = arenas[d // NCH].ap().rearrange("p (j t) -> p j t", j=NCH)
                    e.tensor_scalar(ar[:, d % NCH, :], ps[d % 2][:],
                                    sb["clip"][:, 0:1], sb["clip"][:, 1:2],
                                    op0=ALU.max, op1=ALU.min).then_inc(s_ve.sem, 1)
                evac_done[d] = s_ve.bump()

            barrier(P, c)

            if P2_STAGE < 2:
                continue
            for ai, (arena, w_t, b_t, out_t) in enumerate(
                    ((qf, sb["qlnw"], sb["qlnb"], qn_o), (kf, sb["klnw"], sb["klnb"], kn_o))):
                emit_norm(P, c, sb, ps, arena, wb=(w_t, b_t))
                arv = apv(arena).rearrange("p (j t) -> p j t", j=NCH)

                @P.on("sync")
                def _(e, arv=arv, out_t=out_t, g=g):
                    for j in range(NCH):
                        e.dma_start(out=out_t[j, :, slice(g * G, (g + 1) * G)],
                                    in_=arv[:, j, :]).then_inc(s_out.sem, 16)
                s_out.bump(16 * NCH)
                if g == 1 and ai == 1 and "hook_kn" in io:
                    io["hook_kn"](P, c)

            if P2_STAGE < 3:
                continue
            vf = qf
            vfv = apv(vf).rearrange("p (j t) -> p j t", j=NCH)
            pe_v = [None] * NCH
            ve_v = [None] * NCH
            vwaits = [sl_w[d % 2].bump(16) for d in range(NCH)]

            @P.on("gpsimd")
            def _(e, vwaits=vwaits, pe_v=pe_v, g=g):
                for d in range(NCH):
                    if d >= 2:
                        e.wait_ge(s_pe.sem, pe_v[d - 2])
                    e.dma_start(out=wbv[:, d % 2, :, :],
                                in_=wqkvt[64 + d, :, :, :]).then_inc(sl_w[d % 2].sem, 16)

            for d in range(NCH):
                @P.on("tensor")
                def _(e, d=d, vw=vwaits[d], ev=(ve_v[d - 2] if d >= 2 else 0)):
                    e.wait_ge(sl_w[d % 2].sem, vw)
                    if d >= 2:
                        e.wait_ge(s_ve.sem, ev)
                    pt = ps[d % 2]
                    mm = None
                    for j in range(NCH):
                        mm = e.matmul(pt[:], wbv[:, d % 2, j, :], xnv[:, j, :],
                                      start=(j == 0), stop=(j == NCH - 1))
                    mm.then_inc(s_pe.sem, 1)
                pe_v[d] = s_pe.bump()

                @P.on("vector")
                def _(e, d=d, pe_cnt=pe_v[d]):
                    e.wait_ge(s_pe.sem, pe_cnt)
                    e.tensor_scalar(vfv[:, d, :], ps[d % 2][:],
                                    sb["clip"][:, 0:1], sb["clip"][:, 1:2],
                                    op0=ALU.max, op1=ALU.min).then_inc(s_ve.sem, 1)
                ve_v[d] = s_ve.bump()

            barrier(P, c)

            if P2_STAGE < 4:
                barrier(P, c)
                continue
            vstage = sb["vstage"]
            vsv2 = apv(vstage).rearrange("p (i n) -> p i n", i=2)
            psb_v = (sb["_psb0"], sb["_psb1"])
            vti = 0
            vt_ve = []
            for i in range(NT):
                for h in range(NCH):
                    @P.on("tensor")
                    def _(e, h=h, i=i, vti=vti, wv=(vt_ve[vti - 2] if vti >= 2 else 0)):
                        if vti >= 2:
                            e.wait_ge(s_ve.sem, wv)
                        mm = e.transpose(psb_v[vti % 2][:, 0:128],
                                         vfv[:, h, 128 * i:128 * (i + 1)], sb["identb"][:])
                        mm.then_inc(s_pe.sem, 1)
                    vt_pe = s_pe.bump()

                    @P.on("vector")
                    def _(e, h=h, i=i, vti=vti, vt_pe=vt_pe,
                          ow=(c["sl_o"][i % 2].n if h == 0 else 0)):
                        e.wait_ge(s_pe.sem, vt_pe)
                        if h == 0 and i >= 2:
                            e.wait_ge(c["sl_o"][i % 2].sem, ow)
                        e.tensor_copy(vsv2[:, i % 2, 128 * h:128 * (h + 1)],
                                      psb_v[vti % 2][:, 0:128]).then_inc(s_ve.sem, 1)
                    ve_cnt = s_ve.bump()
                    vt_ve.append(ve_cnt)
                    vti += 1

                c["sl_o"][i % 2].bump(16)

                @P.on("sync")
                def _(e, i=i, ve_cnt=ve_cnt, g=g):
                    e.wait_ge(s_ve.sem, ve_cnt)
                    e.dma_start(out=vt_o[slice((g * 4 + i) * 128, (g * 4 + i + 1) * 128), :],
                                in_=vsv2[:, i % 2, :]).then_inc(c["sl_o"][i % 2].sem, 16)

            if g == 1 and "hook_vt" in io:
                io["hook_vt"](P, c)
            barrier(P, c)


def build_qkv_launch():
    nc = bass.Bass("TRN2", target_bir_lowering=False)
    io = qkv_dram_decls(nc)
    io["xin"] = nc.dram_tensor("xin", [D, 2 * G], F32, kind="ExternalInput").ap()
    io["qn"] = nc.dram_tensor("qn", [NH, 128, 2 * G], BF16, kind="ExternalOutput").ap()
    io["kn"] = nc.dram_tensor("kn", [NH, 128, 2 * G], BF16, kind="ExternalOutput").ap()
    io["vt"] = nc.dram_tensor("vt", [2 * G, D], BF16, kind="ExternalOutput").ap()
    with contextlib.ExitStack() as ctx:
        E = ctx.enter_context
        c = make_ctx(nc, ctx)
        ps = [E(nc.psum_tensor(f"psf{i}", [128, 512], F32)) for i in range(6)]
        sb = sb_common(nc, ctx)
        sb.update(alloc_sb(nc, ctx, QKV_SB))
        sb["_psb0"] = E(nc.psum_tensor("psb_v0", [128, 1024], BF16))
        sb["_psb1"] = E(nc.psum_tensor("psb_v1", [128, 1024], BF16))
        xn = E(nc.sbuf_tensor("xn0", [128, NCH * G], BF16))
        qf = E(nc.sbuf_tensor("qf", [128, NCH * G], BF16))
        kf = E(nc.sbuf_tensor("kf", [128, NCH * G], BF16))
        block = E(nc.Block())
        P = Prog()
        emit_consts(P, c, sb)
        emit_qkv_phase(P, c, ps, sb, xn, qf, kf, io)
        emit_final_wait(P, c)
        P.emit(block)
    return nc


ATTC_SB = {
    "crow": ([1, 2 * S], BF16), "mcol": ([128, 8], F32), "nmb": ([1, 2 * G], BF16),
    "colt": ([128, NH * 16], F32),
    "maskab": ([128, 2 * 256], F32), "maskq": ([128, 2 * 256], F32),
    "identf": ([128, 128], F32),
}


def attc_dram_decls(nc):
    d = {}
    d["woutt"] = nc.dram_tensor("woutt", [NCH, 128, NCH, 128], F32, kind="ExternalInput").ap()
    d["colt"] = nc.dram_tensor("colt", [128, NH * 16], F32, kind="ExternalInput").ap()
    d["colrow"] = nc.dram_tensor("colrow", [NH, S], BF16, kind="ExternalInput").ap()
    d["maskab"] = nc.dram_tensor("maskab", [2, 128, 256], F32, kind="ExternalInput").ap()
    d["maskq"] = nc.dram_tensor("maskq", [2, 128, 256], F32, kind="ExternalInput").ap()
    d["identf"] = nc.dram_tensor("identf", [128, 128], F32, kind="ExternalInput").ap()
    return d


def emit_attc_phase(P, c, ps, sb, qn_sb, ob, ksb, vsb, expt, io):
    xin, xout = io["xin"], io["xout"]
    qn_i = io["qn"]
    kng = io["kng"]
    vtg = io["vtg"]
    woutt, colt, colrow = io["woutt"], io["colt"], io["colrow"]
    maskab, maskq, identf = io["maskab"], io["maskq"], io["identf"]
    psm = ps[4][0:1, :]
    psz = ps[3][0:1, :]
    s_dma, s_pe, s_ve, s_se, s_wd, s_out = (c[k] for k in
                                            ("s_dma", "s_pe", "s_ve", "s_se", "s_wd", "s_out"))
    if True:
        @P.on("sync")
        def _(e):
            e.dma_start(out=sb["colt"][:], in_=colt).then_inc(s_dma.sem, 16)
            e.dma_start(out=apv(sb["maskab"]).rearrange("p (g n) -> p g n", g=2),
                        in_=maskab.rearrange("g p n -> p g n")).then_inc(s_dma.sem, 16)
            e.dma_start(out=apv(sb["maskq"]).rearrange("p (g n) -> p g n", g=2),
                        in_=maskq.rearrange("g p n -> p g n")).then_inc(s_dma.sem, 16)
            e.dma_start(out=sb["identf"][:], in_=identf).then_inc(s_dma.sem, 16)
        s_dma.bump(16 * 4)

        barrier(P, c)

        xin_v = xin.rearrange("(j p) t -> j p t", p=128)
        xout_v = xout.rearrange("(j p) t -> j p t", p=128)
        qnv = apv(qn_sb).rearrange("p (h t) -> p h t", h=NH)
        obv = apv(ob).rearrange("p (h t) -> p h t", h=NH)
        ksv = apv(ksb).rearrange("p (i t) -> p i t", i=2)
        vsv = apv(vsb).rearrange("p (i t) -> p i t", i=2)
        etv = apv(expt).rearrange("p (i t) -> p i t", i=2)
        crv = apv(sb["crow"]).rearrange("o (i t) -> o i t", i=2)
        rows = sb["rows"]
        nmrow_f = rows[0:1, 0:G]
        zrow = rows[0:1, G:2 * G]

        sl_k = c["sl_x"][0:2]
        sl_v = c["sl_x"][2:4]
        sl_c = c["sl_c"]

        for g in ATTN_GROUPS:
            kmax = 8 + 8 * g
            toks = slice(g * G, (g + 1) * G)

            @P.on("sync")
            def _(e, g=g, toks=toks):
                e.dma_start(out=qnv[:, :, :], in_=qn_i[:, :, toks].rearrange("h p t -> p h t")).then_inc(s_dma.sem, 16)
            s_dma.bump(16)
            barrier(P, c)

            pe_last = [0] * NH
            kw = [None] * NH
            vw = [None] * NH
            cw = [None] * NH
            for h in range(NH):
                kw[h] = sl_k[h % 2].bump(16 * 2 * (g + 1))
                vw[h] = sl_v[h % 2].bump(16 * 2 * (g + 1))
                cw[h] = sl_c[h % 2].bump(16)

            ks6 = apv(ksb).rearrange("p (i gg li r tb) -> p i gg li r tb",
                                     i=2, gg=2, li=4, r=2)
            vs6 = apv(vsb).rearrange("p (i gg li r dd) -> p i gg li r dd",
                                     i=2, gg=2, li=4, r=2)
            vtg_v = vtg.rearrange("r (li2 p) dd -> r li2 p dd", p=128)

            @P.on("sync")
            def _(e, g=g, kmax=kmax, pe_last=pe_last,
                  gk=c.get("m_ge_k", 0), gv=c.get("m_ge_v", 0)):
                def k_loads(h):
                    for gg in range(g + 1):
                        for r in range(2):
                            rd = r if gg == 0 else 1 - r
                            e.dma_start(
                                out=ks6[:, h % 2, gg, :, rd, :],
                                in_=kng[r, h, :, 512 * gg:512 * (gg + 1)].rearrange(
                                    "p (li tb) -> p li tb", li=4)).then_inc(sl_k[h % 2].sem, 16)

                def v_loads(h):
                    for gg in range(g + 1):
                        for r in range(2):
                            rd = r if gg == 0 else 1 - r
                            e.dma_start(
                                out=vs6[:, h % 2, gg, :, rd, :],
                                in_=vtg_v[r, 4 * gg:4 * (gg + 1), :, 128 * h:128 * (h + 1)].rearrange(
                                    "li p dd -> p li dd")).then_inc(sl_v[h % 2].sem, 16)

                def c_loads(h):
                    e.dma_start(out=crv[0:1, h % 2, 0:128 * kmax],
                                in_=colrow[h:h + 1, 0:128 * kmax]).then_inc(sl_c[h % 2].sem, 16)

                if gk:
                    e.wait_ge(c["s_ge"].sem, gk)
                for h in (0, 1):
                    k_loads(h)
                    c_loads(h)
                if gv:
                    e.wait_ge(c["s_ge"].sem, gv)
                for h in (0, 1):
                    v_loads(h)
                for h in range(2, NH):
                    e.wait_ge(s_pe.sem, pe_last[h - 2])
                    k_loads(h)
                    v_loads(h)
                    c_loads(h)

            nmrow2 = [rows[0:1, 0:G], rows[0:1, 2 * G:3 * G]]
            zrow2 = [rows[0:1, G:2 * G], rows[0:1, 3 * G:4 * G]]
            nmbv = [apv(sb["nmb"])[0:1, 0:G], apv(sb["nmb"])[0:1, G:2 * G]]
            zbv2 = [apv(sb["tmp"])[:, 0:G], apv(sb["tmp"])[:, G:2 * G]]
            psmv = [ps[4][0:1, :], ps[5][0:1, :]]
            pszv = [ps[0][0:1, :], ps[1][0:1, :]]
            pvb = [ps[2], ps[3]]
            scb = [ps[4], ps[5]]
            zbb = [ps[0], ps[1]]

            for hj in range(0, NH, 2):
                hpair = (hj, hj + 1)
                mx_fin = [[0] * 4, [0] * 4]
                tr_pe = [[0] * 4, [0] * 4]
                for s_ in range(4):
                    nk = 128 * (2 * s_ + 2 + 8 * g)
                    npc = (nk + 511) // 512
                    for hp in range(2):
                        h = hpair[hp]
                        redv = [s_ve.bump() for _ in range(npc)]
                        mx_fin[hp][s_] = s_ve.bump()

                        @P.on("tensor")
                        def _(e, g=g, h=h, hp=hp, s_=s_, npc=npc, nk=nk,
                              kwh=kw[h], cwh=cw[h], redv=tuple(redv),
                              mxw=(mx_fin[hp][s_ - 1] if s_ > 0 else 0)):
                            if s_ == 0:
                                e.wait_ge(sl_k[hp].sem, kwh)
                                e.wait_ge(sl_c[hp].sem, cwh)
                            else:
                                e.wait_ge(s_ve.sem, mxw)
                            for pc in range(npc):
                                n0 = 512 * pc
                                n1 = min(nk, n0 + 512)
                                if pc >= 2:
                                    e.wait_ge(s_ve.sem, redv[pc - 2])
                                bank = ps[2 * hp + pc % 2]
                                e.matmul(bank[:, 0:n1 - n0], sb["onesrow_bf"][0:1, 0:128],
                                         crv[0:1, hp, n0:n1], start=True, stop=False)
                                e.matmul(bank[:, 0:n1 - n0], qnv[:, h, 128 * s_:128 * (s_ + 1)],
                                         ksv[:, hp, n0:n1], start=False,
                                         stop=True).then_inc(s_pe.sem, 1)
                        pevals = [s_pe.bump() for _ in range(npc)]

                        @P.on("vector")
                        def _(e, g=g, hp=hp, npc=npc, nk=nk, pevals=tuple(pevals)):
                            q_ve = c["q_ve"]
                            for pc in range(npc):
                                e.wait_ge(s_pe.sem, pevals[pc])
                                n0 = 512 * pc
                                n1 = min(nk, n0 + 512)
                                bank = ps[2 * hp + pc % 2]
                                if pc == npc - 1:
                                    lw = n1 - n0
                                    q_ve.seq(e, e.tensor_tensor(
                                        bank[:, lw - 256:lw], bank[:, lw - 256:lw],
                                        sb["maskq"][:, 256 * g:256 * (g + 1)], ALU.add))
                                q_ve.seq(e, e.reduce_max(
                                    sb["mcol"][:, 4 * hp + pc:4 * hp + pc + 1],
                                    bank[:, 0:n1 - n0], axis=AX.X, negate=True))
                                e.sem_inc(s_ve.sem, 1)
                            for pc in range(1, npc):
                                q_ve.seq(e, e.tensor_tensor(
                                    sb["mcol"][:, 4 * hp:4 * hp + 1],
                                    sb["mcol"][:, 4 * hp:4 * hp + 1],
                                    sb["mcol"][:, 4 * hp + pc:4 * hp + pc + 1], ALU.min))
                            e.sem_inc(s_ve.sem, 1)

                    for hp in range(2):
                        @P.on("tensor")
                        def _(e, hp=hp, s_=s_, mv=mx_fin[hp][s_]):
                            e.wait_ge(s_ve.sem, mv)
                            e.transpose(psmv[hp][0:1, 128 * s_:128 * (s_ + 1)],
                                        sb["mcol"][:, 4 * hp:4 * hp + 1],
                                        sb["identf"][:]).then_inc(s_pe.sem, 1)
                        tr_pe[hp][s_] = s_pe.bump()

                nm_ve = [0, 0]
                for hp in range(2):
                    @P.on("vector")
                    def _(e, hp=hp, trv=tr_pe[hp][3]):
                        e.wait_ge(s_pe.sem, trv)
                        q_ve = c["q_ve"]
                        q_ve.seq(e, e.tensor_copy(nmrow2[hp], psmv[hp]))
                        q_ve.seq(e, e.tensor_copy(nmbv[hp], nmrow2[hp]))
                        e.sem_inc(s_ve.sem, 1)
                    nm_ve[hp] = s_ve.bump()

                stv = [[0] * kmax, [0] * kmax]
                mkv = [[0] * kmax, [0] * kmax]
                exv = [[0] * kmax, [0] * kmax]
                pvv = [[0] * kmax, [0] * kmax]
                for kb in range(kmax):
                    s0 = max(0, -(-(kb - 8 * g - 1) // 2))
                    nq = G - 128 * s0
                    q0 = 128 * s0
                    for hp in range(2):
                        h = hpair[hp]

                        @P.on("tensor")
                        def _(e, h=h, hp=hp, kb=kb, nq=nq, q0=q0,
                              nmw=nm_ve[1], vwh=vw[h]):
                            if kb == 0:
                                e.wait_ge(s_ve.sem, nmw)
                                e.wait_ge(sl_v[hp].sem, vwh)
                            bank = scb[hp]
                            e.matmul(bank[:, 0:nq], sb["onesrow_bf"][0:1, 0:128],
                                     nmbv[hp][0:1, q0:G], start=True, stop=False)
                            e.matmul(bank[:, 0:nq], ksv[:, hp, 128 * kb:128 * (kb + 1)],
                                     qnv[:, h, q0:G], start=False,
                                     stop=True).then_inc(s_pe.sem, 1)
                        stv[hp][kb] = s_pe.bump()

                    for hp in range(2):
                        if kb >= 8 * g:
                            @P.on("vector")
                            def _(e, g=g, hp=hp, kb=kb, st=stv[hp][kb]):
                                e.wait_ge(s_pe.sem, st)
                                bank = scb[hp]
                                moff = 256 * g + 128 * ((kb - 8 * g) % 2)
                                mslice = sb["maskab"][:, moff:moff + 128]
                                c["q_ve"].seq(e, e.tensor_tensor(
                                    bank[:, 0:128], bank[:, 0:128], mslice, ALU.add))
                                e.sem_inc(s_ve.sem, 1)
                            mkv[hp][kb] = s_ve.bump()

                    for hp in range(2):
                        h = hpair[hp]

                        @P.on("scalar")
                        def _(e, h=h, hp=hp, kb=kb, nq=nq, st=stv[hp][kb],
                              mk=mkv[hp][kb], masked=(kb >= 8 * g)):
                            e.wait_ge(s_pe.sem, st)
                            if masked:
                                e.wait_ge(s_ve.sem, mk)
                            bank = scb[hp]
                            a = e.activation(out=etv[:, hp, 0:nq], in_=bank[:, 0:nq],
                                             func=AF.Exp,
                                             bias=sb["colt"][:, 16 * h + kb:16 * h + kb + 1])
                            a.then_inc(s_se.sem, 1)
                        exv[hp][kb] = s_se.bump()

                    for hp in range(2):
                        @P.on("tensor")
                        def _(e, hp=hp, kb=kb, nq=nq, q0=q0, kmax=kmax,
                              ev=exv[hp][kb]):
                            e.wait_ge(s_se.sem, ev)
                            e.matmul(pvb[hp][:, q0:G], vsv[:, hp, 128 * kb:128 * (kb + 1)],
                                     etv[:, hp, 0:nq],
                                     start=(kb == 0), stop=(kb == kmax - 1))
                            mm = e.matmul(pszv[hp][0:1, q0:G], sb["onescol_bf"][:],
                                          etv[:, hp, 0:nq],
                                          start=(kb == 0), stop=(kb == kmax - 1))
                            mm.then_inc(s_pe.sem, 1)
                        pvv[hp][kb] = s_pe.bump()

                ob_ve = [0, 0]
                for hp in range(2):
                    h = hpair[hp]

                    @P.on("vector")
                    def _(e, hp=hp, pv=pvv[hp][kmax - 1]):
                        e.wait_ge(s_pe.sem, pv)
                        c["q_ve"].seq(e, e.reciprocal(zrow2[hp], pszv[hp]))
                        e.sem_inc(s_ve.sem, 1)
                    zr_ve = s_ve.bump()

                    @P.on("tensor")
                    def _(e, hp=hp, zr_ve=zr_ve):
                        e.wait_ge(s_ve.sem, zr_ve)
                        mm = e.matmul(zbb[hp][:], sb["onesrow_f"][:], zrow2[hp],
                                      start=True, stop=True)
                        mm.then_inc(s_pe.sem, 1)
                    zb_pe = s_pe.bump()

                    @P.on("vector")
                    def _(e, h=h, hp=hp, zb_pe=zb_pe):
                        e.wait_ge(s_pe.sem, zb_pe)
                        q_ve = c["q_ve"]
                        q_ve.seq(e, e.tensor_copy(zbv2[hp], zbb[hp][:]))
                        q_ve.seq(e, e.tensor_tensor(obv[:, h, :], pvb[hp][:],
                                                    zbv2[hp], ALU.mult))
                        e.sem_inc(s_ve.sem, 1)
                    ob_ve[hp] = s_ve.bump()
                    pe_last[h] = pvv[hp][kmax - 1]

                @P.on("tensor")
                def _(e, obw=ob_ve[1]):
                    e.wait_ge(s_ve.sem, obw)
                s_pe.bump(0)

            barrier(P, c)

            emit_dense_residual(P, c, sb, ps, xn=ob, wt_dram=woutt, xin_v=xin_v,
                                xout_v=xout_v, toks=toks, scale=1.0, halfbias=None)
            barrier(P, c)


def build_attn_core_launch():
    nc = bass.Bass("TRN2", target_bir_lowering=False)
    io = attc_dram_decls(nc)
    io["xin"] = nc.dram_tensor("xin", [D, 2 * G], F32, kind="ExternalInput").ap()
    io["xout"] = nc.dram_tensor("xout", [D, 2 * G], F32, kind="ExternalOutput").ap()
    io["qn"] = nc.dram_tensor("qn", [NH, 128, 2 * G], BF16, kind="ExternalInput").ap()
    io["kng"] = nc.dram_tensor("kng", [2, NH, 128, 2 * G], BF16, kind="ExternalInput").ap()
    io["vtg"] = nc.dram_tensor("vtg", [2, 2 * G, D], BF16, kind="ExternalInput").ap()
    with contextlib.ExitStack() as ctx:
        E = ctx.enter_context
        c = make_ctx(nc, ctx)
        ps = [E(nc.psum_tensor(f"psf{i}", [128, 512], F32)) for i in range(6)]
        sb = sb_common(nc, ctx)
        sb.update(alloc_sb(nc, ctx, ATTC_SB))
        qn_sb = E(nc.sbuf_tensor("qn_sb", [128, NH * G], BF16))
        ob = E(nc.sbuf_tensor("ob", [128, NH * G], BF16))
        ksb = E(nc.sbuf_tensor("ksb", [128, 2 * S], BF16))
        vsb = E(nc.sbuf_tensor("vsb", [128, 2 * S], BF16))
        expt = E(nc.sbuf_tensor("expt", [128, 2 * G], BF16))
        block = E(nc.Block())
        P = Prog()
        emit_consts(P, c, sb)
        emit_attc_phase(P, c, ps, sb, qn_sb, ob, ksb, vsb, expt, io)
        emit_final_wait(P, c)
        P.emit(block)
    return nc


def attn0_weights(inputs):
    Wqkv = np.asarray(inputs["L0_attn_Wqkv"], np.float32)
    Wout = np.asarray(inputs["L0_attn_Wout"], np.float32)
    colt = np.zeros((128, NH * 16), np.float32)
    colrow = np.zeros((NH, S), np.float32)
    for h in range(NH):
        for kb in range(16):
            colt[:, 16 * h + kb] = SLOPES[h] * (128 * kb + np.arange(128))
        colrow[h] = SLOPES[h] * np.arange(S)
    w = {
        "wqkvt": _wblocks(np.ascontiguousarray(Wqkv.T)),
        "qlnw": _chunkmajor(np.asarray(inputs["L0_attn_q_ln_w"], np.float32)),
        "qlnb": _chunkmajor(np.asarray(inputs["L0_attn_q_ln_b"], np.float32)),
        "klnw": _chunkmajor(np.asarray(inputs["L0_attn_k_ln_w"], np.float32)),
        "klnb": _chunkmajor(np.asarray(inputs["L0_attn_k_ln_b"], np.float32)),
        "identbf": np.eye(128, dtype=BF),
        "woutt": _wblocks(np.ascontiguousarray(Wout.T)),
        "colt": colt,
        "colrow": colrow.astype(BF),
        "identf": np.eye(128, dtype=np.float32),
    }
    NEG = np.float32(-1e30)
    tri_kq = np.tril(np.ones((128, 128), np.float32)).T
    maskab = np.zeros((2, 128, 256), np.float32)
    maskab[0, :, 0:128] = np.where(tri_kq > 0, 0, NEG)
    maskab[0, :, 128:256] = NEG
    maskab[1, :, 0:128] = 0
    maskab[1, :, 128:256] = np.where(tri_kq > 0, 0, NEG)
    w["maskab"] = maskab
    tri_qk = np.tril(np.ones((128, 128), np.float32))
    maskq = np.zeros((2, 128, 256), np.float32)
    maskq[0, :, 0:128] = np.where(tri_qk > 0, 0, NEG)
    maskq[0, :, 128:256] = NEG
    maskq[1, :, 0:128] = 0
    maskq[1, :, 128:256] = np.where(tri_qk > 0, 0, NEG)
    w["maskq"] = maskq
    return w


def run_attn0(x, inputs, w=None):
    if w is None:
        w = attn0_weights(inputs)
    nc2 = _get_prog(("qkv",), build_qkv_launch)
    in_maps = []
    for core in range(8):
        xc, _ = _shard_x(x, core)
        im = {"xin": xc, "wqkvt": w["wqkvt"], "i_qlnw": w["qlnw"], "i_qlnb": w["qlnb"],
              "i_klnw": w["klnw"], "i_klnb": w["klnb"], "identbf2": w["identbf"]}
        in_maps.append(im)
    res = run_bass_kernel_spmd(nc2, in_maps, core_ids=list(range(8)))
    qn = [np.asarray(res.results[cc]["qn"]) for cc in range(8)]
    kn = [np.asarray(res.results[cc]["kn"]) for cc in range(8)]
    vt = [np.asarray(res.results[cc]["vt"]) for cc in range(8)]

    kng = [np.stack([kn[2 * sq], kn[2 * sq + 1]]) for sq in range(4)]
    vtg = [np.stack([vt[2 * sq], vt[2 * sq + 1]]) for sq in range(4)]

    in_maps = []
    for core in range(8):
        seq, par = core // 2, core % 2
        xc, _ = _shard_x(x, core)
        im = {"xin": xc, "qn": qn[core], "kng": kng[seq], "vtg": vtg[seq],
              "woutt": w["woutt"], "colt": w["colt"], "colrow": w["colrow"],
              "maskab": np.stack([w["maskab"][par], w["maskab"][1 - par]]),
              "maskq": np.stack([w["maskq"][par], w["maskq"][1 - par]]),
              "identf": w["identf"]}
        in_maps.append(im)
    out = x.copy()
    nc3 = _get_prog(("attc",), build_attn_core_launch)
    res = run_bass_kernel_spmd(nc3, in_maps, core_ids=list(range(8)))
    for cc in range(8):
        _unshard_x(out, cc, np.asarray(res.results[cc]["xout"]))
    return out


A1_SB = {
    "bq": ([128, NCH], F32), "bk": ([128, 8], F32), "bv": ([128, 8], F32),
    "bqs": ([128, 8], F32), "identb": ([128, 128], BF16),
    "vstage": ([128, 4 * 128], BF16), "oe": ([128, 2 * G], BF16),
}


def a1_dram_decls(nc):
    d = {}
    d["wq"] = nc.dram_tensor("wq", [NCH, 128, NCH, 128], F32, kind="ExternalInput").ap()
    d["wk"] = nc.dram_tensor("wk", [8, 128, NCH, 128], F32, kind="ExternalInput").ap()
    d["wv"] = nc.dram_tensor("wv", [8, 128, NCH, 128], F32, kind="ExternalInput").ap()
    d["bqm"] = nc.dram_tensor("bqm", [128, NCH], F32, kind="ExternalInput").ap()
    d["bkm"] = nc.dram_tensor("bkm", [128, 8], F32, kind="ExternalInput").ap()
    d["bvm"] = nc.dram_tensor("bvm", [128, 8], F32, kind="ExternalInput").ap()
    d["identbf"] = nc.dram_tensor("identbf", [128, 128], BF16, kind="ExternalInput").ap()
    d["q3"] = nc.dram_tensor("q3", [8, G, 128], BF16, kind="Internal").ap()
    d["k3"] = nc.dram_tensor("k3", [8, G, 128], BF16, kind="Internal").ap()
    d["o3e"] = nc.dram_tensor("o3e", [128, NCH, G], BF16, kind="Internal").ap()
    return d


def emit_a1_phase(P, c, ps, psb, sb, xn, qt, kt, vt, ot, qhkT, khkT, probs, ptb, zc, io):
    xin, xout = io["xin"], io["xout"]
    wq, wk, wv = io["wq"], io["wk"], io["wv"]
    bqm, bkm, bvm = io["bqm"], io["bkm"], io["bvm"]
    identbf = io["identbf"]
    q3, k3, o3e = io["q3"], io["k3"], io["o3e"]
    s_dma, s_pe, s_ve, s_se, s_wd, s_out = (c[k] for k in
                                            ("s_dma", "s_pe", "s_ve", "s_se", "s_wd", "s_out"))
    if True:
        @P.on("sync")
        def _(e):
            e.dma_start(out=sb["bq"][:], in_=bqm).then_inc(s_dma.sem, 16)
            e.dma_start(out=sb["bk"][:], in_=bkm).then_inc(s_dma.sem, 16)
            e.dma_start(out=sb["bv"][:], in_=bvm).then_inc(s_dma.sem, 16)
            e.dma_start(out=sb["identb"][:], in_=identbf).then_inc(s_dma.sem, 16)
        s_dma.bump(16 * 4)

        barrier(P, c)

        SC = float(1.0 / np.sqrt(128.0))

        @P.on("vector")
        def _(e):
            bqv = apv(sb["bq"]).rearrange("p (hk jj) -> p hk jj", hk=8)
            q_ve = c["q_ve"]
            q_ve.seq(e, e.tensor_tensor(sb["bqs"][:], bqv[:, :, 0], bqv[:, :, 1], ALU.add))
            q_ve.seq(e, e.tensor_tensor(sb["bqs"][:], sb["bqs"][:], bqv[:, :, 2], ALU.add))
            q_ve.seq(e, e.tensor_tensor(sb["bqs"][:], sb["bqs"][:], bqv[:, :, 3], ALU.add))
            e.sem_inc(s_ve.sem, 1)
        s_ve.bump()

        @P.on("scalar")
        def _(e, w=s_ve.n):
            e.wait_ge(s_ve.sem, w)
            e.mul(sb["bqs"][:], sb["bqs"][:], SC).then_inc(s_se.sem, 1)
        s_se.bump()

        barrier(P, c)

        xin_v = xin.rearrange("(j p) t -> j p t", p=128)
        xout_v = xout.rearrange("(j p) t -> j p t", p=128)
        xnv = apv(xn).rearrange("p (j t) -> p j t", j=NCH)
        qtv = apv(qt).rearrange("p (hk t) -> p hk t", hk=8)
        ktv = apv(kt).rearrange("p (hk t) -> p hk t", hk=8)
        vtv = apv(vt).rearrange("p (hk t) -> p hk t", hk=8)
        otv = apv(ot).rearrange("p (hk t) -> p hk t", hk=8)
        wbv = apv(sb["wbuf"]).rearrange("p (i j n) -> p i j n", i=2, j=NCH)
        sl_w = c["sl_w"]
        xsv = apv(sb["xs"]).rearrange("p (i t) -> p i t", i=4)
        xov = apv(sb["xo"]).rearrange("p (i t) -> p i t", i=4)
        tmpv = apv(sb["tmp"]).rearrange("p (i t) -> p i t", i=2)
        sl_xe, sl_o = c["sl_xe"], c["sl_o"]

        for g in range(2):
            toks = slice(g * G, (g + 1) * G)
            emit_ln(P, c, sb, ps, xin_v, toks, xn)

            wprev = [0, 0]
            pprev = [0, 0]
            for proj, (wsrc, nblk, acc, arena, bias_t, scale) in enumerate((
                    (wq, NCH, 4, qtv, sb["bqs"], SC),
                    (wk, 8, 1, ktv, sb["bk"], 1.0),
                    (wv, 8, 1, vtv, sb["bv"], 1.0))):
                nout = nblk // acc
                wws = []
                pe_m = [None] * nblk
                ve_m = [None] * nout
                for d in range(nblk):
                    wws.append(sl_w[d % 2].bump(16))

                @P.on("gpsimd")
                def _(e, wsrc=wsrc, nblk=nblk, pe_m=pe_m, acc=acc, wprev=tuple(wprev)):
                    for d in range(nblk):
                        if d >= 2:
                            e.wait_ge(s_pe.sem, pe_m[d - 2])
                        elif wprev[d % 2]:
                            e.wait_ge(s_pe.sem, wprev[d % 2])
                        e.dma_start(out=wbv[:, d % 2, :, :],
                                    in_=wsrc[d, :, :, :]).then_inc(sl_w[d % 2].sem, 16)

                for d in range(nblk):
                    hk = d // acc

                    @P.on("tensor")
                    def _(e, d=d, hk=hk, ww=wws[d], acc=acc,
                          ev=(ve_m[(d // acc) - 2] if d // acc >= 2 and d % acc == 0
                              else (pprev[(d // acc) % 2] if d // acc < 2 and d % acc == 0 else None))):
                        e.wait_ge(sl_w[d % 2].sem, ww)
                        if ev:
                            e.wait_ge(s_se.sem, ev)
                        pt = ps[hk % 2]
                        mm = None
                        for j in range(NCH):
                            mm = e.matmul(pt[:], wbv[:, d % 2, j, :], xnv[:, j, :],
                                          start=(j == 0 and d % acc == 0),
                                          stop=(j == NCH - 1 and d % acc == acc - 1))
                        mm.then_inc(s_pe.sem, 1)
                    pe_m[d] = s_pe.bump()

                    if d % acc == acc - 1:
                        @P.on("scalar")
                        def _(e, hk=hk, pe_cnt=pe_m[d], arena=arena, bias_t=bias_t, scale=scale):
                            e.wait_ge(s_pe.sem, pe_cnt)
                            e.activation(out=arena[:, hk, :], in_=ps[hk % 2][:], func=AF.Identity,
                                         scale=scale, bias=bias_t[:, hk:hk + 1]).then_inc(s_se.sem, 1)
                        ve_m[hk] = s_se.bump()
                wprev = [pe_m[nblk - 2], pe_m[nblk - 1]]
                pprev = [ve_m[nout - 2], ve_m[nout - 1]]
            barrier(P, c)

            tvi = 0
            tv_se = []
            for src_ar, dst in ((qtv, q3), (ktv, k3)):
                for hk in range(8):
                    for i in range(NT):
                        @P.on("tensor")
                        def _(e, src_ar=src_ar, hk=hk, i=i, tvi=tvi,
                              wv=(tv_se[tvi - 2] if tvi >= 2 else 0)):
                            if tvi >= 2:
                                e.wait_ge(s_se.sem, wv)
                            mm = e.transpose(psb[tvi % 2][:, 0:128],
                                             src_ar[:, hk, 128 * i:128 * (i + 1)],
                                             sb["identb"][:])
                            mm.then_inc(s_pe.sem, 1)
                        tp = s_pe.bump()

                        @P.on("scalar")
                        def _(e, i=i, tvi=tvi, tp=tp,
                              ow=(c["sl_o"][tvi % 4].n if True else 0)):
                            e.wait_ge(s_pe.sem, tp)
                            if tvi >= 4:
                                e.wait_ge(c["sl_o"][tvi % 4].sem, ow)
                            e.copy(apv(sb["vstage"]).rearrange("p (i n) -> p i n", i=4)[:, tvi % 4, 0:128],
                                   psb[tvi % 2][:, 0:128]).then_inc(s_se.sem, 1)
                        se_cnt = s_se.bump()
                        tv_se.append(se_cnt)
                        c["sl_o"][tvi % 4].bump(16)

                        @P.on("sync")
                        def _(e, dst=dst, hk=hk, i=i, tvi=tvi, se_cnt=se_cnt):
                            e.wait_ge(s_se.sem, se_cnt)
                            e.dma_start(out=dst[hk, 128 * i:128 * (i + 1), :],
                                        in_=apv(sb["vstage"]).rearrange("p (i n) -> p i n", i=4)[:, tvi % 4, 0:128]).then_inc(c["sl_o"][tvi % 4].sem, 16)
                        tvi += 1
            barrier(P, c)

            BT = 32
            q_hk = apv(qhkT).rearrange("p (i n) -> p i n", i=2)
            k_hk = apv(khkT).rearrange("p (i n) -> p i n", i=2)
            pv_pe_l = []
            ld_b = [None] * (G // BT)
            for b in range(G // BT):
                ld_b[b] = c["sl_x"][b % 2].bump(32)

            @P.on("sync")
            def _(e, g=g, pv_pe_l=pv_pe_l, ld_b=ld_b):
                for b in range(G // BT):
                    if b >= 2:
                        e.wait_ge(s_pe.sem, pv_pe_l[(b - 2) * BT + BT - 1])
                    e.dma_start(out=q_hk[0:8, b % 2, 0:BT * 128],
                                in_=q3[:, BT * b:BT * (b + 1), :]).then_inc(c["sl_x"][b % 2].sem, 16)
                    e.dma_start(out=k_hk[0:8, b % 2, 0:BT * 128],
                                in_=k3[:, BT * b:BT * (b + 1), :]).then_inc(c["sl_x"][b % 2].sem, 16)

            rec = []
            vt_q = apv(vt).rearrange("p (hk t) -> p t hk", hk=8)
            ot_q = apv(ot).rearrange("p (hk t) -> p t hk", hk=8)
            zt = apv(sb["tmp"])
            scb3 = [ps[2], ps[3], ps[0]]
            pvb3 = [ps[4], ps[5], ps[1]]
            NQD = G // 4
            for qd in range(NQD):
                b = (4 * qd) // BT
                sl = qd % 2
                s3 = qd % 3
                prev2 = rec[qd - 2] if qd >= 2 else None
                prev3 = rec[qd - 3] if qd >= 3 else None
                t0 = 4 * qd
                tb0 = t0 - b * BT

                @P.on("tensor")
                def _(e, b=b, s3=s3, qd=qd, t0=t0, tb0=tb0, ldw=ld_b[b],
                      zv=(prev3["ex_se"] if prev3 else 0)):
                    if tb0 == 0:
                        e.wait_ge(c["sl_x"][b % 2].sem, ldw)
                    if qd >= 3:
                        e.wait_ge(s_se.sem, zv)
                    mm = None
                    for j in range(4):
                        mm = e.matmul(scb3[s3][:, 128 * j:128 * (j + 1)],
                                      q_hk[0:8, b % 2, 128 * (tb0 + j):128 * (tb0 + j + 1)],
                                      k_hk[0:8, b % 2, 128 * (tb0 + j):128 * (tb0 + j + 1)],
                                      start=(j == 0), stop=(j == 3))
                    mm.then_inc(s_pe.sem, 1)
                sc_pe = s_pe.bump()

                @P.on("vector")
                def _(e, s3=s3, qd=qd, sc_pe=sc_pe):
                    e.wait_ge(s_pe.sem, sc_pe)
                    q_ve = c["q_ve"]
                    mm = None
                    for j in range(4):
                        mm = e.reduce_max(zt[:, 4 * s3 + j:4 * s3 + j + 1],
                                          scb3[s3][:, 128 * j:128 * (j + 1)],
                                          axis=AX.X, negate=True)
                    q_ve.seq(e, mm)
                    e.sem_inc(s_ve.sem, 1)
                nm_ve = s_ve.bump()

                @P.on("scalar")
                def _(e, sl=sl, s3=s3, qd=qd, nm_ve=nm_ve,
                      pvw=(prev2["tr_pe"] if prev2 else 0),
                      zrw=(prev2["zr_ve"] if prev2 else 0)):
                    e.wait_ge(s_ve.sem, nm_ve)
                    if qd >= 2:
                        e.wait_ge(s_pe.sem, pvw)
                        e.wait_ge(s_ve.sem, zrw)
                    a = None
                    for j in range(4):
                        a = e.activation(
                            out=probs[:, 512 * sl + 128 * j:512 * sl + 128 * (j + 1)],
                            in_=scb3[s3][:, 128 * j:128 * (j + 1)], func=AF.Exp,
                            bias=zt[:, 4 * s3 + j:4 * s3 + j + 1])
                    a.then_inc(s_se.sem, 1)
                ex_se = s_se.bump()

                @P.on("vector")
                def _(e, sl=sl, s3=s3, qd=qd, ex_se=ex_se):
                    e.wait_ge(s_se.sem, ex_se)
                    q_ve = c["q_ve"]
                    mm = None
                    for j in range(4):
                        mm = e.reduce_sum(zt[:, 16 + 4 * s3 + j:16 + 4 * s3 + j + 1],
                                          probs[:, 512 * sl + 128 * j:512 * sl + 128 * (j + 1)],
                                          axis=AX.X)
                    q_ve.seq(e, mm)
                    q_ve.seq(e, e.reciprocal(zt[:, 16 + 4 * s3:16 + 4 * s3 + 4],
                                             zt[:, 16 + 4 * s3:16 + 4 * s3 + 4]))
                    e.sem_inc(s_ve.sem, 1)
                zr_ve = s_ve.bump()

                @P.on("tensor")
                def _(e, sl=sl, qd=qd, ex_se=ex_se,
                      cpw=(prev2["cp_ve"] if prev2 else 0)):
                    e.wait_ge(s_se.sem, ex_se)
                    if qd >= 2:
                        e.wait_ge(s_ve.sem, cpw)
                    mm = None
                    for j in range(4):
                        mm = e.transpose(psb[sl][:, 128 * j:128 * (j + 1)],
                                         probs[:, 512 * sl + 128 * j:512 * sl + 128 * (j + 1)],
                                         sb["identb"][:])
                    mm.then_inc(s_pe.sem, 1)
                tr_pe = s_pe.bump()

                @P.on("vector")
                def _(e, sl=sl, qd=qd, tr_pe=tr_pe,
                      opw=(prev2["pv_pe"] if prev2 else 0)):
                    e.wait_ge(s_pe.sem, tr_pe)
                    if qd >= 2:
                        e.wait_ge(s_pe.sem, opw)
                    e.tensor_copy(ptb[:, 512 * sl:512 * (sl + 1)],
                                  psb[sl][:, 0:512]).then_inc(s_ve.sem, 1)
                cp_ve = s_ve.bump()

                @P.on("tensor")
                def _(e, sl=sl, s3=s3, qd=qd, t0=t0, cp_ve=cp_ve,
                      ovw=(prev3["ev_ve"] if prev3 else 0)):
                    e.wait_ge(s_ve.sem, cp_ve)
                    if qd >= 3:
                        e.wait_ge(s_ve.sem, ovw)
                    mm = None
                    for j in range(4):
                        mm = e.matmul(pvb3[s3][:, 8 * j:8 * (j + 1)],
                                      ptb[:, 512 * sl + 128 * j:512 * sl + 128 * (j + 1)],
                                      vt_q[:, t0 + j, :], start=(j == 0), stop=(j == 3))
                    mm.then_inc(s_pe.sem, 1)
                pv_pe = s_pe.bump()
                for j in range(4):
                    pv_pe_l.append(pv_pe)

                @P.on("vector")
                def _(e, s3=s3, qd=qd, t0=t0, pv_pe=pv_pe):
                    e.wait_ge(s_pe.sem, pv_pe)
                    q_ve = c["q_ve"]
                    for j in range(4):
                        q_ve.seq(e, e.tensor_scalar(ot_q[:, t0 + j, :],
                                                    pvb3[s3][:, 8 * j:8 * (j + 1)],
                                                    zt[:, 16 + 4 * s3 + j:16 + 4 * s3 + j + 1],
                                                    None, op0=ALU.mult))
                    e.sem_inc(s_ve.sem, 1)
                ev_ve = s_ve.bump()
                rec.append({"ex_se": ex_se, "zr_ve": zr_ve, "tr_pe": tr_pe,
                            "cp_ve": cp_ve, "pv_pe": pv_pe, "ev_ve": ev_ve})


            barrier(P, c)

            o3x = o3e.rearrange("dd (hk r) t -> dd hk r t", r=4)

            @P.on("sync")
            def _(e):
                for r in range(4):
                    e.dma_start(out=o3x[:, :, r, :], in_=otv[:, :, :]).then_inc(s_dma.sem, 16)
            s_dma.bump(16 * 4)
            barrier(P, c)

            o3f = o3e.rearrange("dd hh t -> (dd hh) t").rearrange("(a p) t -> a p t", p=128)
            oev = apv(sb["oe"]).rearrange("p (i t) -> p i t", i=2)
            a1rv = []
            for cch in range(NCH):
                xe_w = sl_xe[cch % 4].bump(16)
                oe_w = c["sl_g"][cch % 2].bump(16)
                ve_prev2 = a1rv[cch - 2] if cch >= 2 else 0

                @P.on("sync")
                def _(e, cch=cch, toks=toks,
                      ev4=(a1rv[cch - 4] if cch >= 4 else 0)):
                    if cch >= 4:
                        e.wait_ge(s_ve.sem, ev4)
                    e.dma_start(out=xsv[:, cch % 4, :], in_=xin_v[cch, :, toks]).then_inc(sl_xe[cch % 4].sem, 16)

                @P.on("gpsimd")
                def _(e, cch=cch, ve_prev2=ve_prev2):
                    if cch >= 2:
                        e.wait_ge(s_ve.sem, ve_prev2)
                    e.dma_start(out=oev[:, cch % 2, :], in_=o3f[cch, :, :]).then_inc(c["sl_g"][cch % 2].sem, 16)

                @P.on("vector")
                def _(e, cch=cch, xw=xe_w, ow=oe_w,
                      ov=(c["sl_o"][cch % 4].n if cch >= 4 else 0)):
                    e.wait_ge(sl_xe[cch % 4].sem, xw)
                    e.wait_ge(c["sl_g"][cch % 2].sem, ow)
                    if cch >= 4:
                        e.wait_ge(c["sl_o"][cch % 4].sem, ov)
                    e.tensor_tensor(xov[:, cch % 4, :], xsv[:, cch % 4, :],
                                    oev[:, cch % 2, :], ALU.add).then_inc(s_ve.sem, 1)
                ve_cnt = s_ve.bump()
                a1rv.append(ve_cnt)
                c["sl_o"][cch % 4].bump(16)

                @P.on("sync")
                def _(e, cch=cch, ve_cnt=ve_cnt, toks=toks):
                    e.wait_ge(s_ve.sem, ve_cnt)
                    e.dma_start(out=xout_v[cch, :, toks], in_=xov[:, cch % 4, :]).then_inc(c["sl_o"][cch % 4].sem, 16)

            barrier(P, c)


def build_attn1_launch():
    nc = bass.Bass("TRN2", target_bir_lowering=False)
    io = a1_dram_decls(nc)
    io["xin"] = nc.dram_tensor("xin", [D, 2 * G], F32, kind="ExternalInput").ap()
    io["xout"] = nc.dram_tensor("xout", [D, 2 * G], F32, kind="ExternalOutput").ap()
    with contextlib.ExitStack() as ctx:
        E = ctx.enter_context
        c = make_ctx(nc, ctx)
        ps = [E(nc.psum_tensor(f"psf{i}", [128, 512], F32)) for i in range(6)]
        psb = [E(nc.psum_tensor(f"psbt{i}", [128, 1024], BF16)) for i in range(2)]
        sb = sb_common(nc, ctx)
        sb.update(alloc_sb(nc, ctx, A1_SB))
        xn = E(nc.sbuf_tensor("xn0", [128, NCH * G], BF16))
        qt = E(nc.sbuf_tensor("qt", [128, 8 * G], BF16))
        kt = E(nc.sbuf_tensor("kt", [128, 8 * G], BF16))
        vt = E(nc.sbuf_tensor("vt_sb", [128, 8 * G], BF16))
        ot = E(nc.sbuf_tensor("ot", [128, 8 * G], BF16))
        qhkT = E(nc.sbuf_tensor("qhkT", [8, 64 * 128 * 2], BF16))
        khkT = E(nc.sbuf_tensor("khkT", [8, 64 * 128 * 2], BF16))
        probs = E(nc.sbuf_tensor("probs_sb", [128, 1024], BF16))
        ptb = E(nc.sbuf_tensor("pt_sb", [128, 1024], BF16))
        zc = E(nc.sbuf_tensor("zc", [128, 16], F32))
        block = E(nc.Block())
        P = Prog()
        emit_consts(P, c, sb)
        emit_a1_phase(P, c, ps, psb, sb, xn, qt, kt, vt, ot, qhkT, khkT, probs, ptb, zc, io)
        emit_final_wait(P, c)
        P.emit(block)
    return nc


def attn1_weights(inputs):
    Wq = np.asarray(inputs["L1_attn_Wq"], np.float32)
    Wk = np.asarray(inputs["L1_attn_Wk"], np.float32)
    Wv = np.asarray(inputs["L1_attn_Wv"], np.float32)
    wqT = Wq.transpose(2, 1, 0).reshape(D, D).T
    wqT = np.ascontiguousarray(Wq.transpose(0, 2, 1).reshape(D, D))
    wkT = np.ascontiguousarray(Wk.transpose(0, 2, 1).reshape(D, 8 * 128))
    wvT = np.ascontiguousarray(Wv.transpose(0, 2, 1).reshape(D, 8 * 128))
    bq = np.asarray(inputs["L1_attn_bq"], np.float32)
    bk = np.asarray(inputs["L1_attn_bk"], np.float32)
    bv = np.asarray(inputs["L1_attn_bv"], np.float32)
    bqv = np.ascontiguousarray(bq.T.reshape(D))
    bkv = np.ascontiguousarray(bk.T.reshape(8 * 128))
    bvv = np.ascontiguousarray(bv.T.reshape(8 * 128))
    return {
        "wq": _wblocks(wqT),
        "wk": _wblocks(wkT),
        "wv": _wblocks(wvT),
        "bqm": _chunkmajor(bqv),
        "bkm": np.ascontiguousarray(bkv.reshape(8, 128).T),
        "bvm": np.ascontiguousarray(bvv.reshape(8, 128).T),
        "identbf": np.eye(128, dtype=BF),
    }


def run_attn1(x, inputs, w=None):
    if w is None:
        w = attn1_weights(inputs)
    nc = _get_prog(("attn1",), build_attn1_launch)
    in_maps = []
    for core in range(8):
        xc, _ = _shard_x(x, core)
        im = {"xin": xc}
        im.update(w)
        in_maps.append(im)
    res = run_bass_kernel_spmd(nc, in_maps, core_ids=list(range(8)))
    out = x.copy()
    for core in range(8):
        _unshard_x(out, core, np.asarray(res.results[core]["xout"]))
    return out


